# version 10
# baseline (speedup 1.0000x reference)
import numpy as np
import ml_dtypes
import concourse.bass as bass
import concourse.mybir as mybir
from concourse.bass_utils import run_bass_kernel_spmd

F32 = mybir.dt.float32
BF16 = mybir.dt.bfloat16
I32 = mybir.dt.int32
AF = mybir.ActivationFunctionType
ALU = mybir.AluOpType
AX = mybir.AxisListType
BFNP = ml_dtypes.bfloat16

B, T, D = 4, 4096, 2048
HV, HK, DK, DV, KCONV = 32, 16, 128, 128, 4
CH = 128
TT = 512
NTILES = T // TT
CPT = TT // CH
NVH, NKH = 16, 8
CONVC = 4096
VCC = 2048
TB = 2048
TTB = 512
NBT = TB // TTB
LN_EPS, L2_EPS, RMS_EPS = 1e-5, 1e-6, 1e-6

ENGS = ("tensor", "vector", "scalar", "gpsimd", "sync")
GRAN = 2048


class Prog:
    def __init__(self):
        self.steps = []
        self.last_write = {}
        self.last_reads = {}
        self.dma_issued = {}
        self.eng_steps = {e: [] for e in ENGS}

    def _dep(self, step, prod):
        if prod is None or prod is step:
            return
        if (prod["eng"] == "tensor" and step["eng"] == "tensor"
                and prod["kind"] == "c" and step["kind"] == "c"):
            return
        if prod not in step["dep_steps"]:
            step["dep_steps"].append(prod)
            if prod["kind"] == "d":
                step["dma_req"][prod["sem"]] = max(
                    step["dma_req"].get(prod["sem"], 0),
                    16 * self.dma_issued[prod["sem"]])

    def _add(self, eng, kind, emit, reads, writes, sem=None):
        step = dict(eng=eng, kind=kind, emit=emit, sem=sem,
                    dep_steps=[], dma_req={}, marked=False, count=None)
        for r in reads:
            self._dep(step, self.last_write.get(r))
        for w in writes:
            self._dep(step, self.last_write.get(w))
            for rd in self.last_reads.get(w, ()):
                self._dep(step, rd)
        if kind == "d":
            self.dma_issued[sem] = self.dma_issued.get(sem, 0) + 1
        for w in writes:
            self.last_write[w] = step
            self.last_reads[w] = []
        for r in reads:
            self.last_reads.setdefault(r, []).append(step)
        self.steps.append(step)
        self.eng_steps[eng].append(step)
        return step

    def c(self, eng, emit, reads=(), writes=()):
        return self._add(eng, "c", emit, reads, writes)

    def dma(self, eng, sem, emit, reads=(), writes=()):
        return self._add(eng, "d", emit, reads, writes, sem=sem)

    def coll(self, emit, reads=(), writes=()):
        return self._add("gpsimd", "cc", emit, reads, writes, sem="_cc")

    def finalize(self, block, sems, eng_sems, final_waits):
        for st in self.steps:
            for p in st["dep_steps"]:
                p["marked"] = True
        for e in ENGS:
            cnt = 0
            for st in self.eng_steps[e]:
                if st["kind"] == "c" and st["marked"]:
                    cnt += 1
                    st["count"] = cnt
        totals = {}
        for st in self.steps:
            if st["kind"] == "d":
                totals[st["sem"]] = totals.get(st["sem"], 0) + 16
        FULL = ("dAUX", "gAUX")
        dcnt = {}
        for st in self.steps:
            if st["kind"] == "d":
                dcnt[st["sem"]] = dcnt.get(st["sem"], 0) + 16
                st["count"] = dcnt[st["sem"]]
                st["marked"] = True
            elif st["kind"] == "cc":
                dcnt[st["sem"]] = dcnt.get(st["sem"], 0) + 1
                st["count"] = dcnt[st["sem"]]
                st["marked"] = True

        def emit_engine(name):
            def run(eng):
                seen = {}
                for st in self.eng_steps[name]:
                    req = {}
                    for p in st["dep_steps"]:
                        if p["kind"] == "c":
                            s, v = eng_sems[p["eng"]], p["count"]
                        elif p["kind"] == "cc":
                            s, v = sems[p["sem"]], p["count"]
                        else:
                            continue
                        if req.get(id(s), (None, -1))[1] < v:
                            req[id(s)] = (s, v)
                    for sname, v in st["dma_req"].items():
                        s = sems[sname]
                        if sname in FULL and not (st["kind"] == "d" and st["sem"] == sname):
                            v = totals[sname]
                        if req.get(id(s), (None, -1))[1] < v:
                            req[id(s)] = (s, v)
                    for s, v in req.values():
                        if seen.get(id(s), -1) < v:
                            eng.wait_ge(s, v)
                            seen[id(s)] = v
                    inst = st["emit"](eng)
                    if st["marked"]:
                        if st["kind"] == "c":
                            inst.then_inc(eng_sems[name], 1)
                        elif st["kind"] == "d":
                            inst.then_inc(sems[st["sem"]], 16)
                        else:
                            inst.then_inc(sems[st["sem"]], 1)
                if name == "sync":
                    for sname, v in final_waits:
                        if v > 0:
                            eng.wait_ge(sems[sname], v)
            return run

        block.sync(emit_engine("sync"))
        block.tensor(emit_engine("tensor"))
        block.vector(emit_engine("vector"))
        block.scalar(emit_engine("scalar"))
        block.gpsimd(emit_engine("gpsimd"))


def _stage_core(inp, c):
    f32 = np.float32
    b, h = c // 2, c % 2
    vh0, kh0 = NVH * h, NKH * h
    W = np.asarray(inp["L0_attn_W_qkv"], f32)
    wqkv = np.ascontiguousarray(np.concatenate(
        [W[:, kh0 * DK:(kh0 + NKH) * DK],
         W[:, 2048 + kh0 * DK:2048 + (kh0 + NKH) * DK],
         W[:, 4096 + vh0 * DV:4096 + (vh0 + NVH) * DV]], axis=1))
    cw = np.asarray(inp["L0_attn_conv_w"], f32)
    cw_sl = np.concatenate(
        [cw[kh0 * DK:(kh0 + NKH) * DK], cw[2048 + kh0 * DK:2048 + (kh0 + NKH) * DK],
         cw[4096 + vh0 * DV:4096 + (vh0 + NVH) * DV]], axis=0)
    convw = np.ascontiguousarray(cw_sl.reshape(32, 128, 4).transpose(1, 0, 2).reshape(128, 128))
    wout = np.ascontiguousarray(np.asarray(inp["L0_attn_W_out"], f32))

    def cols128(v):
        n = v.shape[0]
        return np.ascontiguousarray(v.reshape(n, 8, 128).transpose(2, 0, 1).reshape(128, n * 8))

    x = np.asarray(inp["L0_attn_x"], f32)
    gth = (2 * np.arange(4096, dtype=np.int64) + h).astype(np.int32)
    gth = np.ascontiguousarray(gth.reshape(32, 128).T)
    sel = np.zeros((8, 768), np.float32)
    for n in range(6):
        sel[n, n * 128:(n + 1) * 128] = 1.0

    return dict(
        x=np.ascontiguousarray(x[b]),
        xh=np.ascontiguousarray(np.concatenate(
            [x[b, (2 * tb + h) * 512:(2 * tb + h + 1) * 512] for tb in range(4)])),
        wqkv=wqkv, convw=convw,
        wz=np.ascontiguousarray(np.asarray(inp["L0_attn_W_z"], f32)[:, vh0 * DV:(vh0 + NVH) * DV]),
        wba=np.ascontiguousarray(np.concatenate(
            [np.asarray(inp["L0_attn_W_b"], f32)[:, vh0:vh0 + NVH],
             np.asarray(inp["L0_attn_W_a"], f32)[:, vh0:vh0 + NVH]], axis=1)),
        dtb_row=np.ascontiguousarray(np.asarray(inp["L0_attn_dt_bias"], f32)[vh0:vh0 + NVH].reshape(1, NVH)),
        alog_row=np.ascontiguousarray(np.asarray(inp["L0_attn_A_log"], f32)[vh0:vh0 + NVH].reshape(1, NVH)),
        normw_col=np.asarray(inp["L0_attn_norm_w"], f32).reshape(128, 1),
        wout=wout,
        wg=np.ascontiguousarray(np.asarray(inp["L0_ffn_Wg"], f32).transpose(1, 0, 2).reshape(2048, 16)),
        bg_rows=np.ascontiguousarray(np.asarray(inp["L0_ffn_bg"], f32).reshape(1, 16)),
        wup=np.ascontiguousarray(np.asarray(inp["L0_ffn_Wup"], f32).transpose(1, 0, 2).reshape(2048, 6144)),
        bup_cols=cols128(np.asarray(inp["L0_ffn_bup"], f32)),
        wsw=np.ascontiguousarray(np.asarray(inp["L0_ffn_Wsw"], f32).transpose(1, 0, 2).reshape(1024, 6144)),
        bsw_cols=cols128(np.asarray(inp["L0_ffn_bsw"], f32)),
        wdn=np.ascontiguousarray(np.asarray(inp["L0_ffn_Wdn"], f32).reshape(6144, 2048)),
        bdn_rows=np.ascontiguousarray(np.asarray(inp["L0_ffn_bdn"], f32)),
        wsu=np.asarray(inp["L0_ffn_Wsu"], f32),
        bsu_cols=np.ascontiguousarray(np.asarray(inp["L0_ffn_bsu"], f32).reshape(16, 128).T),
        wsa=np.asarray(inp["L0_ffn_Wsa"], f32),
        bsa_cols=np.ascontiguousarray(np.asarray(inp["L0_ffn_bsa"], f32).reshape(8, 128).T),
        wsd=np.asarray(inp["L0_ffn_Wsd"], f32),
        bsd_row=np.ascontiguousarray(np.asarray(inp["L0_ffn_bsd"], f32).reshape(1, 2048)),
        gther=gth,
        ident_bf=np.eye(128, dtype=np.float32).astype(BFNP),
        identf=np.eye(128, dtype=np.float32),
        mski=np.ascontiguousarray(np.triu(np.ones((128, 128), np.float32))),
        msks=np.ascontiguousarray(np.triu(np.ones((128, 128), np.float32), 1)),
        ones_row=np.ones((1, 128), np.float32),
        ones_row_bf=np.ones((1, 128), np.float32).astype(BFNP),
        ones_col=np.ones((128, 1), np.float32),
        ones_col_bf=np.ones((128, 1), np.float32).astype(BFNP),
        selbf=sel.astype(BFNP),
    )


INPUT_DECL = dict(
    x=([T, D], F32), xh=([TB, D], F32),
    wqkv=([D, CONVC], F32), convw=([128, 128], F32), wz=([D, VCC], F32),
    wba=([D, 32], F32), dtb_row=([1, NVH], F32), alog_row=([1, NVH], F32),
    normw_col=([128, 1], F32), wout=([4096, 2048], F32),
    wg=([2048, 16], F32), bg_rows=([1, 16], F32),
    wup=([2048, 6144], F32), bup_cols=([128, 48], F32),
    wsw=([1024, 6144], F32), bsw_cols=([128, 48], F32),
    wdn=([6144, 2048], F32), bdn_rows=([6, 2048], F32),
    wsu=([2048, 2048], F32), bsu_cols=([128, 16], F32),
    wsa=([1024, 1024], F32), bsa_cols=([128, 8], F32),
    wsd=([2048, 2048], F32), bsd_row=([1, 2048], F32),
    gther=([128, 32], I32),
    ident_bf=([128, 128], BF16), identf=([128, 128], F32),
    mski=([128, 128], F32), msks=([128, 128], F32),
    ones_row=([1, 128], F32), ones_row_bf=([1, 128], BF16),
    ones_col=([128, 1], F32), ones_col_bf=([128, 1], BF16),
    selbf=([8, 768], BF16),
)


ARENA_BASE = 22528
ARENA_TOP = 222000


def _layout():
    L = {}
    off = [ARENA_BASE]

    def al(name, nbytes, at=None):
        a = off[0] if at is None else at
        if at is None:
            off[0] = a + nbytes
        L[name] = (a, nbytes)

    al("xn_fm", 16384)
    al("qkvp", 32768)
    al("sz", 16384)
    al("wt0", 8192)
    al("wt1", 8192)
    al("xtm", 8192)
    al("stg", 6144)
    al("cvt", 2048)
    al("strow", 6144)
    al("q_tm", 2048); al("k_tm", 2048); al("v_tm", 4096)
    al("qs_tm", 2048); al("kn_tm", 2048)
    al("vb", 4096); al("kbg", 4096); al("kdec", 4096)
    al("qgi_fm", 4096)
    al("C", 4096); al("A", 4096); al("C2", 4096); al("A2", 4096)
    al("P", 4096); al("P2", 4096)
    al("vnew", 4096); al("y_tm", 4096); al("nrm_fm", 4096)
    al("mdiff", 2048)
    al("egl", 512); al("gch", 2048); al("gT", 2048); al("ba_all", 512)
    al("lncol", 128); al("rstc", 128)
    al("r_bc", 1024); al("mr_bc", 1024); al("halo", 256); al("S_bf", 4096)
    al("bdn_bf", 4096); al("bsd_bf", 4096); al("selbf_t", 2048)
    al("wgt", 512)
    al("m1", 4096, at=L["xtm"][0])
    al("m0", 4096, at=L["xtm"][0] + 4096)
    al("qs_fm", 2048, at=L["q_tm"][0])
    al("kn_fm", 2048, at=L["k_tm"][0])
    al("qgi_tm", 4096, at=L["v_tm"][0])
    al("vc", 4096, at=L["C2"][0])
    al("kcdT", 4096, at=L["A2"][0])
    al("ysq", 4096, at=L["vnew"][0])
    al("Gm", 2048, at=L["vnew"][0])
    al("qkTm", 2048, at=L["vnew"][0] + 2048)
    al("nrmB", 32768, at=L["qkvp"][0])
    for n in range(4):
        al("h2w%d" % n, 8192, at=L["qkvp"][0] + n * 8192)
    al("h2w4", 8192, at=L["sz"][0])
    al("h2w5", 8192, at=L["sz"][0] + 8192)
    al("xt_fm", 16384, at=L["xn_fm"][0])
    al("res0s", 8192, at=L["xtm"][0])
    al("resfs", 8192, at=L["stg"][0])
    al("h_fm", 8192, at=L["cvt"][0])
    al("sil", 8192, at=L["q_tm"][0])
    al("sh_fm", 16384, at=L["C"][0])
    al("sh2", 16384, at=L["P"][0])
    al("xt1b", 1024, at=L["nrm_fm"][0])
    al("wbc", 1024, at=L["r_bc"][0])
    al("gat", 2048, at=L["mdiff"][0])
    al("wfm", 1024, at=L["egl"][0])
    assert off[0] <= ARENA_TOP, off[0]
    return L


LAY = _layout()
assert LAY["strow"][0] == LAY["cvt"][0] + 2048
assert LAY["k_tm"][0] == LAY["q_tm"][0] + 2048 and LAY["v_tm"][0] == LAY["k_tm"][0] + 2048
assert LAY["gch"][0] == LAY["egl"][0] + 512


def granules(*names):
    out = set()
    for n in names:
        a, sz = LAY[n]
        for g in range(a // GRAN, (a + sz + GRAN - 1) // GRAN):
            out.add("g%d" % g)
    return tuple(out)


ARENA_DECL = dict(
    xn_fm=([128, 8192], BF16), qkvp=([128, 16384], BF16), sz=([128, 8192], BF16),
    wt0=([128, 4096], BF16), wt1=([128, 4096], BF16),
    xtm=([128, 4096], BF16), stg=([2, 1536], F32), cvt=([128, 512], F32),
    strow=([1, 1536], F32),
    q_tm=([128, 1024], BF16), k_tm=([128, 1024], BF16), v_tm=([128, 2048], BF16),
    qs_tm=([128, 1024], BF16), kn_tm=([128, 1024], BF16),
    vb=([128, 2048], BF16), kbg=([128, 2048], BF16), kdec=([128, 2048], BF16),
    qgi_fm=([128, 2048], BF16),
    Gm=([128, 1024], BF16), qkTm=([128, 1024], BF16),
    C=([128, 2048], BF16), A=([128, 2048], BF16),
    C2=([128, 2048], BF16), A2=([128, 2048], BF16),
    P=([128, 2048], BF16), P2=([128, 2048], BF16),
    vnew=([128, 2048], BF16), y_tm=([128, 2048], BF16), nrm_fm=([128, 2048], BF16),
    mdiff=([128, 512], F32),
    egl=([128, 64], F32), gch=([128, 448], F32), gT=([16, 512], F32),
    ba_all=([128, 128], F32), lncol=([128, 16], F32), rstc=([128, 16], F32),
    r_bc=([128, 512], BF16), mr_bc=([128, 512], BF16),
    halo=([128, 96], BF16), S_bf=([128, 2048], BF16),
    bdn_bf=([6, 2048], BF16), bsd_bf=([1, 2048], BF16), selbf_t=([8, 768], BF16),
    wgt=([128, 128], BF16),
    m1=([128, 2048], BF16), m0=([128, 2048], BF16),
    qs_fm=([128, 1024], BF16), kn_fm=([128, 1024], BF16), qgi_tm=([128, 2048], BF16),
    vc=([128, 2048], BF16), kcdT=([128, 2048], BF16), ysq=([128, 2048], BF16),
    nrmB=([128, 16384], BF16),
    h2w0=([128, 4096], BF16), h2w1=([128, 4096], BF16), h2w2=([128, 4096], BF16),
    h2w3=([128, 4096], BF16), h2w4=([128, 4096], BF16), h2w5=([128, 4096], BF16),
    xt_fm=([128, 8192], BF16), res0s=([128, 2048], F32), resfs=([128, 2048], F32),
    h_fm=([128, 4096], BF16), sil=([128, 4096], BF16),
    sh_fm=([128, 8192], BF16), sh2=([128, 8192], BF16),
    xt1b=([128, 512], BF16), wbc=([128, 512], BF16),
    gat=([128, 512], F32), wfm=([8, 512], BF16),
)


class Ctx:
    pass


def build(phase_a_only=False, debug_ccin=False, ntiles=NTILES):
    nc = bass.Bass("TRN2", target_bir_lowering=False)
    din = {n: nc.dram_tensor(n, list(s), d, kind="ExternalInput")
           for n, (s, d) in INPUT_DECL.items()}
    y = nc.dram_tensor("y", [TB, D], F32, kind="ExternalOutput")
    dr = {}
    dr["wqkv_bf"] = nc.dram_tensor("wqkv_bf", [16, 128, 16, 256], BF16)
    dr["wz_bf"] = nc.dram_tensor("wz_bf", [8, 128, 16, 256], BF16)
    dr["wba_bf"] = nc.dram_tensor("wba_bf", [2048, 32], BF16)
    dr["wg_bf"] = nc.dram_tensor("wg_bf", [2048, 16], BF16)
    dr["wup_bf"] = nc.dram_tensor("wup_bf", [24, 128, 16, 256], BF16)
    dr["wsw_bf"] = nc.dram_tensor("wsw_bf", [24, 128, 8, 256], BF16)
    dr["wsu_bf"] = nc.dram_tensor("wsu_bf", [8, 128, 16, 256], BF16)
    dr["wsa_bf"] = nc.dram_tensor("wsa_bf", [4, 128, 8, 256], BF16)
    dr["wdn_bf"] = nc.dram_tensor("wdn_bf", [4, 48, 128, 512], BF16)
    dr["wsd_bf"] = nc.dram_tensor("wsd_bf", [4, 16, 128, 512], BF16)
    dr["wout_bf"] = nc.dram_tensor("wout_bf", [4, 32, 128, 512], BF16)
    dr["cc_in"] = nc.dram_tensor("cc_in", [4, VCC, 1024], BF16,
                                 kind="ExternalOutput" if debug_ccin else "Internal")
    for q in range(4):
        dr["cc_out%d" % q] = nc.dram_tensor("cc_out%d" % q, [2 * VCC, 1024], BF16)
    dr["res0_d"] = nc.dram_tensor("res0_d", [TB, D], F32)
    dr["resf_d"] = nc.dram_tensor("resf_d", [TB, D], F32)

    from contextlib import ExitStack
    with ExitStack() as stk:
        E = stk.enter_context
        sb = {}
        for n, shape, dt in (
                ("ident_bf", [128, 128], BF16), ("identf", [128, 128], F32),
                ("mski", [128, 128], F32), ("msks", [128, 128], F32),
                ("convw", [128, 128], F32),
                ("ones_row", [1, 128], F32), ("ones_row_bf", [1, 128], BF16),
                ("ones_col", [128, 1], F32), ("ones_col_bf", [128, 1], BF16),
                ("normw_col", [128, 1], F32),
                ("bup_cols", [128, 48], F32), ("bsw_cols", [128, 48], F32),
                ("bsu_cols", [128, 16], F32), ("bsa_cols", [128, 8], F32),
                ("bg_rows", [1, 16], F32), ("gth_idx", [128, 32], I32),
                ("dtb_bc", [128, 16], F32), ("acoef_bc", [128, 16], F32),
                ("rowtmp", [1, 16], F32), ("rowtmp2", [1, 16], F32),
                ("halo2", [128, 96], BF16),
                ("wba_t", [128, 512], BF16), ("bg_bf", [1, 16], BF16)):
            sb[n] = nc.alloc_sbuf_tensor("c_" + n, shape, dt, side="right")
        for n, (shape, dt) in ARENA_DECL.items():
            a, sz_ = LAY[n]
            nb = shape[1] * (2 if dt in (BF16,) else 4)
            assert nb <= sz_, (n, nb, sz_)
            sb[n] = nc.alloc_sbuf_tensor_at(n, shape, dt, offset=a)

        pb = {}
        for i in range(6):
            pb["pf%d" % i] = E(nc.psum_tensor("pf%d" % i, [128, 512], F32))
        pb["pt0"] = E(nc.psum_tensor("pt0", [128, 1024], BF16))
        pb["pt1"] = E(nc.psum_tensor("pt1", [128, 1024], BF16))

        sems = {}
        for s in ("dW0", "dW1", "dX", "dST", "dAUX", "dRS", "dGT", "_cc",
                  "dOUT", "dRES", "gAUX", "gW0", "gW1", "gX", "dNF", "gCAST"):
            sems[s] = E(nc.semaphore(s))
        eng_sems = {e: E(nc.semaphore("se_" + e)) for e in ENGS}

        block = E(nc.Block())
        P = Prog()
        cx = Ctx()
        cx.nc, cx.P, cx.sb, cx.pb, cx.din, cx.dr, cx.y = nc, P, sb, pb, din, dr, y
        cx.G = granules

        def make_coll(q):
            return lambda: P.coll(lambda e, q=q: e.collective_compute(
                "AllGather", ALU.bypass,
                replica_groups=[[0, 1], [2, 3], [4, 5], [6, 7]],
                ins=[dr["cc_in"].ap()[q]], outs=[dr["cc_out%d" % q].ap()]),
                reads=("cc_in_q%d" % q,), writes=("cc_out%d" % q,))

        cx.coll_at_tile = {} if phase_a_only else {2 * q + 2: make_coll(q)
                                                  for q in range(3)}
        cx.coll_b0 = make_coll(3)

        emit_prep(cx)
        for t in range(ntiles):
            emit_phaseA_tile(cx, t)
        if not phase_a_only:
            for tb in range(NBT):
                emit_phaseB_tile(cx, tb)
        final = []
        if phase_a_only or debug_ccin:
            final.append(("dST", 16 * P.dma_issued.get("dST", 0)))
            final.append(("_cc", P.dma_issued.get("_cc", 0)))
        if not phase_a_only:
            final.append(("dOUT", 16 * P.dma_issued.get("dOUT", 0)))
        P.finalize(block, sems, eng_sems, final)
    return nc


def emit_prep(cx):
    P, sb, pb, din, dr, G = cx.P, cx.sb, cx.pb, cx.din, cx.dr, cx.G

    for n in ("identf", "mski", "msks", "convw", "ones_row", "ones_col",
              "normw_col", "bup_cols", "bsw_cols", "bsu_cols", "bsa_cols",
              "bg_rows"):
        P.dma("sync", "dAUX", lambda e, n=n: e.dma_start(out=sb[n][:], in_=din[n].ap()),
              reads=(), writes=(n,))
    for n in ("ident_bf", "ones_row_bf", "ones_col_bf", "selbf"):
        dst = "selbf_t" if n == "selbf" else n
        P.dma("sync", "dAUX", lambda e, n=n, dst=dst: e.dma_start(out=sb[dst][:], in_=din[n].ap()),
              reads=(), writes=(dst,) if dst != "selbf_t" else G("selbf_t"))
    P.dma("gpsimd", "gAUX", lambda e: e.dma_start(out=sb["bdn_bf"][:], in_=din["bdn_rows"].ap()),
          reads=(), writes=G("bdn_bf"))
    P.dma("gpsimd", "gAUX", lambda e: e.dma_start(out=sb["bsd_bf"][:], in_=din["bsd_row"].ap()),
          reads=(), writes=G("bsd_bf"))
    P.dma("gpsimd", "gAUX", lambda e: e.dma_start(out=sb["bg_bf"][:], in_=din["bg_rows"].ap()),
          reads=(), writes=("bg_bf",))
    P.dma("sync", "dAUX", lambda e: e.dma_start(
        out=sb["gth_idx"][:], in_=din["gther"].ap()),
        reads=(), writes=("gth_idx",))

    P.dma("sync", "dAUX", lambda e: e.dma_start(out=sb["rowtmp"][:], in_=din["dtb_row"].ap()),
          reads=(), writes=("rowtmp",))
    P.c("tensor", lambda e: e.matmul(pb["pf0"][:, 0:16], sb["ones_row"][:],
                                     sb["rowtmp"][:], start=True, stop=True),
        reads=("rowtmp", "ones_row"), writes=("pf0",))
    P.c("vector", lambda e: e.tensor_copy(sb["dtb_bc"][:], pb["pf0"][:, 0:16]),
        reads=("pf0",), writes=("dtb_bc",))
    P.dma("sync", "dAUX", lambda e: e.dma_start(out=sb["rowtmp2"][:], in_=din["alog_row"].ap()),
          reads=(), writes=("rowtmp2",))
    P.c("scalar", lambda e: e.activation(sb["rowtmp2"][:], sb["rowtmp2"][:], AF.Exp),
        reads=("rowtmp2",), writes=("rowtmp2",))
    P.c("tensor", lambda e: e.matmul(pb["pf0"][:, 0:16], sb["ones_row"][:],
                                     sb["rowtmp2"][:], start=True, stop=True),
        reads=("rowtmp2", "ones_row"), writes=("pf0",))
    P.c("vector", lambda e: e.tensor_scalar_mul(sb["acoef_bc"][:], pb["pf0"][:, 0:16], -1.0),
        reads=("pf0",), writes=("acoef_bc",))

    cx.deferred = []

    def cast_round(dst, emit, defer):
        def rnd():
            st = P.dma("gpsimd", "gCAST", emit, reads=(), writes=())
            P.last_write[dst] = st
        cx.deferred.append(rnd) if defer else rnd()

    def cast_gpkc(src, dst, rows, kdim, gdim, defer=False):
        for k in range(rows // 128):
            for gb in range(0, gdim, 16):
                gn = min(16, gdim - gb)
                cast_round(dst, lambda e, k=k, gb=gb, gn=gn: e.dma_start(
                    out=dr[dst].ap()[gb:gb + gn, :, k, :].transpose([1, 0, 2]),
                    in_=din[src].ap()[k * 128:(k + 1) * 128, gb * 256:(gb + gn) * 256]
                    .rearrange("p (g c) -> p g c", g=gn)), defer)

    def cast_hgrc(src, dst, rows, hg, defer=False):
        for r in range(rows // 128):
            cast_round(dst, lambda e, r=r: e.dma_start(
                out=dr[dst].ap()[:, r, :, :].transpose([1, 0, 2]),
                in_=din[src].ap()[r * 128:(r + 1) * 128, :]
                .rearrange("p (g c) -> p g c", g=hg)), defer)

    cast_gpkc("wqkv", "wqkv_bf", 2048, 16, 16)
    cast_gpkc("wz", "wz_bf", 2048, 16, 8)
    for k in range(16):
        P.dma("gpsimd", "gW0", lambda e, k=k: e.dma_start(
            out=sb["wba_t"][:, k * 32:(k + 1) * 32], in_=din["wba"].ap()[k * 128:(k + 1) * 128, :]),
            reads=("wba_t",), writes=("wba_t",))
    for k in range(16):
        cast_round("wg_bf", lambda e, k=k: e.dma_start(
            out=dr["wg_bf"].ap()[k * 128:(k + 1) * 128, :],
            in_=din["wg"].ap()[k * 128:(k + 1) * 128, :]), False)

    cast_hgrc("wout", "wout_bf", 4096, 4, defer=True)
    cast_gpkc("wup", "wup_bf", 2048, 16, 24, defer=True)
    cast_gpkc("wsw", "wsw_bf", 1024, 8, 24, defer=True)
    cast_gpkc("wsu", "wsu_bf", 2048, 16, 8, defer=True)
    cast_gpkc("wsa", "wsa_bf", 1024, 8, 4, defer=True)
    cast_hgrc("wdn", "wdn_bf", 6144, 4, defer=True)
    cast_hgrc("wsd", "wsd_bf", 2048, 4, defer=True)

    P.c("vector", lambda e: e.memset(sb["S_bf"][:], 0.0), reads=(), writes=G("S_bf"))
    P.c("vector", lambda e: e.memset(sb["halo"][:], 0.0), reads=(), writes=G("halo"))
    P.c("vector", lambda e: e.memset(sb["stg"][0:2, 0:512], 1.0), reads=(), writes=("stgc0",))
    P.c("vector", lambda e: e.memset(sb["stg"][0:2, 512:1536], -1.0),
        reads=(), writes=("stgc1", "stgc2"))


def emit_phaseA_tile(cx, t):
    P, sb, pb, din, dr, G = cx.P, cx.sb, cx.pb, cx.din, cx.dr, cx.G
    t0 = t * TT

    nleft = NTILES - t
    nbatch = (len(cx.deferred) + nleft - 1) // nleft if cx.deferred else 0
    batch = cx.deferred[:nbatch]
    del cx.deferred[:nbatch]
    cx.tile_casts = batch[(len(batch) + 1) // 2:]

    for rnd in range(2):
        for s2 in range(2):
            s = rnd * 2 + s2
            P.dma("gpsimd", "gX", lambda e, s=s, s2=s2: e.dma_start(
                out=sb["xtm"][:, s2 * 2048:(s2 + 1) * 2048],
                in_=din["x"].ap()[t0 + s * 128:t0 + (s + 1) * 128, :]),
                reads=G("xtm"), writes=G("xtm"))
        for s2 in range(2):
            s = rnd * 2 + s2
            for fg in range(2):
                bank = "pt%d" % fg
                for f2 in range(8):
                    f = fg * 8 + f2
                    P.c("tensor", lambda e, s2=s2, f=f, f2=f2, bank=bank: e.transpose(
                        pb[bank][:, f2 * 128:(f2 + 1) * 128],
                        sb["xtm"][:, s2 * 2048 + f * 128:s2 * 2048 + (f + 1) * 128],
                        sb["ident_bf"][:]),
                        reads=G("xtm") + ("ident_bf",), writes=(bank,))
                P.c("scalar", lambda e, s=s, fg=fg, bank=bank: e.activation(
                    bass.AP(sb["xn_fm"], fg * 8 * 512 + s * 128, [[8192, 128], [512, 8], [1, 128]]),
                    pb[bank][:].rearrange("p (f c) -> p f c", f=8), AF.Copy),
                    reads=(bank,), writes=G("xn_fm"))
    for cast_rnd in batch[:(len(batch) + 1) // 2]:
        cast_rnd()

    for k in range(16):
        P.c("tensor", lambda e, k=k: e.matmul(
            pb["pf4"][0:1, :], sb["ones_col_bf"][:], sb["xn_fm"][:, k * 512:(k + 1) * 512],
            start=(k == 0), stop=(k == 15)),
            reads=G("xn_fm") + ("ones_col_bf",), writes=("pf4",))
    for k in range(16):
        P.c("scalar", lambda e, k=k: e.activation(
            sb["r_bc"][:, 0:512], sb["xn_fm"][:, k * 512:(k + 1) * 512], AF.Square),
            reads=G("xn_fm"), writes=G("r_bc"))
        P.c("tensor", lambda e, k=k: e.matmul(
            pb["pf5"][0:1, :], sb["ones_col_bf"][:], sb["r_bc"][:, 0:512],
            start=(k == 0), stop=(k == 15)),
            reads=G("r_bc") + ("ones_col_bf",), writes=("pf5",))
    P.c("vector", lambda e: e.tensor_scalar_mul(sb["strow"][0:1, 0:512], pb["pf4"][0:1, :], 1.0 / D),
        reads=("pf4",), writes=G("strow"))
    P.c("vector", lambda e: e.tensor_scalar_mul(sb["strow"][0:1, 512:1024], pb["pf5"][0:1, :], 1.0 / D),
        reads=("pf5",), writes=G("strow"))
    P.c("vector", lambda e: e.scalar_tensor_tensor(
        sb["strow"][0:1, 1024:1536], sb["strow"][0:1, 0:512], -1.0, sb["strow"][0:1, 0:512],
        ALU.mult, ALU.mult), reads=G("strow"), writes=G("strow"))
    P.c("vector", lambda e: e.tensor_add(sb["strow"][0:1, 1024:1536],
                                         sb["strow"][0:1, 1024:1536], sb["strow"][0:1, 512:1024]),
        reads=G("strow"), writes=G("strow"))
    P.c("vector", lambda e: e.tensor_scalar_add(sb["strow"][0:1, 1024:1536],
                                                sb["strow"][0:1, 1024:1536], LN_EPS),
        reads=G("strow"), writes=G("strow"))
    P.c("scalar", lambda e: e.activation(sb["strow"][0:1, 512:1024], sb["strow"][0:1, 1024:1536],
                                         AF.Ln), reads=G("strow"), writes=G("strow"))
    P.c("scalar", lambda e: e.activation(sb["strow"][0:1, 512:1024], sb["strow"][0:1, 512:1024],
                                         AF.Exp, scale=-0.5), reads=G("strow"), writes=G("strow"))
    P.c("vector", lambda e: e.tensor_mul(sb["strow"][0:1, 0:512], sb["strow"][0:1, 0:512],
                                         sb["strow"][0:1, 512:1024]),
        reads=G("strow"), writes=G("strow"))
    P.c("tensor", lambda e: e.matmul(pb["pf4"][:, :], sb["ones_row"][:],
                                     sb["strow"][0:1, 512:1024], start=True, stop=True),
        reads=G("strow") + ("ones_row",), writes=("pf4",))
    P.c("scalar", lambda e: e.activation(sb["r_bc"][:, 0:512], pb["pf4"][:, :], AF.Copy),
        reads=("pf4",), writes=G("r_bc"))
    P.c("tensor", lambda e: e.matmul(pb["pf5"][:, :], sb["ones_row"][:],
                                     sb["strow"][0:1, 0:512], start=True, stop=True),
        reads=G("strow") + ("ones_row",), writes=("pf5",))
    P.c("scalar", lambda e: e.activation(sb["mr_bc"][:, 0:512], pb["pf5"][:, :], AF.Copy),
        reads=("pf5",), writes=G("mr_bc"))
    for k in range(16):
        eng = "vector"
        P.c(eng, lambda e, k=k: e.tensor_mul(
            sb["xn_fm"][:, k * 512:(k + 1) * 512], sb["xn_fm"][:, k * 512:(k + 1) * 512],
            sb["r_bc"][:, 0:512]),
            reads=G("xn_fm", "r_bc"), writes=G("xn_fm"))
        P.c(eng, lambda e, k=k: e.tensor_sub(
            sb["xn_fm"][:, k * 512:(k + 1) * 512], sb["xn_fm"][:, k * 512:(k + 1) * 512],
            sb["mr_bc"][:, 0:512]),
            reads=G("xn_fm", "mr_bc"), writes=G("xn_fm"))

    for g in range(16):
        slot = g % 2
        buf = "wt%d" % slot
        P.dma("sync", "dW%d" % slot, lambda e, g=g, buf=buf: e.dma_start(
            out=sb[buf][:, 0:4096], in_=dr["wqkv_bf"].ap()[g]),
            reads=("wqkv_bf",), writes=G(buf))
        for oc2 in range(2):
            oc = g * 2 + oc2
            bank = "pf%d" % (oc % 4)
            for k in range(16):
                P.c("tensor", lambda e, oc2=oc2, k=k, buf=buf, bank=bank: e.matmul(
                    pb[bank][:, :], sb[buf][:, k * 256 + oc2 * 128:k * 256 + oc2 * 128 + 128],
                    sb["xn_fm"][:, k * 512:(k + 1) * 512], start=(k == 0), stop=(k == 15)),
                    reads=G(buf, "xn_fm"), writes=(bank,))
            P.c("scalar", lambda e, oc=oc, bank=bank: e.activation(
                sb["qkvp"][:, oc * 512:(oc + 1) * 512], pb[bank][:, :], AF.Copy),
                reads=(bank,), writes=G("qkvp"))

    for g in range(8):
        slot = g % 2
        buf = "wt%d" % slot
        P.dma("sync", "dW%d" % slot, lambda e, g=g, buf=buf: e.dma_start(
            out=sb[buf][:, 0:4096], in_=dr["wz_bf"].ap()[g]),
            reads=("wz_bf",), writes=G(buf))
        for s in range(4):
            bank = "pf%d" % (s % 4)
            for k in range(16):
                P.c("tensor", lambda e, s=s, k=k, buf=buf, bank=bank: e.matmul(
                    pb[bank][:, 0:256], sb["xn_fm"][:, k * 512 + s * 128:k * 512 + (s + 1) * 128],
                    sb[buf][:, k * 256:(k + 1) * 256], start=(k == 0), stop=(k == 15)),
                    reads=G(buf, "xn_fm"), writes=(bank,))
            P.c("scalar", lambda e, s=s, g=g, bank=bank: e.activation(
                sb["sz"][:, s * 2048 + g * 256:s * 2048 + (g + 1) * 256],
                pb[bank][:, 0:256], AF.Silu),
                reads=(bank,), writes=G("sz"))

    for s in range(4):
        for k in range(16):
            P.c("tensor", lambda e, s=s, k=k: e.matmul(
                pb["pf4"][:, 0:32], sb["xn_fm"][:, k * 512 + s * 128:k * 512 + (s + 1) * 128],
                sb["wba_t"][:, k * 32:(k + 1) * 32], start=(k == 0), stop=(k == 15)),
                reads=("wba_t",) + G("xn_fm"), writes=("pf4",))
        P.c("vector", lambda e, s=s: e.tensor_copy(sb["ba_all"][:, s * 32:(s + 1) * 32],
                                                   pb["pf4"][:, 0:32]),
            reads=("pf4",), writes=G("ba_all"))

    for rnd in cx.tile_casts:
        rnd()
    cx.tile_casts = []
    if t in cx.coll_at_tile:
        cx.coll_at_tile.pop(t)()

    P.c("vector", lambda e: e.tensor_copy(
        sb["halo2"][:].rearrange("p (c k) -> p c k", c=32),
        bass.AP(sb["qkvp"], 509, [[16384, 128], [512, 32], [1, 3]])),
        reads=G("qkvp"), writes=("halo2",))
    for cc in range(32):
        eng = "vector"
        scr = "cvt" if cc % 2 == 0 else "mdiff"
        base = cc * 512
        P.c(eng, lambda e, cc=cc, base=base, scr=scr: e.tensor_scalar_mul(
            sb[scr][:, 0:512], sb["qkvp"][:, base:base + 512],
            sb["convw"][:, cc * 4 + 3:cc * 4 + 4]),
            reads=G("qkvp") + ("convw",), writes=G(scr))
        for k in range(3):
            nh = 3 - k
            P.c(eng, lambda e, cc=cc, base=base, k=k, nh=nh, scr=scr: e.scalar_tensor_tensor(
                sb[scr][:, nh:512], sb["qkvp"][:, base:base + 512 - nh],
                sb["convw"][:, cc * 4 + k:cc * 4 + k + 1], sb[scr][:, nh:512],
                ALU.mult, ALU.add),
                reads=G("qkvp", scr) + ("convw",), writes=G(scr))
            P.c(eng, lambda e, cc=cc, k=k, nh=nh, scr=scr: e.scalar_tensor_tensor(
                sb[scr][:, 0:nh], sb["halo"][:, cc * 3 + k:cc * 3 + k + nh],
                sb["convw"][:, cc * 4 + k:cc * 4 + k + 1], sb[scr][:, 0:nh],
                ALU.mult, ALU.add),
                reads=G("halo", scr) + ("convw",), writes=G(scr))
        P.c(eng, lambda e, cc=cc, base=base, scr=scr: e.tensor_copy(
            sb["qkvp"][:, base:base + 512], sb[scr][:, 0:512]),
            reads=G(scr) + ("halo2",), writes=G("qkvp"))
    P.c("vector", lambda e: e.tensor_copy(sb["halo"][:], sb["halo2"][:]),
        reads=("halo2",) + G("halo"), writes=G("halo"))

    gch = sb["gch"]
    bvw = bass.AP(sb["ba_all"], 0, [[128, 128], [32, 4], [1, 16]])
    avw = bass.AP(sb["ba_all"], 16, [[128, 128], [32, 4], [1, 16]])
    LNB, GC, R1, BGF, BET, KDF, EG = 0, 64, 128, 192, 256, 320, 384

    def gv(off):
        return gch[:, off:off + 64]
    P.c("vector", lambda e: e.tensor_scalar_min(gv(LNB), bvw, 0.0),
        reads=G("ba_all"), writes=G("gch"))
    P.c("scalar", lambda e: e.activation(gv(R1), bvw, AF.Abs),
        reads=G("ba_all"), writes=G("gch"))
    P.c("scalar", lambda e: e.activation(gv(R1), gv(R1), AF.Exp, scale=-1.0),
        reads=G("gch"), writes=G("gch"))
    P.c("scalar", lambda e: e.activation(gv(R1), gv(R1), AF.Ln, bias=1.0),
        reads=G("gch"), writes=G("gch"))
    P.c("vector", lambda e: e.tensor_sub(gv(LNB), gv(LNB), gv(R1)),
        reads=G("gch"), writes=G("gch"))
    P.c("vector", lambda e: e.tensor_tensor(
        gv(GC).rearrange("p (c v) -> p c v", c=4), avw,
        sb["dtb_bc"][:].unsqueeze(1).broadcast_to([128, 4, 16]), ALU.add),
        reads=G("ba_all") + ("dtb_bc",), writes=G("gch"))
    P.c("scalar", lambda e: e.activation(gv(BGF), gv(GC), AF.Abs),
        reads=G("gch"), writes=G("gch"))
    P.c("scalar", lambda e: e.activation(gv(BGF), gv(BGF), AF.Exp, scale=-1.0),
        reads=G("gch"), writes=G("gch"))
    P.c("scalar", lambda e: e.activation(gv(BGF), gv(BGF), AF.Ln, bias=1.0),
        reads=G("gch"), writes=G("gch"))
    P.c("vector", lambda e: e.tensor_scalar_max(gv(GC), gv(GC), 0.0),
        reads=G("gch"), writes=G("gch"))
    P.c("vector", lambda e: e.tensor_add(gv(GC), gv(GC), gv(BGF)),
        reads=G("gch"), writes=G("gch"))
    P.c("vector", lambda e: e.tensor_tensor(
        gv(GC).rearrange("p (c v) -> p c v", c=4), gv(GC).rearrange("p (c v) -> p c v", c=4),
        sb["acoef_bc"][:].unsqueeze(1).broadcast_to([128, 4, 16]), ALU.mult),
        reads=G("gch") + ("acoef_bc",), writes=G("gch"))
    for ci in range(CPT):
        P.c("tensor", lambda e, ci=ci: e.matmul(
            pb["pf5"][:, ci * 16:(ci + 1) * 16], sb["mski"][:],
            gch[:, GC + ci * 16:GC + (ci + 1) * 16], start=True, stop=True),
            reads=G("gch") + ("mski",), writes=("pf5",))
    P.c("vector", lambda e: e.tensor_copy(gv(GC), pb["pf5"][:, 0:64]),
        reads=("pf5",), writes=G("gch"))
    P.c("vector", lambda e: e.tensor_add(gv(R1), gv(GC), gv(LNB)),
        reads=G("gch"), writes=G("gch"))
    P.c("scalar", lambda e: e.activation(gv(BGF), gv(R1), AF.Exp),
        reads=G("gch"), writes=G("gch"))
    P.c("scalar", lambda e: e.activation(gv(BET), gv(LNB), AF.Exp),
        reads=G("gch"), writes=G("gch"))
    P.c("scalar", lambda e: e.activation(gv(EG), gv(GC), AF.Exp),
        reads=G("gch"), writes=G("gch"))

    for ci in range(CPT):
        emit_chunk(cx, t, ci)


def trans_group(cx, srcs, dst, evac_af=AF.Copy, evac_engs=("scalar",)):
    P, sb, pb, G = cx.P, cx.sb, cx.pb, cx.G
    for grp in range((len(srcs) + 7) // 8):
        bank = "pt%d" % (grp % 2)
        blk = srcs[grp * 8:(grp + 1) * 8]
        for j2, (tn, co) in enumerate(blk):
            P.c("tensor", lambda e, tn=tn, co=co, j2=j2, bank=bank: e.transpose(
                pb[bank][:, j2 * 128:(j2 + 1) * 128], sb[tn][:, co:co + 128],
                sb["ident_bf"][:]),
                reads=G(tn) + ("ident_bf",), writes=(bank,))
        eng = evac_engs[grp % len(evac_engs)]
        if eng == "scalar":
            P.c("scalar", lambda e, grp=grp, n=len(blk), bank=bank, dst=dst, af=evac_af: e.activation(
                sb[dst][:, grp * 1024:grp * 1024 + n * 128], pb[bank][:, 0:n * 128], af),
                reads=(bank,), writes=G(dst))
        else:
            P.c(eng, lambda e, grp=grp, n=len(blk), bank=bank, dst=dst: e.tensor_copy(
                sb[dst][:, grp * 1024:grp * 1024 + n * 128], pb[bank][:, 0:n * 128]),
                reads=(bank,), writes=G(dst))


def emit_chunk(cx, t, ci):
    P, sb, pb, G = cx.P, cx.sb, cx.pb, cx.G
    t0 = t * TT + ci * CH
    gch = sb["gch"]
    LNB, GC, R1, BGF, BET, KDF, EG = 0, 64, 128, 192, 256, 320, 384
    cs = ci * 16

    P.c("tensor", lambda e, cs=cs: e.transpose(pb["pf5"][0:16, 128:256],
                                               gch[:, GC + cs:GC + cs + 16], sb["identf"][:]),
        reads=G("gch") + ("identf",), writes=("pf5",))
    P.c("vector", lambda e: e.tensor_copy(sb["gT"][:, 0:128], pb["pf5"][0:16, 128:256]),
        reads=("pf5",), writes=G("gT"))
    P.c("tensor", lambda e, cs=cs: e.transpose(pb["pf5"][0:16, 256:384],
                                               gch[:, R1 + cs:R1 + cs + 16], sb["identf"][:]),
        reads=G("gch") + ("identf",), writes=("pf5",))
    P.c("vector", lambda e: e.tensor_copy(sb["gT"][:, 128:256], pb["pf5"][0:16, 256:384]),
        reads=("pf5",), writes=G("gT"))
    P.c("tensor", lambda e: e.matmul(
        pb["pf5"][0:1, 0:16], sb["gT"][:, 127:128], sb["identf"][0:16, 0:16],
        start=True, stop=True),
        reads=G("gT") + ("identf",), writes=("pf5",))
    P.c("vector", lambda e: e.tensor_copy(sb["rowtmp"][:], pb["pf5"][0:1, 0:16]),
        reads=("pf5",), writes=("rowtmp",))
    P.c("tensor", lambda e: e.matmul(
        pb["pf5"][:, 0:16], sb["ones_row"][:], sb["rowtmp"][:], start=True, stop=True),
        reads=("rowtmp", "ones_row"), writes=("pf5",))
    P.c("scalar", lambda e, ci=ci: e.activation(
        sb["egl"][:, ci * 16:(ci + 1) * 16], pb["pf5"][:, 0:16], AF.Exp),
        reads=("pf5",), writes=G("egl"))
    P.c("vector", lambda e, cs=cs: e.tensor_sub(gch[:, KDF + cs:KDF + cs + 16],
                                                pb["pf5"][:, 0:16], gch[:, GC + cs:GC + cs + 16]),
        reads=("pf5",) + G("gch"), writes=G("gch"))
    P.c("scalar", lambda e, cs=cs: e.activation(gch[:, KDF + cs:KDF + cs + 16],
                                                gch[:, KDF + cs:KDF + cs + 16], AF.Exp),
        reads=G("gch"), writes=G("gch"))

    trans_group(cx, [("qkvp", j * 512 + ci * 128) for j in range(0, 8)], "q_tm", AF.Silu)
    trans_group(cx, [("qkvp", j * 512 + ci * 128) for j in range(8, 16)], "k_tm", AF.Silu)
    trans_group(cx, [("qkvp", j * 512 + ci * 128) for j in range(16, 32)], "v_tm", AF.Silu)

    for (src, col0) in (("q_tm", 0), ("k_tm", 8)):
        P.c("scalar", lambda e, src=src: e.activation(sb["ysq"][:, 0:1024], sb[src][:], AF.Square),
            reads=G(src), writes=G("ysq"))
        P.c("vector", lambda e, col0=col0: e.tensor_reduce(
            sb["lncol"][:, col0:col0 + 8],
            sb["ysq"][:, 0:1024].rearrange("p (h d) -> p h d", h=8), AX.X, ALU.add),
            reads=G("ysq"), writes=G("lncol"))
    P.c("vector", lambda e: e.tensor_scalar_add(sb["lncol"][:, 0:16], sb["lncol"][:, 0:16],
                                                L2_EPS),
        reads=G("lncol"), writes=G("lncol"))
    P.c("scalar", lambda e: e.activation(sb["lncol"][:, 0:16], sb["lncol"][:, 0:16],
                                         AF.Ln), reads=G("lncol"), writes=G("lncol"))
    P.c("scalar", lambda e: e.activation(sb["lncol"][:, 0:16], sb["lncol"][:, 0:16],
                                         AF.Exp, scale=-0.5), reads=G("lncol"), writes=G("lncol"))
    P.c("vector", lambda e: e.tensor_scalar_mul(sb["lncol"][:, 0:8], sb["lncol"][:, 0:8],
                                                float(128.0 ** -0.5)),
        reads=G("lncol"), writes=G("lncol"))
    P.c("vector", lambda e: e.tensor_tensor(
        sb["qs_tm"][:].rearrange("p (h d) -> p h d", h=8),
        sb["q_tm"][:].rearrange("p (h d) -> p h d", h=8),
        sb["lncol"][:, 0:8].unsqueeze(2).broadcast_to([128, 8, 128]), ALU.mult),
        reads=G("q_tm", "lncol"), writes=G("qs_tm"))
    P.c("vector", lambda e: e.tensor_tensor(
        sb["kn_tm"][:].rearrange("p (h d) -> p h d", h=8),
        sb["k_tm"][:].rearrange("p (h d) -> p h d", h=8),
        sb["lncol"][:, 8:16].unsqueeze(2).broadcast_to([128, 8, 128]), ALU.mult),
        reads=G("k_tm", "lncol"), writes=G("kn_tm"))
    P.c("vector", lambda e, cs=cs: e.tensor_tensor(
        sb["vb"][:].rearrange("p (v d) -> p v d", v=16),
        sb["v_tm"][:].rearrange("p (v d) -> p v d", v=16),
        gch[:, BET + cs:BET + cs + 16].unsqueeze(2).broadcast_to([128, 16, 128]), ALU.mult),
        reads=G("v_tm", "gch"), writes=G("vb"))
    P.c("vector", lambda e, cs=cs: e.tensor_tensor(
        sb["kbg"][:].rearrange("p (h u d) -> p h u d", h=8, u=2),
        sb["kn_tm"][:].rearrange("p (h d) -> p h d", h=8)
        .unsqueeze(2).broadcast_to([128, 8, 2, 128]),
        gch[:, BGF + cs:BGF + cs + 16].rearrange("p (h u) -> p h u", h=8)
        .unsqueeze(3).broadcast_to([128, 8, 2, 128]), ALU.mult),
        reads=G("kn_tm", "gch"), writes=G("kbg"))
    P.c("vector", lambda e, cs=cs: e.tensor_tensor(
        sb["kdec"][:].rearrange("p (h u d) -> p h u d", h=8, u=2),
        sb["kn_tm"][:].rearrange("p (h d) -> p h d", h=8)
        .unsqueeze(2).broadcast_to([128, 8, 2, 128]),
        gch[:, KDF + cs:KDF + cs + 16].rearrange("p (h u) -> p h u", h=8)
        .unsqueeze(3).broadcast_to([128, 8, 2, 128]), ALU.mult),
        reads=G("kn_tm", "gch"), writes=G("kdec"))
    P.c("vector", lambda e, cs=cs: e.tensor_tensor(
        sb["qgi_tm"][:].rearrange("p (h u d) -> p h u d", h=8, u=2),
        sb["qs_tm"][:].rearrange("p (h d) -> p h d", h=8)
        .unsqueeze(2).broadcast_to([128, 8, 2, 128]),
        gch[:, EG + cs:EG + cs + 16].rearrange("p (h u) -> p h u", h=8)
        .unsqueeze(3).broadcast_to([128, 8, 2, 128]), ALU.mult),
        reads=G("qs_tm", "gch"), writes=G("qgi_tm"))

    trans_group(cx, [("qs_tm", j * 128) for j in range(8)], "qs_fm")
    trans_group(cx, [("kn_tm", j * 128) for j in range(8)], "kn_fm",
                evac_engs=("vector",))
    trans_group(cx, [("qgi_tm", j * 128) for j in range(16)], "qgi_fm",
                evac_engs=("scalar", "vector"))

    for rd in range(4):
        P.dma("scalar", "dRS", lambda e, rd=rd: e.dma_start(
            out=sb["stg"][1:2, 0:512], in_=sb["gT"][rd * 4:rd * 4 + 4, 0:128]),
            reads=G("gT"), writes=("stg_l",))
        P.dma("scalar", "dRS", lambda e, rd=rd: e.dma_start(
            out=sb["stg"][0:1, 512:1024], in_=sb["gT"][rd * 4:rd * 4 + 4, 0:128]),
            reads=G("gT"), writes=("stg_r0",))
        P.dma("scalar", "dRS", lambda e, rd=rd: e.dma_start(
            out=sb["stg"][0:1, 1024:1536], in_=sb["gT"][rd * 4:rd * 4 + 4, 128:256]),
            reads=G("gT"), writes=("stg_r1",))
        for v4 in range(4):
            P.c("tensor", lambda e, v4=v4: e.matmul(
                pb["pf0"][:, v4 * 128:(v4 + 1) * 128],
                bass.AP(sb["stg"], v4 * 128, [[1536, 2], [1, 128]]),
                bass.AP(sb["stg"], 1024 + v4 * 128, [[1536, 2], [1, 128]]),
                start=True, stop=True),
                reads=("stg_l", "stg_r1", "stgc0", "stgc2"), writes=("pf0",))
            P.c("tensor", lambda e, v4=v4: e.matmul(
                pb["pf1"][:, v4 * 128:(v4 + 1) * 128],
                bass.AP(sb["stg"], v4 * 128, [[1536, 2], [1, 128]]),
                bass.AP(sb["stg"], 512 + v4 * 128, [[1536, 2], [1, 128]]),
                start=True, stop=True),
                reads=("stg_l", "stg_r0", "stgc0", "stgc1"), writes=("pf1",))
        P.c("vector", lambda e: e.tensor_tensor(
            sb["mdiff"][:].rearrange("p (v c) -> p v c", v=4),
            pb["pf0"][:].rearrange("p (v c) -> p v c", v=4),
            sb["msks"][:].unsqueeze(1).broadcast_to([128, 4, 128]), ALU.mult),
            reads=("pf0", "msks"), writes=G("mdiff"))
        P.c("vector", lambda e: e.tensor_tensor(
            sb["cvt"][:].rearrange("p (v c) -> p v c", v=4),
            pb["pf1"][:].rearrange("p (v c) -> p v c", v=4),
            sb["mski"][:].unsqueeze(1).broadcast_to([128, 4, 128]), ALU.mult),
            reads=("pf1", "mski"), writes=G("cvt"))
        P.c("scalar", lambda e, rd=rd: e.activation(
            sb["m1"][:, rd * 512:(rd + 1) * 512], sb["mdiff"][:, 0:512], AF.Exp),
            reads=G("mdiff"), writes=G("m1"))
        P.c("scalar", lambda e, rd=rd: e.activation(
            sb["m0"][:, rd * 512:(rd + 1) * 512], sb["cvt"][:, 0:512], AF.Exp),
            reads=G("cvt"), writes=G("m0"))

    for rnd in range(2):
        for k4 in range(4):
            kh = rnd * 4 + k4
            P.c("tensor", lambda e, kh=kh, k4=k4: e.matmul(
                pb["pf2"][:, k4 * 128:(k4 + 1) * 128], sb["kn_fm"][:, kh * 128:(kh + 1) * 128],
                sb["kn_fm"][:, kh * 128:(kh + 1) * 128], start=True, stop=True),
                reads=G("kn_fm"), writes=("pf2",))
            P.c("tensor", lambda e, kh=kh, k4=k4: e.matmul(
                pb["pf3"][:, k4 * 128:(k4 + 1) * 128], sb["kn_fm"][:, kh * 128:(kh + 1) * 128],
                sb["qs_fm"][:, kh * 128:(kh + 1) * 128], start=True, stop=True),
                reads=G("kn_fm", "qs_fm"), writes=("pf3",))
        P.c("vector", lambda e, rnd=rnd: e.scalar_tensor_tensor(
            sb["Gm"][:, rnd * 512:(rnd + 1) * 512].rearrange("p (v c) -> p v c", v=4),
            pb["pf2"][:].rearrange("p (v c) -> p v c", v=4), -1.0,
            sb["msks"][:].unsqueeze(1).broadcast_to([128, 4, 128]), ALU.mult, ALU.mult),
            reads=("pf2", "msks"), writes=G("Gm"))
        P.c("vector", lambda e, rnd=rnd: e.tensor_tensor(
            sb["qkTm"][:, rnd * 512:(rnd + 1) * 512].rearrange("p (v c) -> p v c", v=4),
            pb["pf3"][:].rearrange("p (v c) -> p v c", v=4),
            sb["mski"][:].unsqueeze(1).broadcast_to([128, 4, 128]), ALU.mult),
            reads=("pf3", "mski"), writes=G("qkTm"))
    P.c("vector", lambda e: e.tensor_tensor(
        sb["C"][:].rearrange("p (h u d) -> p h u d", h=8, u=2),
        sb["Gm"][:].rearrange("p (h d) -> p h d", h=8)
        .unsqueeze(2).broadcast_to([128, 8, 2, 128]),
        sb["m1"][:].rearrange("p (h u d) -> p h u d", h=8, u=2), ALU.mult),
        reads=G("Gm", "m1"), writes=G("C"))
    P.c("vector", lambda e: e.tensor_tensor(
        sb["m0"][:].rearrange("p (h u d) -> p h u d", h=8, u=2),
        sb["qkTm"][:].rearrange("p (h d) -> p h d", h=8)
        .unsqueeze(2).broadcast_to([128, 8, 2, 128]),
        sb["m0"][:].rearrange("p (h u d) -> p h u d", h=8, u=2), ALU.mult),
        reads=G("qkTm", "m0"), writes=G("m0"))

    trans_group(cx, [("C", j * 128) for j in range(16)], "A",
                evac_engs=("scalar", "vector"))
    P.c("vector", lambda e: e.tensor_tensor(
        sb["P"][:].rearrange("p (v c) -> p v c", v=16),
        sb["C"][:].rearrange("p (v c) -> p v c", v=16),
        sb["ident_bf"][:].unsqueeze(1).broadcast_to([128, 16, 128]), ALU.add),
        reads=G("C") + ("ident_bf",), writes=G("P"))

    def quad_evac(grp, bank, dst):
        if grp % 2 == 0:
            P.c("scalar", lambda e, grp=grp, bank=bank, dst=dst: e.activation(
                sb[dst][:, grp * 512:(grp + 1) * 512], pb[bank][:], AF.Copy),
                reads=(bank,), writes=G(dst))
        else:
            P.c("vector", lambda e, grp=grp, bank=bank, dst=dst: e.tensor_copy(
                sb[dst][:, grp * 512:(grp + 1) * 512], pb[bank][:]),
                reads=(bank,), writes=G(dst))

    def quad_mm(lhs_of, rhs_of, dst):
        for grp in range(4):
            bank = "pf%d" % (grp % 4)
            rd = set()
            for v2 in range(4):
                vh = grp * 4 + v2
                ln, lo = lhs_of(vh)
                rn, ro = rhs_of(vh)
                rd.add(ln)
                rd.add(rn)
                P.c("tensor", lambda e, v2=v2, ln=ln, lo=lo, rn=rn, ro=ro, bank=bank: e.matmul(
                    pb[bank][:, v2 * 128:(v2 + 1) * 128],
                    sb[ln][:, lo:lo + 128], sb[rn][:, ro:ro + 128],
                    start=True, stop=True),
                    reads=G(*rd), writes=(bank,))
            quad_evac(grp, bank, dst)

    def quad_mm_addp(a_name, p_name, dst):
        for grp in range(4):
            bank = "pf%d" % (grp % 4)
            for v2 in range(4):
                vh = grp * 4 + v2
                P.c("tensor", lambda e, v2=v2, vh=vh, bank=bank, a=a_name, p=p_name: e.matmul(
                    pb[bank][:, v2 * 128:(v2 + 1) * 128],
                    sb[a][:, vh * 128:(vh + 1) * 128], sb[p][:, vh * 128:(vh + 1) * 128],
                    start=True, stop=False),
                    reads=G(a_name, p_name), writes=(bank,))
                P.c("tensor", lambda e, v2=v2, vh=vh, bank=bank, p=p_name: e.matmul(
                    pb[bank][:, v2 * 128:(v2 + 1) * 128],
                    sb["ident_bf"][:], sb[p][:, vh * 128:(vh + 1) * 128],
                    start=False, stop=True),
                    reads=("ident_bf",) + G(p_name), writes=(bank,))
            quad_evac(grp, bank, dst)

    cur_c, cur_a, cur_p = "C", "A", "P"
    oth_c, oth_a, oth_p = "C2", "A2", "P2"
    for lvl in range(6):
        if lvl < 5:
            quad_mm(lambda vh, a=cur_a: (a, vh * 128), lambda vh, c=cur_c: (c, vh * 128), oth_c)
        quad_mm(lambda vh, c=cur_c: (c, vh * 128), lambda vh, a=cur_a: (a, vh * 128), oth_a)
        quad_mm_addp(oth_a, cur_p, oth_p)
        cur_c, oth_c = oth_c, cur_c
        cur_a, oth_a = oth_a, cur_a
        cur_p, oth_p = oth_p, cur_p
    Pfin = cur_p

    quad_mm(lambda vh, p=Pfin: (p, vh * 128), lambda vh: ("vb", vh * 128), "vc")
    quad_mm(lambda vh: ("kbg", vh * 128), lambda vh, p=Pfin: (p, vh * 128), "kcdT")

    for grp in range(4):
        bs = "pf%d" % (2 + grp % 2)
        bo = "pf%d" % (4 + grp % 2)
        bd = "pf%d" % (grp % 2)
        for v2 in range(4):
            vh = grp * 4 + v2
            P.c("tensor", lambda e, vh=vh, v2=v2, bs=bs: e.matmul(
                pb[bs][:, v2 * 128:(v2 + 1) * 128], sb["kcdT"][:, vh * 128:(vh + 1) * 128],
                sb["S_bf"][:, vh * 128:(vh + 1) * 128], start=True, stop=True),
                reads=G("kcdT", "S_bf"), writes=(bs,))
        P.c("vector", lambda e, grp=grp, bs=bs: e.tensor_sub(
            sb["vnew"][:, grp * 512:(grp + 1) * 512], sb["vc"][:, grp * 512:(grp + 1) * 512],
            pb[bs][:]),
            reads=G("vc") + (bs,), writes=G("vnew"))
        for v2 in range(4):
            vh = grp * 4 + v2
            P.c("tensor", lambda e, vh=vh, v2=v2, bo=bo: e.matmul(
                pb[bo][:, v2 * 128:(v2 + 1) * 128], sb["qgi_fm"][:, vh * 128:(vh + 1) * 128],
                sb["S_bf"][:, vh * 128:(vh + 1) * 128], start=True, stop=False),
                reads=G("qgi_fm", "S_bf"), writes=(bo,))
            P.c("tensor", lambda e, vh=vh, v2=v2, bo=bo: e.matmul(
                pb[bo][:, v2 * 128:(v2 + 1) * 128], sb["m0"][:, vh * 128:(vh + 1) * 128],
                sb["vnew"][:, vh * 128:(vh + 1) * 128], start=False, stop=True),
                reads=G("m0", "vnew"), writes=(bo,))
            P.c("tensor", lambda e, vh=vh, v2=v2, bd=bd: e.matmul(
                pb[bd][:, v2 * 128:(v2 + 1) * 128], sb["kdec"][:, vh * 128:(vh + 1) * 128],
                sb["vnew"][:, vh * 128:(vh + 1) * 128], start=True, stop=True),
                reads=G("kdec", "vnew"), writes=(bd,))
        P.c("vector", lambda e, grp=grp, bo=bo, ci=ci: e.tensor_mul(
            sb["y_tm"][:, grp * 512:(grp + 1) * 512], pb[bo][:],
            sb["sz"][:, ci * 2048 + grp * 512:ci * 2048 + (grp + 1) * 512]),
            reads=(bo,) + G("sz"), writes=G("y_tm"))
        for v2 in range(4):
            vh = grp * 4 + v2
            P.c("vector", lambda e, vh=vh, v2=v2, bd=bd, ci=ci: e.scalar_tensor_tensor(
                sb["S_bf"][:, vh * 128:(vh + 1) * 128], sb["S_bf"][:, vh * 128:(vh + 1) * 128],
                sb["egl"][:, ci * 16 + vh:ci * 16 + vh + 1], pb[bd][:, v2 * 128:(v2 + 1) * 128],
                ALU.mult, ALU.add),
                reads=G("S_bf", "egl") + (bd,), writes=G("S_bf"))

    P.c("scalar", lambda e: e.activation(sb["ysq"][:, 0:2048], sb["y_tm"][:], AF.Square),
        reads=G("y_tm"), writes=G("ysq"))
    P.c("vector", lambda e: e.tensor_reduce(
        sb["rstc"][:, 0:16], sb["ysq"][:].rearrange("p (v d) -> p v d", v=16), AX.X, ALU.add),
        reads=G("ysq"), writes=G("rstc"))
    P.c("vector", lambda e: e.tensor_scalar_add(sb["rstc"][:, 0:16], sb["rstc"][:, 0:16],
                                                RMS_EPS * 128.0),
        reads=G("rstc"), writes=G("rstc"))
    P.c("scalar", lambda e: e.activation(sb["rstc"][:, 0:16], sb["rstc"][:, 0:16],
                                         AF.Ln), reads=G("rstc"), writes=G("rstc"))
    P.c("scalar", lambda e: e.activation(sb["rstc"][:, 0:16], sb["rstc"][:, 0:16],
                                         AF.Exp, scale=-0.5), reads=G("rstc"), writes=G("rstc"))
    P.c("vector", lambda e: e.tensor_scalar_mul(sb["rstc"][:, 0:16], sb["rstc"][:, 0:16],
                                                float(128.0 ** 0.5)),
        reads=G("rstc"), writes=G("rstc"))
    P.c("vector", lambda e: e.tensor_tensor(
        sb["y_tm"][:].rearrange("p (v d) -> p v d", v=16),
        sb["y_tm"][:].rearrange("p (v d) -> p v d", v=16),
        sb["rstc"][:, 0:16].unsqueeze(2).broadcast_to([128, 16, 128]), ALU.mult),
        reads=G("y_tm", "rstc"), writes=G("y_tm"))
    trans_group(cx, [("y_tm", j * 128) for j in range(16)], "nrm_fm",
                evac_engs=("scalar", "vector"))
    P.c("vector", lambda e: e.tensor_scalar_mul(
        sb["nrm_fm"][:], sb["nrm_fm"][:], sb["normw_col"][:]),
        reads=G("nrm_fm") + ("normw_col",), writes=G("nrm_fm"))
    q, col = t0 // 1024, t0 % 1024
    for half in range(2):
        P.dma("scalar", "dST", lambda e, half=half, q=q, col=col: e.dma_start(
            out=cx.dr["cc_in"].ap()[q, half * 1024:(half + 1) * 1024, col:col + 128]
            .rearrange("(v p) c -> v p c", p=128).transpose([1, 0, 2]),
            in_=sb["nrm_fm"][:, half * 1024:(half + 1) * 1024]
            .rearrange("p (v c) -> p v c", v=8)),
            reads=G("nrm_fm"), writes=("cc_in_q%d" % q,))


def emit_phaseB_tile(cx, tb):
    P, sb, pb, din, dr, G = cx.P, cx.sb, cx.pb, cx.din, cx.dr, cx.G
    t0 = tb * TTB

    srcq = dr["cc_out%d" % tb].ap().rearrange("r (u c) -> (r u) c", u=2)
    for g in range(32):
        P.dma("gpsimd", "dGT", lambda e, g=g, srcq=srcq: e.indirect_dma_start(
            sb["nrmB"][:, g * 512:(g + 1) * 512], None, srcq,
            bass.IndirectOffsetOnAxis(sb["gth_idx"][:, g:g + 1], 0)),
            reads=("cc_out%d" % tb, "gth_idx") + G("nrmB"), writes=G("nrmB"))
    if tb == 0:
        cx.coll_b0()

    for hg in range(4):
        for rg in range(4):
            slot = rg % 2
            buf = "wt%d" % slot
            P.dma("sync", "dW%d" % slot, lambda e, hg=hg, rg=rg, buf=buf: e.dma_start(
                out=sb[buf][:, 0:4096].rearrange("p (r c) -> p r c", r=8),
                in_=dr["wout_bf"].ap()[hg, rg * 8:(rg + 1) * 8].transpose([1, 0, 2])),
                reads=("wout_bf",), writes=G(buf))
            for s in range(4):
                bank = "pf%d" % s
                for r2 in range(8):
                    rc = rg * 8 + r2
                    P.c("tensor", lambda e, s=s, rc=rc, r2=r2, buf=buf, bank=bank: e.matmul(
                        pb[bank][:, :],
                        sb["nrmB"][:, rc * 512 + s * 128:rc * 512 + (s + 1) * 128],
                        sb[buf][:, r2 * 512:(r2 + 1) * 512],
                        start=(rc == 0), stop=(rc == 31)),
                        reads=G("nrmB", buf), writes=(bank,))
        for s in range(4):
            bank = "pf%d" % s
            P.dma("sync", "dX", lambda e, s=s, hg=hg: e.dma_start(
                out=sb["resfs"][:, 0:512],
                in_=din["xh"].ap()[t0 + s * 128:t0 + (s + 1) * 128, hg * 512:(hg + 1) * 512]),
                reads=G("resfs"), writes=G("resfs"))
            P.c("vector", lambda e, s=s, hg=hg, bank=bank: e.tensor_add(
                sb["resfs"][:, 512:1024], pb[bank][:, :], sb["resfs"][:, 0:512]),
                reads=(bank,) + G("resfs"), writes=G("resfs"))
            P.dma("scalar", "dRES", lambda e, s=s, hg=hg: e.dma_start(
                out=dr["res0_d"].ap()[t0 + s * 128:t0 + (s + 1) * 128, hg * 512:(hg + 1) * 512],
                in_=sb["resfs"][:, 512:1024]),
                reads=G("resfs"), writes=("res0_d",))

    for s in range(4):
        P.dma("sync", "dRES", lambda e, s=s: e.dma_start(
            out=sb["res0s"][:, 0:2048], in_=dr["res0_d"].ap()[t0 + s * 128:t0 + (s + 1) * 128, :]),
            reads=("res0_d",) + G("res0s"), writes=G("res0s"))
        P.c("vector", lambda e: e.tensor_reduce(
            sb["rstc"][:, 0:1], sb["res0s"][:, 0:2048], AX.X, ALU.add),
            reads=G("res0s"), writes=G("rstc"))
        P.c("scalar", lambda e: e.activation(sb["resfs"][:, 0:2048], sb["res0s"][:, 0:2048],
                                             AF.Square),
            reads=G("res0s"), writes=G("resfs"))
        P.c("vector", lambda e: e.tensor_reduce(
            sb["rstc"][:, 1:2], sb["resfs"][:, 0:2048], AX.X, ALU.add),
            reads=G("resfs"), writes=G("rstc"))
        P.c("vector", lambda e: e.tensor_scalar_mul(sb["rstc"][:, 0:2], sb["rstc"][:, 0:2], 1.0 / D),
            reads=G("rstc"), writes=G("rstc"))
        P.c("vector", lambda e: e.scalar_tensor_tensor(
            sb["rstc"][:, 2:3], sb["rstc"][:, 0:1], -1.0, sb["rstc"][:, 0:1],
            ALU.mult, ALU.mult), reads=G("rstc"), writes=G("rstc"))
        P.c("vector", lambda e: e.tensor_add(sb["rstc"][:, 2:3], sb["rstc"][:, 2:3],
                                             sb["rstc"][:, 1:2]),
            reads=G("rstc"), writes=G("rstc"))
        P.c("vector", lambda e: e.tensor_scalar_add(sb["rstc"][:, 2:3], sb["rstc"][:, 2:3],
                                                    LN_EPS), reads=G("rstc"), writes=G("rstc"))
        P.c("scalar", lambda e: e.activation(sb["rstc"][:, 2:3], sb["rstc"][:, 2:3],
                                             AF.Ln), reads=G("rstc"), writes=G("rstc"))
        P.c("scalar", lambda e: e.activation(sb["rstc"][:, 2:3], sb["rstc"][:, 2:3],
                                             AF.Exp, scale=-0.5), reads=G("rstc"), writes=G("rstc"))
        P.c("vector", lambda e: e.tensor_mul(sb["rstc"][:, 3:4], sb["rstc"][:, 0:1],
                                             sb["rstc"][:, 2:3]),
            reads=G("rstc"), writes=G("rstc"))
        P.c("vector", lambda e: e.tensor_scalar(
            sb["resfs"][:, 0:2048], sb["res0s"][:, 0:2048], sb["rstc"][:, 2:3],
            sb["rstc"][:, 3:4], ALU.mult, ALU.subtract),
            reads=G("res0s", "rstc"), writes=G("resfs"))
        P.dma("scalar", "dRES", lambda e, s=s: e.dma_start(
            out=dr["resf_d"].ap()[t0 + s * 128:t0 + (s + 1) * 128, :], in_=sb["resfs"][:, 0:2048]),
            reads=G("resfs"), writes=("resf_d",))
        P.c("vector", lambda e: e.tensor_copy(sb["xt1b"][:, 0:512] if False else
                                              sb["h2w0"][:, 0:2048], sb["resfs"][:, 0:2048]),
            reads=G("resfs"), writes=G("h2w0"))
        trans_to_xtfm(cx, "h2w0", 0, s)

    for j in range(2):
        emit_ffn_iter(cx, tb, j)


def trans_to_xtfm(cx, srcname, src_off, s):
    P, sb, pb, G = cx.P, cx.sb, cx.pb, cx.G
    for grp in range(2):
        bank = "pt%d" % grp
        for f2 in range(8):
            f = grp * 8 + f2
            P.c("tensor", lambda e, f=f, f2=f2, bank=bank, srcname=srcname, src_off=src_off: e.transpose(
                pb[bank][:, f2 * 128:(f2 + 1) * 128],
                sb[srcname][:, src_off + f * 128:src_off + (f + 1) * 128], sb["ident_bf"][:]),
                reads=G(srcname) + ("ident_bf",), writes=(bank,))
        P.c("scalar", lambda e, grp=grp, s=s, bank=bank: e.activation(
            bass.AP(sb["xt_fm"], grp * 8 * 512 + s * 128, [[8192, 128], [512, 8], [1, 128]]),
            pb[bank][:].rearrange("p (f c) -> p f c", f=8), AF.Copy),
            reads=(bank,), writes=G("xt_fm"))


def emit_ffn_iter(cx, tb, j):
    P, sb, pb, din, dr, G = cx.P, cx.sb, cx.pb, cx.din, cx.dr, cx.G
    t0 = tb * TTB

    P.dma("sync", "dW0", lambda e, j=j: e.dma_start(
        out=sb["wgt"][:, 0:128].rearrange("p (k c) -> p k c", k=16),
        in_=dr["wg_bf"].ap()[:, j * 8:(j + 1) * 8].rearrange("(k p) c -> k p c", p=128)
        .transpose([1, 0, 2])),
        reads=("wg_bf",), writes=("wgt",))
    for s in range(4):
        for k in range(16):
            P.c("tensor", lambda e, s=s, k=k: e.matmul(
                pb["pf4"][:, 0:8], sb["xt_fm"][:, k * 512 + s * 128:k * 512 + (s + 1) * 128],
                sb["wgt"][:, k * 8:(k + 1) * 8], start=(k == 0), stop=False),
                reads=("wgt",) + G("xt_fm"), writes=("pf4",))
        P.c("tensor", lambda e, s=s, j=j: e.matmul(
            pb["pf4"][:, 0:8], sb["ones_row_bf"][:], sb["bg_bf"][0:1, j * 8:j * 8 + 8],
            start=False, stop=True),
            reads=("bg_bf", "ones_row_bf"), writes=("pf4",))
        P.c("vector", lambda e, s=s: e.tensor_copy(sb["gat"][:, s * 8:s * 8 + 8],
                                                   pb["pf4"][:, 0:8]),
            reads=("pf4",), writes=G("gat"))
    for s in range(4):
        g0 = s * 8
        gat = sb["gat"]
        P.c("vector", lambda e, g0=g0: e.tensor_reduce(
            sb["gat"][:, 32 + 0:32 + 1], gat[:, g0:g0 + 8], AX.X, ALU.min),
            reads=G("gat"), writes=G("gat"))
        P.c("scalar", lambda e, g0=g0: e.sign(sb["gat"][:, 40:48], gat[:, g0:g0 + 8]) if False else
            e.activation(sb["gat"][:, 40:48], gat[:, g0:g0 + 8], AF.Sign, bias=sb["gat"][:, 32:33]),
            reads=G("gat"), writes=G("gat")) if False else None
        P.c("vector", lambda e, g0=g0: e.tensor_scalar(
            sb["gat"][:, 40:48], gat[:, g0:g0 + 8], sb["gat"][:, 32:33], None,
            ALU.subtract), reads=G("gat"), writes=G("gat"))
        P.c("scalar", lambda e: e.activation(sb["gat"][:, 40:48], sb["gat"][:, 40:48], AF.Sign),
            reads=G("gat"), writes=G("gat"))
        P.c("vector", lambda e, g0=g0: e.tensor_mul(
            sb["gat"][:, 48:56], sb["gat"][:, 40:48], gat[:, g0:g0 + 8]),
            reads=G("gat"), writes=G("gat"))
        P.c("vector", lambda e: e.tensor_scalar(
            sb["gat"][:, 40:48], sb["gat"][:, 40:48], -1e30, 1e30, ALU.mult, ALU.add),
            reads=G("gat"), writes=G("gat"))
        P.c("vector", lambda e: e.tensor_add(
            sb["gat"][:, 48:56], sb["gat"][:, 48:56], sb["gat"][:, 40:48]),
            reads=G("gat"), writes=G("gat"))
        P.c("vector", lambda e: e.tensor_reduce(
            sb["gat"][:, 33:34], sb["gat"][:, 48:56], AX.X, ALU.min),
            reads=G("gat"), writes=G("gat"))
        P.c("vector", lambda e, g0=g0: e.tensor_scalar(
            sb["gat"][:, 40:48], gat[:, g0:g0 + 8], sb["gat"][:, 33:34], None,
            ALU.subtract), reads=G("gat"), writes=G("gat"))
        P.c("scalar", lambda e: e.activation(sb["gat"][:, 40:48], sb["gat"][:, 40:48], AF.Sign),
            reads=G("gat"), writes=G("gat"))
        P.c("vector", lambda e: e.tensor_scalar_max(sb["gat"][:, 40:48], sb["gat"][:, 40:48], 0.0),
            reads=G("gat"), writes=G("gat"))
        P.c("vector", lambda e, g0=g0: e.tensor_reduce(
            sb["gat"][:, 34:35], gat[:, g0:g0 + 8], AX.X, ALU.max),
            reads=G("gat"), writes=G("gat"))
        P.c("vector", lambda e, g0=g0: e.tensor_scalar(
            sb["gat"][:, 48:56], gat[:, g0:g0 + 8], sb["gat"][:, 34:35], None,
            ALU.subtract), reads=G("gat"), writes=G("gat"))
        P.c("scalar", lambda e: e.activation(sb["gat"][:, 48:56], sb["gat"][:, 48:56], AF.Exp),
            reads=G("gat"), writes=G("gat"))
        P.c("vector", lambda e: e.tensor_mul(sb["gat"][:, 48:56], sb["gat"][:, 48:56],
                                             sb["gat"][:, 40:48]),
            reads=G("gat"), writes=G("gat"))
        P.c("vector", lambda e: e.tensor_reduce(
            sb["gat"][:, 35:36], sb["gat"][:, 48:56], AX.X, ALU.add),
            reads=G("gat"), writes=G("gat"))
        P.c("vector", lambda e: e.reciprocal(sb["gat"][:, 36:37], sb["gat"][:, 35:36]),
            reads=G("gat"), writes=G("gat"))
        P.c("vector", lambda e, s=s: e.tensor_scalar(
            sb["wbc"][:, 0:8] if False else sb["gat"][:, 56 + 0:56 + 8],
            sb["gat"][:, 48:56], sb["gat"][:, 36:37], None, ALU.mult),
            reads=G("gat"), writes=G("gat"))
        P.c("vector", lambda e, s=s: e.tensor_copy(sb["wgt"][:, 0:8] if False else
                                                   sb["wbc"][:, 0:8], sb["gat"][:, 56:64]),
            reads=G("gat"), writes=G("wbc"))
        P.c("tensor", lambda e, s=s: e.transpose(
            pb["pt0"][:, 0:128], sb["wbc"][:, 0:8] if False else sb["wbc"][:, 0:8],
            sb["ident_bf"][:]),
            reads=G("wbc") + ("ident_bf",), writes=("pt0",)) if False else None
        P.c("tensor", lambda e, s=s: e.transpose(
            pb["pt0"][0:8, 0:128], sb["wbc"][:, 0:8], sb["ident_bf"][:]),
            reads=G("wbc") + ("ident_bf",), writes=("pt0",))
        P.c("scalar", lambda e, s=s: e.activation(
            sb["wfm"][:, s * 128:(s + 1) * 128], pb["pt0"][0:8, 0:128], AF.Copy),
            reads=("pt0",), writes=G("wfm"))

    for n in range(6):
        for g4 in range(2):
            for g2 in range(2):
                g = n * 4 + g4 * 2 + g2
                slot = g % 2
                buf = "wt%d" % slot
                P.dma("sync", "dW%d" % slot, lambda e, g=g, buf=buf: e.dma_start(
                    out=sb[buf][:, 0:4096], in_=dr["wup_bf"].ap()[g]),
                    reads=("wup_bf",), writes=G(buf))
                for e2 in range(2):
                    ec = (g4 * 2 + g2) * 2 + e2
                    bank = "pf%d" % (ec % 4)
                    for k in range(16):
                        P.c("tensor", lambda e, e2=e2, k=k, buf=buf, bank=bank: e.matmul(
                            pb[bank][:, :], sb[buf][:, k * 256 + e2 * 128:k * 256 + e2 * 128 + 128],
                            sb["xt_fm"][:, k * 512:(k + 1) * 512], start=(k == 0), stop=(k == 15)),
                            reads=G(buf, "xt_fm"), writes=(bank,))
                    P.c("vector", lambda e, n=n, ec=ec, bank=bank: e.tensor_scalar_add(
                        sb["h_fm"][:, ec * 512:(ec + 1) * 512], pb[bank][:, :],
                        sb["bup_cols"][:, n * 8 + ec:n * 8 + ec + 1]),
                        reads=(bank, "bup_cols"), writes=G("h_fm"))
        for g4 in range(4):
            g = n * 4 + g4
            slot = g % 2
            buf = "wt%d" % slot
            P.dma("sync", "dW%d" % slot, lambda e, g=g, buf=buf: e.dma_start(
                out=sb[buf][:, 0:2048], in_=dr["wsw_bf"].ap()[g]),
                reads=("wsw_bf",), writes=G(buf))
            for e2 in range(2):
                ec = g4 * 2 + e2
                bank = "pf%d" % (ec % 4)
                for k in range(8):
                    P.c("tensor", lambda e, e2=e2, k=k, buf=buf, bank=bank: e.matmul(
                        pb[bank][:, :], sb[buf][:, k * 256 + e2 * 128:k * 256 + e2 * 128 + 128],
                        sb["h_fm"][:, k * 512:(k + 1) * 512], start=(k == 0), stop=(k == 7)),
                        reads=G(buf, "h_fm"), writes=(bank,))
                P.c("scalar", lambda e, n=n, ec=ec, bank=bank: e.activation(
                    sb["sil"][:, ec * 512:(ec + 1) * 512], pb[bank][:, :], AF.Silu,
                    bias=sb["bsw_cols"][:, n * 8 + ec:n * 8 + ec + 1]),
                    reads=(bank, "bsw_cols"), writes=G("sil"))
        P.c("tensor", lambda e, n=n: e.matmul(
            pb["pf4"][:, :], sb["selbf_t"][:, n * 128:(n + 1) * 128],
            sb["wfm"][:, 0:512], start=True, stop=True),
            reads=G("selbf_t", "wfm"), writes=("pf4",))
        P.c("scalar", lambda e: e.activation(sb["wbc"][:, 0:512], pb["pf4"][:, :], AF.Copy),
            reads=("pf4",), writes=G("wbc"))
        P.c("vector", lambda e: e.tensor_mul(sb["sil"][:, 0:4096], sb["sil"][:, 0:4096],
                                             sb["h_fm"][:, 0:4096]),
            reads=G("sil", "h_fm"), writes=G("sil"))
        P.c("vector", lambda e, n=n: e.tensor_tensor(
            sb["h2w%d" % n][:].rearrange("p (g c) -> p g c", g=8),
            sb["sil"][:, 0:4096].rearrange("p (g c) -> p g c", g=8),
            sb["wbc"][:, 0:512].unsqueeze(1).broadcast_to([128, 8, 512]), ALU.mult),
            reads=G("sil", "wbc"), writes=G("h2w%d" % n))

    for g in range(8):
        slot = g % 2
        buf = "wt%d" % slot
        P.dma("sync", "dW%d" % slot, lambda e, g=g, buf=buf: e.dma_start(
            out=sb[buf][:, 0:4096], in_=dr["wsu_bf"].ap()[g]),
            reads=("wsu_bf",), writes=G(buf))
        for e2 in range(2):
            sc = g * 2 + e2
            bank = "pf%d" % (sc % 4)
            for k in range(16):
                P.c("tensor", lambda e, e2=e2, k=k, buf=buf, bank=bank: e.matmul(
                    pb[bank][:, :], sb[buf][:, k * 256 + e2 * 128:k * 256 + e2 * 128 + 128],
                    sb["xt_fm"][:, k * 512:(k + 1) * 512], start=(k == 0), stop=(k == 15)),
                    reads=G(buf, "xt_fm"), writes=(bank,))
            P.c("vector", lambda e, sc=sc, bank=bank: e.tensor_scalar_add(
                sb["sh_fm"][:, sc * 512:(sc + 1) * 512], pb[bank][:, :],
                sb["bsu_cols"][:, sc:sc + 1]),
                reads=(bank, "bsu_cols"), writes=G("sh_fm"))
    for g in range(4):
        slot = g % 2
        buf = "wt%d" % slot
        P.dma("sync", "dW%d" % slot, lambda e, g=g, buf=buf: e.dma_start(
            out=sb[buf][:, 0:2048], in_=dr["wsa_bf"].ap()[g]),
            reads=("wsa_bf",), writes=G(buf))
        for sl in range(2):
            for e2 in range(2):
                ec = g * 2 + e2
                bank = "pf%d" % ((sl * 2 + e2) % 4)
                for k in range(8):
                    P.c("tensor", lambda e, sl=sl, e2=e2, k=k, buf=buf, bank=bank: e.matmul(
                        pb[bank][:, :], sb[buf][:, k * 256 + e2 * 128:k * 256 + e2 * 128 + 128],
                        sb["sh_fm"][:, (sl * 8 + k) * 512:(sl * 8 + k + 1) * 512],
                        start=(k == 0), stop=(k == 7)),
                        reads=G(buf, "sh_fm"), writes=(bank,))
                P.c("scalar", lambda e, sl=sl, ec=ec, bank=bank: e.activation(
                    sb["sh2"][:, (sl * 8 + ec) * 512:(sl * 8 + ec + 1) * 512],
                    pb[bank][:, :], AF.Silu, bias=sb["bsa_cols"][:, ec:ec + 1]),
                    reads=(bank, "bsa_cols"), writes=G("sh2"))
    P.c("vector", lambda e: e.tensor_mul(sb["sh2"][:, 0:8192], sb["sh2"][:, 0:8192],
                                         sb["sh_fm"][:, 0:8192]),
        reads=G("sh2", "sh_fm"), writes=G("sh2"))


    last = (j == 1)
    for hg in range(4):
        banks = ["pf%d" % i for i in range(4)]
        first = [True] * 4
        for n in range(6):
            slot = n % 2
            buf = "wt%d" % slot
            P.dma("sync", "dW%d" % slot, lambda e, hg=hg, n=n, buf=buf: e.dma_start(
                out=sb[buf][:, 0:4096].rearrange("p (r c) -> p r c", r=8),
                in_=dr["wdn_bf"].ap()[hg, n * 8:(n + 1) * 8].transpose([1, 0, 2])),
                reads=("wdn_bf",), writes=G(buf))
            for s in range(4):
                for r2 in range(8):
                    P.c("tensor", lambda e, n=n, s=s, r2=r2, buf=buf, f=first[s], bank=banks[s]: e.matmul(
                        pb[bank][:, :],
                        sb["h2w%d" % n][:, r2 * 512 + s * 128:r2 * 512 + (s + 1) * 128],
                        sb[buf][:, r2 * 512:(r2 + 1) * 512],
                        start=f and r2 == 0, stop=False),
                        reads=G("h2w%d" % n, buf), writes=(banks[s],))
                first[s] = False
        for half in range(2):
            slot = half % 2
            buf = "wt%d" % slot
            P.dma("sync", "dW%d" % slot, lambda e, hg=hg, half=half, buf=buf: e.dma_start(
                out=sb[buf][:, 0:4096].rearrange("p (r c) -> p r c", r=8),
                in_=dr["wsd_bf"].ap()[hg, half * 8:(half + 1) * 8].transpose([1, 0, 2])),
                reads=("wsd_bf",), writes=G(buf))
            for s in range(4):
                for r2 in range(8):
                    rc = half * 8 + r2
                    P.c("tensor", lambda e, rc=rc, r2=r2, s=s, buf=buf, bank=banks[s]: e.matmul(
                        pb[bank][:, :],
                        sb["sh2"][:, rc * 512 + s * 128:rc * 512 + (s + 1) * 128],
                        sb[buf][:, r2 * 512:(r2 + 1) * 512],
                        start=False, stop=False),
                        reads=G("sh2", buf), writes=(banks[s],))
        for s in range(4):
            P.c("tensor", lambda e, s=s, hg=hg, bank=banks[s]: e.matmul(
                pb[bank][:, :], sb["wfm"][0:6, s * 128:(s + 1) * 128],
                sb["bdn_bf"][:, hg * 512:(hg + 1) * 512],
                start=False, stop=False),
                reads=G("wfm", "bdn_bf"), writes=(banks[s],))
            P.c("tensor", lambda e, s=s, hg=hg, bank=banks[s]: e.matmul(
                pb[bank][:, :], sb["ones_row_bf"][:],
                sb["bsd_bf"][0:1, hg * 512:(hg + 1) * 512],
                start=False, stop=True),
                reads=("ones_row_bf",) + G("bsd_bf"), writes=(banks[s],))
            P.dma("sync", "dRES", lambda e, s=s, hg=hg: e.dma_start(
                out=sb["resfs"][:, 0:512],
                in_=dr["resf_d"].ap()[t0 + s * 128:t0 + (s + 1) * 128, hg * 512:(hg + 1) * 512]),
                reads=("resf_d",) + G("resfs"), writes=G("resfs"))
            if not last:
                P.c("vector", lambda e, s=s, bank=banks[s]: e.tensor_add(
                    sb["xt1b"][:, 0:512], pb[bank][:, :], sb["resfs"][:, 0:512]),
                    reads=(banks[s],) + G("resfs"), writes=G("xt1b"))
                bankt = "pt0"
                for f2 in range(4):
                    P.c("tensor", lambda e, f2=f2, bankt=bankt: e.transpose(
                        pb[bankt][:, f2 * 128:(f2 + 1) * 128],
                        sb["xt1b"][:, f2 * 128:(f2 + 1) * 128], sb["ident_bf"][:]),
                        reads=G("xt1b") + ("ident_bf",), writes=(bankt,))
                P.c("scalar", lambda e, s=s, hg=hg, bankt=bankt: e.activation(
                    bass.AP(sb["xt_fm"], hg * 4 * 512 + s * 128, [[8192, 128], [512, 4], [1, 128]]),
                    pb[bankt][:, 0:512].rearrange("p (f c) -> p f c", f=4), AF.Copy),
                    reads=(bankt,), writes=G("xt_fm"))
            else:
                P.dma("sync", "dRES", lambda e, s=s, hg=hg: e.dma_start(
                    out=sb["res0s"][:, 0:512],
                    in_=dr["res0_d"].ap()[t0 + s * 128:t0 + (s + 1) * 128, hg * 512:(hg + 1) * 512]),
                    reads=("res0_d",) + G("res0s"), writes=G("res0s"))
                P.c("vector", lambda e, s=s, bank=banks[s]: e.tensor_add(
                    sb["resfs"][:, 512:1024], pb[bank][:, :], sb["resfs"][:, 0:512]),
                    reads=(banks[s],) + G("resfs"), writes=G("resfs"))
                P.c("vector", lambda e, s=s: e.tensor_add(
                    sb["resfs"][:, 512:1024], sb["resfs"][:, 512:1024], sb["res0s"][:, 0:512]),
                    reads=G("resfs", "res0s"), writes=G("resfs"))
                P.dma("sync", "dOUT", lambda e, s=s, hg=hg: e.dma_start(
                    out=cx.y.ap()[t0 + s * 128:t0 + (s + 1) * 128, hg * 512:(hg + 1) * 512],
                    in_=sb["resfs"][:, 512:1024]),
                    reads=G("resfs"), writes=("y",))


def kernel(**inputs):
    nc = build()
    in_maps = [_stage_core(inputs, c) for c in range(8)]
    res = run_bass_kernel_spmd(nc, in_maps, core_ids=list(range(8)))
    out = np.zeros((B, T, D), np.float32)
    for c in range(8):
        b, h = c // 2, c % 2
        yv = np.asarray(res.results[c]["y"], np.float32)
        for tb in range(4):
            out[b, (2 * tb + h) * 512:(2 * tb + h + 1) * 512] = yv[tb * 512:(tb + 1) * 512]
    return out


# version 11
# speedup vs baseline: 1.0127x; 1.0127x over previous
import numpy as np
import ml_dtypes
import concourse.bass as bass
import concourse.mybir as mybir
from concourse.bass_utils import run_bass_kernel_spmd

F32 = mybir.dt.float32
BF16 = mybir.dt.bfloat16
I32 = mybir.dt.int32
AF = mybir.ActivationFunctionType
ALU = mybir.AluOpType
AX = mybir.AxisListType
BFNP = ml_dtypes.bfloat16

B, T, D = 4, 4096, 2048
HV, HK, DK, DV, KCONV = 32, 16, 128, 128, 4
CH = 128
TT = 512
NTILES = T // TT
CPT = TT // CH
NVH, NKH = 16, 8
CONVC = 4096
VCC = 2048
TB = 2048
TTB = 512
NBT = TB // TTB
LN_EPS, L2_EPS, RMS_EPS = 1e-5, 1e-6, 1e-6

ENGS = ("tensor", "vector", "scalar", "gpsimd", "sync")
GRAN = 2048


class Prog:
    def __init__(self):
        self.steps = []
        self.last_write = {}
        self.last_reads = {}
        self.dma_issued = {}
        self.eng_steps = {e: [] for e in ENGS}

    def _dep(self, step, prod):
        if prod is None or prod is step:
            return
        if (prod["eng"] == "tensor" and step["eng"] == "tensor"
                and prod["kind"] == "c" and step["kind"] == "c"):
            return
        if prod not in step["dep_steps"]:
            step["dep_steps"].append(prod)
            if prod["kind"] == "d":
                step["dma_req"][prod["sem"]] = max(
                    step["dma_req"].get(prod["sem"], 0),
                    16 * self.dma_issued[prod["sem"]])

    def _add(self, eng, kind, emit, reads, writes, sem=None):
        step = dict(eng=eng, kind=kind, emit=emit, sem=sem,
                    dep_steps=[], dma_req={}, marked=False, count=None)
        for r in reads:
            self._dep(step, self.last_write.get(r))
        for w in writes:
            self._dep(step, self.last_write.get(w))
            for rd in self.last_reads.get(w, ()):
                self._dep(step, rd)
        if kind == "d":
            self.dma_issued[sem] = self.dma_issued.get(sem, 0) + 1
        for w in writes:
            self.last_write[w] = step
            self.last_reads[w] = []
        for r in reads:
            self.last_reads.setdefault(r, []).append(step)
        self.steps.append(step)
        self.eng_steps[eng].append(step)
        return step

    def c(self, eng, emit, reads=(), writes=()):
        return self._add(eng, "c", emit, reads, writes)

    def dma(self, eng, sem, emit, reads=(), writes=()):
        return self._add(eng, "d", emit, reads, writes, sem=sem)

    def coll(self, emit, reads=(), writes=()):
        return self._add("gpsimd", "cc", emit, reads, writes, sem="_cc")

    def finalize(self, block, sems, eng_sems, final_waits):
        for st in self.steps:
            for p in st["dep_steps"]:
                p["marked"] = True
        for e in ENGS:
            cnt = 0
            for st in self.eng_steps[e]:
                if st["kind"] == "c" and st["marked"]:
                    cnt += 1
                    st["count"] = cnt
        totals = {}
        for st in self.steps:
            if st["kind"] == "d":
                totals[st["sem"]] = totals.get(st["sem"], 0) + 16
        FULL = ("dAUX", "gAUX")
        dcnt = {}
        for st in self.steps:
            if st["kind"] == "d":
                dcnt[st["sem"]] = dcnt.get(st["sem"], 0) + 16
                st["count"] = dcnt[st["sem"]]
                st["marked"] = True
            elif st["kind"] == "cc":
                dcnt[st["sem"]] = dcnt.get(st["sem"], 0) + 1
                st["count"] = dcnt[st["sem"]]
                st["marked"] = True

        def emit_engine(name):
            def run(eng):
                seen = {}
                for st in self.eng_steps[name]:
                    req = {}
                    for p in st["dep_steps"]:
                        if p["kind"] == "c":
                            s, v = eng_sems[p["eng"]], p["count"]
                        elif p["kind"] == "cc":
                            s, v = sems[p["sem"]], p["count"]
                        else:
                            continue
                        if req.get(id(s), (None, -1))[1] < v:
                            req[id(s)] = (s, v)
                    for sname, v in st["dma_req"].items():
                        s = sems[sname]
                        if sname in FULL and not (st["kind"] == "d" and st["sem"] == sname):
                            v = totals[sname]
                        if req.get(id(s), (None, -1))[1] < v:
                            req[id(s)] = (s, v)
                    for s, v in req.values():
                        if seen.get(id(s), -1) < v:
                            eng.wait_ge(s, v)
                            seen[id(s)] = v
                    inst = st["emit"](eng)
                    if st["marked"]:
                        if st["kind"] == "c":
                            inst.then_inc(eng_sems[name], 1)
                        elif st["kind"] == "d":
                            inst.then_inc(sems[st["sem"]], 16)
                        else:
                            inst.then_inc(sems[st["sem"]], 1)
                if name == "sync":
                    for sname, v in final_waits:
                        if v > 0:
                            eng.wait_ge(sems[sname], v)
            return run

        block.sync(emit_engine("sync"))
        block.tensor(emit_engine("tensor"))
        block.vector(emit_engine("vector"))
        block.scalar(emit_engine("scalar"))
        block.gpsimd(emit_engine("gpsimd"))


def _stage_core(inp, c):
    f32 = np.float32
    b, h = c // 2, c % 2
    vh0, kh0 = NVH * h, NKH * h
    W = np.asarray(inp["L0_attn_W_qkv"], f32)
    wqkv = np.ascontiguousarray(np.concatenate(
        [W[:, kh0 * DK:(kh0 + NKH) * DK],
         W[:, 2048 + kh0 * DK:2048 + (kh0 + NKH) * DK],
         W[:, 4096 + vh0 * DV:4096 + (vh0 + NVH) * DV]], axis=1))
    cw = np.asarray(inp["L0_attn_conv_w"], f32)
    cw_sl = np.concatenate(
        [cw[kh0 * DK:(kh0 + NKH) * DK], cw[2048 + kh0 * DK:2048 + (kh0 + NKH) * DK],
         cw[4096 + vh0 * DV:4096 + (vh0 + NVH) * DV]], axis=0)
    convw = np.ascontiguousarray(cw_sl.reshape(32, 128, 4).transpose(1, 0, 2).reshape(128, 128))
    wout = np.ascontiguousarray(np.asarray(inp["L0_attn_W_out"], f32))

    def cols128(v):
        n = v.shape[0]
        return np.ascontiguousarray(v.reshape(n, 8, 128).transpose(2, 0, 1).reshape(128, n * 8))

    x = np.asarray(inp["L0_attn_x"], f32)
    gth = (2 * np.arange(4096, dtype=np.int64) + h).astype(np.int32)
    gth = np.ascontiguousarray(gth.reshape(32, 128).T)
    sel = np.zeros((8, 768), np.float32)
    for n in range(6):
        sel[n, n * 128:(n + 1) * 128] = 1.0

    return dict(
        x=np.ascontiguousarray(x[b]),
        xh=np.ascontiguousarray(np.concatenate(
            [x[b, (2 * tb + h) * 512:(2 * tb + h + 1) * 512] for tb in range(4)])),
        wqkv=wqkv, convw=convw,
        wz=np.ascontiguousarray(np.asarray(inp["L0_attn_W_z"], f32)[:, vh0 * DV:(vh0 + NVH) * DV]),
        wba=np.ascontiguousarray(np.concatenate(
            [np.asarray(inp["L0_attn_W_b"], f32)[:, vh0:vh0 + NVH],
             np.asarray(inp["L0_attn_W_a"], f32)[:, vh0:vh0 + NVH]], axis=1)),
        dtb_row=np.ascontiguousarray(np.asarray(inp["L0_attn_dt_bias"], f32)[vh0:vh0 + NVH].reshape(1, NVH)),
        alog_row=np.ascontiguousarray(np.asarray(inp["L0_attn_A_log"], f32)[vh0:vh0 + NVH].reshape(1, NVH)),
        normw_col=np.asarray(inp["L0_attn_norm_w"], f32).reshape(128, 1),
        wout=wout,
        wg=np.ascontiguousarray(np.asarray(inp["L0_ffn_Wg"], f32).transpose(1, 0, 2).reshape(2048, 16)),
        bg_rows=np.ascontiguousarray(np.asarray(inp["L0_ffn_bg"], f32).reshape(1, 16)),
        wup=np.ascontiguousarray(np.asarray(inp["L0_ffn_Wup"], f32).transpose(1, 0, 2).reshape(2048, 6144)),
        bup_cols=cols128(np.asarray(inp["L0_ffn_bup"], f32)),
        wsw=np.ascontiguousarray(np.asarray(inp["L0_ffn_Wsw"], f32).transpose(1, 0, 2).reshape(1024, 6144)),
        bsw_cols=cols128(np.asarray(inp["L0_ffn_bsw"], f32)),
        wdn=np.ascontiguousarray(np.asarray(inp["L0_ffn_Wdn"], f32).reshape(6144, 2048)),
        bdn_rows=np.ascontiguousarray(np.asarray(inp["L0_ffn_bdn"], f32)),
        wsu=np.asarray(inp["L0_ffn_Wsu"], f32),
        bsu_cols=np.ascontiguousarray(np.asarray(inp["L0_ffn_bsu"], f32).reshape(16, 128).T),
        wsa=np.asarray(inp["L0_ffn_Wsa"], f32),
        bsa_cols=np.ascontiguousarray(np.asarray(inp["L0_ffn_bsa"], f32).reshape(8, 128).T),
        wsd=np.asarray(inp["L0_ffn_Wsd"], f32),
        bsd_row=np.ascontiguousarray(np.asarray(inp["L0_ffn_bsd"], f32).reshape(1, 2048)),
        gther=gth,
        ident_bf=np.eye(128, dtype=np.float32).astype(BFNP),
        identf=np.eye(128, dtype=np.float32),
        mski=np.ascontiguousarray(np.triu(np.ones((128, 128), np.float32))),
        msks=np.ascontiguousarray(np.triu(np.ones((128, 128), np.float32), 1)),
        ones_row=np.ones((1, 128), np.float32),
        ones_row_bf=np.ones((1, 128), np.float32).astype(BFNP),
        ones_col=np.ones((128, 1), np.float32),
        ones_col_bf=np.ones((128, 1), np.float32).astype(BFNP),
        selbf=sel.astype(BFNP),
    )


INPUT_DECL = dict(
    x=([T, D], F32), xh=([TB, D], F32),
    wqkv=([D, CONVC], F32), convw=([128, 128], F32), wz=([D, VCC], F32),
    wba=([D, 32], F32), dtb_row=([1, NVH], F32), alog_row=([1, NVH], F32),
    normw_col=([128, 1], F32), wout=([4096, 2048], F32),
    wg=([2048, 16], F32), bg_rows=([1, 16], F32),
    wup=([2048, 6144], F32), bup_cols=([128, 48], F32),
    wsw=([1024, 6144], F32), bsw_cols=([128, 48], F32),
    wdn=([6144, 2048], F32), bdn_rows=([6, 2048], F32),
    wsu=([2048, 2048], F32), bsu_cols=([128, 16], F32),
    wsa=([1024, 1024], F32), bsa_cols=([128, 8], F32),
    wsd=([2048, 2048], F32), bsd_row=([1, 2048], F32),
    gther=([128, 32], I32),
    ident_bf=([128, 128], BF16), identf=([128, 128], F32),
    mski=([128, 128], F32), msks=([128, 128], F32),
    ones_row=([1, 128], F32), ones_row_bf=([1, 128], BF16),
    ones_col=([128, 1], F32), ones_col_bf=([128, 1], BF16),
    selbf=([8, 768], BF16),
)


ARENA_BASE = 22528
ARENA_TOP = 222000


def _layout():
    L = {}
    off = [ARENA_BASE]

    def al(name, nbytes, at=None):
        a = off[0] if at is None else at
        if at is None:
            off[0] = a + nbytes
        L[name] = (a, nbytes)

    al("xn_fm", 16384)
    al("qkvp", 32768)
    al("sz", 16384)
    al("wt0", 8192)
    al("wt1", 8192)
    al("xtm", 8192)
    al("stg", 6144)
    al("cvt", 2048)
    al("strow", 6144)
    al("q_tm", 2048); al("k_tm", 2048); al("v_tm", 4096)
    al("qs_tm", 2048); al("kn_tm", 2048)
    al("vb", 4096); al("kbg", 4096); al("kdec", 4096)
    al("qgi_fm", 4096)
    al("C", 4096); al("A", 4096); al("C2", 4096); al("A2", 4096)
    al("P", 4096); al("P2", 4096)
    al("vnew", 4096); al("y_tm", 4096); al("nrm_fm", 4096)
    al("mdiff", 2048)
    al("egl", 512); al("gch", 2048); al("gT", 2048); al("ba_all", 512)
    al("lncol", 128); al("rstc", 128)
    al("r_bc", 1024); al("mr_bc", 1024); al("halo", 256); al("S_bf", 4096)
    al("bdn_bf", 4096); al("bsd_bf", 4096); al("selbf_t", 2048)
    al("wgt", 512)
    al("m1", 4096, at=L["xtm"][0])
    al("m0", 4096, at=L["xtm"][0] + 4096)
    al("qs_fm", 2048, at=L["q_tm"][0])
    al("kn_fm", 2048, at=L["k_tm"][0])
    al("qgi_tm", 4096, at=L["v_tm"][0])
    al("vc", 4096, at=L["C2"][0])
    al("kcdT", 4096, at=L["A2"][0])
    al("ysq", 4096, at=L["vnew"][0])
    al("Gm", 2048, at=L["vnew"][0])
    al("qkTm", 2048, at=L["vnew"][0] + 2048)
    al("nrmB", 32768, at=L["qkvp"][0])
    for n in range(4):
        al("h2w%d" % n, 8192, at=L["qkvp"][0] + n * 8192)
    al("h2w4", 8192, at=L["sz"][0])
    al("h2w5", 8192, at=L["sz"][0] + 8192)
    al("xt_fm", 16384, at=L["xn_fm"][0])
    al("res0s", 8192, at=L["xtm"][0])
    al("resfs", 8192, at=L["stg"][0])
    al("h_fm", 8192, at=L["cvt"][0])
    al("sil", 8192, at=L["q_tm"][0])
    al("sh_fm", 16384, at=L["C"][0])
    al("sh2", 16384, at=L["P"][0])
    al("xt1b", 1024, at=L["nrm_fm"][0])
    al("wbc", 1024, at=L["r_bc"][0])
    al("gat", 2048, at=L["mdiff"][0])
    al("wfm", 1024, at=L["egl"][0])
    assert off[0] <= ARENA_TOP, off[0]
    return L


LAY = _layout()
assert LAY["strow"][0] == LAY["cvt"][0] + 2048
assert LAY["k_tm"][0] == LAY["q_tm"][0] + 2048 and LAY["v_tm"][0] == LAY["k_tm"][0] + 2048
assert LAY["gch"][0] == LAY["egl"][0] + 512


def granules(*names):
    out = set()
    for n in names:
        a, sz = LAY[n]
        for g in range(a // GRAN, (a + sz + GRAN - 1) // GRAN):
            out.add("g%d" % g)
    return tuple(out)


ARENA_DECL = dict(
    xn_fm=([128, 8192], BF16), qkvp=([128, 16384], BF16), sz=([128, 8192], BF16),
    wt0=([128, 4096], BF16), wt1=([128, 4096], BF16),
    xtm=([128, 4096], BF16), stg=([2, 1536], F32), cvt=([128, 512], F32),
    strow=([1, 1536], F32),
    q_tm=([128, 1024], BF16), k_tm=([128, 1024], BF16), v_tm=([128, 2048], BF16),
    qs_tm=([128, 1024], BF16), kn_tm=([128, 1024], BF16),
    vb=([128, 2048], BF16), kbg=([128, 2048], BF16), kdec=([128, 2048], BF16),
    qgi_fm=([128, 2048], BF16),
    Gm=([128, 1024], BF16), qkTm=([128, 1024], BF16),
    C=([128, 2048], BF16), A=([128, 2048], BF16),
    C2=([128, 2048], BF16), A2=([128, 2048], BF16),
    P=([128, 2048], BF16), P2=([128, 2048], BF16),
    vnew=([128, 2048], BF16), y_tm=([128, 2048], BF16), nrm_fm=([128, 2048], BF16),
    mdiff=([128, 512], F32),
    egl=([128, 64], F32), gch=([128, 448], F32), gT=([16, 512], F32),
    ba_all=([128, 128], F32), lncol=([128, 16], F32), rstc=([128, 16], F32),
    r_bc=([128, 512], BF16), mr_bc=([128, 512], BF16),
    halo=([128, 96], BF16), S_bf=([128, 2048], BF16),
    bdn_bf=([6, 2048], BF16), bsd_bf=([1, 2048], BF16), selbf_t=([8, 768], BF16),
    wgt=([128, 128], BF16),
    m1=([128, 2048], BF16), m0=([128, 2048], BF16),
    qs_fm=([128, 1024], BF16), kn_fm=([128, 1024], BF16), qgi_tm=([128, 2048], BF16),
    vc=([128, 2048], BF16), kcdT=([128, 2048], BF16), ysq=([128, 2048], BF16),
    nrmB=([128, 16384], BF16),
    h2w0=([128, 4096], BF16), h2w1=([128, 4096], BF16), h2w2=([128, 4096], BF16),
    h2w3=([128, 4096], BF16), h2w4=([128, 4096], BF16), h2w5=([128, 4096], BF16),
    xt_fm=([128, 8192], BF16), res0s=([128, 2048], F32), resfs=([128, 2048], F32),
    h_fm=([128, 4096], BF16), sil=([128, 4096], BF16),
    sh_fm=([128, 8192], BF16), sh2=([128, 8192], BF16),
    xt1b=([128, 512], BF16), wbc=([128, 512], BF16),
    gat=([128, 512], F32), wfm=([8, 512], BF16),
)


class Ctx:
    pass


def build(phase_a_only=False, debug_ccin=False, ntiles=NTILES):
    nc = bass.Bass("TRN2", target_bir_lowering=False)
    din = {n: nc.dram_tensor(n, list(s), d, kind="ExternalInput")
           for n, (s, d) in INPUT_DECL.items()}
    y = nc.dram_tensor("y", [TB, D], F32, kind="ExternalOutput")
    dr = {}
    dr["wqkv_bf"] = nc.dram_tensor("wqkv_bf", [16, 128, 16, 256], BF16)
    dr["wz_bf"] = nc.dram_tensor("wz_bf", [8, 128, 16, 256], BF16)
    dr["wba_bf"] = nc.dram_tensor("wba_bf", [2048, 32], BF16)
    dr["wg_bf"] = nc.dram_tensor("wg_bf", [2048, 16], BF16)
    dr["wup_bf"] = nc.dram_tensor("wup_bf", [24, 128, 16, 256], BF16)
    dr["wsw_bf"] = nc.dram_tensor("wsw_bf", [24, 128, 8, 256], BF16)
    dr["wsu_bf"] = nc.dram_tensor("wsu_bf", [8, 128, 16, 256], BF16)
    dr["wsa_bf"] = nc.dram_tensor("wsa_bf", [4, 128, 8, 256], BF16)
    dr["wdn_bf"] = nc.dram_tensor("wdn_bf", [4, 48, 128, 512], BF16)
    dr["wsd_bf"] = nc.dram_tensor("wsd_bf", [4, 16, 128, 512], BF16)
    dr["wout_bf"] = nc.dram_tensor("wout_bf", [4, 32, 128, 512], BF16)
    dr["cc_in"] = nc.dram_tensor("cc_in", [4, VCC, 1024], BF16,
                                 kind="ExternalOutput" if debug_ccin else "Internal")
    for q in range(4):
        dr["cc_out%d" % q] = nc.dram_tensor("cc_out%d" % q, [2 * VCC, 1024], BF16)
    dr["res0_d"] = nc.dram_tensor("res0_d", [TB, D], F32)
    dr["resf_d"] = nc.dram_tensor("resf_d", [TB, D], F32)

    from contextlib import ExitStack
    with ExitStack() as stk:
        E = stk.enter_context
        sb = {}
        for n, shape, dt in (
                ("ident_bf", [128, 128], BF16), ("identf", [128, 128], F32),
                ("mski", [128, 128], F32), ("msks", [128, 128], F32),
                ("convw", [128, 128], F32),
                ("ones_row", [1, 128], F32), ("ones_row_bf", [1, 128], BF16),
                ("ones_col", [128, 1], F32), ("ones_col_bf", [128, 1], BF16),
                ("normw_col", [128, 1], F32),
                ("bup_cols", [128, 48], F32), ("bsw_cols", [128, 48], F32),
                ("bsu_cols", [128, 16], F32), ("bsa_cols", [128, 8], F32),
                ("bg_rows", [1, 16], F32), ("gth_idx", [128, 32], I32),
                ("dtb_bc", [128, 16], F32), ("acoef_bc", [128, 16], F32),
                ("rowtmp", [1, 16], F32), ("rowtmp2", [1, 16], F32),
                ("halo2", [128, 96], BF16),
                ("wba_t", [128, 512], BF16), ("bg_bf", [1, 16], BF16)):
            sb[n] = nc.alloc_sbuf_tensor("c_" + n, shape, dt, side="right")
        for n, (shape, dt) in ARENA_DECL.items():
            a, sz_ = LAY[n]
            nb = shape[1] * (2 if dt in (BF16,) else 4)
            assert nb <= sz_, (n, nb, sz_)
            sb[n] = nc.alloc_sbuf_tensor_at(n, shape, dt, offset=a)

        pb = {}
        for i in range(6):
            pb["pf%d" % i] = E(nc.psum_tensor("pf%d" % i, [128, 512], F32))
        pb["pt0"] = E(nc.psum_tensor("pt0", [128, 1024], BF16))
        pb["pt1"] = E(nc.psum_tensor("pt1", [128, 1024], BF16))

        sems = {}
        for s in ("dW0", "dW1", "dX", "dST", "dAUX", "dRS", "dGT", "_cc",
                  "dOUT", "dRES", "gAUX", "gW0", "gW1", "gX", "dNF", "gCAST"):
            sems[s] = E(nc.semaphore(s))
        eng_sems = {e: E(nc.semaphore("se_" + e)) for e in ENGS}

        block = E(nc.Block())
        P = Prog()
        cx = Ctx()
        cx.nc, cx.P, cx.sb, cx.pb, cx.din, cx.dr, cx.y = nc, P, sb, pb, din, dr, y
        cx.G = granules

        def make_coll(q):
            return lambda: P.coll(lambda e, q=q: e.collective_compute(
                "AllGather", ALU.bypass,
                replica_groups=[[0, 1], [2, 3], [4, 5], [6, 7]],
                ins=[dr["cc_in"].ap()[q]], outs=[dr["cc_out%d" % q].ap()]),
                reads=("cc_in_q%d" % q,), writes=("cc_out%d" % q,))

        cx.coll_at_tile = {} if phase_a_only else {2 * q + 2: make_coll(q)
                                                  for q in range(3)}
        cx.coll_b0 = make_coll(3)

        emit_prep(cx)
        for t in range(ntiles):
            emit_phaseA_tile(cx, t)
        if not phase_a_only:
            for tb in range(NBT):
                emit_phaseB_tile(cx, tb)
        final = []
        if phase_a_only or debug_ccin:
            final.append(("dST", 16 * P.dma_issued.get("dST", 0)))
            final.append(("_cc", P.dma_issued.get("_cc", 0)))
        if not phase_a_only:
            final.append(("dOUT", 16 * P.dma_issued.get("dOUT", 0)))
        P.finalize(block, sems, eng_sems, final)
    return nc


def emit_prep(cx):
    P, sb, pb, din, dr, G = cx.P, cx.sb, cx.pb, cx.din, cx.dr, cx.G

    for n in ("identf", "mski", "msks", "convw", "ones_row", "ones_col",
              "normw_col", "bup_cols", "bsw_cols", "bsu_cols", "bsa_cols",
              "bg_rows"):
        P.dma("sync", "dAUX", lambda e, n=n: e.dma_start(out=sb[n][:], in_=din[n].ap()),
              reads=(), writes=(n,))
    for n in ("ident_bf", "ones_row_bf", "ones_col_bf", "selbf"):
        dst = "selbf_t" if n == "selbf" else n
        P.dma("sync", "dAUX", lambda e, n=n, dst=dst: e.dma_start(out=sb[dst][:], in_=din[n].ap()),
              reads=(), writes=(dst,) if dst != "selbf_t" else G("selbf_t"))
    P.dma("gpsimd", "gAUX", lambda e: e.dma_start(out=sb["bdn_bf"][:], in_=din["bdn_rows"].ap()),
          reads=(), writes=G("bdn_bf"))
    P.dma("gpsimd", "gAUX", lambda e: e.dma_start(out=sb["bsd_bf"][:], in_=din["bsd_row"].ap()),
          reads=(), writes=G("bsd_bf"))
    P.dma("gpsimd", "gAUX", lambda e: e.dma_start(out=sb["bg_bf"][:], in_=din["bg_rows"].ap()),
          reads=(), writes=("bg_bf",))
    P.dma("sync", "dAUX", lambda e: e.dma_start(
        out=sb["gth_idx"][:], in_=din["gther"].ap()),
        reads=(), writes=("gth_idx",))

    P.dma("sync", "dAUX", lambda e: e.dma_start(out=sb["rowtmp"][:], in_=din["dtb_row"].ap()),
          reads=(), writes=("rowtmp",))
    P.c("tensor", lambda e: e.matmul(pb["pf0"][:, 0:16], sb["ones_row"][:],
                                     sb["rowtmp"][:], start=True, stop=True),
        reads=("rowtmp", "ones_row"), writes=("pf0",))
    P.c("vector", lambda e: e.tensor_copy(sb["dtb_bc"][:], pb["pf0"][:, 0:16]),
        reads=("pf0",), writes=("dtb_bc",))
    P.dma("sync", "dAUX", lambda e: e.dma_start(out=sb["rowtmp2"][:], in_=din["alog_row"].ap()),
          reads=(), writes=("rowtmp2",))
    P.c("scalar", lambda e: e.activation(sb["rowtmp2"][:], sb["rowtmp2"][:], AF.Exp),
        reads=("rowtmp2",), writes=("rowtmp2",))
    P.c("tensor", lambda e: e.matmul(pb["pf0"][:, 0:16], sb["ones_row"][:],
                                     sb["rowtmp2"][:], start=True, stop=True),
        reads=("rowtmp2", "ones_row"), writes=("pf0",))
    P.c("vector", lambda e: e.tensor_scalar_mul(sb["acoef_bc"][:], pb["pf0"][:, 0:16], -1.0),
        reads=("pf0",), writes=("acoef_bc",))

    cx.deferred = []

    def cast_round(dst, emit, defer):
        def rnd():
            st = P.dma("gpsimd", "gCAST", emit, reads=(), writes=())
            P.last_write[dst] = st
        cx.deferred.append(rnd) if defer else rnd()

    def cast_gpkc(src, dst, rows, kdim, gdim, defer=False):
        for k in range(rows // 128):
            for gb in range(0, gdim, 16):
                gn = min(16, gdim - gb)
                cast_round(dst, lambda e, k=k, gb=gb, gn=gn: e.dma_start(
                    out=dr[dst].ap()[gb:gb + gn, :, k, :].transpose([1, 0, 2]),
                    in_=din[src].ap()[k * 128:(k + 1) * 128, gb * 256:(gb + gn) * 256]
                    .rearrange("p (g c) -> p g c", g=gn)), defer)

    def cast_hgrc(src, dst, rows, hg, defer=False):
        for r in range(rows // 128):
            cast_round(dst, lambda e, r=r: e.dma_start(
                out=dr[dst].ap()[:, r, :, :].transpose([1, 0, 2]),
                in_=din[src].ap()[r * 128:(r + 1) * 128, :]
                .rearrange("p (g c) -> p g c", g=hg)), defer)

    cast_gpkc("wqkv", "wqkv_bf", 2048, 16, 16)
    cast_gpkc("wz", "wz_bf", 2048, 16, 8)
    for k in range(16):
        P.dma("gpsimd", "gW0", lambda e, k=k: e.dma_start(
            out=sb["wba_t"][:, k * 32:(k + 1) * 32], in_=din["wba"].ap()[k * 128:(k + 1) * 128, :]),
            reads=("wba_t",), writes=("wba_t",))
    for k in range(16):
        cast_round("wg_bf", lambda e, k=k: e.dma_start(
            out=dr["wg_bf"].ap()[k * 128:(k + 1) * 128, :],
            in_=din["wg"].ap()[k * 128:(k + 1) * 128, :]), False)

    cast_hgrc("wout", "wout_bf", 4096, 4, defer=True)
    cast_gpkc("wup", "wup_bf", 2048, 16, 24, defer=True)
    cast_gpkc("wsw", "wsw_bf", 1024, 8, 24, defer=True)
    cast_gpkc("wsu", "wsu_bf", 2048, 16, 8, defer=True)
    cast_gpkc("wsa", "wsa_bf", 1024, 8, 4, defer=True)
    cast_hgrc("wdn", "wdn_bf", 6144, 4, defer=True)
    cast_hgrc("wsd", "wsd_bf", 2048, 4, defer=True)

    P.c("vector", lambda e: e.memset(sb["S_bf"][:], 0.0), reads=(), writes=G("S_bf"))
    P.c("vector", lambda e: e.memset(sb["halo"][:], 0.0), reads=(), writes=G("halo"))
    P.c("vector", lambda e: e.memset(sb["stg"][0:2, 0:512], 1.0), reads=(), writes=("stgc0",))
    P.c("vector", lambda e: e.memset(sb["stg"][0:2, 512:1536], -1.0),
        reads=(), writes=("stgc1", "stgc2"))


def emit_phaseA_tile(cx, t):
    P, sb, pb, din, dr, G = cx.P, cx.sb, cx.pb, cx.din, cx.dr, cx.G
    t0 = t * TT

    nleft = NTILES - t
    nbatch = (len(cx.deferred) + nleft - 1) // nleft if cx.deferred else 0
    batch = cx.deferred[:nbatch]
    del cx.deferred[:nbatch]
    cx.tile_casts = batch[(len(batch) + 1) // 2:]

    for rnd in range(2):
        for s2 in range(2):
            s = rnd * 2 + s2
            P.dma("gpsimd", "gX", lambda e, s=s, s2=s2: e.dma_start(
                out=sb["xtm"][:, s2 * 2048:(s2 + 1) * 2048],
                in_=din["x"].ap()[t0 + s * 128:t0 + (s + 1) * 128, :]),
                reads=G("xtm"), writes=G("xtm"))
        for s2 in range(2):
            s = rnd * 2 + s2
            for fg in range(2):
                bank = "pt%d" % fg
                for f2 in range(8):
                    f = fg * 8 + f2
                    P.c("tensor", lambda e, s2=s2, f=f, f2=f2, bank=bank: e.transpose(
                        pb[bank][:, f2 * 128:(f2 + 1) * 128],
                        sb["xtm"][:, s2 * 2048 + f * 128:s2 * 2048 + (f + 1) * 128],
                        sb["ident_bf"][:]),
                        reads=G("xtm") + ("ident_bf",), writes=(bank,))
                P.c("scalar", lambda e, s=s, fg=fg, bank=bank: e.activation(
                    bass.AP(sb["xn_fm"], fg * 8 * 512 + s * 128, [[8192, 128], [512, 8], [1, 128]]),
                    pb[bank][:].rearrange("p (f c) -> p f c", f=8), AF.Copy),
                    reads=(bank,), writes=G("xn_fm"))
    for cast_rnd in batch[:(len(batch) + 1) // 2]:
        cast_rnd()

    for k in range(16):
        P.c("tensor", lambda e, k=k: e.matmul(
            pb["pf4"][0:1, :], sb["ones_col_bf"][:], sb["xn_fm"][:, k * 512:(k + 1) * 512],
            start=(k == 0), stop=(k == 15)),
            reads=G("xn_fm") + ("ones_col_bf",), writes=("pf4",))
    for k in range(16):
        P.c("scalar", lambda e, k=k: e.activation(
            sb["r_bc"][:, 0:512], sb["xn_fm"][:, k * 512:(k + 1) * 512], AF.Square),
            reads=G("xn_fm"), writes=G("r_bc"))
        P.c("tensor", lambda e, k=k: e.matmul(
            pb["pf5"][0:1, :], sb["ones_col_bf"][:], sb["r_bc"][:, 0:512],
            start=(k == 0), stop=(k == 15)),
            reads=G("r_bc") + ("ones_col_bf",), writes=("pf5",))
    P.c("vector", lambda e: e.tensor_scalar_mul(sb["strow"][0:1, 0:512], pb["pf4"][0:1, :], 1.0 / D),
        reads=("pf4",), writes=G("strow"))
    P.c("vector", lambda e: e.tensor_scalar_mul(sb["strow"][0:1, 512:1024], pb["pf5"][0:1, :], 1.0 / D),
        reads=("pf5",), writes=G("strow"))
    P.c("vector", lambda e: e.scalar_tensor_tensor(
        sb["strow"][0:1, 1024:1536], sb["strow"][0:1, 0:512], -1.0, sb["strow"][0:1, 0:512],
        ALU.mult, ALU.mult), reads=G("strow"), writes=G("strow"))
    P.c("vector", lambda e: e.tensor_add(sb["strow"][0:1, 1024:1536],
                                         sb["strow"][0:1, 1024:1536], sb["strow"][0:1, 512:1024]),
        reads=G("strow"), writes=G("strow"))
    P.c("vector", lambda e: e.tensor_scalar_add(sb["strow"][0:1, 1024:1536],
                                                sb["strow"][0:1, 1024:1536], LN_EPS),
        reads=G("strow"), writes=G("strow"))
    P.c("scalar", lambda e: e.activation(sb["strow"][0:1, 512:1024], sb["strow"][0:1, 1024:1536],
                                         AF.Ln), reads=G("strow"), writes=G("strow"))
    P.c("scalar", lambda e: e.activation(sb["strow"][0:1, 512:1024], sb["strow"][0:1, 512:1024],
                                         AF.Exp, scale=-0.5), reads=G("strow"), writes=G("strow"))
    P.c("vector", lambda e: e.tensor_mul(sb["strow"][0:1, 0:512], sb["strow"][0:1, 0:512],
                                         sb["strow"][0:1, 512:1024]),
        reads=G("strow"), writes=G("strow"))
    P.c("tensor", lambda e: e.matmul(pb["pf4"][:, :], sb["ones_row"][:],
                                     sb["strow"][0:1, 512:1024], start=True, stop=True),
        reads=G("strow") + ("ones_row",), writes=("pf4",))
    P.c("scalar", lambda e: e.activation(sb["r_bc"][:, 0:512], pb["pf4"][:, :], AF.Copy),
        reads=("pf4",), writes=G("r_bc"))
    P.c("tensor", lambda e: e.matmul(pb["pf5"][:, :], sb["ones_row"][:],
                                     sb["strow"][0:1, 0:512], start=True, stop=True),
        reads=G("strow") + ("ones_row",), writes=("pf5",))
    P.c("scalar", lambda e: e.activation(sb["mr_bc"][:, 0:512], pb["pf5"][:, :], AF.Copy),
        reads=("pf5",), writes=G("mr_bc"))
    for k in range(16):
        eng = "gpsimd" if k % 2 == 0 else "vector"
        P.c(eng, lambda e, k=k: e.tensor_mul(
            sb["xn_fm"][:, k * 512:(k + 1) * 512], sb["xn_fm"][:, k * 512:(k + 1) * 512],
            sb["r_bc"][:, 0:512]),
            reads=G("xn_fm", "r_bc"), writes=G("xn_fm"))
        P.c(eng, lambda e, k=k: e.tensor_sub(
            sb["xn_fm"][:, k * 512:(k + 1) * 512], sb["xn_fm"][:, k * 512:(k + 1) * 512],
            sb["mr_bc"][:, 0:512]),
            reads=G("xn_fm", "mr_bc"), writes=G("xn_fm"))

    for g in range(16):
        slot = g % 2
        buf = "wt%d" % slot
        P.dma("sync", "dW%d" % slot, lambda e, g=g, buf=buf: e.dma_start(
            out=sb[buf][:, 0:4096], in_=dr["wqkv_bf"].ap()[g]),
            reads=("wqkv_bf",), writes=G(buf))
        for oc2 in range(2):
            oc = g * 2 + oc2
            bank = "pf%d" % (oc % 4)
            for k in range(16):
                P.c("tensor", lambda e, oc2=oc2, k=k, buf=buf, bank=bank: e.matmul(
                    pb[bank][:, :], sb[buf][:, k * 256 + oc2 * 128:k * 256 + oc2 * 128 + 128],
                    sb["xn_fm"][:, k * 512:(k + 1) * 512], start=(k == 0), stop=(k == 15)),
                    reads=G(buf, "xn_fm"), writes=(bank,))
            P.c("scalar", lambda e, oc=oc, bank=bank: e.activation(
                sb["qkvp"][:, oc * 512:(oc + 1) * 512], pb[bank][:, :], AF.Copy),
                reads=(bank,), writes=G("qkvp"))

    for g in range(8):
        slot = g % 2
        buf = "wt%d" % slot
        P.dma("sync", "dW%d" % slot, lambda e, g=g, buf=buf: e.dma_start(
            out=sb[buf][:, 0:4096], in_=dr["wz_bf"].ap()[g]),
            reads=("wz_bf",), writes=G(buf))
        for s in range(4):
            bank = "pf%d" % (s % 4)
            for k in range(16):
                P.c("tensor", lambda e, s=s, k=k, buf=buf, bank=bank: e.matmul(
                    pb[bank][:, 0:256], sb["xn_fm"][:, k * 512 + s * 128:k * 512 + (s + 1) * 128],
                    sb[buf][:, k * 256:(k + 1) * 256], start=(k == 0), stop=(k == 15)),
                    reads=G(buf, "xn_fm"), writes=(bank,))
            P.c("scalar", lambda e, s=s, g=g, bank=bank: e.activation(
                sb["sz"][:, s * 2048 + g * 256:s * 2048 + (g + 1) * 256],
                pb[bank][:, 0:256], AF.Silu),
                reads=(bank,), writes=G("sz"))

    for s in range(4):
        for k in range(16):
            P.c("tensor", lambda e, s=s, k=k: e.matmul(
                pb["pf4"][:, 0:32], sb["xn_fm"][:, k * 512 + s * 128:k * 512 + (s + 1) * 128],
                sb["wba_t"][:, k * 32:(k + 1) * 32], start=(k == 0), stop=(k == 15)),
                reads=("wba_t",) + G("xn_fm"), writes=("pf4",))
        P.c("vector", lambda e, s=s: e.tensor_copy(sb["ba_all"][:, s * 32:(s + 1) * 32],
                                                   pb["pf4"][:, 0:32]),
            reads=("pf4",), writes=G("ba_all"))

    for rnd in cx.tile_casts:
        rnd()
    cx.tile_casts = []
    if t in cx.coll_at_tile:
        cx.coll_at_tile.pop(t)()

    P.c("gpsimd", lambda e: e.tensor_copy(
        sb["halo2"][:].rearrange("p (c k) -> p c k", c=32),
        bass.AP(sb["qkvp"], 509, [[16384, 128], [512, 32], [1, 3]])),
        reads=G("qkvp"), writes=("halo2",))
    for cc in range(32):
        eng = "vector"
        scr = "cvt" if cc % 2 == 0 else "mdiff"
        base = cc * 512
        P.c(eng, lambda e, cc=cc, base=base, scr=scr: e.tensor_scalar_mul(
            sb[scr][:, 0:512], sb["qkvp"][:, base:base + 512],
            sb["convw"][:, cc * 4 + 3:cc * 4 + 4]),
            reads=G("qkvp") + ("convw",), writes=G(scr))
        for k in range(3):
            nh = 3 - k
            P.c(eng, lambda e, cc=cc, base=base, k=k, nh=nh, scr=scr: e.scalar_tensor_tensor(
                sb[scr][:, nh:512], sb["qkvp"][:, base:base + 512 - nh],
                sb["convw"][:, cc * 4 + k:cc * 4 + k + 1], sb[scr][:, nh:512],
                ALU.mult, ALU.add),
                reads=G("qkvp", scr) + ("convw",), writes=G(scr))
            P.c(eng, lambda e, cc=cc, k=k, nh=nh, scr=scr: e.scalar_tensor_tensor(
                sb[scr][:, 0:nh], sb["halo"][:, cc * 3 + k:cc * 3 + k + nh],
                sb["convw"][:, cc * 4 + k:cc * 4 + k + 1], sb[scr][:, 0:nh],
                ALU.mult, ALU.add),
                reads=G("halo", scr) + ("convw",), writes=G(scr))
        P.c(eng, lambda e, cc=cc, base=base, scr=scr: e.tensor_copy(
            sb["qkvp"][:, base:base + 512], sb[scr][:, 0:512]),
            reads=G(scr) + ("halo2",), writes=G("qkvp"))
    P.c("gpsimd", lambda e: e.tensor_copy(sb["halo"][:], sb["halo2"][:]),
        reads=("halo2",) + G("halo"), writes=G("halo"))

    gch = sb["gch"]
    bvw = bass.AP(sb["ba_all"], 0, [[128, 128], [32, 4], [1, 16]])
    avw = bass.AP(sb["ba_all"], 16, [[128, 128], [32, 4], [1, 16]])
    LNB, GC, R1, BGF, BET, KDF, EG = 0, 64, 128, 192, 256, 320, 384

    def gv(off):
        return gch[:, off:off + 64]
    P.c("vector", lambda e: e.tensor_scalar_min(gv(LNB), bvw, 0.0),
        reads=G("ba_all"), writes=G("gch"))
    P.c("scalar", lambda e: e.activation(gv(R1), bvw, AF.Abs),
        reads=G("ba_all"), writes=G("gch"))
    P.c("scalar", lambda e: e.activation(gv(R1), gv(R1), AF.Exp, scale=-1.0),
        reads=G("gch"), writes=G("gch"))
    P.c("scalar", lambda e: e.activation(gv(R1), gv(R1), AF.Ln, bias=1.0),
        reads=G("gch"), writes=G("gch"))
    P.c("vector", lambda e: e.tensor_sub(gv(LNB), gv(LNB), gv(R1)),
        reads=G("gch"), writes=G("gch"))
    P.c("vector", lambda e: e.tensor_tensor(
        gv(GC).rearrange("p (c v) -> p c v", c=4), avw,
        sb["dtb_bc"][:].unsqueeze(1).broadcast_to([128, 4, 16]), ALU.add),
        reads=G("ba_all") + ("dtb_bc",), writes=G("gch"))
    P.c("scalar", lambda e: e.activation(gv(BGF), gv(GC), AF.Abs),
        reads=G("gch"), writes=G("gch"))
    P.c("scalar", lambda e: e.activation(gv(BGF), gv(BGF), AF.Exp, scale=-1.0),
        reads=G("gch"), writes=G("gch"))
    P.c("scalar", lambda e: e.activation(gv(BGF), gv(BGF), AF.Ln, bias=1.0),
        reads=G("gch"), writes=G("gch"))
    P.c("vector", lambda e: e.tensor_scalar_max(gv(GC), gv(GC), 0.0),
        reads=G("gch"), writes=G("gch"))
    P.c("vector", lambda e: e.tensor_add(gv(GC), gv(GC), gv(BGF)),
        reads=G("gch"), writes=G("gch"))
    P.c("vector", lambda e: e.tensor_tensor(
        gv(GC).rearrange("p (c v) -> p c v", c=4), gv(GC).rearrange("p (c v) -> p c v", c=4),
        sb["acoef_bc"][:].unsqueeze(1).broadcast_to([128, 4, 16]), ALU.mult),
        reads=G("gch") + ("acoef_bc",), writes=G("gch"))
    for ci in range(CPT):
        P.c("tensor", lambda e, ci=ci: e.matmul(
            pb["pf5"][:, ci * 16:(ci + 1) * 16], sb["mski"][:],
            gch[:, GC + ci * 16:GC + (ci + 1) * 16], start=True, stop=True),
            reads=G("gch") + ("mski",), writes=("pf5",))
    P.c("vector", lambda e: e.tensor_copy(gv(GC), pb["pf5"][:, 0:64]),
        reads=("pf5",), writes=G("gch"))
    P.c("vector", lambda e: e.tensor_add(gv(R1), gv(GC), gv(LNB)),
        reads=G("gch"), writes=G("gch"))
    P.c("scalar", lambda e: e.activation(gv(BGF), gv(R1), AF.Exp),
        reads=G("gch"), writes=G("gch"))
    P.c("scalar", lambda e: e.activation(gv(BET), gv(LNB), AF.Exp),
        reads=G("gch"), writes=G("gch"))
    P.c("scalar", lambda e: e.activation(gv(EG), gv(GC), AF.Exp),
        reads=G("gch"), writes=G("gch"))

    for ci in range(CPT):
        emit_chunk(cx, t, ci)


def trans_group(cx, srcs, dst, evac_af=AF.Copy, evac_engs=("scalar",)):
    P, sb, pb, G = cx.P, cx.sb, cx.pb, cx.G
    for grp in range((len(srcs) + 7) // 8):
        bank = "pt%d" % (grp % 2)
        blk = srcs[grp * 8:(grp + 1) * 8]
        for j2, (tn, co) in enumerate(blk):
            P.c("tensor", lambda e, tn=tn, co=co, j2=j2, bank=bank: e.transpose(
                pb[bank][:, j2 * 128:(j2 + 1) * 128], sb[tn][:, co:co + 128],
                sb["ident_bf"][:]),
                reads=G(tn) + ("ident_bf",), writes=(bank,))
        eng = evac_engs[grp % len(evac_engs)]
        if eng == "scalar":
            P.c("scalar", lambda e, grp=grp, n=len(blk), bank=bank, dst=dst, af=evac_af: e.activation(
                sb[dst][:, grp * 1024:grp * 1024 + n * 128], pb[bank][:, 0:n * 128], af),
                reads=(bank,), writes=G(dst))
        else:
            P.c(eng, lambda e, grp=grp, n=len(blk), bank=bank, dst=dst: e.tensor_copy(
                sb[dst][:, grp * 1024:grp * 1024 + n * 128], pb[bank][:, 0:n * 128]),
                reads=(bank,), writes=G(dst))


def emit_chunk(cx, t, ci):
    P, sb, pb, G = cx.P, cx.sb, cx.pb, cx.G
    t0 = t * TT + ci * CH
    gch = sb["gch"]
    LNB, GC, R1, BGF, BET, KDF, EG = 0, 64, 128, 192, 256, 320, 384
    cs = ci * 16

    P.c("tensor", lambda e, cs=cs: e.transpose(pb["pf5"][0:16, 128:256],
                                               gch[:, GC + cs:GC + cs + 16], sb["identf"][:]),
        reads=G("gch") + ("identf",), writes=("pf5",))
    P.c("vector", lambda e: e.tensor_copy(sb["gT"][:, 0:128], pb["pf5"][0:16, 128:256]),
        reads=("pf5",), writes=G("gT"))
    P.c("tensor", lambda e, cs=cs: e.transpose(pb["pf5"][0:16, 256:384],
                                               gch[:, R1 + cs:R1 + cs + 16], sb["identf"][:]),
        reads=G("gch") + ("identf",), writes=("pf5",))
    P.c("vector", lambda e: e.tensor_copy(sb["gT"][:, 128:256], pb["pf5"][0:16, 256:384]),
        reads=("pf5",), writes=G("gT"))
    P.c("tensor", lambda e: e.matmul(
        pb["pf5"][0:1, 0:16], sb["gT"][:, 127:128], sb["identf"][0:16, 0:16],
        start=True, stop=True),
        reads=G("gT") + ("identf",), writes=("pf5",))
    P.c("vector", lambda e: e.tensor_copy(sb["rowtmp"][:], pb["pf5"][0:1, 0:16]),
        reads=("pf5",), writes=("rowtmp",))
    P.c("tensor", lambda e: e.matmul(
        pb["pf5"][:, 0:16], sb["ones_row"][:], sb["rowtmp"][:], start=True, stop=True),
        reads=("rowtmp", "ones_row"), writes=("pf5",))
    P.c("scalar", lambda e, ci=ci: e.activation(
        sb["egl"][:, ci * 16:(ci + 1) * 16], pb["pf5"][:, 0:16], AF.Exp),
        reads=("pf5",), writes=G("egl"))
    P.c("vector", lambda e, cs=cs: e.tensor_sub(gch[:, KDF + cs:KDF + cs + 16],
                                                pb["pf5"][:, 0:16], gch[:, GC + cs:GC + cs + 16]),
        reads=("pf5",) + G("gch"), writes=G("gch"))
    P.c("scalar", lambda e, cs=cs: e.activation(gch[:, KDF + cs:KDF + cs + 16],
                                                gch[:, KDF + cs:KDF + cs + 16], AF.Exp),
        reads=G("gch"), writes=G("gch"))

    trans_group(cx, [("qkvp", j * 512 + ci * 128) for j in range(0, 8)], "q_tm", AF.Silu)
    trans_group(cx, [("qkvp", j * 512 + ci * 128) for j in range(8, 16)], "k_tm", AF.Silu)
    trans_group(cx, [("qkvp", j * 512 + ci * 128) for j in range(16, 32)], "v_tm", AF.Silu)

    for (src, col0) in (("q_tm", 0), ("k_tm", 8)):
        P.c("scalar", lambda e, src=src: e.activation(sb["ysq"][:, 0:1024], sb[src][:], AF.Square),
            reads=G(src), writes=G("ysq"))
        P.c("vector", lambda e, col0=col0: e.tensor_reduce(
            sb["lncol"][:, col0:col0 + 8],
            sb["ysq"][:, 0:1024].rearrange("p (h d) -> p h d", h=8), AX.X, ALU.add),
            reads=G("ysq"), writes=G("lncol"))
    P.c("vector", lambda e: e.tensor_scalar_add(sb["lncol"][:, 0:16], sb["lncol"][:, 0:16],
                                                L2_EPS),
        reads=G("lncol"), writes=G("lncol"))
    P.c("scalar", lambda e: e.activation(sb["lncol"][:, 0:16], sb["lncol"][:, 0:16],
                                         AF.Ln), reads=G("lncol"), writes=G("lncol"))
    P.c("scalar", lambda e: e.activation(sb["lncol"][:, 0:16], sb["lncol"][:, 0:16],
                                         AF.Exp, scale=-0.5), reads=G("lncol"), writes=G("lncol"))
    P.c("vector", lambda e: e.tensor_scalar_mul(sb["lncol"][:, 0:8], sb["lncol"][:, 0:8],
                                                float(128.0 ** -0.5)),
        reads=G("lncol"), writes=G("lncol"))
    P.c("vector", lambda e: e.tensor_tensor(
        sb["qs_tm"][:].rearrange("p (h d) -> p h d", h=8),
        sb["q_tm"][:].rearrange("p (h d) -> p h d", h=8),
        sb["lncol"][:, 0:8].unsqueeze(2).broadcast_to([128, 8, 128]), ALU.mult),
        reads=G("q_tm", "lncol"), writes=G("qs_tm"))
    P.c("vector", lambda e: e.tensor_tensor(
        sb["kn_tm"][:].rearrange("p (h d) -> p h d", h=8),
        sb["k_tm"][:].rearrange("p (h d) -> p h d", h=8),
        sb["lncol"][:, 8:16].unsqueeze(2).broadcast_to([128, 8, 128]), ALU.mult),
        reads=G("k_tm", "lncol"), writes=G("kn_tm"))
    P.c("gpsimd", lambda e, cs=cs: e.tensor_tensor(
        sb["vb"][:].rearrange("p (v d) -> p v d", v=16),
        sb["v_tm"][:].rearrange("p (v d) -> p v d", v=16),
        gch[:, BET + cs:BET + cs + 16].unsqueeze(2).broadcast_to([128, 16, 128]), ALU.mult),
        reads=G("v_tm", "gch"), writes=G("vb"))
    P.c("gpsimd", lambda e, cs=cs: e.tensor_tensor(
        sb["kbg"][:].rearrange("p (h u d) -> p h u d", h=8, u=2),
        sb["kn_tm"][:].rearrange("p (h d) -> p h d", h=8)
        .unsqueeze(2).broadcast_to([128, 8, 2, 128]),
        gch[:, BGF + cs:BGF + cs + 16].rearrange("p (h u) -> p h u", h=8)
        .unsqueeze(3).broadcast_to([128, 8, 2, 128]), ALU.mult),
        reads=G("kn_tm", "gch"), writes=G("kbg"))
    P.c("vector", lambda e, cs=cs: e.tensor_tensor(
        sb["kdec"][:].rearrange("p (h u d) -> p h u d", h=8, u=2),
        sb["kn_tm"][:].rearrange("p (h d) -> p h d", h=8)
        .unsqueeze(2).broadcast_to([128, 8, 2, 128]),
        gch[:, KDF + cs:KDF + cs + 16].rearrange("p (h u) -> p h u", h=8)
        .unsqueeze(3).broadcast_to([128, 8, 2, 128]), ALU.mult),
        reads=G("kn_tm", "gch"), writes=G("kdec"))
    P.c("vector", lambda e, cs=cs: e.tensor_tensor(
        sb["qgi_tm"][:].rearrange("p (h u d) -> p h u d", h=8, u=2),
        sb["qs_tm"][:].rearrange("p (h d) -> p h d", h=8)
        .unsqueeze(2).broadcast_to([128, 8, 2, 128]),
        gch[:, EG + cs:EG + cs + 16].rearrange("p (h u) -> p h u", h=8)
        .unsqueeze(3).broadcast_to([128, 8, 2, 128]), ALU.mult),
        reads=G("qs_tm", "gch"), writes=G("qgi_tm"))

    trans_group(cx, [("qs_tm", j * 128) for j in range(8)], "qs_fm")
    trans_group(cx, [("kn_tm", j * 128) for j in range(8)], "kn_fm",
                evac_engs=("vector",))
    trans_group(cx, [("qgi_tm", j * 128) for j in range(16)], "qgi_fm",
                evac_engs=("scalar", "vector"))

    for rd in range(4):
        P.dma("scalar", "dRS", lambda e, rd=rd: e.dma_start(
            out=sb["stg"][1:2, 0:512], in_=sb["gT"][rd * 4:rd * 4 + 4, 0:128]),
            reads=G("gT"), writes=("stg_l",))
        P.dma("scalar", "dRS", lambda e, rd=rd: e.dma_start(
            out=sb["stg"][0:1, 512:1024], in_=sb["gT"][rd * 4:rd * 4 + 4, 0:128]),
            reads=G("gT"), writes=("stg_r0",))
        P.dma("scalar", "dRS", lambda e, rd=rd: e.dma_start(
            out=sb["stg"][0:1, 1024:1536], in_=sb["gT"][rd * 4:rd * 4 + 4, 128:256]),
            reads=G("gT"), writes=("stg_r1",))
        for v4 in range(4):
            P.c("tensor", lambda e, v4=v4: e.matmul(
                pb["pf0"][:, v4 * 128:(v4 + 1) * 128],
                bass.AP(sb["stg"], v4 * 128, [[1536, 2], [1, 128]]),
                bass.AP(sb["stg"], 1024 + v4 * 128, [[1536, 2], [1, 128]]),
                start=True, stop=True),
                reads=("stg_l", "stg_r1", "stgc0", "stgc2"), writes=("pf0",))
            P.c("tensor", lambda e, v4=v4: e.matmul(
                pb["pf1"][:, v4 * 128:(v4 + 1) * 128],
                bass.AP(sb["stg"], v4 * 128, [[1536, 2], [1, 128]]),
                bass.AP(sb["stg"], 512 + v4 * 128, [[1536, 2], [1, 128]]),
                start=True, stop=True),
                reads=("stg_l", "stg_r0", "stgc0", "stgc1"), writes=("pf1",))
        P.c("vector", lambda e: e.tensor_tensor(
            sb["mdiff"][:].rearrange("p (v c) -> p v c", v=4),
            pb["pf0"][:].rearrange("p (v c) -> p v c", v=4),
            sb["msks"][:].unsqueeze(1).broadcast_to([128, 4, 128]), ALU.mult),
            reads=("pf0", "msks"), writes=G("mdiff"))
        P.c("vector", lambda e: e.tensor_tensor(
            sb["cvt"][:].rearrange("p (v c) -> p v c", v=4),
            pb["pf1"][:].rearrange("p (v c) -> p v c", v=4),
            sb["mski"][:].unsqueeze(1).broadcast_to([128, 4, 128]), ALU.mult),
            reads=("pf1", "mski"), writes=G("cvt"))
        P.c("scalar", lambda e, rd=rd: e.activation(
            sb["m1"][:, rd * 512:(rd + 1) * 512], sb["mdiff"][:, 0:512], AF.Exp),
            reads=G("mdiff"), writes=G("m1"))
        P.c("scalar", lambda e, rd=rd: e.activation(
            sb["m0"][:, rd * 512:(rd + 1) * 512], sb["cvt"][:, 0:512], AF.Exp),
            reads=G("cvt"), writes=G("m0"))

    for rnd in range(2):
        for k4 in range(4):
            kh = rnd * 4 + k4
            P.c("tensor", lambda e, kh=kh, k4=k4: e.matmul(
                pb["pf2"][:, k4 * 128:(k4 + 1) * 128], sb["kn_fm"][:, kh * 128:(kh + 1) * 128],
                sb["kn_fm"][:, kh * 128:(kh + 1) * 128], start=True, stop=True),
                reads=G("kn_fm"), writes=("pf2",))
            P.c("tensor", lambda e, kh=kh, k4=k4: e.matmul(
                pb["pf3"][:, k4 * 128:(k4 + 1) * 128], sb["kn_fm"][:, kh * 128:(kh + 1) * 128],
                sb["qs_fm"][:, kh * 128:(kh + 1) * 128], start=True, stop=True),
                reads=G("kn_fm", "qs_fm"), writes=("pf3",))
        P.c("vector", lambda e, rnd=rnd: e.scalar_tensor_tensor(
            sb["Gm"][:, rnd * 512:(rnd + 1) * 512].rearrange("p (v c) -> p v c", v=4),
            pb["pf2"][:].rearrange("p (v c) -> p v c", v=4), -1.0,
            sb["msks"][:].unsqueeze(1).broadcast_to([128, 4, 128]), ALU.mult, ALU.mult),
            reads=("pf2", "msks"), writes=G("Gm"))
        P.c("vector", lambda e, rnd=rnd: e.tensor_tensor(
            sb["qkTm"][:, rnd * 512:(rnd + 1) * 512].rearrange("p (v c) -> p v c", v=4),
            pb["pf3"][:].rearrange("p (v c) -> p v c", v=4),
            sb["mski"][:].unsqueeze(1).broadcast_to([128, 4, 128]), ALU.mult),
            reads=("pf3", "mski"), writes=G("qkTm"))
    P.c("gpsimd", lambda e: e.tensor_tensor(
        sb["C"][:].rearrange("p (h u d) -> p h u d", h=8, u=2),
        sb["Gm"][:].rearrange("p (h d) -> p h d", h=8)
        .unsqueeze(2).broadcast_to([128, 8, 2, 128]),
        sb["m1"][:].rearrange("p (h u d) -> p h u d", h=8, u=2), ALU.mult),
        reads=G("Gm", "m1"), writes=G("C"))
    P.c("vector", lambda e: e.tensor_tensor(
        sb["m0"][:].rearrange("p (h u d) -> p h u d", h=8, u=2),
        sb["qkTm"][:].rearrange("p (h d) -> p h d", h=8)
        .unsqueeze(2).broadcast_to([128, 8, 2, 128]),
        sb["m0"][:].rearrange("p (h u d) -> p h u d", h=8, u=2), ALU.mult),
        reads=G("qkTm", "m0"), writes=G("m0"))

    trans_group(cx, [("C", j * 128) for j in range(16)], "A",
                evac_engs=("scalar", "vector"))
    P.c("gpsimd", lambda e: e.tensor_tensor(
        sb["P"][:].rearrange("p (v c) -> p v c", v=16),
        sb["C"][:].rearrange("p (v c) -> p v c", v=16),
        sb["ident_bf"][:].unsqueeze(1).broadcast_to([128, 16, 128]), ALU.add),
        reads=G("C") + ("ident_bf",), writes=G("P"))

    def quad_evac(grp, bank, dst):
        if grp % 2 == 0:
            P.c("scalar", lambda e, grp=grp, bank=bank, dst=dst: e.activation(
                sb[dst][:, grp * 512:(grp + 1) * 512], pb[bank][:], AF.Copy),
                reads=(bank,), writes=G(dst))
        else:
            P.c("vector", lambda e, grp=grp, bank=bank, dst=dst: e.tensor_copy(
                sb[dst][:, grp * 512:(grp + 1) * 512], pb[bank][:]),
                reads=(bank,), writes=G(dst))

    def quad_mm(lhs_of, rhs_of, dst):
        for grp in range(4):
            bank = "pf%d" % (grp % 4)
            rd = set()
            for v2 in range(4):
                vh = grp * 4 + v2
                ln, lo = lhs_of(vh)
                rn, ro = rhs_of(vh)
                rd.add(ln)
                rd.add(rn)
                P.c("tensor", lambda e, v2=v2, ln=ln, lo=lo, rn=rn, ro=ro, bank=bank: e.matmul(
                    pb[bank][:, v2 * 128:(v2 + 1) * 128],
                    sb[ln][:, lo:lo + 128], sb[rn][:, ro:ro + 128],
                    start=True, stop=True),
                    reads=G(*rd), writes=(bank,))
            quad_evac(grp, bank, dst)

    def quad_mm_addp(a_name, p_name, dst):
        for grp in range(4):
            bank = "pf%d" % (grp % 4)
            for v2 in range(4):
                vh = grp * 4 + v2
                P.c("tensor", lambda e, v2=v2, vh=vh, bank=bank, a=a_name, p=p_name: e.matmul(
                    pb[bank][:, v2 * 128:(v2 + 1) * 128],
                    sb[a][:, vh * 128:(vh + 1) * 128], sb[p][:, vh * 128:(vh + 1) * 128],
                    start=True, stop=False),
                    reads=G(a_name, p_name), writes=(bank,))
                P.c("tensor", lambda e, v2=v2, vh=vh, bank=bank, p=p_name: e.matmul(
                    pb[bank][:, v2 * 128:(v2 + 1) * 128],
                    sb["ident_bf"][:], sb[p][:, vh * 128:(vh + 1) * 128],
                    start=False, stop=True),
                    reads=("ident_bf",) + G(p_name), writes=(bank,))
            quad_evac(grp, bank, dst)

    cur_c, cur_a, cur_p = "C", "A", "P"
    oth_c, oth_a, oth_p = "C2", "A2", "P2"
    for lvl in range(6):
        if lvl < 5:
            quad_mm(lambda vh, a=cur_a: (a, vh * 128), lambda vh, c=cur_c: (c, vh * 128), oth_c)
        quad_mm(lambda vh, c=cur_c: (c, vh * 128), lambda vh, a=cur_a: (a, vh * 128), oth_a)
        quad_mm_addp(oth_a, cur_p, oth_p)
        cur_c, oth_c = oth_c, cur_c
        cur_a, oth_a = oth_a, cur_a
        cur_p, oth_p = oth_p, cur_p
    Pfin = cur_p

    quad_mm(lambda vh, p=Pfin: (p, vh * 128), lambda vh: ("vb", vh * 128), "vc")
    quad_mm(lambda vh: ("kbg", vh * 128), lambda vh, p=Pfin: (p, vh * 128), "kcdT")

    for grp in range(4):
        bs = "pf%d" % (2 + grp % 2)
        bo = "pf%d" % (4 + grp % 2)
        bd = "pf%d" % (grp % 2)
        for v2 in range(4):
            vh = grp * 4 + v2
            P.c("tensor", lambda e, vh=vh, v2=v2, bs=bs: e.matmul(
                pb[bs][:, v2 * 128:(v2 + 1) * 128], sb["kcdT"][:, vh * 128:(vh + 1) * 128],
                sb["S_bf"][:, vh * 128:(vh + 1) * 128], start=True, stop=True),
                reads=G("kcdT", "S_bf"), writes=(bs,))
        P.c("vector", lambda e, grp=grp, bs=bs: e.tensor_sub(
            sb["vnew"][:, grp * 512:(grp + 1) * 512], sb["vc"][:, grp * 512:(grp + 1) * 512],
            pb[bs][:]),
            reads=G("vc") + (bs,), writes=G("vnew"))
        for v2 in range(4):
            vh = grp * 4 + v2
            P.c("tensor", lambda e, vh=vh, v2=v2, bo=bo: e.matmul(
                pb[bo][:, v2 * 128:(v2 + 1) * 128], sb["qgi_fm"][:, vh * 128:(vh + 1) * 128],
                sb["S_bf"][:, vh * 128:(vh + 1) * 128], start=True, stop=False),
                reads=G("qgi_fm", "S_bf"), writes=(bo,))
            P.c("tensor", lambda e, vh=vh, v2=v2, bo=bo: e.matmul(
                pb[bo][:, v2 * 128:(v2 + 1) * 128], sb["m0"][:, vh * 128:(vh + 1) * 128],
                sb["vnew"][:, vh * 128:(vh + 1) * 128], start=False, stop=True),
                reads=G("m0", "vnew"), writes=(bo,))
            P.c("tensor", lambda e, vh=vh, v2=v2, bd=bd: e.matmul(
                pb[bd][:, v2 * 128:(v2 + 1) * 128], sb["kdec"][:, vh * 128:(vh + 1) * 128],
                sb["vnew"][:, vh * 128:(vh + 1) * 128], start=True, stop=True),
                reads=G("kdec", "vnew"), writes=(bd,))
        P.c("vector", lambda e, grp=grp, bo=bo, ci=ci: e.tensor_mul(
            sb["y_tm"][:, grp * 512:(grp + 1) * 512], pb[bo][:],
            sb["sz"][:, ci * 2048 + grp * 512:ci * 2048 + (grp + 1) * 512]),
            reads=(bo,) + G("sz"), writes=G("y_tm"))
        for v2 in range(4):
            vh = grp * 4 + v2
            P.c("vector", lambda e, vh=vh, v2=v2, bd=bd, ci=ci: e.scalar_tensor_tensor(
                sb["S_bf"][:, vh * 128:(vh + 1) * 128], sb["S_bf"][:, vh * 128:(vh + 1) * 128],
                sb["egl"][:, ci * 16 + vh:ci * 16 + vh + 1], pb[bd][:, v2 * 128:(v2 + 1) * 128],
                ALU.mult, ALU.add),
                reads=G("S_bf", "egl") + (bd,), writes=G("S_bf"))

    P.c("scalar", lambda e: e.activation(sb["ysq"][:, 0:2048], sb["y_tm"][:], AF.Square),
        reads=G("y_tm"), writes=G("ysq"))
    P.c("vector", lambda e: e.tensor_reduce(
        sb["rstc"][:, 0:16], sb["ysq"][:].rearrange("p (v d) -> p v d", v=16), AX.X, ALU.add),
        reads=G("ysq"), writes=G("rstc"))
    P.c("vector", lambda e: e.tensor_scalar_add(sb["rstc"][:, 0:16], sb["rstc"][:, 0:16],
                                                RMS_EPS * 128.0),
        reads=G("rstc"), writes=G("rstc"))
    P.c("scalar", lambda e: e.activation(sb["rstc"][:, 0:16], sb["rstc"][:, 0:16],
                                         AF.Ln), reads=G("rstc"), writes=G("rstc"))
    P.c("scalar", lambda e: e.activation(sb["rstc"][:, 0:16], sb["rstc"][:, 0:16],
                                         AF.Exp, scale=-0.5), reads=G("rstc"), writes=G("rstc"))
    P.c("vector", lambda e: e.tensor_scalar_mul(sb["rstc"][:, 0:16], sb["rstc"][:, 0:16],
                                                float(128.0 ** 0.5)),
        reads=G("rstc"), writes=G("rstc"))
    P.c("vector", lambda e: e.tensor_tensor(
        sb["y_tm"][:].rearrange("p (v d) -> p v d", v=16),
        sb["y_tm"][:].rearrange("p (v d) -> p v d", v=16),
        sb["rstc"][:, 0:16].unsqueeze(2).broadcast_to([128, 16, 128]), ALU.mult),
        reads=G("y_tm", "rstc"), writes=G("y_tm"))
    trans_group(cx, [("y_tm", j * 128) for j in range(16)], "nrm_fm",
                evac_engs=("scalar", "vector"))
    P.c("vector", lambda e: e.tensor_scalar_mul(
        sb["nrm_fm"][:], sb["nrm_fm"][:], sb["normw_col"][:]),
        reads=G("nrm_fm") + ("normw_col",), writes=G("nrm_fm"))
    q, col = t0 // 1024, t0 % 1024
    for half in range(2):
        P.dma("scalar", "dST", lambda e, half=half, q=q, col=col: e.dma_start(
            out=cx.dr["cc_in"].ap()[q, half * 1024:(half + 1) * 1024, col:col + 128]
            .rearrange("(v p) c -> v p c", p=128).transpose([1, 0, 2]),
            in_=sb["nrm_fm"][:, half * 1024:(half + 1) * 1024]
            .rearrange("p (v c) -> p v c", v=8)),
            reads=G("nrm_fm"), writes=("cc_in_q%d" % q,))


def emit_phaseB_tile(cx, tb):
    P, sb, pb, din, dr, G = cx.P, cx.sb, cx.pb, cx.din, cx.dr, cx.G
    t0 = tb * TTB

    srcq = dr["cc_out%d" % tb].ap().rearrange("r (u c) -> (r u) c", u=2)
    for g in range(32):
        P.dma("gpsimd", "dGT", lambda e, g=g, srcq=srcq: e.indirect_dma_start(
            sb["nrmB"][:, g * 512:(g + 1) * 512], None, srcq,
            bass.IndirectOffsetOnAxis(sb["gth_idx"][:, g:g + 1], 0)),
            reads=("cc_out%d" % tb, "gth_idx") + G("nrmB"), writes=G("nrmB"))
    if tb == 0:
        cx.coll_b0()

    for hg in range(4):
        for rg in range(4):
            slot = rg % 2
            buf = "wt%d" % slot
            P.dma("sync", "dW%d" % slot, lambda e, hg=hg, rg=rg, buf=buf: e.dma_start(
                out=sb[buf][:, 0:4096].rearrange("p (r c) -> p r c", r=8),
                in_=dr["wout_bf"].ap()[hg, rg * 8:(rg + 1) * 8].transpose([1, 0, 2])),
                reads=("wout_bf",), writes=G(buf))
            for s in range(4):
                bank = "pf%d" % s
                for r2 in range(8):
                    rc = rg * 8 + r2
                    P.c("tensor", lambda e, s=s, rc=rc, r2=r2, buf=buf, bank=bank: e.matmul(
                        pb[bank][:, :],
                        sb["nrmB"][:, rc * 512 + s * 128:rc * 512 + (s + 1) * 128],
                        sb[buf][:, r2 * 512:(r2 + 1) * 512],
                        start=(rc == 0), stop=(rc == 31)),
                        reads=G("nrmB", buf), writes=(bank,))
        for s in range(4):
            bank = "pf%d" % s
            P.dma("sync", "dX", lambda e, s=s, hg=hg: e.dma_start(
                out=sb["resfs"][:, 0:512],
                in_=din["xh"].ap()[t0 + s * 128:t0 + (s + 1) * 128, hg * 512:(hg + 1) * 512]),
                reads=G("resfs"), writes=G("resfs"))
            P.c("vector", lambda e, s=s, hg=hg, bank=bank: e.tensor_add(
                sb["resfs"][:, 512:1024], pb[bank][:, :], sb["resfs"][:, 0:512]),
                reads=(bank,) + G("resfs"), writes=G("resfs"))
            P.dma("scalar", "dRES", lambda e, s=s, hg=hg: e.dma_start(
                out=dr["res0_d"].ap()[t0 + s * 128:t0 + (s + 1) * 128, hg * 512:(hg + 1) * 512],
                in_=sb["resfs"][:, 512:1024]),
                reads=G("resfs"), writes=("res0_d",))

    for s in range(4):
        P.dma("sync", "dRES", lambda e, s=s: e.dma_start(
            out=sb["res0s"][:, 0:2048], in_=dr["res0_d"].ap()[t0 + s * 128:t0 + (s + 1) * 128, :]),
            reads=("res0_d",) + G("res0s"), writes=G("res0s"))
        P.c("vector", lambda e: e.tensor_reduce(
            sb["rstc"][:, 0:1], sb["res0s"][:, 0:2048], AX.X, ALU.add),
            reads=G("res0s"), writes=G("rstc"))
        P.c("scalar", lambda e: e.activation(sb["resfs"][:, 0:2048], sb["res0s"][:, 0:2048],
                                             AF.Square),
            reads=G("res0s"), writes=G("resfs"))
        P.c("vector", lambda e: e.tensor_reduce(
            sb["rstc"][:, 1:2], sb["resfs"][:, 0:2048], AX.X, ALU.add),
            reads=G("resfs"), writes=G("rstc"))
        P.c("vector", lambda e: e.tensor_scalar_mul(sb["rstc"][:, 0:2], sb["rstc"][:, 0:2], 1.0 / D),
            reads=G("rstc"), writes=G("rstc"))
        P.c("vector", lambda e: e.scalar_tensor_tensor(
            sb["rstc"][:, 2:3], sb["rstc"][:, 0:1], -1.0, sb["rstc"][:, 0:1],
            ALU.mult, ALU.mult), reads=G("rstc"), writes=G("rstc"))
        P.c("vector", lambda e: e.tensor_add(sb["rstc"][:, 2:3], sb["rstc"][:, 2:3],
                                             sb["rstc"][:, 1:2]),
            reads=G("rstc"), writes=G("rstc"))
        P.c("vector", lambda e: e.tensor_scalar_add(sb["rstc"][:, 2:3], sb["rstc"][:, 2:3],
                                                    LN_EPS), reads=G("rstc"), writes=G("rstc"))
        P.c("scalar", lambda e: e.activation(sb["rstc"][:, 2:3], sb["rstc"][:, 2:3],
                                             AF.Ln), reads=G("rstc"), writes=G("rstc"))
        P.c("scalar", lambda e: e.activation(sb["rstc"][:, 2:3], sb["rstc"][:, 2:3],
                                             AF.Exp, scale=-0.5), reads=G("rstc"), writes=G("rstc"))
        P.c("vector", lambda e: e.tensor_mul(sb["rstc"][:, 3:4], sb["rstc"][:, 0:1],
                                             sb["rstc"][:, 2:3]),
            reads=G("rstc"), writes=G("rstc"))
        P.c("vector", lambda e: e.tensor_scalar(
            sb["resfs"][:, 0:2048], sb["res0s"][:, 0:2048], sb["rstc"][:, 2:3],
            sb["rstc"][:, 3:4], ALU.mult, ALU.subtract),
            reads=G("res0s", "rstc"), writes=G("resfs"))
        P.dma("scalar", "dRES", lambda e, s=s: e.dma_start(
            out=dr["resf_d"].ap()[t0 + s * 128:t0 + (s + 1) * 128, :], in_=sb["resfs"][:, 0:2048]),
            reads=G("resfs"), writes=("resf_d",))
        P.c("vector", lambda e: e.tensor_copy(sb["xt1b"][:, 0:512] if False else
                                              sb["h2w0"][:, 0:2048], sb["resfs"][:, 0:2048]),
            reads=G("resfs"), writes=G("h2w0"))
        trans_to_xtfm(cx, "h2w0", 0, s)

    for j in range(2):
        emit_ffn_iter(cx, tb, j)


def trans_to_xtfm(cx, srcname, src_off, s):
    P, sb, pb, G = cx.P, cx.sb, cx.pb, cx.G
    for grp in range(2):
        bank = "pt%d" % grp
        for f2 in range(8):
            f = grp * 8 + f2
            P.c("tensor", lambda e, f=f, f2=f2, bank=bank, srcname=srcname, src_off=src_off: e.transpose(
                pb[bank][:, f2 * 128:(f2 + 1) * 128],
                sb[srcname][:, src_off + f * 128:src_off + (f + 1) * 128], sb["ident_bf"][:]),
                reads=G(srcname) + ("ident_bf",), writes=(bank,))
        P.c("scalar", lambda e, grp=grp, s=s, bank=bank: e.activation(
            bass.AP(sb["xt_fm"], grp * 8 * 512 + s * 128, [[8192, 128], [512, 8], [1, 128]]),
            pb[bank][:].rearrange("p (f c) -> p f c", f=8), AF.Copy),
            reads=(bank,), writes=G("xt_fm"))


def emit_ffn_iter(cx, tb, j):
    P, sb, pb, din, dr, G = cx.P, cx.sb, cx.pb, cx.din, cx.dr, cx.G
    t0 = tb * TTB

    P.dma("sync", "dW0", lambda e, j=j: e.dma_start(
        out=sb["wgt"][:, 0:128].rearrange("p (k c) -> p k c", k=16),
        in_=dr["wg_bf"].ap()[:, j * 8:(j + 1) * 8].rearrange("(k p) c -> k p c", p=128)
        .transpose([1, 0, 2])),
        reads=("wg_bf",), writes=("wgt",))
    for s in range(4):
        for k in range(16):
            P.c("tensor", lambda e, s=s, k=k: e.matmul(
                pb["pf4"][:, 0:8], sb["xt_fm"][:, k * 512 + s * 128:k * 512 + (s + 1) * 128],
                sb["wgt"][:, k * 8:(k + 1) * 8], start=(k == 0), stop=False),
                reads=("wgt",) + G("xt_fm"), writes=("pf4",))
        P.c("tensor", lambda e, s=s, j=j: e.matmul(
            pb["pf4"][:, 0:8], sb["ones_row_bf"][:], sb["bg_bf"][0:1, j * 8:j * 8 + 8],
            start=False, stop=True),
            reads=("bg_bf", "ones_row_bf"), writes=("pf4",))
        P.c("vector", lambda e, s=s: e.tensor_copy(sb["gat"][:, s * 8:s * 8 + 8],
                                                   pb["pf4"][:, 0:8]),
            reads=("pf4",), writes=G("gat"))
    for s in range(4):
        g0 = s * 8
        gat = sb["gat"]
        P.c("vector", lambda e, g0=g0: e.tensor_reduce(
            sb["gat"][:, 32 + 0:32 + 1], gat[:, g0:g0 + 8], AX.X, ALU.min),
            reads=G("gat"), writes=G("gat"))
        P.c("scalar", lambda e, g0=g0: e.sign(sb["gat"][:, 40:48], gat[:, g0:g0 + 8]) if False else
            e.activation(sb["gat"][:, 40:48], gat[:, g0:g0 + 8], AF.Sign, bias=sb["gat"][:, 32:33]),
            reads=G("gat"), writes=G("gat")) if False else None
        P.c("vector", lambda e, g0=g0: e.tensor_scalar(
            sb["gat"][:, 40:48], gat[:, g0:g0 + 8], sb["gat"][:, 32:33], None,
            ALU.subtract), reads=G("gat"), writes=G("gat"))
        P.c("scalar", lambda e: e.activation(sb["gat"][:, 40:48], sb["gat"][:, 40:48], AF.Sign),
            reads=G("gat"), writes=G("gat"))
        P.c("vector", lambda e, g0=g0: e.tensor_mul(
            sb["gat"][:, 48:56], sb["gat"][:, 40:48], gat[:, g0:g0 + 8]),
            reads=G("gat"), writes=G("gat"))
        P.c("vector", lambda e: e.tensor_scalar(
            sb["gat"][:, 40:48], sb["gat"][:, 40:48], -1e30, 1e30, ALU.mult, ALU.add),
            reads=G("gat"), writes=G("gat"))
        P.c("vector", lambda e: e.tensor_add(
            sb["gat"][:, 48:56], sb["gat"][:, 48:56], sb["gat"][:, 40:48]),
            reads=G("gat"), writes=G("gat"))
        P.c("vector", lambda e: e.tensor_reduce(
            sb["gat"][:, 33:34], sb["gat"][:, 48:56], AX.X, ALU.min),
            reads=G("gat"), writes=G("gat"))
        P.c("vector", lambda e, g0=g0: e.tensor_scalar(
            sb["gat"][:, 40:48], gat[:, g0:g0 + 8], sb["gat"][:, 33:34], None,
            ALU.subtract), reads=G("gat"), writes=G("gat"))
        P.c("scalar", lambda e: e.activation(sb["gat"][:, 40:48], sb["gat"][:, 40:48], AF.Sign),
            reads=G("gat"), writes=G("gat"))
        P.c("vector", lambda e: e.tensor_scalar_max(sb["gat"][:, 40:48], sb["gat"][:, 40:48], 0.0),
            reads=G("gat"), writes=G("gat"))
        P.c("vector", lambda e, g0=g0: e.tensor_reduce(
            sb["gat"][:, 34:35], gat[:, g0:g0 + 8], AX.X, ALU.max),
            reads=G("gat"), writes=G("gat"))
        P.c("vector", lambda e, g0=g0: e.tensor_scalar(
            sb["gat"][:, 48:56], gat[:, g0:g0 + 8], sb["gat"][:, 34:35], None,
            ALU.subtract), reads=G("gat"), writes=G("gat"))
        P.c("scalar", lambda e: e.activation(sb["gat"][:, 48:56], sb["gat"][:, 48:56], AF.Exp),
            reads=G("gat"), writes=G("gat"))
        P.c("vector", lambda e: e.tensor_mul(sb["gat"][:, 48:56], sb["gat"][:, 48:56],
                                             sb["gat"][:, 40:48]),
            reads=G("gat"), writes=G("gat"))
        P.c("vector", lambda e: e.tensor_reduce(
            sb["gat"][:, 35:36], sb["gat"][:, 48:56], AX.X, ALU.add),
            reads=G("gat"), writes=G("gat"))
        P.c("vector", lambda e: e.reciprocal(sb["gat"][:, 36:37], sb["gat"][:, 35:36]),
            reads=G("gat"), writes=G("gat"))
        P.c("vector", lambda e, s=s: e.tensor_scalar(
            sb["wbc"][:, 0:8] if False else sb["gat"][:, 56 + 0:56 + 8],
            sb["gat"][:, 48:56], sb["gat"][:, 36:37], None, ALU.mult),
            reads=G("gat"), writes=G("gat"))
        P.c("vector", lambda e, s=s: e.tensor_copy(sb["wgt"][:, 0:8] if False else
                                                   sb["wbc"][:, 0:8], sb["gat"][:, 56:64]),
            reads=G("gat"), writes=G("wbc"))
        P.c("tensor", lambda e, s=s: e.transpose(
            pb["pt0"][:, 0:128], sb["wbc"][:, 0:8] if False else sb["wbc"][:, 0:8],
            sb["ident_bf"][:]),
            reads=G("wbc") + ("ident_bf",), writes=("pt0",)) if False else None
        P.c("tensor", lambda e, s=s: e.transpose(
            pb["pt0"][0:8, 0:128], sb["wbc"][:, 0:8], sb["ident_bf"][:]),
            reads=G("wbc") + ("ident_bf",), writes=("pt0",))
        P.c("scalar", lambda e, s=s: e.activation(
            sb["wfm"][:, s * 128:(s + 1) * 128], pb["pt0"][0:8, 0:128], AF.Copy),
            reads=("pt0",), writes=G("wfm"))

    for n in range(6):
        for g4 in range(2):
            for g2 in range(2):
                g = n * 4 + g4 * 2 + g2
                slot = g % 2
                buf = "wt%d" % slot
                P.dma("sync", "dW%d" % slot, lambda e, g=g, buf=buf: e.dma_start(
                    out=sb[buf][:, 0:4096], in_=dr["wup_bf"].ap()[g]),
                    reads=("wup_bf",), writes=G(buf))
                for e2 in range(2):
                    ec = (g4 * 2 + g2) * 2 + e2
                    bank = "pf%d" % (ec % 4)
                    for k in range(16):
                        P.c("tensor", lambda e, e2=e2, k=k, buf=buf, bank=bank: e.matmul(
                            pb[bank][:, :], sb[buf][:, k * 256 + e2 * 128:k * 256 + e2 * 128 + 128],
                            sb["xt_fm"][:, k * 512:(k + 1) * 512], start=(k == 0), stop=(k == 15)),
                            reads=G(buf, "xt_fm"), writes=(bank,))
                    P.c("vector", lambda e, n=n, ec=ec, bank=bank: e.tensor_scalar_add(
                        sb["h_fm"][:, ec * 512:(ec + 1) * 512], pb[bank][:, :],
                        sb["bup_cols"][:, n * 8 + ec:n * 8 + ec + 1]),
                        reads=(bank, "bup_cols"), writes=G("h_fm"))
        for g4 in range(4):
            g = n * 4 + g4
            slot = g % 2
            buf = "wt%d" % slot
            P.dma("sync", "dW%d" % slot, lambda e, g=g, buf=buf: e.dma_start(
                out=sb[buf][:, 0:2048], in_=dr["wsw_bf"].ap()[g]),
                reads=("wsw_bf",), writes=G(buf))
            for e2 in range(2):
                ec = g4 * 2 + e2
                bank = "pf%d" % (ec % 4)
                for k in range(8):
                    P.c("tensor", lambda e, e2=e2, k=k, buf=buf, bank=bank: e.matmul(
                        pb[bank][:, :], sb[buf][:, k * 256 + e2 * 128:k * 256 + e2 * 128 + 128],
                        sb["h_fm"][:, k * 512:(k + 1) * 512], start=(k == 0), stop=(k == 7)),
                        reads=G(buf, "h_fm"), writes=(bank,))
                P.c("scalar", lambda e, n=n, ec=ec, bank=bank: e.activation(
                    sb["sil"][:, ec * 512:(ec + 1) * 512], pb[bank][:, :], AF.Silu,
                    bias=sb["bsw_cols"][:, n * 8 + ec:n * 8 + ec + 1]),
                    reads=(bank, "bsw_cols"), writes=G("sil"))
        P.c("tensor", lambda e, n=n: e.matmul(
            pb["pf4"][:, :], sb["selbf_t"][:, n * 128:(n + 1) * 128],
            sb["wfm"][:, 0:512], start=True, stop=True),
            reads=G("selbf_t", "wfm"), writes=("pf4",))
        P.c("scalar", lambda e: e.activation(sb["wbc"][:, 0:512], pb["pf4"][:, :], AF.Copy),
            reads=("pf4",), writes=G("wbc"))
        P.c("vector", lambda e: e.tensor_mul(sb["sil"][:, 0:4096], sb["sil"][:, 0:4096],
                                             sb["h_fm"][:, 0:4096]),
            reads=G("sil", "h_fm"), writes=G("sil"))
        P.c("vector", lambda e, n=n: e.tensor_tensor(
            sb["h2w%d" % n][:].rearrange("p (g c) -> p g c", g=8),
            sb["sil"][:, 0:4096].rearrange("p (g c) -> p g c", g=8),
            sb["wbc"][:, 0:512].unsqueeze(1).broadcast_to([128, 8, 512]), ALU.mult),
            reads=G("sil", "wbc"), writes=G("h2w%d" % n))

    for g in range(8):
        slot = g % 2
        buf = "wt%d" % slot
        P.dma("sync", "dW%d" % slot, lambda e, g=g, buf=buf: e.dma_start(
            out=sb[buf][:, 0:4096], in_=dr["wsu_bf"].ap()[g]),
            reads=("wsu_bf",), writes=G(buf))
        for e2 in range(2):
            sc = g * 2 + e2
            bank = "pf%d" % (sc % 4)
            for k in range(16):
                P.c("tensor", lambda e, e2=e2, k=k, buf=buf, bank=bank: e.matmul(
                    pb[bank][:, :], sb[buf][:, k * 256 + e2 * 128:k * 256 + e2 * 128 + 128],
                    sb["xt_fm"][:, k * 512:(k + 1) * 512], start=(k == 0), stop=(k == 15)),
                    reads=G(buf, "xt_fm"), writes=(bank,))
            P.c("vector", lambda e, sc=sc, bank=bank: e.tensor_scalar_add(
                sb["sh_fm"][:, sc * 512:(sc + 1) * 512], pb[bank][:, :],
                sb["bsu_cols"][:, sc:sc + 1]),
                reads=(bank, "bsu_cols"), writes=G("sh_fm"))
    for g in range(4):
        slot = g % 2
        buf = "wt%d" % slot
        P.dma("sync", "dW%d" % slot, lambda e, g=g, buf=buf: e.dma_start(
            out=sb[buf][:, 0:2048], in_=dr["wsa_bf"].ap()[g]),
            reads=("wsa_bf",), writes=G(buf))
        for sl in range(2):
            for e2 in range(2):
                ec = g * 2 + e2
                bank = "pf%d" % ((sl * 2 + e2) % 4)
                for k in range(8):
                    P.c("tensor", lambda e, sl=sl, e2=e2, k=k, buf=buf, bank=bank: e.matmul(
                        pb[bank][:, :], sb[buf][:, k * 256 + e2 * 128:k * 256 + e2 * 128 + 128],
                        sb["sh_fm"][:, (sl * 8 + k) * 512:(sl * 8 + k + 1) * 512],
                        start=(k == 0), stop=(k == 7)),
                        reads=G(buf, "sh_fm"), writes=(bank,))
                P.c("scalar", lambda e, sl=sl, ec=ec, bank=bank: e.activation(
                    sb["sh2"][:, (sl * 8 + ec) * 512:(sl * 8 + ec + 1) * 512],
                    pb[bank][:, :], AF.Silu, bias=sb["bsa_cols"][:, ec:ec + 1]),
                    reads=(bank, "bsa_cols"), writes=G("sh2"))
    P.c("vector", lambda e: e.tensor_mul(sb["sh2"][:, 0:8192], sb["sh2"][:, 0:8192],
                                         sb["sh_fm"][:, 0:8192]),
        reads=G("sh2", "sh_fm"), writes=G("sh2"))


    last = (j == 1)
    for hg in range(4):
        banks = ["pf%d" % i for i in range(4)]
        first = [True] * 4
        for n in range(6):
            slot = n % 2
            buf = "wt%d" % slot
            P.dma("sync", "dW%d" % slot, lambda e, hg=hg, n=n, buf=buf: e.dma_start(
                out=sb[buf][:, 0:4096].rearrange("p (r c) -> p r c", r=8),
                in_=dr["wdn_bf"].ap()[hg, n * 8:(n + 1) * 8].transpose([1, 0, 2])),
                reads=("wdn_bf",), writes=G(buf))
            for s in range(4):
                for r2 in range(8):
                    P.c("tensor", lambda e, n=n, s=s, r2=r2, buf=buf, f=first[s], bank=banks[s]: e.matmul(
                        pb[bank][:, :],
                        sb["h2w%d" % n][:, r2 * 512 + s * 128:r2 * 512 + (s + 1) * 128],
                        sb[buf][:, r2 * 512:(r2 + 1) * 512],
                        start=f and r2 == 0, stop=False),
                        reads=G("h2w%d" % n, buf), writes=(banks[s],))
                first[s] = False
        for half in range(2):
            slot = half % 2
            buf = "wt%d" % slot
            P.dma("sync", "dW%d" % slot, lambda e, hg=hg, half=half, buf=buf: e.dma_start(
                out=sb[buf][:, 0:4096].rearrange("p (r c) -> p r c", r=8),
                in_=dr["wsd_bf"].ap()[hg, half * 8:(half + 1) * 8].transpose([1, 0, 2])),
                reads=("wsd_bf",), writes=G(buf))
            for s in range(4):
                for r2 in range(8):
                    rc = half * 8 + r2
                    P.c("tensor", lambda e, rc=rc, r2=r2, s=s, buf=buf, bank=banks[s]: e.matmul(
                        pb[bank][:, :],
                        sb["sh2"][:, rc * 512 + s * 128:rc * 512 + (s + 1) * 128],
                        sb[buf][:, r2 * 512:(r2 + 1) * 512],
                        start=False, stop=False),
                        reads=G("sh2", buf), writes=(banks[s],))
        for s in range(4):
            P.c("tensor", lambda e, s=s, hg=hg, bank=banks[s]: e.matmul(
                pb[bank][:, :], sb["wfm"][0:6, s * 128:(s + 1) * 128],
                sb["bdn_bf"][:, hg * 512:(hg + 1) * 512],
                start=False, stop=False),
                reads=G("wfm", "bdn_bf"), writes=(banks[s],))
            P.c("tensor", lambda e, s=s, hg=hg, bank=banks[s]: e.matmul(
                pb[bank][:, :], sb["ones_row_bf"][:],
                sb["bsd_bf"][0:1, hg * 512:(hg + 1) * 512],
                start=False, stop=True),
                reads=("ones_row_bf",) + G("bsd_bf"), writes=(banks[s],))
            P.dma("sync", "dRES", lambda e, s=s, hg=hg: e.dma_start(
                out=sb["resfs"][:, 0:512],
                in_=dr["resf_d"].ap()[t0 + s * 128:t0 + (s + 1) * 128, hg * 512:(hg + 1) * 512]),
                reads=("resf_d",) + G("resfs"), writes=G("resfs"))
            if not last:
                P.c("vector", lambda e, s=s, bank=banks[s]: e.tensor_add(
                    sb["xt1b"][:, 0:512], pb[bank][:, :], sb["resfs"][:, 0:512]),
                    reads=(banks[s],) + G("resfs"), writes=G("xt1b"))
                bankt = "pt0"
                for f2 in range(4):
                    P.c("tensor", lambda e, f2=f2, bankt=bankt: e.transpose(
                        pb[bankt][:, f2 * 128:(f2 + 1) * 128],
                        sb["xt1b"][:, f2 * 128:(f2 + 1) * 128], sb["ident_bf"][:]),
                        reads=G("xt1b") + ("ident_bf",), writes=(bankt,))
                P.c("scalar", lambda e, s=s, hg=hg, bankt=bankt: e.activation(
                    bass.AP(sb["xt_fm"], hg * 4 * 512 + s * 128, [[8192, 128], [512, 4], [1, 128]]),
                    pb[bankt][:, 0:512].rearrange("p (f c) -> p f c", f=4), AF.Copy),
                    reads=(bankt,), writes=G("xt_fm"))
            else:
                P.dma("sync", "dRES", lambda e, s=s, hg=hg: e.dma_start(
                    out=sb["res0s"][:, 0:512],
                    in_=dr["res0_d"].ap()[t0 + s * 128:t0 + (s + 1) * 128, hg * 512:(hg + 1) * 512]),
                    reads=("res0_d",) + G("res0s"), writes=G("res0s"))
                P.c("vector", lambda e, s=s, bank=banks[s]: e.tensor_add(
                    sb["resfs"][:, 512:1024], pb[bank][:, :], sb["resfs"][:, 0:512]),
                    reads=(banks[s],) + G("resfs"), writes=G("resfs"))
                P.c("vector", lambda e, s=s: e.tensor_add(
                    sb["resfs"][:, 512:1024], sb["resfs"][:, 512:1024], sb["res0s"][:, 0:512]),
                    reads=G("resfs", "res0s"), writes=G("resfs"))
                P.dma("sync", "dOUT", lambda e, s=s, hg=hg: e.dma_start(
                    out=cx.y.ap()[t0 + s * 128:t0 + (s + 1) * 128, hg * 512:(hg + 1) * 512],
                    in_=sb["resfs"][:, 512:1024]),
                    reads=G("resfs"), writes=("y",))


def kernel(**inputs):
    nc = build()
    in_maps = [_stage_core(inputs, c) for c in range(8)]
    res = run_bass_kernel_spmd(nc, in_maps, core_ids=list(range(8)))
    out = np.zeros((B, T, D), np.float32)
    for c in range(8):
        b, h = c // 2, c % 2
        yv = np.asarray(res.results[c]["y"], np.float32)
        for tb in range(4):
            out[b, (2 * tb + h) * 512:(2 * tb + h + 1) * 512] = yv[tb * 512:(tb + 1) * 512]
    return out


# version 12
# speedup vs baseline: 1.0143x; 1.0016x over previous
import numpy as np
import ml_dtypes
import concourse.bass as bass
import concourse.mybir as mybir
from concourse.bass_utils import run_bass_kernel_spmd

F32 = mybir.dt.float32
BF16 = mybir.dt.bfloat16
I32 = mybir.dt.int32
AF = mybir.ActivationFunctionType
ALU = mybir.AluOpType
AX = mybir.AxisListType
BFNP = ml_dtypes.bfloat16

B, T, D = 4, 4096, 2048
HV, HK, DK, DV, KCONV = 32, 16, 128, 128, 4
CH = 128
TT = 512
NTILES = T // TT
CPT = TT // CH
NVH, NKH = 16, 8
CONVC = 4096
VCC = 2048
TB = 2048
TTB = 512
NBT = TB // TTB
LN_EPS, L2_EPS, RMS_EPS = 1e-5, 1e-6, 1e-6

ENGS = ("tensor", "vector", "scalar", "gpsimd", "sync")
GRAN = 2048


class Prog:
    def __init__(self):
        self.steps = []
        self.last_write = {}
        self.last_reads = {}
        self.dma_issued = {}
        self.eng_steps = {e: [] for e in ENGS}

    def _dep(self, step, prod):
        if prod is None or prod is step:
            return
        if (prod["eng"] == "tensor" and step["eng"] == "tensor"
                and prod["kind"] == "c" and step["kind"] == "c"):
            return
        if prod not in step["dep_steps"]:
            step["dep_steps"].append(prod)
            if prod["kind"] == "d":
                step["dma_req"][prod["sem"]] = max(
                    step["dma_req"].get(prod["sem"], 0),
                    16 * self.dma_issued[prod["sem"]])

    def _add(self, eng, kind, emit, reads, writes, sem=None):
        step = dict(eng=eng, kind=kind, emit=emit, sem=sem,
                    dep_steps=[], dma_req={}, marked=False, count=None)
        for r in reads:
            self._dep(step, self.last_write.get(r))
        for w in writes:
            self._dep(step, self.last_write.get(w))
            for rd in self.last_reads.get(w, ()):
                self._dep(step, rd)
        if kind == "d":
            self.dma_issued[sem] = self.dma_issued.get(sem, 0) + 1
        for w in writes:
            self.last_write[w] = step
            self.last_reads[w] = []
        for r in reads:
            self.last_reads.setdefault(r, []).append(step)
        self.steps.append(step)
        self.eng_steps[eng].append(step)
        return step

    def c(self, eng, emit, reads=(), writes=()):
        return self._add(eng, "c", emit, reads, writes)

    def dma(self, eng, sem, emit, reads=(), writes=()):
        return self._add(eng, "d", emit, reads, writes, sem=sem)

    def coll(self, emit, reads=(), writes=()):
        return self._add("gpsimd", "cc", emit, reads, writes, sem="_cc")

    def finalize(self, block, sems, eng_sems, final_waits):
        for st in self.steps:
            for p in st["dep_steps"]:
                p["marked"] = True
        for e in ENGS:
            cnt = 0
            for st in self.eng_steps[e]:
                if st["kind"] == "c" and st["marked"]:
                    cnt += 1
                    st["count"] = cnt
        totals = {}
        for st in self.steps:
            if st["kind"] == "d":
                totals[st["sem"]] = totals.get(st["sem"], 0) + 16
        FULL = ("dAUX", "gAUX")
        dcnt = {}
        for st in self.steps:
            if st["kind"] == "d":
                dcnt[st["sem"]] = dcnt.get(st["sem"], 0) + 16
                st["count"] = dcnt[st["sem"]]
                st["marked"] = True
            elif st["kind"] == "cc":
                dcnt[st["sem"]] = dcnt.get(st["sem"], 0) + 1
                st["count"] = dcnt[st["sem"]]
                st["marked"] = True

        def emit_engine(name):
            def run(eng):
                seen = {}
                for st in self.eng_steps[name]:
                    req = {}
                    for p in st["dep_steps"]:
                        if p["kind"] == "c":
                            s, v = eng_sems[p["eng"]], p["count"]
                        elif p["kind"] == "cc":
                            s, v = sems[p["sem"]], p["count"]
                        else:
                            continue
                        if req.get(id(s), (None, -1))[1] < v:
                            req[id(s)] = (s, v)
                    for sname, v in st["dma_req"].items():
                        s = sems[sname]
                        if sname in FULL and not (st["kind"] == "d" and st["sem"] == sname):
                            v = totals[sname]
                        if req.get(id(s), (None, -1))[1] < v:
                            req[id(s)] = (s, v)
                    for s, v in req.values():
                        if seen.get(id(s), -1) < v:
                            eng.wait_ge(s, v)
                            seen[id(s)] = v
                    inst = st["emit"](eng)
                    if st["marked"]:
                        if st["kind"] == "c":
                            inst.then_inc(eng_sems[name], 1)
                        elif st["kind"] == "d":
                            inst.then_inc(sems[st["sem"]], 16)
                        else:
                            inst.then_inc(sems[st["sem"]], 1)
                if name == "sync":
                    for sname, v in final_waits:
                        if v > 0:
                            eng.wait_ge(sems[sname], v)
            return run

        block.sync(emit_engine("sync"))
        block.tensor(emit_engine("tensor"))
        block.vector(emit_engine("vector"))
        block.scalar(emit_engine("scalar"))
        block.gpsimd(emit_engine("gpsimd"))


def _stage_core(inp, c):
    f32 = np.float32
    b, h = c // 2, c % 2
    vh0, kh0 = NVH * h, NKH * h
    W = np.asarray(inp["L0_attn_W_qkv"], f32)
    wqkv = np.ascontiguousarray(np.concatenate(
        [W[:, kh0 * DK:(kh0 + NKH) * DK],
         W[:, 2048 + kh0 * DK:2048 + (kh0 + NKH) * DK],
         W[:, 4096 + vh0 * DV:4096 + (vh0 + NVH) * DV]], axis=1))
    cw = np.asarray(inp["L0_attn_conv_w"], f32)
    cw_sl = np.concatenate(
        [cw[kh0 * DK:(kh0 + NKH) * DK], cw[2048 + kh0 * DK:2048 + (kh0 + NKH) * DK],
         cw[4096 + vh0 * DV:4096 + (vh0 + NVH) * DV]], axis=0)
    convw = np.ascontiguousarray(cw_sl.reshape(32, 128, 4).transpose(1, 0, 2).reshape(128, 128))
    wout = np.ascontiguousarray(np.asarray(inp["L0_attn_W_out"], f32))

    def cols128(v):
        n = v.shape[0]
        return np.ascontiguousarray(v.reshape(n, 8, 128).transpose(2, 0, 1).reshape(128, n * 8))

    x = np.asarray(inp["L0_attn_x"], f32)
    gth = (2 * np.arange(4096, dtype=np.int64) + h).astype(np.int32)
    gth = np.ascontiguousarray(gth.reshape(32, 128).T)
    sel = np.zeros((8, 768), np.float32)
    for n in range(6):
        sel[n, n * 128:(n + 1) * 128] = 1.0

    return dict(
        x=np.ascontiguousarray(x[b]),
        xh=np.ascontiguousarray(np.concatenate(
            [x[b, (2 * tb + h) * 512:(2 * tb + h + 1) * 512] for tb in range(4)])),
        wqkv=wqkv, convw=convw,
        wz=np.ascontiguousarray(np.asarray(inp["L0_attn_W_z"], f32)[:, vh0 * DV:(vh0 + NVH) * DV]),
        wba=np.ascontiguousarray(np.concatenate(
            [np.asarray(inp["L0_attn_W_b"], f32)[:, vh0:vh0 + NVH],
             np.asarray(inp["L0_attn_W_a"], f32)[:, vh0:vh0 + NVH]], axis=1)),
        dtb_row=np.ascontiguousarray(np.asarray(inp["L0_attn_dt_bias"], f32)[vh0:vh0 + NVH].reshape(1, NVH)),
        alog_row=np.ascontiguousarray(np.asarray(inp["L0_attn_A_log"], f32)[vh0:vh0 + NVH].reshape(1, NVH)),
        normw_col=np.asarray(inp["L0_attn_norm_w"], f32).reshape(128, 1),
        wout=wout,
        wg=np.ascontiguousarray(np.asarray(inp["L0_ffn_Wg"], f32).transpose(1, 0, 2).reshape(2048, 16)),
        bg_rows=np.ascontiguousarray(np.asarray(inp["L0_ffn_bg"], f32).reshape(1, 16)),
        wup=np.ascontiguousarray(np.asarray(inp["L0_ffn_Wup"], f32).transpose(1, 0, 2).reshape(2048, 6144)),
        bup_cols=cols128(np.asarray(inp["L0_ffn_bup"], f32)),
        wsw=np.ascontiguousarray(np.asarray(inp["L0_ffn_Wsw"], f32).transpose(1, 0, 2).reshape(1024, 6144)),
        bsw_cols=cols128(np.asarray(inp["L0_ffn_bsw"], f32)),
        wdn=np.ascontiguousarray(np.asarray(inp["L0_ffn_Wdn"], f32).reshape(6144, 2048)),
        bdn_rows=np.ascontiguousarray(np.asarray(inp["L0_ffn_bdn"], f32)),
        wsu=np.asarray(inp["L0_ffn_Wsu"], f32),
        bsu_cols=np.ascontiguousarray(np.asarray(inp["L0_ffn_bsu"], f32).reshape(16, 128).T),
        wsa=np.asarray(inp["L0_ffn_Wsa"], f32),
        bsa_cols=np.ascontiguousarray(np.asarray(inp["L0_ffn_bsa"], f32).reshape(8, 128).T),
        wsd=np.asarray(inp["L0_ffn_Wsd"], f32),
        bsd_row=np.ascontiguousarray(np.asarray(inp["L0_ffn_bsd"], f32).reshape(1, 2048)),
        gther=gth,
        ident_bf=np.eye(128, dtype=np.float32).astype(BFNP),
        identf=np.eye(128, dtype=np.float32),
        mski=np.ascontiguousarray(np.triu(np.ones((128, 128), np.float32))),
        msks=np.ascontiguousarray(np.triu(np.ones((128, 128), np.float32), 1)),
        ones_row=np.ones((1, 128), np.float32),
        ones_row_bf=np.ones((1, 128), np.float32).astype(BFNP),
        ones_col=np.ones((128, 1), np.float32),
        ones_col_bf=np.ones((128, 1), np.float32).astype(BFNP),
        selbf=sel.astype(BFNP),
    )


INPUT_DECL = dict(
    x=([T, D], F32), xh=([TB, D], F32),
    wqkv=([D, CONVC], F32), convw=([128, 128], F32), wz=([D, VCC], F32),
    wba=([D, 32], F32), dtb_row=([1, NVH], F32), alog_row=([1, NVH], F32),
    normw_col=([128, 1], F32), wout=([4096, 2048], F32),
    wg=([2048, 16], F32), bg_rows=([1, 16], F32),
    wup=([2048, 6144], F32), bup_cols=([128, 48], F32),
    wsw=([1024, 6144], F32), bsw_cols=([128, 48], F32),
    wdn=([6144, 2048], F32), bdn_rows=([6, 2048], F32),
    wsu=([2048, 2048], F32), bsu_cols=([128, 16], F32),
    wsa=([1024, 1024], F32), bsa_cols=([128, 8], F32),
    wsd=([2048, 2048], F32), bsd_row=([1, 2048], F32),
    gther=([128, 32], I32),
    ident_bf=([128, 128], BF16), identf=([128, 128], F32),
    mski=([128, 128], F32), msks=([128, 128], F32),
    ones_row=([1, 128], F32), ones_row_bf=([1, 128], BF16),
    ones_col=([128, 1], F32), ones_col_bf=([128, 1], BF16),
    selbf=([8, 768], BF16),
)


ARENA_BASE = 22528
ARENA_TOP = 222000


def _layout():
    L = {}
    off = [ARENA_BASE]

    def al(name, nbytes, at=None):
        a = off[0] if at is None else at
        if at is None:
            off[0] = a + nbytes
        L[name] = (a, nbytes)

    al("xn_fm", 16384)
    al("qkvp", 32768)
    al("sz", 16384)
    al("wt0", 8192)
    al("wt1", 8192)
    al("xtm", 8192)
    al("stg", 6144)
    al("cvt", 2048)
    al("strow", 6144)
    al("q_tm", 2048); al("k_tm", 2048); al("v_tm", 4096)
    al("qs_tm", 2048); al("kn_tm", 2048)
    al("vb", 4096); al("kbg", 4096); al("kdec", 4096)
    al("qgi_fm", 4096)
    al("C", 4096); al("A", 4096); al("C2", 4096); al("A2", 4096)
    al("P", 4096); al("P2", 4096)
    al("vnew", 4096); al("y_tm", 4096); al("nrm_fm", 4096)
    al("mdiff", 2048)
    al("egl", 512); al("gch", 2048); al("gT", 2048); al("ba_all", 512)
    al("lncol", 128); al("rstc", 128)
    al("r_bc", 1024); al("mr_bc", 1024); al("halo", 256); al("S_bf", 4096)
    al("bdn_bf", 4096); al("bsd_bf", 4096); al("selbf_t", 2048)
    al("wgt", 512)
    al("m1", 4096, at=L["xtm"][0])
    al("m0", 4096, at=L["xtm"][0] + 4096)
    al("qs_fm", 2048, at=L["q_tm"][0])
    al("kn_fm", 2048, at=L["k_tm"][0])
    al("qgi_tm", 4096, at=L["v_tm"][0])
    al("vc", 4096, at=L["C2"][0])
    al("kcdT", 4096, at=L["A2"][0])
    al("ysq", 4096, at=L["vnew"][0])
    al("Gm", 2048, at=L["vnew"][0])
    al("qkTm", 2048, at=L["vnew"][0] + 2048)
    al("nrmB", 32768, at=L["qkvp"][0])
    for n in range(4):
        al("h2w%d" % n, 8192, at=L["qkvp"][0] + n * 8192)
    al("h2w4", 8192, at=L["sz"][0])
    al("h2w5", 8192, at=L["sz"][0] + 8192)
    al("xt_fm", 16384, at=L["xn_fm"][0])
    al("res0s", 8192, at=L["xtm"][0])
    al("resfs", 8192, at=L["stg"][0])
    al("h_fm", 8192, at=L["cvt"][0])
    al("sil", 8192, at=L["q_tm"][0])
    al("sh_fm", 16384, at=L["C"][0])
    al("sh2", 16384, at=L["P"][0])
    al("xt1b", 1024, at=L["nrm_fm"][0])
    al("wbc", 1024, at=L["r_bc"][0])
    al("gat", 2048, at=L["mdiff"][0])
    al("wfm", 1024, at=L["egl"][0])
    assert off[0] <= ARENA_TOP, off[0]
    return L


LAY = _layout()
assert LAY["strow"][0] == LAY["cvt"][0] + 2048
assert LAY["k_tm"][0] == LAY["q_tm"][0] + 2048 and LAY["v_tm"][0] == LAY["k_tm"][0] + 2048
assert LAY["gch"][0] == LAY["egl"][0] + 512


def granules(*names):
    out = set()
    for n in names:
        a, sz = LAY[n]
        for g in range(a // GRAN, (a + sz + GRAN - 1) // GRAN):
            out.add("g%d" % g)
    return tuple(out)


ARENA_DECL = dict(
    xn_fm=([128, 8192], BF16), qkvp=([128, 16384], BF16), sz=([128, 8192], BF16),
    wt0=([128, 4096], BF16), wt1=([128, 4096], BF16),
    xtm=([128, 4096], BF16), stg=([2, 1536], F32), cvt=([128, 512], F32),
    strow=([1, 1536], F32),
    q_tm=([128, 1024], BF16), k_tm=([128, 1024], BF16), v_tm=([128, 2048], BF16),
    qs_tm=([128, 1024], BF16), kn_tm=([128, 1024], BF16),
    vb=([128, 2048], BF16), kbg=([128, 2048], BF16), kdec=([128, 2048], BF16),
    qgi_fm=([128, 2048], BF16),
    Gm=([128, 1024], BF16), qkTm=([128, 1024], BF16),
    C=([128, 2048], BF16), A=([128, 2048], BF16),
    C2=([128, 2048], BF16), A2=([128, 2048], BF16),
    P=([128, 2048], BF16), P2=([128, 2048], BF16),
    vnew=([128, 2048], BF16), y_tm=([128, 2048], BF16), nrm_fm=([128, 2048], BF16),
    mdiff=([128, 512], F32),
    egl=([128, 64], F32), gch=([128, 448], F32), gT=([16, 512], F32),
    ba_all=([128, 128], F32), lncol=([128, 16], F32), rstc=([128, 16], F32),
    r_bc=([128, 512], BF16), mr_bc=([128, 512], BF16),
    halo=([128, 96], BF16), S_bf=([128, 2048], BF16),
    bdn_bf=([6, 2048], BF16), bsd_bf=([1, 2048], BF16), selbf_t=([8, 768], BF16),
    wgt=([128, 128], BF16),
    m1=([128, 2048], BF16), m0=([128, 2048], BF16),
    qs_fm=([128, 1024], BF16), kn_fm=([128, 1024], BF16), qgi_tm=([128, 2048], BF16),
    vc=([128, 2048], BF16), kcdT=([128, 2048], BF16), ysq=([128, 2048], BF16),
    nrmB=([128, 16384], BF16),
    h2w0=([128, 4096], BF16), h2w1=([128, 4096], BF16), h2w2=([128, 4096], BF16),
    h2w3=([128, 4096], BF16), h2w4=([128, 4096], BF16), h2w5=([128, 4096], BF16),
    xt_fm=([128, 8192], BF16), res0s=([128, 2048], F32), resfs=([128, 2048], F32),
    h_fm=([128, 4096], BF16), sil=([128, 4096], BF16),
    sh_fm=([128, 8192], BF16), sh2=([128, 8192], BF16),
    xt1b=([128, 512], BF16), wbc=([128, 512], BF16),
    gat=([128, 512], F32), wfm=([8, 512], BF16),
)


class Ctx:
    pass


def build(phase_a_only=False, debug_ccin=False, ntiles=NTILES):
    nc = bass.Bass("TRN2", target_bir_lowering=False)
    din = {n: nc.dram_tensor(n, list(s), d, kind="ExternalInput")
           for n, (s, d) in INPUT_DECL.items()}
    y = nc.dram_tensor("y", [TB, D], F32, kind="ExternalOutput")
    dr = {}
    dr["wqkv_bf"] = nc.dram_tensor("wqkv_bf", [16, 128, 16, 256], BF16)
    dr["wz_bf"] = nc.dram_tensor("wz_bf", [8, 128, 16, 256], BF16)
    dr["wba_bf"] = nc.dram_tensor("wba_bf", [2048, 32], BF16)
    dr["wg_bf"] = nc.dram_tensor("wg_bf", [2048, 16], BF16)
    dr["wup_bf"] = nc.dram_tensor("wup_bf", [24, 128, 16, 256], BF16)
    dr["wsw_bf"] = nc.dram_tensor("wsw_bf", [24, 128, 8, 256], BF16)
    dr["wsu_bf"] = nc.dram_tensor("wsu_bf", [8, 128, 16, 256], BF16)
    dr["wsa_bf"] = nc.dram_tensor("wsa_bf", [4, 128, 8, 256], BF16)
    dr["wdn_bf"] = nc.dram_tensor("wdn_bf", [4, 48, 128, 512], BF16)
    dr["wsd_bf"] = nc.dram_tensor("wsd_bf", [4, 16, 128, 512], BF16)
    dr["wout_bf"] = nc.dram_tensor("wout_bf", [4, 32, 128, 512], BF16)
    dr["cc_in"] = nc.dram_tensor("cc_in", [4, VCC, 1024], BF16,
                                 kind="ExternalOutput" if debug_ccin else "Internal")
    for q in range(4):
        dr["cc_out%d" % q] = nc.dram_tensor("cc_out%d" % q, [2 * VCC, 1024], BF16)
    dr["res0_d"] = nc.dram_tensor("res0_d", [TB, D], F32)
    dr["resf_d"] = nc.dram_tensor("resf_d", [TB, D], F32)

    from contextlib import ExitStack
    with ExitStack() as stk:
        E = stk.enter_context
        sb = {}
        for n, shape, dt in (
                ("ident_bf", [128, 128], BF16), ("identf", [128, 128], F32),
                ("mski", [128, 128], F32), ("msks", [128, 128], F32),
                ("convw", [128, 128], F32),
                ("ones_row", [1, 128], F32), ("ones_row_bf", [1, 128], BF16),
                ("ones_col", [128, 1], F32), ("ones_col_bf", [128, 1], BF16),
                ("normw_col", [128, 1], F32),
                ("bup_cols", [128, 48], F32), ("bsw_cols", [128, 48], F32),
                ("bsu_cols", [128, 16], F32), ("bsa_cols", [128, 8], F32),
                ("bg_rows", [1, 16], F32), ("gth_idx", [128, 32], I32),
                ("dtb_bc", [128, 16], F32), ("acoef_bc", [128, 16], F32),
                ("rowtmp", [1, 16], F32), ("rowtmp2", [1, 16], F32),
                ("halo2", [128, 96], BF16),
                ("wba_t", [128, 512], BF16), ("bg_bf", [1, 16], BF16)):
            sb[n] = nc.alloc_sbuf_tensor("c_" + n, shape, dt, side="right")
        for n, (shape, dt) in ARENA_DECL.items():
            a, sz_ = LAY[n]
            nb = shape[1] * (2 if dt in (BF16,) else 4)
            assert nb <= sz_, (n, nb, sz_)
            sb[n] = nc.alloc_sbuf_tensor_at(n, shape, dt, offset=a)

        pb = {}
        for i in range(6):
            pb["pf%d" % i] = E(nc.psum_tensor("pf%d" % i, [128, 512], F32))
        pb["pt0"] = E(nc.psum_tensor("pt0", [128, 1024], BF16))
        pb["pt1"] = E(nc.psum_tensor("pt1", [128, 1024], BF16))

        sems = {}
        for s in ("dW0", "dW1", "dX", "dST", "dAUX", "dRS", "dGT", "_cc",
                  "dOUT", "dRES", "gAUX", "gW0", "gW1", "gX", "dNF", "gCAST"):
            sems[s] = E(nc.semaphore(s))
        eng_sems = {e: E(nc.semaphore("se_" + e)) for e in ENGS}

        block = E(nc.Block())
        P = Prog()
        cx = Ctx()
        cx.nc, cx.P, cx.sb, cx.pb, cx.din, cx.dr, cx.y = nc, P, sb, pb, din, dr, y
        cx.G = granules

        def make_coll(q):
            return lambda: P.coll(lambda e, q=q: e.collective_compute(
                "AllGather", ALU.bypass,
                replica_groups=[[0, 1], [2, 3], [4, 5], [6, 7]],
                ins=[dr["cc_in"].ap()[q]], outs=[dr["cc_out%d" % q].ap()]),
                reads=("cc_in_q%d" % q,), writes=("cc_out%d" % q,))

        cx.coll_at_tile = {} if phase_a_only else {2 * q + 2: make_coll(q)
                                                  for q in range(3)}
        cx.coll_b0 = make_coll(3)

        emit_prep(cx)
        for t in range(ntiles):
            emit_phaseA_tile(cx, t)
        if not phase_a_only:
            for tb in range(NBT):
                emit_phaseB_tile(cx, tb)
        final = []
        if phase_a_only or debug_ccin:
            final.append(("dST", 16 * P.dma_issued.get("dST", 0)))
            final.append(("_cc", P.dma_issued.get("_cc", 0)))
        if not phase_a_only:
            final.append(("dOUT", 16 * P.dma_issued.get("dOUT", 0)))
        P.finalize(block, sems, eng_sems, final)
    return nc


def emit_prep(cx):
    P, sb, pb, din, dr, G = cx.P, cx.sb, cx.pb, cx.din, cx.dr, cx.G

    for n in ("identf", "mski", "msks", "convw", "ones_row", "ones_col",
              "normw_col", "bup_cols", "bsw_cols", "bsu_cols", "bsa_cols",
              "bg_rows"):
        P.dma("sync", "dAUX", lambda e, n=n: e.dma_start(out=sb[n][:], in_=din[n].ap()),
              reads=(), writes=(n,))
    for n in ("ident_bf", "ones_row_bf", "ones_col_bf", "selbf"):
        dst = "selbf_t" if n == "selbf" else n
        P.dma("sync", "dAUX", lambda e, n=n, dst=dst: e.dma_start(out=sb[dst][:], in_=din[n].ap()),
              reads=(), writes=(dst,) if dst != "selbf_t" else G("selbf_t"))
    P.dma("gpsimd", "gAUX", lambda e: e.dma_start(out=sb["bdn_bf"][:], in_=din["bdn_rows"].ap()),
          reads=(), writes=G("bdn_bf"))
    P.dma("gpsimd", "gAUX", lambda e: e.dma_start(out=sb["bsd_bf"][:], in_=din["bsd_row"].ap()),
          reads=(), writes=G("bsd_bf"))
    P.dma("gpsimd", "gAUX", lambda e: e.dma_start(out=sb["bg_bf"][:], in_=din["bg_rows"].ap()),
          reads=(), writes=("bg_bf",))
    P.dma("sync", "dAUX", lambda e: e.dma_start(
        out=sb["gth_idx"][:], in_=din["gther"].ap()),
        reads=(), writes=("gth_idx",))

    P.dma("sync", "dAUX", lambda e: e.dma_start(out=sb["rowtmp"][:], in_=din["dtb_row"].ap()),
          reads=(), writes=("rowtmp",))
    P.c("tensor", lambda e: e.matmul(pb["pf0"][:, 0:16], sb["ones_row"][:],
                                     sb["rowtmp"][:], start=True, stop=True),
        reads=("rowtmp", "ones_row"), writes=("pf0",))
    P.c("vector", lambda e: e.tensor_copy(sb["dtb_bc"][:], pb["pf0"][:, 0:16]),
        reads=("pf0",), writes=("dtb_bc",))
    P.dma("sync", "dAUX", lambda e: e.dma_start(out=sb["rowtmp2"][:], in_=din["alog_row"].ap()),
          reads=(), writes=("rowtmp2",))
    P.c("scalar", lambda e: e.activation(sb["rowtmp2"][:], sb["rowtmp2"][:], AF.Exp),
        reads=("rowtmp2",), writes=("rowtmp2",))
    P.c("tensor", lambda e: e.matmul(pb["pf0"][:, 0:16], sb["ones_row"][:],
                                     sb["rowtmp2"][:], start=True, stop=True),
        reads=("rowtmp2", "ones_row"), writes=("pf0",))
    P.c("vector", lambda e: e.tensor_scalar_mul(sb["acoef_bc"][:], pb["pf0"][:, 0:16], -1.0),
        reads=("pf0",), writes=("acoef_bc",))

    cx.deferred = []

    def cast_round(dst, emit, defer):
        def rnd():
            st = P.dma("gpsimd", "gCAST", emit, reads=(), writes=())
            P.last_write[dst] = st
        cx.deferred.append(rnd) if defer else rnd()

    def cast_gpkc(src, dst, rows, kdim, gdim, defer=False):
        for k in range(rows // 128):
            for gb in range(0, gdim, 16):
                gn = min(16, gdim - gb)
                cast_round(dst, lambda e, k=k, gb=gb, gn=gn: e.dma_start(
                    out=dr[dst].ap()[gb:gb + gn, :, k, :].transpose([1, 0, 2]),
                    in_=din[src].ap()[k * 128:(k + 1) * 128, gb * 256:(gb + gn) * 256]
                    .rearrange("p (g c) -> p g c", g=gn)), defer)

    def cast_hgrc(src, dst, rows, hg, defer=False):
        for r in range(rows // 128):
            cast_round(dst, lambda e, r=r: e.dma_start(
                out=dr[dst].ap()[:, r, :, :].transpose([1, 0, 2]),
                in_=din[src].ap()[r * 128:(r + 1) * 128, :]
                .rearrange("p (g c) -> p g c", g=hg)), defer)

    cast_gpkc("wqkv", "wqkv_bf", 2048, 16, 16)
    cast_gpkc("wz", "wz_bf", 2048, 16, 8)
    for k in range(16):
        P.dma("gpsimd", "gW0", lambda e, k=k: e.dma_start(
            out=sb["wba_t"][:, k * 32:(k + 1) * 32], in_=din["wba"].ap()[k * 128:(k + 1) * 128, :]),
            reads=("wba_t",), writes=("wba_t",))
    for k in range(16):
        cast_round("wg_bf", lambda e, k=k: e.dma_start(
            out=dr["wg_bf"].ap()[k * 128:(k + 1) * 128, :],
            in_=din["wg"].ap()[k * 128:(k + 1) * 128, :]), False)

    cast_hgrc("wout", "wout_bf", 4096, 4, defer=True)
    cast_gpkc("wup", "wup_bf", 2048, 16, 24, defer=True)
    cast_gpkc("wsw", "wsw_bf", 1024, 8, 24, defer=True)
    cast_gpkc("wsu", "wsu_bf", 2048, 16, 8, defer=True)
    cast_gpkc("wsa", "wsa_bf", 1024, 8, 4, defer=True)
    cast_hgrc("wdn", "wdn_bf", 6144, 4, defer=True)
    cast_hgrc("wsd", "wsd_bf", 2048, 4, defer=True)

    P.c("vector", lambda e: e.memset(sb["S_bf"][:], 0.0), reads=(), writes=G("S_bf"))
    P.c("vector", lambda e: e.memset(sb["halo"][:], 0.0), reads=(), writes=G("halo"))
    P.c("vector", lambda e: e.memset(sb["stg"][0:2, 0:512], 1.0), reads=(), writes=("stgc0",))
    P.c("vector", lambda e: e.memset(sb["stg"][0:2, 512:1536], -1.0),
        reads=(), writes=("stgc1", "stgc2"))


def emit_phaseA_tile(cx, t):
    P, sb, pb, din, dr, G = cx.P, cx.sb, cx.pb, cx.din, cx.dr, cx.G
    t0 = t * TT

    nleft = NTILES - t
    nbatch = (len(cx.deferred) + nleft - 1) // nleft if cx.deferred else 0
    batch = cx.deferred[:nbatch]
    del cx.deferred[:nbatch]
    cx.tile_casts = batch[(len(batch) + 1) // 2:]

    for rnd in range(2):
        for s2 in range(2):
            s = rnd * 2 + s2
            P.dma("gpsimd", "gX", lambda e, s=s, s2=s2: e.dma_start(
                out=sb["xtm"][:, s2 * 2048:(s2 + 1) * 2048],
                in_=din["x"].ap()[t0 + s * 128:t0 + (s + 1) * 128, :]),
                reads=G("xtm"), writes=G("xtm"))
        for s2 in range(2):
            s = rnd * 2 + s2
            for fg in range(2):
                bank = "pt%d" % fg
                for f2 in range(8):
                    f = fg * 8 + f2
                    P.c("tensor", lambda e, s2=s2, f=f, f2=f2, bank=bank: e.transpose(
                        pb[bank][:, f2 * 128:(f2 + 1) * 128],
                        sb["xtm"][:, s2 * 2048 + f * 128:s2 * 2048 + (f + 1) * 128],
                        sb["ident_bf"][:]),
                        reads=G("xtm") + ("ident_bf",), writes=(bank,))
                P.c("scalar", lambda e, s=s, fg=fg, bank=bank: e.activation(
                    bass.AP(sb["xn_fm"], fg * 8 * 512 + s * 128, [[8192, 128], [512, 8], [1, 128]]),
                    pb[bank][:].rearrange("p (f c) -> p f c", f=8), AF.Copy),
                    reads=(bank,), writes=G("xn_fm"))
    for cast_rnd in batch[:(len(batch) + 1) // 2]:
        cast_rnd()

    for k in range(16):
        P.c("tensor", lambda e, k=k: e.matmul(
            pb["pf4"][0:1, :], sb["ones_col_bf"][:], sb["xn_fm"][:, k * 512:(k + 1) * 512],
            start=(k == 0), stop=(k == 15)),
            reads=G("xn_fm") + ("ones_col_bf",), writes=("pf4",))
    for k in range(16):
        P.c("scalar", lambda e, k=k: e.activation(
            sb["r_bc"][:, 0:512], sb["xn_fm"][:, k * 512:(k + 1) * 512], AF.Square),
            reads=G("xn_fm"), writes=G("r_bc"))
        P.c("tensor", lambda e, k=k: e.matmul(
            pb["pf5"][0:1, :], sb["ones_col_bf"][:], sb["r_bc"][:, 0:512],
            start=(k == 0), stop=(k == 15)),
            reads=G("r_bc") + ("ones_col_bf",), writes=("pf5",))
    P.c("vector", lambda e: e.tensor_scalar_mul(sb["strow"][0:1, 0:512], pb["pf4"][0:1, :], 1.0 / D),
        reads=("pf4",), writes=G("strow"))
    P.c("vector", lambda e: e.tensor_scalar_mul(sb["strow"][0:1, 512:1024], pb["pf5"][0:1, :], 1.0 / D),
        reads=("pf5",), writes=G("strow"))
    P.c("vector", lambda e: e.scalar_tensor_tensor(
        sb["strow"][0:1, 1024:1536], sb["strow"][0:1, 0:512], -1.0, sb["strow"][0:1, 0:512],
        ALU.mult, ALU.mult), reads=G("strow"), writes=G("strow"))
    P.c("vector", lambda e: e.tensor_add(sb["strow"][0:1, 1024:1536],
                                         sb["strow"][0:1, 1024:1536], sb["strow"][0:1, 512:1024]),
        reads=G("strow"), writes=G("strow"))
    P.c("vector", lambda e: e.tensor_scalar_add(sb["strow"][0:1, 1024:1536],
                                                sb["strow"][0:1, 1024:1536], LN_EPS),
        reads=G("strow"), writes=G("strow"))
    P.c("scalar", lambda e: e.activation(sb["strow"][0:1, 512:1024], sb["strow"][0:1, 1024:1536],
                                         AF.Ln), reads=G("strow"), writes=G("strow"))
    P.c("scalar", lambda e: e.activation(sb["strow"][0:1, 512:1024], sb["strow"][0:1, 512:1024],
                                         AF.Exp, scale=-0.5), reads=G("strow"), writes=G("strow"))
    P.c("vector", lambda e: e.tensor_mul(sb["strow"][0:1, 0:512], sb["strow"][0:1, 0:512],
                                         sb["strow"][0:1, 512:1024]),
        reads=G("strow"), writes=G("strow"))
    P.c("tensor", lambda e: e.matmul(pb["pf4"][:, :], sb["ones_row"][:],
                                     sb["strow"][0:1, 512:1024], start=True, stop=True),
        reads=G("strow") + ("ones_row",), writes=("pf4",))
    P.c("scalar", lambda e: e.activation(sb["r_bc"][:, 0:512], pb["pf4"][:, :], AF.Copy),
        reads=("pf4",), writes=G("r_bc"))
    P.c("tensor", lambda e: e.matmul(pb["pf5"][:, :], sb["ones_row"][:],
                                     sb["strow"][0:1, 0:512], start=True, stop=True),
        reads=G("strow") + ("ones_row",), writes=("pf5",))
    P.c("scalar", lambda e: e.activation(sb["mr_bc"][:, 0:512], pb["pf5"][:, :], AF.Copy),
        reads=("pf5",), writes=G("mr_bc"))
    for k in range(16):
        eng = "gpsimd" if k % 2 == 0 else "vector"
        P.c(eng, lambda e, k=k: e.tensor_mul(
            sb["xn_fm"][:, k * 512:(k + 1) * 512], sb["xn_fm"][:, k * 512:(k + 1) * 512],
            sb["r_bc"][:, 0:512]),
            reads=G("xn_fm", "r_bc"), writes=G("xn_fm"))
        P.c(eng, lambda e, k=k: e.tensor_sub(
            sb["xn_fm"][:, k * 512:(k + 1) * 512], sb["xn_fm"][:, k * 512:(k + 1) * 512],
            sb["mr_bc"][:, 0:512]),
            reads=G("xn_fm", "mr_bc"), writes=G("xn_fm"))

    for g in range(16):
        slot = g % 2
        buf = "wt%d" % slot
        P.dma("sync", "dW%d" % slot, lambda e, g=g, buf=buf: e.dma_start(
            out=sb[buf][:, 0:4096], in_=dr["wqkv_bf"].ap()[g]),
            reads=("wqkv_bf",), writes=G(buf))
        for oc2 in range(2):
            oc = g * 2 + oc2
            bank = "pf%d" % (oc % 4)
            for k in range(16):
                P.c("tensor", lambda e, oc2=oc2, k=k, buf=buf, bank=bank: e.matmul(
                    pb[bank][:, :], sb[buf][:, k * 256 + oc2 * 128:k * 256 + oc2 * 128 + 128],
                    sb["xn_fm"][:, k * 512:(k + 1) * 512], start=(k == 0), stop=(k == 15)),
                    reads=G(buf, "xn_fm"), writes=(bank,))
            P.c("scalar", lambda e, oc=oc, bank=bank: e.activation(
                sb["qkvp"][:, oc * 512:(oc + 1) * 512], pb[bank][:, :], AF.Copy),
                reads=(bank,), writes=G("qkvp"))

    for g in range(8):
        slot = g % 2
        buf = "wt%d" % slot
        P.dma("sync", "dW%d" % slot, lambda e, g=g, buf=buf: e.dma_start(
            out=sb[buf][:, 0:4096], in_=dr["wz_bf"].ap()[g]),
            reads=("wz_bf",), writes=G(buf))
        for s in range(4):
            bank = "pf%d" % (s % 4)
            for k in range(16):
                P.c("tensor", lambda e, s=s, k=k, buf=buf, bank=bank: e.matmul(
                    pb[bank][:, 0:256], sb["xn_fm"][:, k * 512 + s * 128:k * 512 + (s + 1) * 128],
                    sb[buf][:, k * 256:(k + 1) * 256], start=(k == 0), stop=(k == 15)),
                    reads=G(buf, "xn_fm"), writes=(bank,))
            P.c("scalar", lambda e, s=s, g=g, bank=bank: e.activation(
                sb["sz"][:, s * 2048 + g * 256:s * 2048 + (g + 1) * 256],
                pb[bank][:, 0:256], AF.Silu),
                reads=(bank,), writes=G("sz"))

    for s in range(4):
        for k in range(16):
            P.c("tensor", lambda e, s=s, k=k: e.matmul(
                pb["pf4"][:, 0:32], sb["xn_fm"][:, k * 512 + s * 128:k * 512 + (s + 1) * 128],
                sb["wba_t"][:, k * 32:(k + 1) * 32], start=(k == 0), stop=(k == 15)),
                reads=("wba_t",) + G("xn_fm"), writes=("pf4",))
        P.c("vector", lambda e, s=s: e.tensor_copy(sb["ba_all"][:, s * 32:(s + 1) * 32],
                                                   pb["pf4"][:, 0:32]),
            reads=("pf4",), writes=G("ba_all"))

    for rnd in cx.tile_casts:
        rnd()
    cx.tile_casts = []
    if t in cx.coll_at_tile:
        cx.coll_at_tile.pop(t)()

    P.c("gpsimd", lambda e: e.tensor_copy(
        sb["halo2"][:].rearrange("p (c k) -> p c k", c=32),
        bass.AP(sb["qkvp"], 509, [[16384, 128], [512, 32], [1, 3]])),
        reads=G("qkvp"), writes=("halo2",))
    for cc in range(32):
        eng = "vector"
        scr = "cvt" if cc % 2 == 0 else "mdiff"
        base = cc * 512
        P.c(eng, lambda e, cc=cc, base=base, scr=scr: e.tensor_scalar_mul(
            sb[scr][:, 0:512], sb["qkvp"][:, base:base + 512],
            sb["convw"][:, cc * 4 + 3:cc * 4 + 4]),
            reads=G("qkvp") + ("convw",), writes=G(scr))
        for k in range(3):
            nh = 3 - k
            P.c(eng, lambda e, cc=cc, base=base, k=k, nh=nh, scr=scr: e.scalar_tensor_tensor(
                sb[scr][:, nh:512], sb["qkvp"][:, base:base + 512 - nh],
                sb["convw"][:, cc * 4 + k:cc * 4 + k + 1], sb[scr][:, nh:512],
                ALU.mult, ALU.add),
                reads=G("qkvp", scr) + ("convw",), writes=G(scr))
            P.c(eng, lambda e, cc=cc, k=k, nh=nh, scr=scr: e.scalar_tensor_tensor(
                sb[scr][:, 0:nh], sb["halo"][:, cc * 3 + k:cc * 3 + k + nh],
                sb["convw"][:, cc * 4 + k:cc * 4 + k + 1], sb[scr][:, 0:nh],
                ALU.mult, ALU.add),
                reads=G("halo", scr) + ("convw",), writes=G(scr))
        P.c(eng, lambda e, cc=cc, base=base, scr=scr: e.tensor_copy(
            sb["qkvp"][:, base:base + 512], sb[scr][:, 0:512]),
            reads=G(scr) + ("halo2",), writes=G("qkvp"))
    P.c("gpsimd", lambda e: e.tensor_copy(sb["halo"][:], sb["halo2"][:]),
        reads=("halo2",) + G("halo"), writes=G("halo"))

    gch = sb["gch"]
    bvw = bass.AP(sb["ba_all"], 0, [[128, 128], [32, 4], [1, 16]])
    avw = bass.AP(sb["ba_all"], 16, [[128, 128], [32, 4], [1, 16]])
    LNB, GC, R1, BGF, BET, KDF, EG = 0, 64, 128, 192, 256, 320, 384

    def gv(off):
        return gch[:, off:off + 64]
    P.c("vector", lambda e: e.tensor_scalar_min(gv(LNB), bvw, 0.0),
        reads=G("ba_all"), writes=G("gch"))
    P.c("scalar", lambda e: e.activation(gv(R1), bvw, AF.Abs),
        reads=G("ba_all"), writes=G("gch"))
    P.c("scalar", lambda e: e.activation(gv(R1), gv(R1), AF.Exp, scale=-1.0),
        reads=G("gch"), writes=G("gch"))
    P.c("scalar", lambda e: e.activation(gv(R1), gv(R1), AF.Ln, bias=1.0),
        reads=G("gch"), writes=G("gch"))
    P.c("vector", lambda e: e.tensor_sub(gv(LNB), gv(LNB), gv(R1)),
        reads=G("gch"), writes=G("gch"))
    P.c("vector", lambda e: e.tensor_tensor(
        gv(GC).rearrange("p (c v) -> p c v", c=4), avw,
        sb["dtb_bc"][:].unsqueeze(1).broadcast_to([128, 4, 16]), ALU.add),
        reads=G("ba_all") + ("dtb_bc",), writes=G("gch"))
    P.c("scalar", lambda e: e.activation(gv(BGF), gv(GC), AF.Abs),
        reads=G("gch"), writes=G("gch"))
    P.c("scalar", lambda e: e.activation(gv(BGF), gv(BGF), AF.Exp, scale=-1.0),
        reads=G("gch"), writes=G("gch"))
    P.c("scalar", lambda e: e.activation(gv(BGF), gv(BGF), AF.Ln, bias=1.0),
        reads=G("gch"), writes=G("gch"))
    P.c("vector", lambda e: e.tensor_scalar_max(gv(GC), gv(GC), 0.0),
        reads=G("gch"), writes=G("gch"))
    P.c("vector", lambda e: e.tensor_add(gv(GC), gv(GC), gv(BGF)),
        reads=G("gch"), writes=G("gch"))
    P.c("vector", lambda e: e.tensor_tensor(
        gv(GC).rearrange("p (c v) -> p c v", c=4), gv(GC).rearrange("p (c v) -> p c v", c=4),
        sb["acoef_bc"][:].unsqueeze(1).broadcast_to([128, 4, 16]), ALU.mult),
        reads=G("gch") + ("acoef_bc",), writes=G("gch"))
    for ci in range(CPT):
        P.c("tensor", lambda e, ci=ci: e.matmul(
            pb["pf5"][:, ci * 16:(ci + 1) * 16], sb["mski"][:],
            gch[:, GC + ci * 16:GC + (ci + 1) * 16], start=True, stop=True),
            reads=G("gch") + ("mski",), writes=("pf5",))
    P.c("vector", lambda e: e.tensor_copy(gv(GC), pb["pf5"][:, 0:64]),
        reads=("pf5",), writes=G("gch"))
    P.c("vector", lambda e: e.tensor_add(gv(R1), gv(GC), gv(LNB)),
        reads=G("gch"), writes=G("gch"))
    P.c("scalar", lambda e: e.activation(gv(BGF), gv(R1), AF.Exp),
        reads=G("gch"), writes=G("gch"))
    P.c("scalar", lambda e: e.activation(gv(BET), gv(LNB), AF.Exp),
        reads=G("gch"), writes=G("gch"))
    P.c("scalar", lambda e: e.activation(gv(EG), gv(GC), AF.Exp),
        reads=G("gch"), writes=G("gch"))

    for ci in range(CPT):
        emit_chunk(cx, t, ci)


def trans_group(cx, srcs, dst, evac_af=AF.Copy, evac_engs=("scalar",)):
    P, sb, pb, G = cx.P, cx.sb, cx.pb, cx.G
    for grp in range((len(srcs) + 7) // 8):
        bank = "pt%d" % (grp % 2)
        blk = srcs[grp * 8:(grp + 1) * 8]
        for j2, (tn, co) in enumerate(blk):
            P.c("tensor", lambda e, tn=tn, co=co, j2=j2, bank=bank: e.transpose(
                pb[bank][:, j2 * 128:(j2 + 1) * 128], sb[tn][:, co:co + 128],
                sb["ident_bf"][:]),
                reads=G(tn) + ("ident_bf",), writes=(bank,))
        eng = evac_engs[grp % len(evac_engs)]
        if eng == "scalar":
            P.c("scalar", lambda e, grp=grp, n=len(blk), bank=bank, dst=dst, af=evac_af: e.activation(
                sb[dst][:, grp * 1024:grp * 1024 + n * 128], pb[bank][:, 0:n * 128], af),
                reads=(bank,), writes=G(dst))
        else:
            P.c(eng, lambda e, grp=grp, n=len(blk), bank=bank, dst=dst: e.tensor_copy(
                sb[dst][:, grp * 1024:grp * 1024 + n * 128], pb[bank][:, 0:n * 128]),
                reads=(bank,), writes=G(dst))


def emit_chunk(cx, t, ci):
    P, sb, pb, G = cx.P, cx.sb, cx.pb, cx.G
    t0 = t * TT + ci * CH
    gch = sb["gch"]
    LNB, GC, R1, BGF, BET, KDF, EG = 0, 64, 128, 192, 256, 320, 384
    cs = ci * 16

    P.c("tensor", lambda e, cs=cs: e.transpose(pb["pf5"][0:16, 128:256],
                                               gch[:, GC + cs:GC + cs + 16], sb["identf"][:]),
        reads=G("gch") + ("identf",), writes=("pf5",))
    P.c("vector", lambda e: e.tensor_copy(sb["gT"][:, 0:128], pb["pf5"][0:16, 128:256]),
        reads=("pf5",), writes=G("gT"))
    P.c("tensor", lambda e, cs=cs: e.transpose(pb["pf5"][0:16, 256:384],
                                               gch[:, R1 + cs:R1 + cs + 16], sb["identf"][:]),
        reads=G("gch") + ("identf",), writes=("pf5",))
    P.c("vector", lambda e: e.tensor_copy(sb["gT"][:, 128:256], pb["pf5"][0:16, 256:384]),
        reads=("pf5",), writes=G("gT"))
    P.c("tensor", lambda e: e.matmul(
        pb["pf5"][0:1, 0:16], sb["gT"][:, 127:128], sb["identf"][0:16, 0:16],
        start=True, stop=True),
        reads=G("gT") + ("identf",), writes=("pf5",))
    P.c("vector", lambda e: e.tensor_copy(sb["rowtmp"][:], pb["pf5"][0:1, 0:16]),
        reads=("pf5",), writes=("rowtmp",))
    P.c("tensor", lambda e: e.matmul(
        pb["pf5"][:, 0:16], sb["ones_row"][:], sb["rowtmp"][:], start=True, stop=True),
        reads=("rowtmp", "ones_row"), writes=("pf5",))
    P.c("scalar", lambda e, ci=ci: e.activation(
        sb["egl"][:, ci * 16:(ci + 1) * 16], pb["pf5"][:, 0:16], AF.Exp),
        reads=("pf5",), writes=G("egl"))
    P.c("vector", lambda e, cs=cs: e.tensor_sub(gch[:, KDF + cs:KDF + cs + 16],
                                                pb["pf5"][:, 0:16], gch[:, GC + cs:GC + cs + 16]),
        reads=("pf5",) + G("gch"), writes=G("gch"))
    P.c("scalar", lambda e, cs=cs: e.activation(gch[:, KDF + cs:KDF + cs + 16],
                                                gch[:, KDF + cs:KDF + cs + 16], AF.Exp),
        reads=G("gch"), writes=G("gch"))

    trans_group(cx, [("qkvp", j * 512 + ci * 128) for j in range(0, 8)], "q_tm", AF.Silu)
    trans_group(cx, [("qkvp", j * 512 + ci * 128) for j in range(8, 16)], "k_tm", AF.Silu)
    trans_group(cx, [("qkvp", j * 512 + ci * 128) for j in range(16, 32)], "v_tm", AF.Silu)

    for (src, col0) in (("q_tm", 0), ("k_tm", 8)):
        P.c("scalar", lambda e, src=src: e.activation(sb["ysq"][:, 0:1024], sb[src][:], AF.Square),
            reads=G(src), writes=G("ysq"))
        P.c("vector", lambda e, col0=col0: e.tensor_reduce(
            sb["lncol"][:, col0:col0 + 8],
            sb["ysq"][:, 0:1024].rearrange("p (h d) -> p h d", h=8), AX.X, ALU.add),
            reads=G("ysq"), writes=G("lncol"))
    P.c("vector", lambda e: e.tensor_scalar_add(sb["lncol"][:, 0:16], sb["lncol"][:, 0:16],
                                                L2_EPS),
        reads=G("lncol"), writes=G("lncol"))
    P.c("scalar", lambda e: e.activation(sb["lncol"][:, 0:16], sb["lncol"][:, 0:16],
                                         AF.Ln), reads=G("lncol"), writes=G("lncol"))
    P.c("scalar", lambda e: e.activation(sb["lncol"][:, 0:16], sb["lncol"][:, 0:16],
                                         AF.Exp, scale=-0.5), reads=G("lncol"), writes=G("lncol"))
    P.c("vector", lambda e: e.tensor_scalar_mul(sb["lncol"][:, 0:8], sb["lncol"][:, 0:8],
                                                float(128.0 ** -0.5)),
        reads=G("lncol"), writes=G("lncol"))
    P.c("vector", lambda e: e.tensor_tensor(
        sb["qs_tm"][:].rearrange("p (h d) -> p h d", h=8),
        sb["q_tm"][:].rearrange("p (h d) -> p h d", h=8),
        sb["lncol"][:, 0:8].unsqueeze(2).broadcast_to([128, 8, 128]), ALU.mult),
        reads=G("q_tm", "lncol"), writes=G("qs_tm"))
    P.c("vector", lambda e: e.tensor_tensor(
        sb["kn_tm"][:].rearrange("p (h d) -> p h d", h=8),
        sb["k_tm"][:].rearrange("p (h d) -> p h d", h=8),
        sb["lncol"][:, 8:16].unsqueeze(2).broadcast_to([128, 8, 128]), ALU.mult),
        reads=G("k_tm", "lncol"), writes=G("kn_tm"))
    P.c("gpsimd", lambda e, cs=cs: e.tensor_tensor(
        sb["vb"][:].rearrange("p (v d) -> p v d", v=16),
        sb["v_tm"][:].rearrange("p (v d) -> p v d", v=16),
        gch[:, BET + cs:BET + cs + 16].unsqueeze(2).broadcast_to([128, 16, 128]), ALU.mult),
        reads=G("v_tm", "gch"), writes=G("vb"))
    P.c("gpsimd", lambda e, cs=cs: e.tensor_tensor(
        sb["kbg"][:].rearrange("p (h u d) -> p h u d", h=8, u=2),
        sb["kn_tm"][:].rearrange("p (h d) -> p h d", h=8)
        .unsqueeze(2).broadcast_to([128, 8, 2, 128]),
        gch[:, BGF + cs:BGF + cs + 16].rearrange("p (h u) -> p h u", h=8)
        .unsqueeze(3).broadcast_to([128, 8, 2, 128]), ALU.mult),
        reads=G("kn_tm", "gch"), writes=G("kbg"))
    P.c("vector", lambda e, cs=cs: e.tensor_tensor(
        sb["kdec"][:].rearrange("p (h u d) -> p h u d", h=8, u=2),
        sb["kn_tm"][:].rearrange("p (h d) -> p h d", h=8)
        .unsqueeze(2).broadcast_to([128, 8, 2, 128]),
        gch[:, KDF + cs:KDF + cs + 16].rearrange("p (h u) -> p h u", h=8)
        .unsqueeze(3).broadcast_to([128, 8, 2, 128]), ALU.mult),
        reads=G("kn_tm", "gch"), writes=G("kdec"))
    P.c("vector", lambda e, cs=cs: e.tensor_tensor(
        sb["qgi_tm"][:].rearrange("p (h u d) -> p h u d", h=8, u=2),
        sb["qs_tm"][:].rearrange("p (h d) -> p h d", h=8)
        .unsqueeze(2).broadcast_to([128, 8, 2, 128]),
        gch[:, EG + cs:EG + cs + 16].rearrange("p (h u) -> p h u", h=8)
        .unsqueeze(3).broadcast_to([128, 8, 2, 128]), ALU.mult),
        reads=G("qs_tm", "gch"), writes=G("qgi_tm"))

    trans_group(cx, [("qs_tm", j * 128) for j in range(8)], "qs_fm")
    trans_group(cx, [("kn_tm", j * 128) for j in range(8)], "kn_fm",
                evac_engs=("vector",))
    trans_group(cx, [("qgi_tm", j * 128) for j in range(16)], "qgi_fm",
                evac_engs=("scalar", "vector"))

    for rd in range(4):
        P.dma("scalar", "dRS", lambda e, rd=rd: e.dma_start(
            out=sb["stg"][1:2, 0:512], in_=sb["gT"][rd * 4:rd * 4 + 4, 0:128]),
            reads=G("gT"), writes=("stg_l",))
        P.dma("scalar", "dRS", lambda e, rd=rd: e.dma_start(
            out=sb["stg"][0:1, 512:1024], in_=sb["gT"][rd * 4:rd * 4 + 4, 0:128]),
            reads=G("gT"), writes=("stg_r0",))
        P.dma("scalar", "dRS", lambda e, rd=rd: e.dma_start(
            out=sb["stg"][0:1, 1024:1536], in_=sb["gT"][rd * 4:rd * 4 + 4, 128:256]),
            reads=G("gT"), writes=("stg_r1",))
        for v4 in range(4):
            P.c("tensor", lambda e, v4=v4: e.matmul(
                pb["pf0"][:, v4 * 128:(v4 + 1) * 128],
                bass.AP(sb["stg"], v4 * 128, [[1536, 2], [1, 128]]),
                bass.AP(sb["stg"], 1024 + v4 * 128, [[1536, 2], [1, 128]]),
                start=True, stop=True),
                reads=("stg_l", "stg_r1", "stgc0", "stgc2"), writes=("pf0",))
            P.c("tensor", lambda e, v4=v4: e.matmul(
                pb["pf1"][:, v4 * 128:(v4 + 1) * 128],
                bass.AP(sb["stg"], v4 * 128, [[1536, 2], [1, 128]]),
                bass.AP(sb["stg"], 512 + v4 * 128, [[1536, 2], [1, 128]]),
                start=True, stop=True),
                reads=("stg_l", "stg_r0", "stgc0", "stgc1"), writes=("pf1",))
        P.c("vector", lambda e: e.tensor_tensor(
            sb["mdiff"][:].rearrange("p (v c) -> p v c", v=4),
            pb["pf0"][:].rearrange("p (v c) -> p v c", v=4),
            sb["msks"][:].unsqueeze(1).broadcast_to([128, 4, 128]), ALU.mult),
            reads=("pf0", "msks"), writes=G("mdiff"))
        P.c("vector", lambda e: e.tensor_tensor(
            sb["cvt"][:].rearrange("p (v c) -> p v c", v=4),
            pb["pf1"][:].rearrange("p (v c) -> p v c", v=4),
            sb["mski"][:].unsqueeze(1).broadcast_to([128, 4, 128]), ALU.mult),
            reads=("pf1", "mski"), writes=G("cvt"))
        P.c("scalar", lambda e, rd=rd: e.activation(
            sb["m1"][:, rd * 512:(rd + 1) * 512], sb["mdiff"][:, 0:512], AF.Exp),
            reads=G("mdiff"), writes=G("m1"))
        P.c("scalar", lambda e, rd=rd: e.activation(
            sb["m0"][:, rd * 512:(rd + 1) * 512], sb["cvt"][:, 0:512], AF.Exp),
            reads=G("cvt"), writes=G("m0"))

    for rnd in range(2):
        for k4 in range(4):
            kh = rnd * 4 + k4
            P.c("tensor", lambda e, kh=kh, k4=k4: e.matmul(
                pb["pf2"][:, k4 * 128:(k4 + 1) * 128], sb["kn_fm"][:, kh * 128:(kh + 1) * 128],
                sb["kn_fm"][:, kh * 128:(kh + 1) * 128], start=True, stop=True),
                reads=G("kn_fm"), writes=("pf2",))
            P.c("tensor", lambda e, kh=kh, k4=k4: e.matmul(
                pb["pf3"][:, k4 * 128:(k4 + 1) * 128], sb["kn_fm"][:, kh * 128:(kh + 1) * 128],
                sb["qs_fm"][:, kh * 128:(kh + 1) * 128], start=True, stop=True),
                reads=G("kn_fm", "qs_fm"), writes=("pf3",))
        P.c("vector", lambda e, rnd=rnd: e.scalar_tensor_tensor(
            sb["Gm"][:, rnd * 512:(rnd + 1) * 512].rearrange("p (v c) -> p v c", v=4),
            pb["pf2"][:].rearrange("p (v c) -> p v c", v=4), -1.0,
            sb["msks"][:].unsqueeze(1).broadcast_to([128, 4, 128]), ALU.mult, ALU.mult),
            reads=("pf2", "msks"), writes=G("Gm"))
        P.c("vector", lambda e, rnd=rnd: e.tensor_tensor(
            sb["qkTm"][:, rnd * 512:(rnd + 1) * 512].rearrange("p (v c) -> p v c", v=4),
            pb["pf3"][:].rearrange("p (v c) -> p v c", v=4),
            sb["mski"][:].unsqueeze(1).broadcast_to([128, 4, 128]), ALU.mult),
            reads=("pf3", "mski"), writes=G("qkTm"))
    P.c("gpsimd", lambda e: e.tensor_tensor(
        sb["C"][:].rearrange("p (h u d) -> p h u d", h=8, u=2),
        sb["Gm"][:].rearrange("p (h d) -> p h d", h=8)
        .unsqueeze(2).broadcast_to([128, 8, 2, 128]),
        sb["m1"][:].rearrange("p (h u d) -> p h u d", h=8, u=2), ALU.mult),
        reads=G("Gm", "m1"), writes=G("C"))
    P.c("vector", lambda e: e.tensor_tensor(
        sb["m0"][:].rearrange("p (h u d) -> p h u d", h=8, u=2),
        sb["qkTm"][:].rearrange("p (h d) -> p h d", h=8)
        .unsqueeze(2).broadcast_to([128, 8, 2, 128]),
        sb["m0"][:].rearrange("p (h u d) -> p h u d", h=8, u=2), ALU.mult),
        reads=G("qkTm", "m0"), writes=G("m0"))

    trans_group(cx, [("C", j * 128) for j in range(16)], "A",
                evac_engs=("scalar", "vector"))
    P.c("gpsimd", lambda e: e.tensor_tensor(
        sb["P"][:].rearrange("p (v c) -> p v c", v=16),
        sb["C"][:].rearrange("p (v c) -> p v c", v=16),
        sb["ident_bf"][:].unsqueeze(1).broadcast_to([128, 16, 128]), ALU.add),
        reads=G("C") + ("ident_bf",), writes=G("P"))

    def quad_evac(grp, bank, dst):
        if grp % 2 == 0:
            P.c("scalar", lambda e, grp=grp, bank=bank, dst=dst: e.activation(
                sb[dst][:, grp * 512:(grp + 1) * 512], pb[bank][:], AF.Copy),
                reads=(bank,), writes=G(dst))
        else:
            P.c("vector", lambda e, grp=grp, bank=bank, dst=dst: e.tensor_copy(
                sb[dst][:, grp * 512:(grp + 1) * 512], pb[bank][:]),
                reads=(bank,), writes=G(dst))

    def quad_mm(lhs_of, rhs_of, dst):
        for grp in range(4):
            bank = "pf%d" % (grp % 4)
            rd = set()
            for v2 in range(4):
                vh = grp * 4 + v2
                ln, lo = lhs_of(vh)
                rn, ro = rhs_of(vh)
                rd.add(ln)
                rd.add(rn)
                P.c("tensor", lambda e, v2=v2, ln=ln, lo=lo, rn=rn, ro=ro, bank=bank: e.matmul(
                    pb[bank][:, v2 * 128:(v2 + 1) * 128],
                    sb[ln][:, lo:lo + 128], sb[rn][:, ro:ro + 128],
                    start=True, stop=True),
                    reads=G(*rd), writes=(bank,))
            quad_evac(grp, bank, dst)

    def quad_mm_addp(a_name, p_name, dst):
        for grp in range(4):
            bank = "pf%d" % (grp % 4)
            for v2 in range(4):
                vh = grp * 4 + v2
                P.c("tensor", lambda e, v2=v2, vh=vh, bank=bank, a=a_name, p=p_name: e.matmul(
                    pb[bank][:, v2 * 128:(v2 + 1) * 128],
                    sb[a][:, vh * 128:(vh + 1) * 128], sb[p][:, vh * 128:(vh + 1) * 128],
                    start=True, stop=False),
                    reads=G(a_name, p_name), writes=(bank,))
                P.c("tensor", lambda e, v2=v2, vh=vh, bank=bank, p=p_name: e.matmul(
                    pb[bank][:, v2 * 128:(v2 + 1) * 128],
                    sb["ident_bf"][:], sb[p][:, vh * 128:(vh + 1) * 128],
                    start=False, stop=True),
                    reads=("ident_bf",) + G(p_name), writes=(bank,))
            quad_evac(grp, bank, dst)

    cur_c, cur_a, cur_p = "C", "A", "P"
    oth_c, oth_a, oth_p = "C2", "A2", "P2"
    for lvl in range(6):
        if lvl < 5:
            quad_mm(lambda vh, a=cur_a: (a, vh * 128), lambda vh, c=cur_c: (c, vh * 128), oth_c)
        quad_mm(lambda vh, c=cur_c: (c, vh * 128), lambda vh, a=cur_a: (a, vh * 128), oth_a)
        quad_mm_addp(oth_a, cur_p, oth_p)
        cur_c, oth_c = oth_c, cur_c
        cur_a, oth_a = oth_a, cur_a
        cur_p, oth_p = oth_p, cur_p
    Pfin = cur_p

    quad_mm(lambda vh, p=Pfin: (p, vh * 128), lambda vh: ("vb", vh * 128), "vc")
    quad_mm(lambda vh: ("kbg", vh * 128), lambda vh, p=Pfin: (p, vh * 128), "kcdT")

    for grp in range(4):
        bs = "pf%d" % (2 + grp % 2)
        bo = "pf%d" % (4 + grp % 2)
        bd = "pf%d" % (grp % 2)
        for v2 in range(4):
            vh = grp * 4 + v2
            P.c("tensor", lambda e, vh=vh, v2=v2, bs=bs: e.matmul(
                pb[bs][:, v2 * 128:(v2 + 1) * 128], sb["kcdT"][:, vh * 128:(vh + 1) * 128],
                sb["S_bf"][:, vh * 128:(vh + 1) * 128], start=True, stop=True),
                reads=G("kcdT", "S_bf"), writes=(bs,))
        P.c("vector", lambda e, grp=grp, bs=bs: e.tensor_sub(
            sb["vnew"][:, grp * 512:(grp + 1) * 512], sb["vc"][:, grp * 512:(grp + 1) * 512],
            pb[bs][:]),
            reads=G("vc") + (bs,), writes=G("vnew"))
        for v2 in range(4):
            vh = grp * 4 + v2
            P.c("tensor", lambda e, vh=vh, v2=v2, bo=bo: e.matmul(
                pb[bo][:, v2 * 128:(v2 + 1) * 128], sb["qgi_fm"][:, vh * 128:(vh + 1) * 128],
                sb["S_bf"][:, vh * 128:(vh + 1) * 128], start=True, stop=False),
                reads=G("qgi_fm", "S_bf"), writes=(bo,))
            P.c("tensor", lambda e, vh=vh, v2=v2, bo=bo: e.matmul(
                pb[bo][:, v2 * 128:(v2 + 1) * 128], sb["m0"][:, vh * 128:(vh + 1) * 128],
                sb["vnew"][:, vh * 128:(vh + 1) * 128], start=False, stop=True),
                reads=G("m0", "vnew"), writes=(bo,))
            P.c("tensor", lambda e, vh=vh, v2=v2, bd=bd: e.matmul(
                pb[bd][:, v2 * 128:(v2 + 1) * 128], sb["kdec"][:, vh * 128:(vh + 1) * 128],
                sb["vnew"][:, vh * 128:(vh + 1) * 128], start=True, stop=True),
                reads=G("kdec", "vnew"), writes=(bd,))
        P.c("vector", lambda e, grp=grp, bo=bo, ci=ci: e.tensor_mul(
            sb["y_tm"][:, grp * 512:(grp + 1) * 512], pb[bo][:],
            sb["sz"][:, ci * 2048 + grp * 512:ci * 2048 + (grp + 1) * 512]),
            reads=(bo,) + G("sz"), writes=G("y_tm"))
        for v2 in range(4):
            vh = grp * 4 + v2
            P.c("vector", lambda e, vh=vh, v2=v2, bd=bd, ci=ci: e.scalar_tensor_tensor(
                sb["S_bf"][:, vh * 128:(vh + 1) * 128], sb["S_bf"][:, vh * 128:(vh + 1) * 128],
                sb["egl"][:, ci * 16 + vh:ci * 16 + vh + 1], pb[bd][:, v2 * 128:(v2 + 1) * 128],
                ALU.mult, ALU.add),
                reads=G("S_bf", "egl") + (bd,), writes=G("S_bf"))

    P.c("scalar", lambda e: e.activation(sb["ysq"][:, 0:2048], sb["y_tm"][:], AF.Square),
        reads=G("y_tm"), writes=G("ysq"))
    P.c("vector", lambda e: e.tensor_reduce(
        sb["rstc"][:, 0:16], sb["ysq"][:].rearrange("p (v d) -> p v d", v=16), AX.X, ALU.add),
        reads=G("ysq"), writes=G("rstc"))
    P.c("vector", lambda e: e.tensor_scalar_add(sb["rstc"][:, 0:16], sb["rstc"][:, 0:16],
                                                RMS_EPS * 128.0),
        reads=G("rstc"), writes=G("rstc"))
    P.c("scalar", lambda e: e.activation(sb["rstc"][:, 0:16], sb["rstc"][:, 0:16],
                                         AF.Ln), reads=G("rstc"), writes=G("rstc"))
    P.c("scalar", lambda e: e.activation(sb["rstc"][:, 0:16], sb["rstc"][:, 0:16],
                                         AF.Exp, scale=-0.5), reads=G("rstc"), writes=G("rstc"))
    P.c("vector", lambda e: e.tensor_scalar_mul(sb["rstc"][:, 0:16], sb["rstc"][:, 0:16],
                                                float(128.0 ** 0.5)),
        reads=G("rstc"), writes=G("rstc"))
    P.c("vector", lambda e: e.tensor_tensor(
        sb["y_tm"][:].rearrange("p (v d) -> p v d", v=16),
        sb["y_tm"][:].rearrange("p (v d) -> p v d", v=16),
        sb["rstc"][:, 0:16].unsqueeze(2).broadcast_to([128, 16, 128]), ALU.mult),
        reads=G("y_tm", "rstc"), writes=G("y_tm"))
    trans_group(cx, [("y_tm", j * 128) for j in range(16)], "nrm_fm",
                evac_engs=("scalar", "vector"))
    P.c("vector", lambda e: e.tensor_scalar_mul(
        sb["nrm_fm"][:], sb["nrm_fm"][:], sb["normw_col"][:]),
        reads=G("nrm_fm") + ("normw_col",), writes=G("nrm_fm"))
    q, col = t0 // 1024, t0 % 1024
    for half in range(2):
        P.dma("scalar", "dST", lambda e, half=half, q=q, col=col: e.dma_start(
            out=cx.dr["cc_in"].ap()[q, half * 1024:(half + 1) * 1024, col:col + 128]
            .rearrange("(v p) c -> v p c", p=128).transpose([1, 0, 2]),
            in_=sb["nrm_fm"][:, half * 1024:(half + 1) * 1024]
            .rearrange("p (v c) -> p v c", v=8)),
            reads=G("nrm_fm"), writes=("cc_in_q%d" % q,))


def emit_phaseB_tile(cx, tb):
    P, sb, pb, din, dr, G = cx.P, cx.sb, cx.pb, cx.din, cx.dr, cx.G
    t0 = tb * TTB

    srcq = dr["cc_out%d" % tb].ap().rearrange("r (u c) -> (r u) c", u=2)
    for g in range(32):
        P.dma("gpsimd", "dGT", lambda e, g=g, srcq=srcq: e.indirect_dma_start(
            sb["nrmB"][:, g * 512:(g + 1) * 512], None, srcq,
            bass.IndirectOffsetOnAxis(sb["gth_idx"][:, g:g + 1], 0)),
            reads=("cc_out%d" % tb, "gth_idx") + G("nrmB"), writes=G("nrmB"))
    if tb == 0:
        cx.coll_b0()

    for hg in range(4):
        for s in range(4):
            P.dma("scalar", "dX", lambda e, s=s, hg=hg: e.dma_start(
                out=sb["res0s"][:, s * 512:(s + 1) * 512],
                in_=din["xh"].ap()[t0 + s * 128:t0 + (s + 1) * 128, hg * 512:(hg + 1) * 512]),
                reads=G("res0s"), writes=G("res0s"))
        for rg in range(4):
            slot = rg % 2
            buf = "wt%d" % slot
            P.dma("sync" if slot == 0 else "gpsimd", "dW0" if slot == 0 else "gW1",
                  lambda e, hg=hg, rg=rg, buf=buf: e.dma_start(
                out=sb[buf][:, 0:4096].rearrange("p (r c) -> p r c", r=8),
                in_=dr["wout_bf"].ap()[hg, rg * 8:(rg + 1) * 8].transpose([1, 0, 2])),
                reads=("wout_bf",), writes=G(buf))
            for s in range(4):
                bank = "pf%d" % s
                for r2 in range(8):
                    rc = rg * 8 + r2
                    P.c("tensor", lambda e, s=s, rc=rc, r2=r2, buf=buf, bank=bank: e.matmul(
                        pb[bank][:, :],
                        sb["nrmB"][:, rc * 512 + s * 128:rc * 512 + (s + 1) * 128],
                        sb[buf][:, r2 * 512:(r2 + 1) * 512],
                        start=(rc == 0), stop=(rc == 31)),
                        reads=G("nrmB", buf), writes=(bank,))
        for s in range(4):
            bank = "pf%d" % s
            P.c("vector", lambda e, s=s, hg=hg, bank=bank: e.tensor_add(
                sb["resfs"][:, s * 512:(s + 1) * 512], pb[bank][:, :],
                sb["res0s"][:, s * 512:(s + 1) * 512]),
                reads=(bank,) + G("res0s", "resfs"), writes=G("resfs"))
            P.dma("scalar", "dRES", lambda e, s=s, hg=hg: e.dma_start(
                out=dr["res0_d"].ap()[t0 + s * 128:t0 + (s + 1) * 128, hg * 512:(hg + 1) * 512],
                in_=sb["resfs"][:, s * 512:(s + 1) * 512]),
                reads=G("resfs"), writes=("res0_d",))

    for s in range(4):
        P.dma("sync", "dRES", lambda e, s=s: e.dma_start(
            out=sb["res0s"][:, 0:2048], in_=dr["res0_d"].ap()[t0 + s * 128:t0 + (s + 1) * 128, :]),
            reads=("res0_d",) + G("res0s"), writes=G("res0s"))
        P.c("vector", lambda e: e.tensor_reduce(
            sb["rstc"][:, 0:1], sb["res0s"][:, 0:2048], AX.X, ALU.add),
            reads=G("res0s"), writes=G("rstc"))
        P.c("scalar", lambda e: e.activation(sb["resfs"][:, 0:2048], sb["res0s"][:, 0:2048],
                                             AF.Square),
            reads=G("res0s"), writes=G("resfs"))
        P.c("vector", lambda e: e.tensor_reduce(
            sb["rstc"][:, 1:2], sb["resfs"][:, 0:2048], AX.X, ALU.add),
            reads=G("resfs"), writes=G("rstc"))
        P.c("vector", lambda e: e.tensor_scalar_mul(sb["rstc"][:, 0:2], sb["rstc"][:, 0:2], 1.0 / D),
            reads=G("rstc"), writes=G("rstc"))
        P.c("vector", lambda e: e.scalar_tensor_tensor(
            sb["rstc"][:, 2:3], sb["rstc"][:, 0:1], -1.0, sb["rstc"][:, 0:1],
            ALU.mult, ALU.mult), reads=G("rstc"), writes=G("rstc"))
        P.c("vector", lambda e: e.tensor_add(sb["rstc"][:, 2:3], sb["rstc"][:, 2:3],
                                             sb["rstc"][:, 1:2]),
            reads=G("rstc"), writes=G("rstc"))
        P.c("vector", lambda e: e.tensor_scalar_add(sb["rstc"][:, 2:3], sb["rstc"][:, 2:3],
                                                    LN_EPS), reads=G("rstc"), writes=G("rstc"))
        P.c("scalar", lambda e: e.activation(sb["rstc"][:, 2:3], sb["rstc"][:, 2:3],
                                             AF.Ln), reads=G("rstc"), writes=G("rstc"))
        P.c("scalar", lambda e: e.activation(sb["rstc"][:, 2:3], sb["rstc"][:, 2:3],
                                             AF.Exp, scale=-0.5), reads=G("rstc"), writes=G("rstc"))
        P.c("vector", lambda e: e.tensor_mul(sb["rstc"][:, 3:4], sb["rstc"][:, 0:1],
                                             sb["rstc"][:, 2:3]),
            reads=G("rstc"), writes=G("rstc"))
        P.c("vector", lambda e: e.tensor_scalar(
            sb["resfs"][:, 0:2048], sb["res0s"][:, 0:2048], sb["rstc"][:, 2:3],
            sb["rstc"][:, 3:4], ALU.mult, ALU.subtract),
            reads=G("res0s", "rstc"), writes=G("resfs"))
        P.dma("scalar", "dRES", lambda e, s=s: e.dma_start(
            out=dr["resf_d"].ap()[t0 + s * 128:t0 + (s + 1) * 128, :], in_=sb["resfs"][:, 0:2048]),
            reads=G("resfs"), writes=("resf_d",))
        P.c("vector", lambda e: e.tensor_copy(sb["xt1b"][:, 0:512] if False else
                                              sb["h2w0"][:, 0:2048], sb["resfs"][:, 0:2048]),
            reads=G("resfs"), writes=G("h2w0"))
        trans_to_xtfm(cx, "h2w0", 0, s)

    for j in range(2):
        emit_ffn_iter(cx, tb, j)


def trans_to_xtfm(cx, srcname, src_off, s):
    P, sb, pb, G = cx.P, cx.sb, cx.pb, cx.G
    for grp in range(2):
        bank = "pt%d" % grp
        for f2 in range(8):
            f = grp * 8 + f2
            P.c("tensor", lambda e, f=f, f2=f2, bank=bank, srcname=srcname, src_off=src_off: e.transpose(
                pb[bank][:, f2 * 128:(f2 + 1) * 128],
                sb[srcname][:, src_off + f * 128:src_off + (f + 1) * 128], sb["ident_bf"][:]),
                reads=G(srcname) + ("ident_bf",), writes=(bank,))
        P.c("scalar", lambda e, grp=grp, s=s, bank=bank: e.activation(
            bass.AP(sb["xt_fm"], grp * 8 * 512 + s * 128, [[8192, 128], [512, 8], [1, 128]]),
            pb[bank][:].rearrange("p (f c) -> p f c", f=8), AF.Copy),
            reads=(bank,), writes=G("xt_fm"))


def emit_ffn_iter(cx, tb, j):
    P, sb, pb, din, dr, G = cx.P, cx.sb, cx.pb, cx.din, cx.dr, cx.G
    t0 = tb * TTB

    P.dma("sync", "dW0", lambda e, j=j: e.dma_start(
        out=sb["wgt"][:, 0:128].rearrange("p (k c) -> p k c", k=16),
        in_=dr["wg_bf"].ap()[:, j * 8:(j + 1) * 8].rearrange("(k p) c -> k p c", p=128)
        .transpose([1, 0, 2])),
        reads=("wg_bf",), writes=("wgt",))
    for s in range(4):
        for k in range(16):
            P.c("tensor", lambda e, s=s, k=k: e.matmul(
                pb["pf4"][:, 0:8], sb["xt_fm"][:, k * 512 + s * 128:k * 512 + (s + 1) * 128],
                sb["wgt"][:, k * 8:(k + 1) * 8], start=(k == 0), stop=False),
                reads=("wgt",) + G("xt_fm"), writes=("pf4",))
        P.c("tensor", lambda e, s=s, j=j: e.matmul(
            pb["pf4"][:, 0:8], sb["ones_row_bf"][:], sb["bg_bf"][0:1, j * 8:j * 8 + 8],
            start=False, stop=True),
            reads=("bg_bf", "ones_row_bf"), writes=("pf4",))
        P.c("vector", lambda e, s=s: e.tensor_copy(sb["gat"][:, s * 8:s * 8 + 8],
                                                   pb["pf4"][:, 0:8]),
            reads=("pf4",), writes=G("gat"))
    for s in range(4):
        g0 = s * 8
        gat = sb["gat"]
        P.c("vector", lambda e, g0=g0: e.tensor_reduce(
            sb["gat"][:, 32 + 0:32 + 1], gat[:, g0:g0 + 8], AX.X, ALU.min),
            reads=G("gat"), writes=G("gat"))
        P.c("scalar", lambda e, g0=g0: e.sign(sb["gat"][:, 40:48], gat[:, g0:g0 + 8]) if False else
            e.activation(sb["gat"][:, 40:48], gat[:, g0:g0 + 8], AF.Sign, bias=sb["gat"][:, 32:33]),
            reads=G("gat"), writes=G("gat")) if False else None
        P.c("vector", lambda e, g0=g0: e.tensor_scalar(
            sb["gat"][:, 40:48], gat[:, g0:g0 + 8], sb["gat"][:, 32:33], None,
            ALU.subtract), reads=G("gat"), writes=G("gat"))
        P.c("scalar", lambda e: e.activation(sb["gat"][:, 40:48], sb["gat"][:, 40:48], AF.Sign),
            reads=G("gat"), writes=G("gat"))
        P.c("vector", lambda e, g0=g0: e.tensor_mul(
            sb["gat"][:, 48:56], sb["gat"][:, 40:48], gat[:, g0:g0 + 8]),
            reads=G("gat"), writes=G("gat"))
        P.c("vector", lambda e: e.tensor_scalar(
            sb["gat"][:, 40:48], sb["gat"][:, 40:48], -1e30, 1e30, ALU.mult, ALU.add),
            reads=G("gat"), writes=G("gat"))
        P.c("vector", lambda e: e.tensor_add(
            sb["gat"][:, 48:56], sb["gat"][:, 48:56], sb["gat"][:, 40:48]),
            reads=G("gat"), writes=G("gat"))
        P.c("vector", lambda e: e.tensor_reduce(
            sb["gat"][:, 33:34], sb["gat"][:, 48:56], AX.X, ALU.min),
            reads=G("gat"), writes=G("gat"))
        P.c("vector", lambda e, g0=g0: e.tensor_scalar(
            sb["gat"][:, 40:48], gat[:, g0:g0 + 8], sb["gat"][:, 33:34], None,
            ALU.subtract), reads=G("gat"), writes=G("gat"))
        P.c("scalar", lambda e: e.activation(sb["gat"][:, 40:48], sb["gat"][:, 40:48], AF.Sign),
            reads=G("gat"), writes=G("gat"))
        P.c("vector", lambda e: e.tensor_scalar_max(sb["gat"][:, 40:48], sb["gat"][:, 40:48], 0.0),
            reads=G("gat"), writes=G("gat"))
        P.c("vector", lambda e, g0=g0: e.tensor_reduce(
            sb["gat"][:, 34:35], gat[:, g0:g0 + 8], AX.X, ALU.max),
            reads=G("gat"), writes=G("gat"))
        P.c("vector", lambda e, g0=g0: e.tensor_scalar(
            sb["gat"][:, 48:56], gat[:, g0:g0 + 8], sb["gat"][:, 34:35], None,
            ALU.subtract), reads=G("gat"), writes=G("gat"))
        P.c("scalar", lambda e: e.activation(sb["gat"][:, 48:56], sb["gat"][:, 48:56], AF.Exp),
            reads=G("gat"), writes=G("gat"))
        P.c("vector", lambda e: e.tensor_mul(sb["gat"][:, 48:56], sb["gat"][:, 48:56],
                                             sb["gat"][:, 40:48]),
            reads=G("gat"), writes=G("gat"))
        P.c("vector", lambda e: e.tensor_reduce(
            sb["gat"][:, 35:36], sb["gat"][:, 48:56], AX.X, ALU.add),
            reads=G("gat"), writes=G("gat"))
        P.c("vector", lambda e: e.reciprocal(sb["gat"][:, 36:37], sb["gat"][:, 35:36]),
            reads=G("gat"), writes=G("gat"))
        P.c("vector", lambda e, s=s: e.tensor_scalar(
            sb["wbc"][:, 0:8] if False else sb["gat"][:, 56 + 0:56 + 8],
            sb["gat"][:, 48:56], sb["gat"][:, 36:37], None, ALU.mult),
            reads=G("gat"), writes=G("gat"))
        P.c("vector", lambda e, s=s: e.tensor_copy(sb["wgt"][:, 0:8] if False else
                                                   sb["wbc"][:, 0:8], sb["gat"][:, 56:64]),
            reads=G("gat"), writes=G("wbc"))
        P.c("tensor", lambda e, s=s: e.transpose(
            pb["pt0"][:, 0:128], sb["wbc"][:, 0:8] if False else sb["wbc"][:, 0:8],
            sb["ident_bf"][:]),
            reads=G("wbc") + ("ident_bf",), writes=("pt0",)) if False else None
        P.c("tensor", lambda e, s=s: e.transpose(
            pb["pt0"][0:8, 0:128], sb["wbc"][:, 0:8], sb["ident_bf"][:]),
            reads=G("wbc") + ("ident_bf",), writes=("pt0",))
        P.c("scalar", lambda e, s=s: e.activation(
            sb["wfm"][:, s * 128:(s + 1) * 128], pb["pt0"][0:8, 0:128], AF.Copy),
            reads=("pt0",), writes=G("wfm"))

    for n in range(6):
        for g4 in range(2):
            for g2 in range(2):
                g = n * 4 + g4 * 2 + g2
                slot = g % 2
                buf = "wt%d" % slot
                P.dma("sync" if slot == 0 else "gpsimd", "dW0" if slot == 0 else "gW1", lambda e, g=g, buf=buf: e.dma_start(
                    out=sb[buf][:, 0:4096], in_=dr["wup_bf"].ap()[g]),
                    reads=("wup_bf",), writes=G(buf))
                for e2 in range(2):
                    ec = (g4 * 2 + g2) * 2 + e2
                    bank = "pf%d" % (ec % 4)
                    for k in range(16):
                        P.c("tensor", lambda e, e2=e2, k=k, buf=buf, bank=bank: e.matmul(
                            pb[bank][:, :], sb[buf][:, k * 256 + e2 * 128:k * 256 + e2 * 128 + 128],
                            sb["xt_fm"][:, k * 512:(k + 1) * 512], start=(k == 0), stop=(k == 15)),
                            reads=G(buf, "xt_fm"), writes=(bank,))
                    P.c("vector", lambda e, n=n, ec=ec, bank=bank: e.tensor_scalar_add(
                        sb["h_fm"][:, ec * 512:(ec + 1) * 512], pb[bank][:, :],
                        sb["bup_cols"][:, n * 8 + ec:n * 8 + ec + 1]),
                        reads=(bank, "bup_cols"), writes=G("h_fm"))
        for g4 in range(4):
            g = n * 4 + g4
            slot = g % 2
            buf = "wt%d" % slot
            P.dma("sync" if slot == 0 else "gpsimd", "dW0" if slot == 0 else "gW1", lambda e, g=g, buf=buf: e.dma_start(
                out=sb[buf][:, 0:2048], in_=dr["wsw_bf"].ap()[g]),
                reads=("wsw_bf",), writes=G(buf))
            for e2 in range(2):
                ec = g4 * 2 + e2
                bank = "pf%d" % (ec % 4)
                for k in range(8):
                    P.c("tensor", lambda e, e2=e2, k=k, buf=buf, bank=bank: e.matmul(
                        pb[bank][:, :], sb[buf][:, k * 256 + e2 * 128:k * 256 + e2 * 128 + 128],
                        sb["h_fm"][:, k * 512:(k + 1) * 512], start=(k == 0), stop=(k == 7)),
                        reads=G(buf, "h_fm"), writes=(bank,))
                P.c("scalar", lambda e, n=n, ec=ec, bank=bank: e.activation(
                    sb["sil"][:, ec * 512:(ec + 1) * 512], pb[bank][:, :], AF.Silu,
                    bias=sb["bsw_cols"][:, n * 8 + ec:n * 8 + ec + 1]),
                    reads=(bank, "bsw_cols"), writes=G("sil"))
        P.c("tensor", lambda e, n=n: e.matmul(
            pb["pf4"][:, :], sb["selbf_t"][:, n * 128:(n + 1) * 128],
            sb["wfm"][:, 0:512], start=True, stop=True),
            reads=G("selbf_t", "wfm"), writes=("pf4",))
        P.c("scalar", lambda e: e.activation(sb["wbc"][:, 0:512], pb["pf4"][:, :], AF.Copy),
            reads=("pf4",), writes=G("wbc"))
        P.c("gpsimd", lambda e: e.tensor_mul(sb["sil"][:, 0:4096], sb["sil"][:, 0:4096],
                                             sb["h_fm"][:, 0:4096]),
            reads=G("sil", "h_fm"), writes=G("sil"))
        P.c("vector", lambda e, n=n: e.tensor_tensor(
            sb["h2w%d" % n][:].rearrange("p (g c) -> p g c", g=8),
            sb["sil"][:, 0:4096].rearrange("p (g c) -> p g c", g=8),
            sb["wbc"][:, 0:512].unsqueeze(1).broadcast_to([128, 8, 512]), ALU.mult),
            reads=G("sil", "wbc"), writes=G("h2w%d" % n))

    for g in range(8):
        slot = g % 2
        buf = "wt%d" % slot
        P.dma("sync" if slot == 0 else "gpsimd", "dW0" if slot == 0 else "gW1", lambda e, g=g, buf=buf: e.dma_start(
            out=sb[buf][:, 0:4096], in_=dr["wsu_bf"].ap()[g]),
            reads=("wsu_bf",), writes=G(buf))
        for e2 in range(2):
            sc = g * 2 + e2
            bank = "pf%d" % (sc % 4)
            for k in range(16):
                P.c("tensor", lambda e, e2=e2, k=k, buf=buf, bank=bank: e.matmul(
                    pb[bank][:, :], sb[buf][:, k * 256 + e2 * 128:k * 256 + e2 * 128 + 128],
                    sb["xt_fm"][:, k * 512:(k + 1) * 512], start=(k == 0), stop=(k == 15)),
                    reads=G(buf, "xt_fm"), writes=(bank,))
            P.c("vector", lambda e, sc=sc, bank=bank: e.tensor_scalar_add(
                sb["sh_fm"][:, sc * 512:(sc + 1) * 512], pb[bank][:, :],
                sb["bsu_cols"][:, sc:sc + 1]),
                reads=(bank, "bsu_cols"), writes=G("sh_fm"))
    for g in range(4):
        slot = g % 2
        buf = "wt%d" % slot
        P.dma("sync" if slot == 0 else "gpsimd", "dW0" if slot == 0 else "gW1", lambda e, g=g, buf=buf: e.dma_start(
            out=sb[buf][:, 0:2048], in_=dr["wsa_bf"].ap()[g]),
            reads=("wsa_bf",), writes=G(buf))
        for sl in range(2):
            for e2 in range(2):
                ec = g * 2 + e2
                bank = "pf%d" % ((sl * 2 + e2) % 4)
                for k in range(8):
                    P.c("tensor", lambda e, sl=sl, e2=e2, k=k, buf=buf, bank=bank: e.matmul(
                        pb[bank][:, :], sb[buf][:, k * 256 + e2 * 128:k * 256 + e2 * 128 + 128],
                        sb["sh_fm"][:, (sl * 8 + k) * 512:(sl * 8 + k + 1) * 512],
                        start=(k == 0), stop=(k == 7)),
                        reads=G(buf, "sh_fm"), writes=(bank,))
                P.c("scalar", lambda e, sl=sl, ec=ec, bank=bank: e.activation(
                    sb["sh2"][:, (sl * 8 + ec) * 512:(sl * 8 + ec + 1) * 512],
                    pb[bank][:, :], AF.Silu, bias=sb["bsa_cols"][:, ec:ec + 1]),
                    reads=(bank, "bsa_cols"), writes=G("sh2"))
    P.c("gpsimd", lambda e: e.tensor_mul(sb["sh2"][:, 0:8192], sb["sh2"][:, 0:8192],
                                         sb["sh_fm"][:, 0:8192]),
        reads=G("sh2", "sh_fm"), writes=G("sh2"))


    last = (j == 1)
    for hg in range(4):
        banks = ["pf%d" % i for i in range(4)]
        first = [True] * 4
        for s in range(4):
            P.dma("scalar", "dRES", lambda e, s=s, hg=hg: e.dma_start(
                out=sb["resfs"][:, s * 512:(s + 1) * 512],
                in_=dr["resf_d"].ap()[t0 + s * 128:t0 + (s + 1) * 128, hg * 512:(hg + 1) * 512]),
                reads=("resf_d",) + G("resfs"), writes=G("resfs"))
            if last:
                P.dma("scalar", "dRES", lambda e, s=s, hg=hg: e.dma_start(
                    out=sb["res0s"][:, s * 512:(s + 1) * 512],
                    in_=dr["res0_d"].ap()[t0 + s * 128:t0 + (s + 1) * 128, hg * 512:(hg + 1) * 512]),
                    reads=("res0_d",) + G("res0s"), writes=G("res0s"))
        for n in range(6):
            slot = n % 2
            buf = "wt%d" % slot
            P.dma("sync" if slot == 0 else "gpsimd", "dW0" if slot == 0 else "gW1",
                  lambda e, hg=hg, n=n, buf=buf: e.dma_start(
                out=sb[buf][:, 0:4096].rearrange("p (r c) -> p r c", r=8),
                in_=dr["wdn_bf"].ap()[hg, n * 8:(n + 1) * 8].transpose([1, 0, 2])),
                reads=("wdn_bf",), writes=G(buf))
            for s in range(4):
                for r2 in range(8):
                    P.c("tensor", lambda e, n=n, s=s, r2=r2, buf=buf, f=first[s], bank=banks[s]: e.matmul(
                        pb[bank][:, :],
                        sb["h2w%d" % n][:, r2 * 512 + s * 128:r2 * 512 + (s + 1) * 128],
                        sb[buf][:, r2 * 512:(r2 + 1) * 512],
                        start=f and r2 == 0, stop=False),
                        reads=G("h2w%d" % n, buf), writes=(banks[s],))
                first[s] = False
        for half in range(2):
            slot = half % 2
            buf = "wt%d" % slot
            P.dma("sync" if slot == 0 else "gpsimd", "dW0" if slot == 0 else "gW1",
                  lambda e, hg=hg, half=half, buf=buf: e.dma_start(
                out=sb[buf][:, 0:4096].rearrange("p (r c) -> p r c", r=8),
                in_=dr["wsd_bf"].ap()[hg, half * 8:(half + 1) * 8].transpose([1, 0, 2])),
                reads=("wsd_bf",), writes=G(buf))
            for s in range(4):
                for r2 in range(8):
                    rc = half * 8 + r2
                    P.c("tensor", lambda e, rc=rc, r2=r2, s=s, buf=buf, bank=banks[s]: e.matmul(
                        pb[bank][:, :],
                        sb["sh2"][:, rc * 512 + s * 128:rc * 512 + (s + 1) * 128],
                        sb[buf][:, r2 * 512:(r2 + 1) * 512],
                        start=False, stop=False),
                        reads=G("sh2", buf), writes=(banks[s],))
        for s in range(4):
            P.c("tensor", lambda e, s=s, hg=hg, bank=banks[s]: e.matmul(
                pb[bank][:, :], sb["wfm"][0:6, s * 128:(s + 1) * 128],
                sb["bdn_bf"][:, hg * 512:(hg + 1) * 512],
                start=False, stop=False),
                reads=G("wfm", "bdn_bf"), writes=(banks[s],))
            P.c("tensor", lambda e, s=s, hg=hg, bank=banks[s]: e.matmul(
                pb[bank][:, :], sb["ones_row_bf"][:],
                sb["bsd_bf"][0:1, hg * 512:(hg + 1) * 512],
                start=False, stop=True),
                reads=("ones_row_bf",) + G("bsd_bf"), writes=(banks[s],))
            if not last:
                P.c("vector", lambda e, s=s, bank=banks[s]: e.tensor_add(
                    sb["xt1b"][:, 0:512], pb[bank][:, :], sb["resfs"][:, s * 512:(s + 1) * 512]),
                    reads=(banks[s],) + G("resfs"), writes=G("xt1b"))
                bankt = "pt0"
                for f2 in range(4):
                    P.c("tensor", lambda e, f2=f2, bankt=bankt: e.transpose(
                        pb[bankt][:, f2 * 128:(f2 + 1) * 128],
                        sb["xt1b"][:, f2 * 128:(f2 + 1) * 128], sb["ident_bf"][:]),
                        reads=G("xt1b") + ("ident_bf",), writes=(bankt,))
                P.c("scalar", lambda e, s=s, hg=hg, bankt=bankt: e.activation(
                    bass.AP(sb["xt_fm"], hg * 4 * 512 + s * 128, [[8192, 128], [512, 4], [1, 128]]),
                    pb[bankt][:, 0:512].rearrange("p (f c) -> p f c", f=4), AF.Copy),
                    reads=(bankt,), writes=G("xt_fm"))
            else:
                P.c("vector", lambda e, s=s, bank=banks[s]: e.tensor_add(
                    sb["resfs"][:, s * 512:(s + 1) * 512], pb[bank][:, :],
                    sb["resfs"][:, s * 512:(s + 1) * 512]),
                    reads=(banks[s],) + G("resfs"), writes=G("resfs"))
                P.c("vector", lambda e, s=s: e.tensor_add(
                    sb["resfs"][:, s * 512:(s + 1) * 512],
                    sb["resfs"][:, s * 512:(s + 1) * 512],
                    sb["res0s"][:, s * 512:(s + 1) * 512]),
                    reads=G("resfs", "res0s"), writes=G("resfs"))
                P.dma("sync", "dOUT", lambda e, s=s, hg=hg: e.dma_start(
                    out=cx.y.ap()[t0 + s * 128:t0 + (s + 1) * 128, hg * 512:(hg + 1) * 512],
                    in_=sb["resfs"][:, s * 512:(s + 1) * 512]),
                    reads=G("resfs"), writes=("y",))


def kernel(**inputs):
    nc = build()
    in_maps = [_stage_core(inputs, c) for c in range(8)]
    res = run_bass_kernel_spmd(nc, in_maps, core_ids=list(range(8)))
    out = np.zeros((B, T, D), np.float32)
    for c in range(8):
        b, h = c // 2, c % 2
        yv = np.asarray(res.results[c]["y"], np.float32)
        for tb in range(4):
            out[b, (2 * tb + h) * 512:(2 * tb + h + 1) * 512] = yv[tb * 512:(tb + 1) * 512]
    return out


# version 13
# speedup vs baseline: 1.0580x; 1.0431x over previous
import numpy as np
import ml_dtypes
import concourse.bass as bass
import concourse.mybir as mybir
from concourse.bass_utils import run_bass_kernel_spmd

F32 = mybir.dt.float32
BF16 = mybir.dt.bfloat16
I32 = mybir.dt.int32
AF = mybir.ActivationFunctionType
ALU = mybir.AluOpType
AX = mybir.AxisListType
BFNP = ml_dtypes.bfloat16

B, T, D = 4, 4096, 2048
HV, HK, DK, DV, KCONV = 32, 16, 128, 128, 4
CH = 128
TT = 512
NTILES = T // TT
CPT = TT // CH
NVH, NKH = 16, 8
CONVC = 4096
VCC = 2048
TB = 2048
TTB = 512
NBT = TB // TTB
LN_EPS, L2_EPS, RMS_EPS = 1e-5, 1e-6, 1e-6

ENGS = ("tensor", "vector", "scalar", "gpsimd", "sync")
GRAN = 2048


class Prog:
    def __init__(self):
        self.steps = []
        self.last_write = {}
        self.last_reads = {}
        self.dma_issued = {}
        self.eng_steps = {e: [] for e in ENGS}

    def _dep(self, step, prod):
        if prod is None or prod is step:
            return
        if (prod["eng"] == "tensor" and step["eng"] == "tensor"
                and prod["kind"] == "c" and step["kind"] == "c"):
            return
        if prod not in step["dep_steps"]:
            step["dep_steps"].append(prod)
            if prod["kind"] == "d":
                step["dma_req"][prod["sem"]] = max(
                    step["dma_req"].get(prod["sem"], 0),
                    16 * self.dma_issued[prod["sem"]])

    def _add(self, eng, kind, emit, reads, writes, sem=None):
        step = dict(eng=eng, kind=kind, emit=emit, sem=sem,
                    dep_steps=[], dma_req={}, marked=False, count=None)
        for r in reads:
            self._dep(step, self.last_write.get(r))
        for w in writes:
            self._dep(step, self.last_write.get(w))
            for rd in self.last_reads.get(w, ()):
                self._dep(step, rd)
        if kind == "d":
            self.dma_issued[sem] = self.dma_issued.get(sem, 0) + 1
        for w in writes:
            self.last_write[w] = step
            self.last_reads[w] = []
        for r in reads:
            self.last_reads.setdefault(r, []).append(step)
        self.steps.append(step)
        self.eng_steps[eng].append(step)
        return step

    def c(self, eng, emit, reads=(), writes=()):
        return self._add(eng, "c", emit, reads, writes)

    def dma(self, eng, sem, emit, reads=(), writes=()):
        return self._add(eng, "d", emit, reads, writes, sem=sem)

    def coll(self, emit, reads=(), writes=()):
        return self._add("gpsimd", "cc", emit, reads, writes, sem="_cc")

    def finalize(self, block, sems, eng_sems, final_waits):
        for st in self.steps:
            for p in st["dep_steps"]:
                p["marked"] = True
        for e in ENGS:
            cnt = 0
            for st in self.eng_steps[e]:
                if st["kind"] == "c" and st["marked"]:
                    cnt += 1
                    st["count"] = cnt
        totals = {}
        for st in self.steps:
            if st["kind"] == "d":
                totals[st["sem"]] = totals.get(st["sem"], 0) + 16
        FULL = ("dAUX", "gAUX")
        dcnt = {}
        for st in self.steps:
            if st["kind"] == "d":
                dcnt[st["sem"]] = dcnt.get(st["sem"], 0) + 16
                st["count"] = dcnt[st["sem"]]
                st["marked"] = True
            elif st["kind"] == "cc":
                dcnt[st["sem"]] = dcnt.get(st["sem"], 0) + 1
                st["count"] = dcnt[st["sem"]]
                st["marked"] = True

        def emit_engine(name):
            def run(eng):
                seen = {}
                for st in self.eng_steps[name]:
                    req = {}
                    for p in st["dep_steps"]:
                        if p["kind"] == "c":
                            s, v = eng_sems[p["eng"]], p["count"]
                        elif p["kind"] == "cc":
                            s, v = sems[p["sem"]], p["count"]
                        else:
                            continue
                        if req.get(id(s), (None, -1))[1] < v:
                            req[id(s)] = (s, v)
                    for sname, v in st["dma_req"].items():
                        s = sems[sname]
                        if sname in FULL and not (st["kind"] == "d" and st["sem"] == sname):
                            v = totals[sname]
                        if req.get(id(s), (None, -1))[1] < v:
                            req[id(s)] = (s, v)
                    for s, v in req.values():
                        if seen.get(id(s), -1) < v:
                            eng.wait_ge(s, v)
                            seen[id(s)] = v
                    inst = st["emit"](eng)
                    if st["marked"]:
                        if st["kind"] == "c":
                            inst.then_inc(eng_sems[name], 1)
                        elif st["kind"] == "d":
                            inst.then_inc(sems[st["sem"]], 16)
                        else:
                            inst.then_inc(sems[st["sem"]], 1)
                if name == "sync":
                    for sname, v in final_waits:
                        if v > 0:
                            eng.wait_ge(sems[sname], v)
            return run

        block.sync(emit_engine("sync"))
        block.tensor(emit_engine("tensor"))
        block.vector(emit_engine("vector"))
        block.scalar(emit_engine("scalar"))
        block.gpsimd(emit_engine("gpsimd"))


def _stage_core(inp, c):
    f32 = np.float32
    b, h = c // 2, c % 2
    vh0, kh0 = NVH * h, NKH * h
    W = np.asarray(inp["L0_attn_W_qkv"], f32)
    wqkv = np.ascontiguousarray(np.concatenate(
        [W[:, kh0 * DK:(kh0 + NKH) * DK],
         W[:, 2048 + kh0 * DK:2048 + (kh0 + NKH) * DK],
         W[:, 4096 + vh0 * DV:4096 + (vh0 + NVH) * DV]], axis=1))
    cw = np.asarray(inp["L0_attn_conv_w"], f32)
    cw_sl = np.concatenate(
        [cw[kh0 * DK:(kh0 + NKH) * DK], cw[2048 + kh0 * DK:2048 + (kh0 + NKH) * DK],
         cw[4096 + vh0 * DV:4096 + (vh0 + NVH) * DV]], axis=0)
    convw = np.ascontiguousarray(cw_sl.reshape(32, 128, 4).transpose(1, 0, 2).reshape(128, 128))
    wout = np.ascontiguousarray(np.asarray(inp["L0_attn_W_out"], f32))

    def cols128(v):
        n = v.shape[0]
        return np.ascontiguousarray(v.reshape(n, 8, 128).transpose(2, 0, 1).reshape(128, n * 8))

    x = np.asarray(inp["L0_attn_x"], f32)
    gth = (2 * np.arange(4096, dtype=np.int64) + h).astype(np.int32)
    gth = np.ascontiguousarray(gth.reshape(32, 128).T)
    sel = np.zeros((8, 768), np.float32)
    for n in range(6):
        sel[n, n * 128:(n + 1) * 128] = 1.0

    return dict(
        x=np.ascontiguousarray(x[b]),
        xh=np.ascontiguousarray(np.concatenate(
            [x[b, (2 * tb + h) * 512:(2 * tb + h + 1) * 512] for tb in range(4)])),
        wqkv=wqkv, convw=convw,
        wz=np.ascontiguousarray(np.asarray(inp["L0_attn_W_z"], f32)[:, vh0 * DV:(vh0 + NVH) * DV]),
        wba=np.ascontiguousarray(np.concatenate(
            [np.asarray(inp["L0_attn_W_b"], f32)[:, vh0:vh0 + NVH],
             np.asarray(inp["L0_attn_W_a"], f32)[:, vh0:vh0 + NVH]], axis=1)),
        dtb_row=np.ascontiguousarray(np.asarray(inp["L0_attn_dt_bias"], f32)[vh0:vh0 + NVH].reshape(1, NVH)),
        alog_row=np.ascontiguousarray(np.asarray(inp["L0_attn_A_log"], f32)[vh0:vh0 + NVH].reshape(1, NVH)),
        normw_col=np.asarray(inp["L0_attn_norm_w"], f32).reshape(128, 1),
        wout=wout,
        wg=np.ascontiguousarray(np.asarray(inp["L0_ffn_Wg"], f32).transpose(1, 0, 2).reshape(2048, 16)),
        bg_rows=np.ascontiguousarray(np.asarray(inp["L0_ffn_bg"], f32).reshape(1, 16)),
        wup=np.ascontiguousarray(np.asarray(inp["L0_ffn_Wup"], f32).transpose(1, 0, 2).reshape(2048, 6144)),
        bup_cols=cols128(np.asarray(inp["L0_ffn_bup"], f32)),
        wsw=np.ascontiguousarray(np.asarray(inp["L0_ffn_Wsw"], f32).transpose(1, 0, 2).reshape(1024, 6144)),
        bsw_cols=cols128(np.asarray(inp["L0_ffn_bsw"], f32)),
        wdn=np.ascontiguousarray(np.asarray(inp["L0_ffn_Wdn"], f32).reshape(6144, 2048)),
        bdn_rows=np.ascontiguousarray(np.asarray(inp["L0_ffn_bdn"], f32)),
        wsu=np.asarray(inp["L0_ffn_Wsu"], f32),
        bsu_cols=np.ascontiguousarray(np.asarray(inp["L0_ffn_bsu"], f32).reshape(16, 128).T),
        wsa=np.asarray(inp["L0_ffn_Wsa"], f32),
        bsa_cols=np.ascontiguousarray(np.asarray(inp["L0_ffn_bsa"], f32).reshape(8, 128).T),
        wsd=np.asarray(inp["L0_ffn_Wsd"], f32),
        bsd_row=np.ascontiguousarray(np.asarray(inp["L0_ffn_bsd"], f32).reshape(1, 2048)),
        gther=gth,
        ident_bf=np.eye(128, dtype=np.float32).astype(BFNP),
        identf=np.eye(128, dtype=np.float32),
        mski=np.ascontiguousarray(np.triu(np.ones((128, 128), np.float32))),
        msks=np.ascontiguousarray(np.triu(np.ones((128, 128), np.float32), 1)),
        ones_row=np.ones((1, 128), np.float32),
        ones_row_bf=np.ones((1, 128), np.float32).astype(BFNP),
        ones_col=np.ones((128, 1), np.float32),
        ones_col_bf=np.ones((128, 1), np.float32).astype(BFNP),
        selbf=sel.astype(BFNP),
    )


INPUT_DECL = dict(
    x=([T, D], F32), xh=([TB, D], F32),
    wqkv=([D, CONVC], F32), convw=([128, 128], F32), wz=([D, VCC], F32),
    wba=([D, 32], F32), dtb_row=([1, NVH], F32), alog_row=([1, NVH], F32),
    normw_col=([128, 1], F32), wout=([4096, 2048], F32),
    wg=([2048, 16], F32), bg_rows=([1, 16], F32),
    wup=([2048, 6144], F32), bup_cols=([128, 48], F32),
    wsw=([1024, 6144], F32), bsw_cols=([128, 48], F32),
    wdn=([6144, 2048], F32), bdn_rows=([6, 2048], F32),
    wsu=([2048, 2048], F32), bsu_cols=([128, 16], F32),
    wsa=([1024, 1024], F32), bsa_cols=([128, 8], F32),
    wsd=([2048, 2048], F32), bsd_row=([1, 2048], F32),
    gther=([128, 32], I32),
    ident_bf=([128, 128], BF16), identf=([128, 128], F32),
    mski=([128, 128], F32), msks=([128, 128], F32),
    ones_row=([1, 128], F32), ones_row_bf=([1, 128], BF16),
    ones_col=([128, 1], F32), ones_col_bf=([128, 1], BF16),
    selbf=([8, 768], BF16),
)


ARENA_BASE = 22528
ARENA_TOP = 222000


def _layout():
    L = {}
    off = [ARENA_BASE]

    def al(name, nbytes, at=None):
        a = off[0] if at is None else at
        if at is None:
            off[0] = a + nbytes
        L[name] = (a, nbytes)

    al("xn_fm", 16384)
    al("qkvp", 32768)
    al("sz", 16384)
    al("wt0", 8192)
    al("wt1", 8192)
    al("xtm", 8192)
    al("stg", 6144)
    al("cvt", 2048)
    al("strow", 6144)
    al("q_tm", 2048); al("k_tm", 2048); al("v_tm", 4096)
    al("qs_tm", 2048); al("kn_tm", 2048)
    al("vb", 4096); al("kbg", 4096); al("kdec", 4096)
    al("qgi_fm", 4096)
    al("C", 4096); al("A", 4096); al("C2", 4096); al("A2", 4096)
    al("P", 4096); al("P2", 4096)
    al("vnew", 4096); al("y_tm", 4096); al("nrm_fm", 4096)
    al("mdiff", 2048)
    al("egl", 512); al("gch", 2048); al("gT", 2048); al("ba_all", 512)
    al("lncol", 128); al("rstc", 128)
    al("r_bc", 1024); al("mr_bc", 1024); al("halo", 256); al("S_bf", 4096)
    al("bdn_bf", 4096); al("bsd_bf", 4096); al("selbf_t", 2048)
    al("wgt", 512)
    al("m1", 4096, at=L["xtm"][0])
    al("m0", 4096, at=L["xtm"][0] + 4096)
    al("qs_fm", 2048, at=L["q_tm"][0])
    al("kn_fm", 2048, at=L["k_tm"][0])
    al("qgi_tm", 4096, at=L["v_tm"][0])
    al("vc", 4096, at=L["C2"][0])
    al("kcdT", 4096, at=L["A2"][0])
    al("ysq", 4096, at=L["vnew"][0])
    al("Gm", 2048, at=L["vnew"][0])
    al("qkTm", 2048, at=L["vnew"][0] + 2048)
    al("nrmB", 32768, at=L["qkvp"][0])
    for n in range(4):
        al("h2w%d" % n, 8192, at=L["qkvp"][0] + n * 8192)
    al("h2w4", 8192, at=L["sz"][0])
    al("h2w5", 8192, at=L["sz"][0] + 8192)
    al("xt_fm", 16384, at=L["xn_fm"][0])
    al("res0s", 8192, at=L["xtm"][0])
    al("resfs", 8192, at=L["stg"][0])
    al("h_fm", 8192, at=L["cvt"][0])
    al("sil", 8192, at=L["q_tm"][0])
    al("sh_fm", 16384, at=L["C"][0])
    al("sh2", 16384, at=L["P"][0])
    al("xt1b", 1024, at=L["nrm_fm"][0])
    al("wbc", 1024, at=L["r_bc"][0])
    al("gat", 2048, at=L["mdiff"][0])
    al("wfm", 1024, at=L["egl"][0])
    assert off[0] <= ARENA_TOP, off[0]
    return L


LAY = _layout()
assert LAY["strow"][0] == LAY["cvt"][0] + 2048
assert LAY["k_tm"][0] == LAY["q_tm"][0] + 2048 and LAY["v_tm"][0] == LAY["k_tm"][0] + 2048
assert LAY["gch"][0] == LAY["egl"][0] + 512


def granules(*names):
    out = set()
    for n in names:
        a, sz = LAY[n]
        for g in range(a // GRAN, (a + sz + GRAN - 1) // GRAN):
            out.add("g%d" % g)
    return tuple(out)


ARENA_DECL = dict(
    xn_fm=([128, 8192], BF16), qkvp=([128, 16384], BF16), sz=([128, 8192], BF16),
    wt0=([128, 4096], BF16), wt1=([128, 4096], BF16),
    xtm=([128, 4096], BF16), stg=([2, 1536], F32), cvt=([128, 512], F32),
    strow=([1, 1536], F32),
    q_tm=([128, 1024], BF16), k_tm=([128, 1024], BF16), v_tm=([128, 2048], BF16),
    qs_tm=([128, 1024], BF16), kn_tm=([128, 1024], BF16),
    vb=([128, 2048], BF16), kbg=([128, 2048], BF16), kdec=([128, 2048], BF16),
    qgi_fm=([128, 2048], BF16),
    Gm=([128, 1024], BF16), qkTm=([128, 1024], BF16),
    C=([128, 2048], BF16), A=([128, 2048], BF16),
    C2=([128, 2048], BF16), A2=([128, 2048], BF16),
    P=([128, 2048], BF16), P2=([128, 2048], BF16),
    vnew=([128, 2048], BF16), y_tm=([128, 2048], BF16), nrm_fm=([128, 2048], BF16),
    mdiff=([128, 512], F32),
    egl=([128, 64], F32), gch=([128, 448], F32), gT=([16, 512], F32),
    ba_all=([128, 128], F32), lncol=([128, 16], F32), rstc=([128, 16], F32),
    r_bc=([128, 512], BF16), mr_bc=([128, 512], BF16),
    halo=([128, 96], BF16), S_bf=([128, 2048], BF16),
    bdn_bf=([6, 2048], BF16), bsd_bf=([1, 2048], BF16), selbf_t=([8, 768], BF16),
    wgt=([128, 128], BF16),
    m1=([128, 2048], BF16), m0=([128, 2048], BF16),
    qs_fm=([128, 1024], BF16), kn_fm=([128, 1024], BF16), qgi_tm=([128, 2048], BF16),
    vc=([128, 2048], BF16), kcdT=([128, 2048], BF16), ysq=([128, 2048], BF16),
    nrmB=([128, 16384], BF16),
    h2w0=([128, 4096], BF16), h2w1=([128, 4096], BF16), h2w2=([128, 4096], BF16),
    h2w3=([128, 4096], BF16), h2w4=([128, 4096], BF16), h2w5=([128, 4096], BF16),
    xt_fm=([128, 8192], BF16), res0s=([128, 2048], F32), resfs=([128, 2048], F32),
    h_fm=([128, 4096], BF16), sil=([128, 4096], BF16),
    sh_fm=([128, 8192], BF16), sh2=([128, 8192], BF16),
    xt1b=([128, 512], BF16), wbc=([128, 512], BF16),
    gat=([128, 512], F32), wfm=([8, 512], BF16),
)


class Ctx:
    pass


def build(phase_a_only=False, debug_ccin=False, ntiles=NTILES):
    nc = bass.Bass("TRN2", target_bir_lowering=False)
    din = {n: nc.dram_tensor(n, list(s), d, kind="ExternalInput")
           for n, (s, d) in INPUT_DECL.items()}
    y = nc.dram_tensor("y", [TB, D], F32, kind="ExternalOutput")
    dr = {}
    dr["wqkv_bf"] = nc.dram_tensor("wqkv_bf", [16, 128, 16, 256], BF16)
    dr["wz_bf"] = nc.dram_tensor("wz_bf", [8, 128, 16, 256], BF16)
    dr["wba_bf"] = nc.dram_tensor("wba_bf", [2048, 32], BF16)
    dr["wg_bf"] = nc.dram_tensor("wg_bf", [2048, 16], BF16)
    dr["wup_bf"] = nc.dram_tensor("wup_bf", [24, 128, 16, 256], BF16)
    dr["wsw_bf"] = nc.dram_tensor("wsw_bf", [24, 128, 8, 256], BF16)
    dr["wsu_bf"] = nc.dram_tensor("wsu_bf", [8, 128, 16, 256], BF16)
    dr["wsa_bf"] = nc.dram_tensor("wsa_bf", [4, 128, 8, 256], BF16)
    dr["wdn_bf"] = nc.dram_tensor("wdn_bf", [4, 48, 128, 512], BF16)
    dr["wsd_bf"] = nc.dram_tensor("wsd_bf", [4, 16, 128, 512], BF16)
    dr["wout_bf"] = nc.dram_tensor("wout_bf", [4, 32, 128, 512], BF16)
    dr["cc_in"] = nc.dram_tensor("cc_in", [4, VCC, 1024], BF16,
                                 kind="ExternalOutput" if debug_ccin else "Internal")
    for q in range(4):
        dr["cc_out%d" % q] = nc.dram_tensor("cc_out%d" % q, [2 * VCC, 1024], BF16)
    dr["res0_d"] = nc.dram_tensor("res0_d", [TB, D], F32)
    dr["resf_d"] = nc.dram_tensor("resf_d", [TB, D], F32)

    from contextlib import ExitStack
    with ExitStack() as stk:
        E = stk.enter_context
        sb = {}
        for n, shape, dt in (
                ("ident_bf", [128, 128], BF16), ("identf", [128, 128], F32),
                ("mski", [128, 128], F32), ("msks", [128, 128], F32),
                ("convw", [128, 128], F32),
                ("ones_row", [1, 128], F32), ("ones_row_bf", [1, 128], BF16),
                ("ones_col", [128, 1], F32), ("ones_col_bf", [128, 1], BF16),
                ("normw_col", [128, 1], F32),
                ("bup_cols", [128, 48], F32), ("bsw_cols", [128, 48], F32),
                ("bsu_cols", [128, 16], F32), ("bsa_cols", [128, 8], F32),
                ("bg_rows", [1, 16], F32), ("gth_idx", [128, 32], I32),
                ("dtb_bc", [128, 16], F32), ("acoef_bc", [128, 16], F32),
                ("rowtmp", [1, 16], F32), ("rowtmp2", [1, 16], F32),
                ("halo2", [128, 96], BF16),
                ("wba_t", [128, 512], BF16), ("bg_bf", [1, 16], BF16)):
            sb[n] = nc.alloc_sbuf_tensor("c_" + n, shape, dt, side="right")
        for n, (shape, dt) in ARENA_DECL.items():
            a, sz_ = LAY[n]
            nb = shape[1] * (2 if dt in (BF16,) else 4)
            assert nb <= sz_, (n, nb, sz_)
            sb[n] = nc.alloc_sbuf_tensor_at(n, shape, dt, offset=a)

        pb = {}
        for i in range(6):
            pb["pf%d" % i] = E(nc.psum_tensor("pf%d" % i, [128, 512], F32))
        pb["pt0"] = E(nc.psum_tensor("pt0", [128, 1024], BF16))
        pb["pt1"] = E(nc.psum_tensor("pt1", [128, 1024], BF16))

        sems = {}
        for s in ("dW0", "dW1", "dX", "dST", "dAUX", "dRS", "dGT", "_cc",
                  "dOUT", "dRES", "gAUX", "gW0", "gW1", "gX", "dNF", "gCAST"):
            sems[s] = E(nc.semaphore(s))
        eng_sems = {e: E(nc.semaphore("se_" + e)) for e in ENGS}

        block = E(nc.Block())
        P = Prog()
        cx = Ctx()
        cx.nc, cx.P, cx.sb, cx.pb, cx.din, cx.dr, cx.y = nc, P, sb, pb, din, dr, y
        cx.G = granules

        def make_coll(q):
            return lambda: P.coll(lambda e, q=q: e.collective_compute(
                "AllGather", ALU.bypass,
                replica_groups=[[0, 1], [2, 3], [4, 5], [6, 7]],
                ins=[dr["cc_in"].ap()[q]], outs=[dr["cc_out%d" % q].ap()]),
                reads=("cc_in_q%d" % q,), writes=("cc_out%d" % q,))

        cx.coll_at_tile = {} if phase_a_only else {2 * q + 2: make_coll(q)
                                                  for q in range(3)}
        cx.coll_b0 = make_coll(3)

        emit_prep(cx)
        for t in range(ntiles):
            emit_phaseA_tile(cx, t)
        if not phase_a_only:
            for tb in range(NBT):
                emit_phaseB_tile(cx, tb)
        final = []
        if phase_a_only or debug_ccin:
            final.append(("dST", 16 * P.dma_issued.get("dST", 0)))
            final.append(("_cc", P.dma_issued.get("_cc", 0)))
        if not phase_a_only:
            final.append(("dOUT", 16 * P.dma_issued.get("dOUT", 0)))
        P.finalize(block, sems, eng_sems, final)
    return nc


def emit_prep(cx):
    P, sb, pb, din, dr, G = cx.P, cx.sb, cx.pb, cx.din, cx.dr, cx.G

    for n in ("identf", "mski", "msks", "convw", "ones_row", "ones_col",
              "normw_col", "bup_cols", "bsw_cols", "bsu_cols", "bsa_cols",
              "bg_rows"):
        P.dma("sync", "dAUX", lambda e, n=n: e.dma_start(out=sb[n][:], in_=din[n].ap()),
              reads=(), writes=(n,))
    for n in ("ident_bf", "ones_row_bf", "ones_col_bf", "selbf"):
        dst = "selbf_t" if n == "selbf" else n
        P.dma("sync", "dAUX", lambda e, n=n, dst=dst: e.dma_start(out=sb[dst][:], in_=din[n].ap()),
              reads=(), writes=(dst,) if dst != "selbf_t" else G("selbf_t"))
    P.dma("gpsimd", "gAUX", lambda e: e.dma_start(out=sb["bdn_bf"][:], in_=din["bdn_rows"].ap()),
          reads=(), writes=G("bdn_bf"))
    P.dma("gpsimd", "gAUX", lambda e: e.dma_start(out=sb["bsd_bf"][:], in_=din["bsd_row"].ap()),
          reads=(), writes=G("bsd_bf"))
    P.dma("gpsimd", "gAUX", lambda e: e.dma_start(out=sb["bg_bf"][:], in_=din["bg_rows"].ap()),
          reads=(), writes=("bg_bf",))
    P.dma("sync", "dAUX", lambda e: e.dma_start(
        out=sb["gth_idx"][:], in_=din["gther"].ap()),
        reads=(), writes=("gth_idx",))

    P.dma("sync", "dAUX", lambda e: e.dma_start(out=sb["rowtmp"][:], in_=din["dtb_row"].ap()),
          reads=(), writes=("rowtmp",))
    P.c("tensor", lambda e: e.matmul(pb["pf0"][:, 0:16], sb["ones_row"][:],
                                     sb["rowtmp"][:], start=True, stop=True),
        reads=("rowtmp", "ones_row"), writes=("pf0",))
    P.c("vector", lambda e: e.tensor_copy(sb["dtb_bc"][:], pb["pf0"][:, 0:16]),
        reads=("pf0",), writes=("dtb_bc",))
    P.dma("sync", "dAUX", lambda e: e.dma_start(out=sb["rowtmp2"][:], in_=din["alog_row"].ap()),
          reads=(), writes=("rowtmp2",))
    P.c("scalar", lambda e: e.activation(sb["rowtmp2"][:], sb["rowtmp2"][:], AF.Exp),
        reads=("rowtmp2",), writes=("rowtmp2",))
    P.c("tensor", lambda e: e.matmul(pb["pf0"][:, 0:16], sb["ones_row"][:],
                                     sb["rowtmp2"][:], start=True, stop=True),
        reads=("rowtmp2", "ones_row"), writes=("pf0",))
    P.c("vector", lambda e: e.tensor_scalar_mul(sb["acoef_bc"][:], pb["pf0"][:, 0:16], -1.0),
        reads=("pf0",), writes=("acoef_bc",))

    cx.deferred = []

    def cast_round(dst, emit, defer):
        def rnd():
            st = P.dma("gpsimd", "gCAST", emit, reads=(), writes=())
            P.last_write[dst] = st
        cx.deferred.append(rnd) if defer else rnd()

    def cast_gpkc(src, dst, rows, kdim, gdim, defer=False):
        for k in range(rows // 128):
            for gb in range(0, gdim, 16):
                gn = min(16, gdim - gb)
                cast_round(dst, lambda e, k=k, gb=gb, gn=gn: e.dma_start(
                    out=dr[dst].ap()[gb:gb + gn, :, k, :].transpose([1, 0, 2]),
                    in_=din[src].ap()[k * 128:(k + 1) * 128, gb * 256:(gb + gn) * 256]
                    .rearrange("p (g c) -> p g c", g=gn)), defer)

    def cast_hgrc(src, dst, rows, hg, defer=False):
        for r in range(rows // 128):
            cast_round(dst, lambda e, r=r: e.dma_start(
                out=dr[dst].ap()[:, r, :, :].transpose([1, 0, 2]),
                in_=din[src].ap()[r * 128:(r + 1) * 128, :]
                .rearrange("p (g c) -> p g c", g=hg)), defer)

    cast_gpkc("wqkv", "wqkv_bf", 2048, 16, 16)
    cast_gpkc("wz", "wz_bf", 2048, 16, 8)
    for k in range(16):
        P.dma("gpsimd", "gW0", lambda e, k=k: e.dma_start(
            out=sb["wba_t"][:, k * 32:(k + 1) * 32], in_=din["wba"].ap()[k * 128:(k + 1) * 128, :]),
            reads=("wba_t",), writes=("wba_t",))
    for k in range(16):
        cast_round("wg_bf", lambda e, k=k: e.dma_start(
            out=dr["wg_bf"].ap()[k * 128:(k + 1) * 128, :],
            in_=din["wg"].ap()[k * 128:(k + 1) * 128, :]), False)

    cast_hgrc("wout", "wout_bf", 4096, 4, defer=True)
    cast_gpkc("wup", "wup_bf", 2048, 16, 24, defer=True)
    cast_gpkc("wsw", "wsw_bf", 1024, 8, 24, defer=True)
    cast_gpkc("wsu", "wsu_bf", 2048, 16, 8, defer=True)
    cast_gpkc("wsa", "wsa_bf", 1024, 8, 4, defer=True)
    cast_hgrc("wdn", "wdn_bf", 6144, 4, defer=True)
    cast_hgrc("wsd", "wsd_bf", 2048, 4, defer=True)

    P.c("vector", lambda e: e.memset(sb["S_bf"][:], 0.0), reads=(), writes=G("S_bf"))
    P.c("vector", lambda e: e.memset(sb["halo"][:], 0.0), reads=(), writes=G("halo"))
    P.c("vector", lambda e: e.memset(sb["stg"][0:2, 0:512], 1.0), reads=(), writes=("stgc0",))
    P.c("vector", lambda e: e.memset(sb["stg"][0:2, 512:1536], -1.0),
        reads=(), writes=("stgc1", "stgc2"))


def emit_phaseA_tile(cx, t):
    P, sb, pb, din, dr, G = cx.P, cx.sb, cx.pb, cx.din, cx.dr, cx.G
    t0 = t * TT

    nleft = NTILES - t
    nbatch = (len(cx.deferred) + nleft - 1) // nleft if cx.deferred else 0
    batch = cx.deferred[:nbatch]
    del cx.deferred[:nbatch]
    cx.tile_casts = batch[(len(batch) + 1) // 2:]

    for rnd in range(2):
        for s2 in range(2):
            s = rnd * 2 + s2
            P.dma("gpsimd", "gX", lambda e, s=s, s2=s2: e.dma_start(
                out=sb["xtm"][:, s2 * 2048:(s2 + 1) * 2048],
                in_=din["x"].ap()[t0 + s * 128:t0 + (s + 1) * 128, :]),
                reads=G("xtm"), writes=G("xtm"))
        for s2 in range(2):
            s = rnd * 2 + s2
            for fg in range(2):
                bank = "pt%d" % fg
                for f2 in range(8):
                    f = fg * 8 + f2
                    P.c("tensor", lambda e, s2=s2, f=f, f2=f2, bank=bank: e.transpose(
                        pb[bank][:, f2 * 128:(f2 + 1) * 128],
                        sb["xtm"][:, s2 * 2048 + f * 128:s2 * 2048 + (f + 1) * 128],
                        sb["ident_bf"][:]),
                        reads=G("xtm") + ("ident_bf",), writes=(bank,))
                P.c("scalar", lambda e, s=s, fg=fg, bank=bank: e.activation(
                    bass.AP(sb["xn_fm"], fg * 8 * 512 + s * 128, [[8192, 128], [512, 8], [1, 128]]),
                    pb[bank][:].rearrange("p (f c) -> p f c", f=8), AF.Copy),
                    reads=(bank,), writes=G("xn_fm"))
    for cast_rnd in batch[:(len(batch) + 1) // 2]:
        cast_rnd()

    for k in range(16):
        P.c("tensor", lambda e, k=k: e.matmul(
            pb["pf4"][0:1, :], sb["ones_col_bf"][:], sb["xn_fm"][:, k * 512:(k + 1) * 512],
            start=(k == 0), stop=(k == 15)),
            reads=G("xn_fm") + ("ones_col_bf",), writes=("pf4",))
    for k in range(16):
        P.c("scalar", lambda e, k=k: e.activation(
            sb["r_bc"][:, 0:512], sb["xn_fm"][:, k * 512:(k + 1) * 512], AF.Square),
            reads=G("xn_fm"), writes=G("r_bc"))
        P.c("tensor", lambda e, k=k: e.matmul(
            pb["pf5"][0:1, :], sb["ones_col_bf"][:], sb["r_bc"][:, 0:512],
            start=(k == 0), stop=(k == 15)),
            reads=G("r_bc") + ("ones_col_bf",), writes=("pf5",))
    P.c("vector", lambda e: e.tensor_scalar_mul(sb["strow"][0:1, 0:512], pb["pf4"][0:1, :], 1.0 / D),
        reads=("pf4",), writes=G("strow"))
    P.c("vector", lambda e: e.tensor_scalar_mul(sb["strow"][0:1, 512:1024], pb["pf5"][0:1, :], 1.0 / D),
        reads=("pf5",), writes=G("strow"))
    P.c("vector", lambda e: e.scalar_tensor_tensor(
        sb["strow"][0:1, 1024:1536], sb["strow"][0:1, 0:512], -1.0, sb["strow"][0:1, 0:512],
        ALU.mult, ALU.mult), reads=G("strow"), writes=G("strow"))
    P.c("vector", lambda e: e.tensor_add(sb["strow"][0:1, 1024:1536],
                                         sb["strow"][0:1, 1024:1536], sb["strow"][0:1, 512:1024]),
        reads=G("strow"), writes=G("strow"))
    P.c("vector", lambda e: e.tensor_scalar_add(sb["strow"][0:1, 1024:1536],
                                                sb["strow"][0:1, 1024:1536], LN_EPS),
        reads=G("strow"), writes=G("strow"))
    P.c("scalar", lambda e: e.activation(sb["strow"][0:1, 512:1024], sb["strow"][0:1, 1024:1536],
                                         AF.Ln), reads=G("strow"), writes=G("strow"))
    P.c("scalar", lambda e: e.activation(sb["strow"][0:1, 512:1024], sb["strow"][0:1, 512:1024],
                                         AF.Exp, scale=-0.5), reads=G("strow"), writes=G("strow"))
    P.c("vector", lambda e: e.tensor_mul(sb["strow"][0:1, 0:512], sb["strow"][0:1, 0:512],
                                         sb["strow"][0:1, 512:1024]),
        reads=G("strow"), writes=G("strow"))
    P.c("tensor", lambda e: e.matmul(pb["pf4"][:, :], sb["ones_row"][:],
                                     sb["strow"][0:1, 512:1024], start=True, stop=True),
        reads=G("strow") + ("ones_row",), writes=("pf4",))
    P.c("scalar", lambda e: e.activation(sb["r_bc"][:, 0:512], pb["pf4"][:, :], AF.Copy),
        reads=("pf4",), writes=G("r_bc"))
    P.c("tensor", lambda e: e.matmul(pb["pf5"][:, :], sb["ones_row"][:],
                                     sb["strow"][0:1, 0:512], start=True, stop=True),
        reads=G("strow") + ("ones_row",), writes=("pf5",))
    P.c("scalar", lambda e: e.activation(sb["mr_bc"][:, 0:512], pb["pf5"][:, :], AF.Copy),
        reads=("pf5",), writes=G("mr_bc"))
    for k in range(16):
        eng = "gpsimd" if k % 2 == 0 else "vector"
        P.c(eng, lambda e, k=k: e.tensor_mul(
            sb["xn_fm"][:, k * 512:(k + 1) * 512], sb["xn_fm"][:, k * 512:(k + 1) * 512],
            sb["r_bc"][:, 0:512]),
            reads=G("xn_fm", "r_bc"), writes=G("xn_fm"))
        P.c(eng, lambda e, k=k: e.tensor_sub(
            sb["xn_fm"][:, k * 512:(k + 1) * 512], sb["xn_fm"][:, k * 512:(k + 1) * 512],
            sb["mr_bc"][:, 0:512]),
            reads=G("xn_fm", "mr_bc"), writes=G("xn_fm"))

    for g in range(16):
        slot = g % 2
        buf = "wt%d" % slot
        P.dma("sync", "dW%d" % slot, lambda e, g=g, buf=buf: e.dma_start(
            out=sb[buf][:, 0:4096], in_=dr["wqkv_bf"].ap()[g]),
            reads=("wqkv_bf",), writes=G(buf))
        for oc2 in range(2):
            oc = g * 2 + oc2
            bank = "pf%d" % (oc % 4)
            for k in range(16):
                P.c("tensor", lambda e, oc2=oc2, k=k, buf=buf, bank=bank: e.matmul(
                    pb[bank][:, :], sb[buf][:, k * 256 + oc2 * 128:k * 256 + oc2 * 128 + 128],
                    sb["xn_fm"][:, k * 512:(k + 1) * 512], start=(k == 0), stop=(k == 15)),
                    reads=G(buf, "xn_fm"), writes=(bank,))
            P.c("scalar", lambda e, oc=oc, bank=bank: e.activation(
                sb["qkvp"][:, oc * 512:(oc + 1) * 512], pb[bank][:, :], AF.Copy),
                reads=(bank,), writes=G("qkvp"))

    for g in range(8):
        slot = g % 2
        buf = "wt%d" % slot
        P.dma("sync", "dW%d" % slot, lambda e, g=g, buf=buf: e.dma_start(
            out=sb[buf][:, 0:4096], in_=dr["wz_bf"].ap()[g]),
            reads=("wz_bf",), writes=G(buf))
        for s in range(4):
            bank = "pf%d" % (s % 4)
            for k in range(16):
                P.c("tensor", lambda e, s=s, k=k, buf=buf, bank=bank: e.matmul(
                    pb[bank][:, 0:256], sb["xn_fm"][:, k * 512 + s * 128:k * 512 + (s + 1) * 128],
                    sb[buf][:, k * 256:(k + 1) * 256], start=(k == 0), stop=(k == 15)),
                    reads=G(buf, "xn_fm"), writes=(bank,))
            P.c("scalar", lambda e, s=s, g=g, bank=bank: e.activation(
                sb["sz"][:, s * 2048 + g * 256:s * 2048 + (g + 1) * 256],
                pb[bank][:, 0:256], AF.Silu),
                reads=(bank,), writes=G("sz"))

    for s in range(4):
        for k in range(16):
            P.c("tensor", lambda e, s=s, k=k: e.matmul(
                pb["pf4"][:, 0:32], sb["xn_fm"][:, k * 512 + s * 128:k * 512 + (s + 1) * 128],
                sb["wba_t"][:, k * 32:(k + 1) * 32], start=(k == 0), stop=(k == 15)),
                reads=("wba_t",) + G("xn_fm"), writes=("pf4",))
        P.c("vector", lambda e, s=s: e.tensor_copy(sb["ba_all"][:, s * 32:(s + 1) * 32],
                                                   pb["pf4"][:, 0:32]),
            reads=("pf4",), writes=G("ba_all"))

    for rnd in cx.tile_casts:
        rnd()
    cx.tile_casts = []
    if t in cx.coll_at_tile:
        cx.coll_at_tile.pop(t)()

    P.c("gpsimd", lambda e: e.tensor_copy(
        sb["halo2"][:].rearrange("p (c k) -> p c k", c=32),
        bass.AP(sb["qkvp"], 509, [[16384, 128], [512, 32], [1, 3]])),
        reads=G("qkvp"), writes=("halo2",))
    for cc in range(32):
        eng = "vector"
        scr = "cvt" if cc % 2 == 0 else "mdiff"
        base = cc * 512
        P.c(eng, lambda e, cc=cc, base=base, scr=scr: e.tensor_scalar_mul(
            sb[scr][:, 0:512], sb["qkvp"][:, base:base + 512],
            sb["convw"][:, cc * 4 + 3:cc * 4 + 4]),
            reads=G("qkvp") + ("convw",), writes=G(scr))
        for k in range(3):
            nh = 3 - k
            P.c(eng, lambda e, cc=cc, base=base, k=k, nh=nh, scr=scr: e.scalar_tensor_tensor(
                sb[scr][:, nh:512], sb["qkvp"][:, base:base + 512 - nh],
                sb["convw"][:, cc * 4 + k:cc * 4 + k + 1], sb[scr][:, nh:512],
                ALU.mult, ALU.add),
                reads=G("qkvp", scr) + ("convw",), writes=G(scr))
            P.c(eng, lambda e, cc=cc, k=k, nh=nh, scr=scr: e.scalar_tensor_tensor(
                sb[scr][:, 0:nh], sb["halo"][:, cc * 3 + k:cc * 3 + k + nh],
                sb["convw"][:, cc * 4 + k:cc * 4 + k + 1], sb[scr][:, 0:nh],
                ALU.mult, ALU.add),
                reads=G("halo", scr) + ("convw",), writes=G(scr))
        P.c(eng, lambda e, cc=cc, base=base, scr=scr: e.tensor_copy(
            sb["qkvp"][:, base:base + 512], sb[scr][:, 0:512]),
            reads=G(scr) + ("halo2",), writes=G("qkvp"))
    P.c("gpsimd", lambda e: e.tensor_copy(sb["halo"][:], sb["halo2"][:]),
        reads=("halo2",) + G("halo"), writes=G("halo"))

    gch = sb["gch"]
    bvw = bass.AP(sb["ba_all"], 0, [[128, 128], [32, 4], [1, 16]])
    avw = bass.AP(sb["ba_all"], 16, [[128, 128], [32, 4], [1, 16]])
    LNB, GC, R1, BGF, BET, KDF, EG = 0, 64, 128, 192, 256, 320, 384

    def gv(off):
        return gch[:, off:off + 64]
    P.c("vector", lambda e: e.tensor_scalar_min(gv(LNB), bvw, 0.0),
        reads=G("ba_all"), writes=G("gch"))
    P.c("scalar", lambda e: e.activation(gv(R1), bvw, AF.Abs),
        reads=G("ba_all"), writes=G("gch"))
    P.c("scalar", lambda e: e.activation(gv(R1), gv(R1), AF.Exp, scale=-1.0),
        reads=G("gch"), writes=G("gch"))
    P.c("scalar", lambda e: e.activation(gv(R1), gv(R1), AF.Ln, bias=1.0),
        reads=G("gch"), writes=G("gch"))
    P.c("vector", lambda e: e.tensor_sub(gv(LNB), gv(LNB), gv(R1)),
        reads=G("gch"), writes=G("gch"))
    P.c("vector", lambda e: e.tensor_tensor(
        gv(GC).rearrange("p (c v) -> p c v", c=4), avw,
        sb["dtb_bc"][:].unsqueeze(1).broadcast_to([128, 4, 16]), ALU.add),
        reads=G("ba_all") + ("dtb_bc",), writes=G("gch"))
    P.c("scalar", lambda e: e.activation(gv(BGF), gv(GC), AF.Abs),
        reads=G("gch"), writes=G("gch"))
    P.c("scalar", lambda e: e.activation(gv(BGF), gv(BGF), AF.Exp, scale=-1.0),
        reads=G("gch"), writes=G("gch"))
    P.c("scalar", lambda e: e.activation(gv(BGF), gv(BGF), AF.Ln, bias=1.0),
        reads=G("gch"), writes=G("gch"))
    P.c("vector", lambda e: e.tensor_scalar_max(gv(GC), gv(GC), 0.0),
        reads=G("gch"), writes=G("gch"))
    P.c("vector", lambda e: e.tensor_add(gv(GC), gv(GC), gv(BGF)),
        reads=G("gch"), writes=G("gch"))
    P.c("vector", lambda e: e.tensor_tensor(
        gv(GC).rearrange("p (c v) -> p c v", c=4), gv(GC).rearrange("p (c v) -> p c v", c=4),
        sb["acoef_bc"][:].unsqueeze(1).broadcast_to([128, 4, 16]), ALU.mult),
        reads=G("gch") + ("acoef_bc",), writes=G("gch"))
    for ci in range(CPT):
        P.c("tensor", lambda e, ci=ci: e.matmul(
            pb["pf5"][:, ci * 16:(ci + 1) * 16], sb["mski"][:],
            gch[:, GC + ci * 16:GC + (ci + 1) * 16], start=True, stop=True),
            reads=G("gch") + ("mski",), writes=("pf5",))
    P.c("vector", lambda e: e.tensor_copy(gv(GC), pb["pf5"][:, 0:64]),
        reads=("pf5",), writes=G("gch"))
    P.c("vector", lambda e: e.tensor_add(gv(R1), gv(GC), gv(LNB)),
        reads=G("gch"), writes=G("gch"))
    P.c("scalar", lambda e: e.activation(gv(BGF), gv(R1), AF.Exp),
        reads=G("gch"), writes=G("gch"))
    P.c("scalar", lambda e: e.activation(gv(BET), gv(LNB), AF.Exp),
        reads=G("gch"), writes=G("gch"))
    P.c("scalar", lambda e: e.activation(gv(EG), gv(GC), AF.Exp),
        reads=G("gch"), writes=G("gch"))

    for ci in range(CPT):
        emit_chunk(cx, t, ci)


def trans_group(cx, srcs, dst, evac_af=AF.Copy, evac_engs=("scalar",)):
    P, sb, pb, G = cx.P, cx.sb, cx.pb, cx.G
    for grp in range((len(srcs) + 7) // 8):
        bank = "pt%d" % (grp % 2)
        blk = srcs[grp * 8:(grp + 1) * 8]
        for j2, (tn, co) in enumerate(blk):
            P.c("tensor", lambda e, tn=tn, co=co, j2=j2, bank=bank: e.transpose(
                pb[bank][:, j2 * 128:(j2 + 1) * 128], sb[tn][:, co:co + 128],
                sb["ident_bf"][:]),
                reads=G(tn) + ("ident_bf",), writes=(bank,))
        eng = evac_engs[grp % len(evac_engs)]
        if eng == "scalar":
            P.c("scalar", lambda e, grp=grp, n=len(blk), bank=bank, dst=dst, af=evac_af: e.activation(
                sb[dst][:, grp * 1024:grp * 1024 + n * 128], pb[bank][:, 0:n * 128], af),
                reads=(bank,), writes=G(dst))
        else:
            P.c(eng, lambda e, grp=grp, n=len(blk), bank=bank, dst=dst: e.tensor_copy(
                sb[dst][:, grp * 1024:grp * 1024 + n * 128], pb[bank][:, 0:n * 128]),
                reads=(bank,), writes=G(dst))


def emit_chunk(cx, t, ci):
    P, sb, pb, G = cx.P, cx.sb, cx.pb, cx.G
    t0 = t * TT + ci * CH
    gch = sb["gch"]
    LNB, GC, R1, BGF, BET, KDF, EG = 0, 64, 128, 192, 256, 320, 384
    cs = ci * 16

    P.c("tensor", lambda e, cs=cs: e.transpose(pb["pf5"][0:16, 128:256],
                                               gch[:, GC + cs:GC + cs + 16], sb["identf"][:]),
        reads=G("gch") + ("identf",), writes=("pf5",))
    P.c("vector", lambda e: e.tensor_copy(sb["gT"][:, 0:128], pb["pf5"][0:16, 128:256]),
        reads=("pf5",), writes=G("gT"))
    P.c("tensor", lambda e, cs=cs: e.transpose(pb["pf5"][0:16, 256:384],
                                               gch[:, R1 + cs:R1 + cs + 16], sb["identf"][:]),
        reads=G("gch") + ("identf",), writes=("pf5",))
    P.c("vector", lambda e: e.tensor_copy(sb["gT"][:, 128:256], pb["pf5"][0:16, 256:384]),
        reads=("pf5",), writes=G("gT"))
    P.c("tensor", lambda e: e.matmul(
        pb["pf5"][0:1, 0:16], sb["gT"][:, 127:128], sb["identf"][0:16, 0:16],
        start=True, stop=True),
        reads=G("gT") + ("identf",), writes=("pf5",))
    P.c("vector", lambda e: e.tensor_copy(sb["rowtmp"][:], pb["pf5"][0:1, 0:16]),
        reads=("pf5",), writes=("rowtmp",))
    P.c("tensor", lambda e: e.matmul(
        pb["pf5"][:, 0:16], sb["ones_row"][:], sb["rowtmp"][:], start=True, stop=True),
        reads=("rowtmp", "ones_row"), writes=("pf5",))
    P.c("scalar", lambda e, ci=ci: e.activation(
        sb["egl"][:, ci * 16:(ci + 1) * 16], pb["pf5"][:, 0:16], AF.Exp),
        reads=("pf5",), writes=G("egl"))
    P.c("vector", lambda e, cs=cs: e.tensor_sub(gch[:, KDF + cs:KDF + cs + 16],
                                                pb["pf5"][:, 0:16], gch[:, GC + cs:GC + cs + 16]),
        reads=("pf5",) + G("gch"), writes=G("gch"))
    P.c("scalar", lambda e, cs=cs: e.activation(gch[:, KDF + cs:KDF + cs + 16],
                                                gch[:, KDF + cs:KDF + cs + 16], AF.Exp),
        reads=G("gch"), writes=G("gch"))

    trans_group(cx, [("qkvp", j * 512 + ci * 128) for j in range(0, 8)], "q_tm", AF.Silu)
    trans_group(cx, [("qkvp", j * 512 + ci * 128) for j in range(8, 16)], "k_tm", AF.Silu)
    trans_group(cx, [("qkvp", j * 512 + ci * 128) for j in range(16, 32)], "v_tm", AF.Silu)

    for (src, col0) in (("q_tm", 0), ("k_tm", 8)):
        P.c("scalar", lambda e, src=src: e.activation(sb["ysq"][:, 0:1024], sb[src][:], AF.Square),
            reads=G(src), writes=G("ysq"))
        P.c("vector", lambda e, col0=col0: e.tensor_reduce(
            sb["lncol"][:, col0:col0 + 8],
            sb["ysq"][:, 0:1024].rearrange("p (h d) -> p h d", h=8), AX.X, ALU.add),
            reads=G("ysq"), writes=G("lncol"))
    P.c("vector", lambda e: e.tensor_scalar_add(sb["lncol"][:, 0:16], sb["lncol"][:, 0:16],
                                                L2_EPS),
        reads=G("lncol"), writes=G("lncol"))
    P.c("scalar", lambda e: e.activation(sb["lncol"][:, 0:16], sb["lncol"][:, 0:16],
                                         AF.Ln), reads=G("lncol"), writes=G("lncol"))
    P.c("scalar", lambda e: e.activation(sb["lncol"][:, 0:16], sb["lncol"][:, 0:16],
                                         AF.Exp, scale=-0.5), reads=G("lncol"), writes=G("lncol"))
    P.c("vector", lambda e: e.tensor_scalar_mul(sb["lncol"][:, 0:8], sb["lncol"][:, 0:8],
                                                float(128.0 ** -0.5)),
        reads=G("lncol"), writes=G("lncol"))
    P.c("vector", lambda e: e.tensor_tensor(
        sb["qs_tm"][:].rearrange("p (h d) -> p h d", h=8),
        sb["q_tm"][:].rearrange("p (h d) -> p h d", h=8),
        sb["lncol"][:, 0:8].unsqueeze(2).broadcast_to([128, 8, 128]), ALU.mult),
        reads=G("q_tm", "lncol"), writes=G("qs_tm"))
    P.c("vector", lambda e: e.tensor_tensor(
        sb["kn_tm"][:].rearrange("p (h d) -> p h d", h=8),
        sb["k_tm"][:].rearrange("p (h d) -> p h d", h=8),
        sb["lncol"][:, 8:16].unsqueeze(2).broadcast_to([128, 8, 128]), ALU.mult),
        reads=G("k_tm", "lncol"), writes=G("kn_tm"))
    P.c("gpsimd", lambda e, cs=cs: e.tensor_tensor(
        sb["vb"][:].rearrange("p (v d) -> p v d", v=16),
        sb["v_tm"][:].rearrange("p (v d) -> p v d", v=16),
        gch[:, BET + cs:BET + cs + 16].unsqueeze(2).broadcast_to([128, 16, 128]), ALU.mult),
        reads=G("v_tm", "gch"), writes=G("vb"))
    P.c("gpsimd", lambda e, cs=cs: e.tensor_tensor(
        sb["kbg"][:].rearrange("p (h u d) -> p h u d", h=8, u=2),
        sb["kn_tm"][:].rearrange("p (h d) -> p h d", h=8)
        .unsqueeze(2).broadcast_to([128, 8, 2, 128]),
        gch[:, BGF + cs:BGF + cs + 16].rearrange("p (h u) -> p h u", h=8)
        .unsqueeze(3).broadcast_to([128, 8, 2, 128]), ALU.mult),
        reads=G("kn_tm", "gch"), writes=G("kbg"))
    P.c("vector", lambda e, cs=cs: e.tensor_tensor(
        sb["kdec"][:].rearrange("p (h u d) -> p h u d", h=8, u=2),
        sb["kn_tm"][:].rearrange("p (h d) -> p h d", h=8)
        .unsqueeze(2).broadcast_to([128, 8, 2, 128]),
        gch[:, KDF + cs:KDF + cs + 16].rearrange("p (h u) -> p h u", h=8)
        .unsqueeze(3).broadcast_to([128, 8, 2, 128]), ALU.mult),
        reads=G("kn_tm", "gch"), writes=G("kdec"))
    P.c("vector", lambda e, cs=cs: e.tensor_tensor(
        sb["qgi_tm"][:].rearrange("p (h u d) -> p h u d", h=8, u=2),
        sb["qs_tm"][:].rearrange("p (h d) -> p h d", h=8)
        .unsqueeze(2).broadcast_to([128, 8, 2, 128]),
        gch[:, EG + cs:EG + cs + 16].rearrange("p (h u) -> p h u", h=8)
        .unsqueeze(3).broadcast_to([128, 8, 2, 128]), ALU.mult),
        reads=G("qs_tm", "gch"), writes=G("qgi_tm"))

    trans_group(cx, [("qs_tm", j * 128) for j in range(8)], "qs_fm")
    trans_group(cx, [("kn_tm", j * 128) for j in range(8)], "kn_fm",
                evac_engs=("vector",))
    trans_group(cx, [("qgi_tm", j * 128) for j in range(16)], "qgi_fm",
                evac_engs=("scalar", "vector"))

    for rd in range(4):
        P.dma("scalar", "dRS", lambda e, rd=rd: e.dma_start(
            out=sb["stg"][1:2, 0:512], in_=sb["gT"][rd * 4:rd * 4 + 4, 0:128]),
            reads=G("gT"), writes=("stg_l",))
        P.dma("scalar", "dRS", lambda e, rd=rd: e.dma_start(
            out=sb["stg"][0:1, 512:1024], in_=sb["gT"][rd * 4:rd * 4 + 4, 0:128]),
            reads=G("gT"), writes=("stg_r0",))
        P.dma("scalar", "dRS", lambda e, rd=rd: e.dma_start(
            out=sb["stg"][0:1, 1024:1536], in_=sb["gT"][rd * 4:rd * 4 + 4, 128:256]),
            reads=G("gT"), writes=("stg_r1",))
        for v4 in range(4):
            P.c("tensor", lambda e, v4=v4: e.matmul(
                pb["pf0"][:, v4 * 128:(v4 + 1) * 128],
                bass.AP(sb["stg"], v4 * 128, [[1536, 2], [1, 128]]),
                bass.AP(sb["stg"], 1024 + v4 * 128, [[1536, 2], [1, 128]]),
                start=True, stop=True),
                reads=("stg_l", "stg_r1", "stgc0", "stgc2"), writes=("pf0",))
            P.c("tensor", lambda e, v4=v4: e.matmul(
                pb["pf1"][:, v4 * 128:(v4 + 1) * 128],
                bass.AP(sb["stg"], v4 * 128, [[1536, 2], [1, 128]]),
                bass.AP(sb["stg"], 512 + v4 * 128, [[1536, 2], [1, 128]]),
                start=True, stop=True),
                reads=("stg_l", "stg_r0", "stgc0", "stgc1"), writes=("pf1",))
        P.c("vector", lambda e: e.tensor_tensor(
            sb["mdiff"][:].rearrange("p (v c) -> p v c", v=4),
            pb["pf0"][:].rearrange("p (v c) -> p v c", v=4),
            sb["msks"][:].unsqueeze(1).broadcast_to([128, 4, 128]), ALU.mult),
            reads=("pf0", "msks"), writes=G("mdiff"))
        P.c("vector", lambda e: e.tensor_tensor(
            sb["cvt"][:].rearrange("p (v c) -> p v c", v=4),
            pb["pf1"][:].rearrange("p (v c) -> p v c", v=4),
            sb["mski"][:].unsqueeze(1).broadcast_to([128, 4, 128]), ALU.mult),
            reads=("pf1", "mski"), writes=G("cvt"))
        P.c("scalar", lambda e, rd=rd: e.activation(
            sb["m1"][:, rd * 512:(rd + 1) * 512], sb["mdiff"][:, 0:512], AF.Exp),
            reads=G("mdiff"), writes=G("m1"))
        P.c("scalar", lambda e, rd=rd: e.activation(
            sb["m0"][:, rd * 512:(rd + 1) * 512], sb["cvt"][:, 0:512], AF.Exp),
            reads=G("cvt"), writes=G("m0"))

    for rnd in range(2):
        for k4 in range(4):
            kh = rnd * 4 + k4
            P.c("tensor", lambda e, kh=kh, k4=k4: e.matmul(
                pb["pf2"][:, k4 * 128:(k4 + 1) * 128], sb["kn_fm"][:, kh * 128:(kh + 1) * 128],
                sb["kn_fm"][:, kh * 128:(kh + 1) * 128], start=True, stop=True),
                reads=G("kn_fm"), writes=("pf2",))
            P.c("tensor", lambda e, kh=kh, k4=k4: e.matmul(
                pb["pf3"][:, k4 * 128:(k4 + 1) * 128], sb["kn_fm"][:, kh * 128:(kh + 1) * 128],
                sb["qs_fm"][:, kh * 128:(kh + 1) * 128], start=True, stop=True),
                reads=G("kn_fm", "qs_fm"), writes=("pf3",))
        P.c("vector", lambda e, rnd=rnd: e.scalar_tensor_tensor(
            sb["Gm"][:, rnd * 512:(rnd + 1) * 512].rearrange("p (v c) -> p v c", v=4),
            pb["pf2"][:].rearrange("p (v c) -> p v c", v=4), -1.0,
            sb["msks"][:].unsqueeze(1).broadcast_to([128, 4, 128]), ALU.mult, ALU.mult),
            reads=("pf2", "msks"), writes=G("Gm"))
        P.c("vector", lambda e, rnd=rnd: e.tensor_tensor(
            sb["qkTm"][:, rnd * 512:(rnd + 1) * 512].rearrange("p (v c) -> p v c", v=4),
            pb["pf3"][:].rearrange("p (v c) -> p v c", v=4),
            sb["mski"][:].unsqueeze(1).broadcast_to([128, 4, 128]), ALU.mult),
            reads=("pf3", "mski"), writes=G("qkTm"))
    P.c("gpsimd", lambda e: e.tensor_tensor(
        sb["C"][:].rearrange("p (h u d) -> p h u d", h=8, u=2),
        sb["Gm"][:].rearrange("p (h d) -> p h d", h=8)
        .unsqueeze(2).broadcast_to([128, 8, 2, 128]),
        sb["m1"][:].rearrange("p (h u d) -> p h u d", h=8, u=2), ALU.mult),
        reads=G("Gm", "m1"), writes=G("C"))
    P.c("vector", lambda e: e.tensor_tensor(
        sb["m0"][:].rearrange("p (h u d) -> p h u d", h=8, u=2),
        sb["qkTm"][:].rearrange("p (h d) -> p h d", h=8)
        .unsqueeze(2).broadcast_to([128, 8, 2, 128]),
        sb["m0"][:].rearrange("p (h u d) -> p h u d", h=8, u=2), ALU.mult),
        reads=G("qkTm", "m0"), writes=G("m0"))

    trans_group(cx, [("C", j * 128) for j in range(16)], "A",
                evac_engs=("scalar", "vector"))
    P.c("gpsimd", lambda e: e.tensor_tensor(
        sb["P"][:].rearrange("p (v c) -> p v c", v=16),
        sb["C"][:].rearrange("p (v c) -> p v c", v=16),
        sb["ident_bf"][:].unsqueeze(1).broadcast_to([128, 16, 128]), ALU.add),
        reads=G("C") + ("ident_bf",), writes=G("P"))

    def quad_evac(grp, bank, dst):
        if grp % 2 == 0:
            P.c("scalar", lambda e, grp=grp, bank=bank, dst=dst: e.activation(
                sb[dst][:, grp * 512:(grp + 1) * 512], pb[bank][:], AF.Copy),
                reads=(bank,), writes=G(dst))
        else:
            P.c("vector", lambda e, grp=grp, bank=bank, dst=dst: e.tensor_copy(
                sb[dst][:, grp * 512:(grp + 1) * 512], pb[bank][:]),
                reads=(bank,), writes=G(dst))

    def quad_mm(lhs_of, rhs_of, dst):
        for grp in range(4):
            bank = "pf%d" % (grp % 4)
            rd = set()
            for v2 in range(4):
                vh = grp * 4 + v2
                ln, lo = lhs_of(vh)
                rn, ro = rhs_of(vh)
                rd.add(ln)
                rd.add(rn)
                P.c("tensor", lambda e, v2=v2, ln=ln, lo=lo, rn=rn, ro=ro, bank=bank: e.matmul(
                    pb[bank][:, v2 * 128:(v2 + 1) * 128],
                    sb[ln][:, lo:lo + 128], sb[rn][:, ro:ro + 128],
                    start=True, stop=True),
                    reads=G(*rd), writes=(bank,))
            quad_evac(grp, bank, dst)

    def quad_mm_addp(a_name, p_name, dst):
        for grp in range(4):
            bank = "pf%d" % (grp % 4)
            for v2 in range(4):
                vh = grp * 4 + v2
                P.c("tensor", lambda e, v2=v2, vh=vh, bank=bank, a=a_name, p=p_name: e.matmul(
                    pb[bank][:, v2 * 128:(v2 + 1) * 128],
                    sb[a][:, vh * 128:(vh + 1) * 128], sb[p][:, vh * 128:(vh + 1) * 128],
                    start=True, stop=False),
                    reads=G(a_name, p_name), writes=(bank,))
                P.c("tensor", lambda e, v2=v2, vh=vh, bank=bank, p=p_name: e.matmul(
                    pb[bank][:, v2 * 128:(v2 + 1) * 128],
                    sb["ident_bf"][:], sb[p][:, vh * 128:(vh + 1) * 128],
                    start=False, stop=True),
                    reads=("ident_bf",) + G(p_name), writes=(bank,))
            quad_evac(grp, bank, dst)

    cur_c, cur_a, cur_p = "C", "A", "P"
    oth_c, oth_a, oth_p = "C2", "A2", "P2"
    for lvl in range(6):
        if lvl < 5:
            quad_mm(lambda vh, a=cur_a: (a, vh * 128), lambda vh, c=cur_c: (c, vh * 128), oth_c)
        quad_mm(lambda vh, c=cur_c: (c, vh * 128), lambda vh, a=cur_a: (a, vh * 128), oth_a)
        quad_mm_addp(oth_a, cur_p, oth_p)
        cur_c, oth_c = oth_c, cur_c
        cur_a, oth_a = oth_a, cur_a
        cur_p, oth_p = oth_p, cur_p
    Pfin = cur_p

    quad_mm(lambda vh, p=Pfin: (p, vh * 128), lambda vh: ("vb", vh * 128), "vc")
    quad_mm(lambda vh: ("kbg", vh * 128), lambda vh, p=Pfin: (p, vh * 128), "kcdT")

    for grp in range(4):
        bs = "pf%d" % (2 + grp % 2)
        bo = "pf%d" % (4 + grp % 2)
        bd = "pf%d" % (grp % 2)
        for v2 in range(4):
            vh = grp * 4 + v2
            P.c("tensor", lambda e, vh=vh, v2=v2, bs=bs: e.matmul(
                pb[bs][:, v2 * 128:(v2 + 1) * 128], sb["kcdT"][:, vh * 128:(vh + 1) * 128],
                sb["S_bf"][:, vh * 128:(vh + 1) * 128], start=True, stop=True),
                reads=G("kcdT", "S_bf"), writes=(bs,))
        P.c("vector", lambda e, grp=grp, bs=bs: e.tensor_sub(
            sb["vnew"][:, grp * 512:(grp + 1) * 512], sb["vc"][:, grp * 512:(grp + 1) * 512],
            pb[bs][:]),
            reads=G("vc") + (bs,), writes=G("vnew"))
        for v2 in range(4):
            vh = grp * 4 + v2
            P.c("tensor", lambda e, vh=vh, v2=v2, bo=bo: e.matmul(
                pb[bo][:, v2 * 128:(v2 + 1) * 128], sb["qgi_fm"][:, vh * 128:(vh + 1) * 128],
                sb["S_bf"][:, vh * 128:(vh + 1) * 128], start=True, stop=False),
                reads=G("qgi_fm", "S_bf"), writes=(bo,))
            P.c("tensor", lambda e, vh=vh, v2=v2, bo=bo: e.matmul(
                pb[bo][:, v2 * 128:(v2 + 1) * 128], sb["m0"][:, vh * 128:(vh + 1) * 128],
                sb["vnew"][:, vh * 128:(vh + 1) * 128], start=False, stop=True),
                reads=G("m0", "vnew"), writes=(bo,))
            P.c("tensor", lambda e, vh=vh, v2=v2, bd=bd: e.matmul(
                pb[bd][:, v2 * 128:(v2 + 1) * 128], sb["kdec"][:, vh * 128:(vh + 1) * 128],
                sb["vnew"][:, vh * 128:(vh + 1) * 128], start=True, stop=True),
                reads=G("kdec", "vnew"), writes=(bd,))
        P.c("vector", lambda e, grp=grp, bo=bo, ci=ci: e.tensor_mul(
            sb["y_tm"][:, grp * 512:(grp + 1) * 512], pb[bo][:],
            sb["sz"][:, ci * 2048 + grp * 512:ci * 2048 + (grp + 1) * 512]),
            reads=(bo,) + G("sz"), writes=G("y_tm"))
        for v2 in range(4):
            vh = grp * 4 + v2
            P.c("vector", lambda e, vh=vh, v2=v2, bd=bd, ci=ci: e.scalar_tensor_tensor(
                sb["S_bf"][:, vh * 128:(vh + 1) * 128], sb["S_bf"][:, vh * 128:(vh + 1) * 128],
                sb["egl"][:, ci * 16 + vh:ci * 16 + vh + 1], pb[bd][:, v2 * 128:(v2 + 1) * 128],
                ALU.mult, ALU.add),
                reads=G("S_bf", "egl") + (bd,), writes=G("S_bf"))

    P.c("scalar", lambda e: e.activation(sb["ysq"][:, 0:2048], sb["y_tm"][:], AF.Square),
        reads=G("y_tm"), writes=G("ysq"))
    P.c("vector", lambda e: e.tensor_reduce(
        sb["rstc"][:, 0:16], sb["ysq"][:].rearrange("p (v d) -> p v d", v=16), AX.X, ALU.add),
        reads=G("ysq"), writes=G("rstc"))
    P.c("vector", lambda e: e.tensor_scalar_add(sb["rstc"][:, 0:16], sb["rstc"][:, 0:16],
                                                RMS_EPS * 128.0),
        reads=G("rstc"), writes=G("rstc"))
    P.c("scalar", lambda e: e.activation(sb["rstc"][:, 0:16], sb["rstc"][:, 0:16],
                                         AF.Ln), reads=G("rstc"), writes=G("rstc"))
    P.c("scalar", lambda e: e.activation(sb["rstc"][:, 0:16], sb["rstc"][:, 0:16],
                                         AF.Exp, scale=-0.5), reads=G("rstc"), writes=G("rstc"))
    P.c("vector", lambda e: e.tensor_scalar_mul(sb["rstc"][:, 0:16], sb["rstc"][:, 0:16],
                                                float(128.0 ** 0.5)),
        reads=G("rstc"), writes=G("rstc"))
    P.c("vector", lambda e: e.tensor_tensor(
        sb["y_tm"][:].rearrange("p (v d) -> p v d", v=16),
        sb["y_tm"][:].rearrange("p (v d) -> p v d", v=16),
        sb["rstc"][:, 0:16].unsqueeze(2).broadcast_to([128, 16, 128]), ALU.mult),
        reads=G("y_tm", "rstc"), writes=G("y_tm"))
    trans_group(cx, [("y_tm", j * 128) for j in range(16)], "nrm_fm",
                evac_engs=("scalar", "vector"))
    P.c("vector", lambda e: e.tensor_scalar_mul(
        sb["nrm_fm"][:], sb["nrm_fm"][:], sb["normw_col"][:]),
        reads=G("nrm_fm") + ("normw_col",), writes=G("nrm_fm"))
    q, col = t0 // 1024, t0 % 1024
    for half in range(2):
        P.dma("scalar", "dST", lambda e, half=half, q=q, col=col: e.dma_start(
            out=cx.dr["cc_in"].ap()[q, half * 1024:(half + 1) * 1024, col:col + 128]
            .rearrange("(v p) c -> v p c", p=128).transpose([1, 0, 2]),
            in_=sb["nrm_fm"][:, half * 1024:(half + 1) * 1024]
            .rearrange("p (v c) -> p v c", v=8)),
            reads=G("nrm_fm"), writes=("cc_in_q%d" % q,))


def emit_phaseB_tile(cx, tb):
    P, sb, pb, din, dr, G = cx.P, cx.sb, cx.pb, cx.din, cx.dr, cx.G
    t0 = tb * TTB

    srcq = dr["cc_out%d" % tb].ap().rearrange("r (u c) -> (r u) c", u=2)
    for g in range(32):
        P.dma("gpsimd", "dGT", lambda e, g=g, srcq=srcq: e.indirect_dma_start(
            sb["nrmB"][:, g * 512:(g + 1) * 512], None, srcq,
            bass.IndirectOffsetOnAxis(sb["gth_idx"][:, g:g + 1], 0)),
            reads=("cc_out%d" % tb, "gth_idx") + G("nrmB"), writes=G("nrmB"))
    if tb == 0:
        cx.coll_b0()

    for hg in range(4):
        for s in range(4):
            P.dma("scalar", "dX", lambda e, s=s, hg=hg: e.dma_start(
                out=sb["res0s"][:, s * 512:(s + 1) * 512],
                in_=din["xh"].ap()[t0 + s * 128:t0 + (s + 1) * 128, hg * 512:(hg + 1) * 512]),
                reads=G("res0s"), writes=G("res0s"))
        for rg in range(4):
            slot = rg % 2
            buf = "wt%d" % slot
            P.dma("sync", "dW%d" % slot,
                  lambda e, hg=hg, rg=rg, buf=buf: e.dma_start(
                out=sb[buf][:, 0:4096].rearrange("p (r c) -> p r c", r=8),
                in_=dr["wout_bf"].ap()[hg, rg * 8:(rg + 1) * 8].transpose([1, 0, 2])),
                reads=("wout_bf",), writes=G(buf))
            for s in range(4):
                bank = "pf%d" % s
                for r2 in range(8):
                    rc = rg * 8 + r2
                    P.c("tensor", lambda e, s=s, rc=rc, r2=r2, buf=buf, bank=bank: e.matmul(
                        pb[bank][:, :],
                        sb["nrmB"][:, rc * 512 + s * 128:rc * 512 + (s + 1) * 128],
                        sb[buf][:, r2 * 512:(r2 + 1) * 512],
                        start=(rc == 0), stop=(rc == 31)),
                        reads=G("nrmB", buf), writes=(bank,))
        for s in range(4):
            bank = "pf%d" % s
            P.c("vector", lambda e, s=s, hg=hg, bank=bank: e.tensor_add(
                sb["resfs"][:, s * 512:(s + 1) * 512], pb[bank][:, :],
                sb["res0s"][:, s * 512:(s + 1) * 512]),
                reads=(bank,) + G("res0s", "resfs"), writes=G("resfs"))
            P.dma("scalar", "dRES", lambda e, s=s, hg=hg: e.dma_start(
                out=dr["res0_d"].ap()[t0 + s * 128:t0 + (s + 1) * 128, hg * 512:(hg + 1) * 512],
                in_=sb["resfs"][:, s * 512:(s + 1) * 512]),
                reads=G("resfs"), writes=("res0_d",))

    for s in range(4):
        P.dma("sync", "dRES", lambda e, s=s: e.dma_start(
            out=sb["res0s"][:, 0:2048], in_=dr["res0_d"].ap()[t0 + s * 128:t0 + (s + 1) * 128, :]),
            reads=("res0_d",) + G("res0s"), writes=G("res0s"))
        P.c("vector", lambda e: e.tensor_reduce(
            sb["rstc"][:, 0:1], sb["res0s"][:, 0:2048], AX.X, ALU.add),
            reads=G("res0s"), writes=G("rstc"))
        P.c("scalar", lambda e: e.activation(sb["resfs"][:, 0:2048], sb["res0s"][:, 0:2048],
                                             AF.Square),
            reads=G("res0s"), writes=G("resfs"))
        P.c("vector", lambda e: e.tensor_reduce(
            sb["rstc"][:, 1:2], sb["resfs"][:, 0:2048], AX.X, ALU.add),
            reads=G("resfs"), writes=G("rstc"))
        P.c("vector", lambda e: e.tensor_scalar_mul(sb["rstc"][:, 0:2], sb["rstc"][:, 0:2], 1.0 / D),
            reads=G("rstc"), writes=G("rstc"))
        P.c("vector", lambda e: e.scalar_tensor_tensor(
            sb["rstc"][:, 2:3], sb["rstc"][:, 0:1], -1.0, sb["rstc"][:, 0:1],
            ALU.mult, ALU.mult), reads=G("rstc"), writes=G("rstc"))
        P.c("vector", lambda e: e.tensor_add(sb["rstc"][:, 2:3], sb["rstc"][:, 2:3],
                                             sb["rstc"][:, 1:2]),
            reads=G("rstc"), writes=G("rstc"))
        P.c("vector", lambda e: e.tensor_scalar_add(sb["rstc"][:, 2:3], sb["rstc"][:, 2:3],
                                                    LN_EPS), reads=G("rstc"), writes=G("rstc"))
        P.c("scalar", lambda e: e.activation(sb["rstc"][:, 2:3], sb["rstc"][:, 2:3],
                                             AF.Ln), reads=G("rstc"), writes=G("rstc"))
        P.c("scalar", lambda e: e.activation(sb["rstc"][:, 2:3], sb["rstc"][:, 2:3],
                                             AF.Exp, scale=-0.5), reads=G("rstc"), writes=G("rstc"))
        P.c("vector", lambda e: e.tensor_mul(sb["rstc"][:, 3:4], sb["rstc"][:, 0:1],
                                             sb["rstc"][:, 2:3]),
            reads=G("rstc"), writes=G("rstc"))
        P.c("vector", lambda e: e.tensor_scalar(
            sb["resfs"][:, 0:2048], sb["res0s"][:, 0:2048], sb["rstc"][:, 2:3],
            sb["rstc"][:, 3:4], ALU.mult, ALU.subtract),
            reads=G("res0s", "rstc"), writes=G("resfs"))
        P.dma("scalar", "dRES", lambda e, s=s: e.dma_start(
            out=dr["resf_d"].ap()[t0 + s * 128:t0 + (s + 1) * 128, :], in_=sb["resfs"][:, 0:2048]),
            reads=G("resfs"), writes=("resf_d",))
        P.c("vector", lambda e: e.tensor_copy(sb["xt1b"][:, 0:512] if False else
                                              sb["h2w0"][:, 0:2048], sb["resfs"][:, 0:2048]),
            reads=G("resfs"), writes=G("h2w0"))
        trans_to_xtfm(cx, "h2w0", 0, s)

    for j in range(2):
        emit_ffn_iter(cx, tb, j)


def trans_to_xtfm(cx, srcname, src_off, s):
    P, sb, pb, G = cx.P, cx.sb, cx.pb, cx.G
    for grp in range(2):
        bank = "pt%d" % grp
        for f2 in range(8):
            f = grp * 8 + f2
            P.c("tensor", lambda e, f=f, f2=f2, bank=bank, srcname=srcname, src_off=src_off: e.transpose(
                pb[bank][:, f2 * 128:(f2 + 1) * 128],
                sb[srcname][:, src_off + f * 128:src_off + (f + 1) * 128], sb["ident_bf"][:]),
                reads=G(srcname) + ("ident_bf",), writes=(bank,))
        P.c("scalar", lambda e, grp=grp, s=s, bank=bank: e.activation(
            bass.AP(sb["xt_fm"], grp * 8 * 512 + s * 128, [[8192, 128], [512, 8], [1, 128]]),
            pb[bank][:].rearrange("p (f c) -> p f c", f=8), AF.Copy),
            reads=(bank,), writes=G("xt_fm"))


def emit_ffn_iter(cx, tb, j):
    P, sb, pb, din, dr, G = cx.P, cx.sb, cx.pb, cx.din, cx.dr, cx.G
    t0 = tb * TTB

    P.dma("sync", "dW0", lambda e, j=j: e.dma_start(
        out=sb["wgt"][:, 0:128].rearrange("p (k c) -> p k c", k=16),
        in_=dr["wg_bf"].ap()[:, j * 8:(j + 1) * 8].rearrange("(k p) c -> k p c", p=128)
        .transpose([1, 0, 2])),
        reads=("wg_bf",), writes=("wgt",))
    for s in range(4):
        for k in range(16):
            P.c("tensor", lambda e, s=s, k=k: e.matmul(
                pb["pf4"][:, 0:8], sb["xt_fm"][:, k * 512 + s * 128:k * 512 + (s + 1) * 128],
                sb["wgt"][:, k * 8:(k + 1) * 8], start=(k == 0), stop=False),
                reads=("wgt",) + G("xt_fm"), writes=("pf4",))
        P.c("tensor", lambda e, s=s, j=j: e.matmul(
            pb["pf4"][:, 0:8], sb["ones_row_bf"][:], sb["bg_bf"][0:1, j * 8:j * 8 + 8],
            start=False, stop=True),
            reads=("bg_bf", "ones_row_bf"), writes=("pf4",))
        P.c("vector", lambda e, s=s: e.tensor_copy(sb["gat"][:, s * 8:s * 8 + 8],
                                                   pb["pf4"][:, 0:8]),
            reads=("pf4",), writes=G("gat"))
    for s in range(4):
        g0 = s * 8
        gat = sb["gat"]
        P.c("vector", lambda e, g0=g0: e.tensor_reduce(
            sb["gat"][:, 32 + 0:32 + 1], gat[:, g0:g0 + 8], AX.X, ALU.min),
            reads=G("gat"), writes=G("gat"))
        P.c("scalar", lambda e, g0=g0: e.sign(sb["gat"][:, 40:48], gat[:, g0:g0 + 8]) if False else
            e.activation(sb["gat"][:, 40:48], gat[:, g0:g0 + 8], AF.Sign, bias=sb["gat"][:, 32:33]),
            reads=G("gat"), writes=G("gat")) if False else None
        P.c("vector", lambda e, g0=g0: e.tensor_scalar(
            sb["gat"][:, 40:48], gat[:, g0:g0 + 8], sb["gat"][:, 32:33], None,
            ALU.subtract), reads=G("gat"), writes=G("gat"))
        P.c("scalar", lambda e: e.activation(sb["gat"][:, 40:48], sb["gat"][:, 40:48], AF.Sign),
            reads=G("gat"), writes=G("gat"))
        P.c("vector", lambda e, g0=g0: e.tensor_mul(
            sb["gat"][:, 48:56], sb["gat"][:, 40:48], gat[:, g0:g0 + 8]),
            reads=G("gat"), writes=G("gat"))
        P.c("vector", lambda e: e.tensor_scalar(
            sb["gat"][:, 40:48], sb["gat"][:, 40:48], -1e30, 1e30, ALU.mult, ALU.add),
            reads=G("gat"), writes=G("gat"))
        P.c("vector", lambda e: e.tensor_add(
            sb["gat"][:, 48:56], sb["gat"][:, 48:56], sb["gat"][:, 40:48]),
            reads=G("gat"), writes=G("gat"))
        P.c("vector", lambda e: e.tensor_reduce(
            sb["gat"][:, 33:34], sb["gat"][:, 48:56], AX.X, ALU.min),
            reads=G("gat"), writes=G("gat"))
        P.c("vector", lambda e, g0=g0: e.tensor_scalar(
            sb["gat"][:, 40:48], gat[:, g0:g0 + 8], sb["gat"][:, 33:34], None,
            ALU.subtract), reads=G("gat"), writes=G("gat"))
        P.c("scalar", lambda e: e.activation(sb["gat"][:, 40:48], sb["gat"][:, 40:48], AF.Sign),
            reads=G("gat"), writes=G("gat"))
        P.c("vector", lambda e: e.tensor_scalar_max(sb["gat"][:, 40:48], sb["gat"][:, 40:48], 0.0),
            reads=G("gat"), writes=G("gat"))
        P.c("vector", lambda e, g0=g0: e.tensor_reduce(
            sb["gat"][:, 34:35], gat[:, g0:g0 + 8], AX.X, ALU.max),
            reads=G("gat"), writes=G("gat"))
        P.c("vector", lambda e, g0=g0: e.tensor_scalar(
            sb["gat"][:, 48:56], gat[:, g0:g0 + 8], sb["gat"][:, 34:35], None,
            ALU.subtract), reads=G("gat"), writes=G("gat"))
        P.c("scalar", lambda e: e.activation(sb["gat"][:, 48:56], sb["gat"][:, 48:56], AF.Exp),
            reads=G("gat"), writes=G("gat"))
        P.c("vector", lambda e: e.tensor_mul(sb["gat"][:, 48:56], sb["gat"][:, 48:56],
                                             sb["gat"][:, 40:48]),
            reads=G("gat"), writes=G("gat"))
        P.c("vector", lambda e: e.tensor_reduce(
            sb["gat"][:, 35:36], sb["gat"][:, 48:56], AX.X, ALU.add),
            reads=G("gat"), writes=G("gat"))
        P.c("vector", lambda e: e.reciprocal(sb["gat"][:, 36:37], sb["gat"][:, 35:36]),
            reads=G("gat"), writes=G("gat"))
        P.c("vector", lambda e, s=s: e.tensor_scalar(
            sb["wbc"][:, 0:8] if False else sb["gat"][:, 56 + 0:56 + 8],
            sb["gat"][:, 48:56], sb["gat"][:, 36:37], None, ALU.mult),
            reads=G("gat"), writes=G("gat"))
        P.c("vector", lambda e, s=s: e.tensor_copy(sb["wgt"][:, 0:8] if False else
                                                   sb["wbc"][:, 0:8], sb["gat"][:, 56:64]),
            reads=G("gat"), writes=G("wbc"))
        P.c("tensor", lambda e, s=s: e.transpose(
            pb["pt0"][:, 0:128], sb["wbc"][:, 0:8] if False else sb["wbc"][:, 0:8],
            sb["ident_bf"][:]),
            reads=G("wbc") + ("ident_bf",), writes=("pt0",)) if False else None
        P.c("tensor", lambda e, s=s: e.transpose(
            pb["pt0"][0:8, 0:128], sb["wbc"][:, 0:8], sb["ident_bf"][:]),
            reads=G("wbc") + ("ident_bf",), writes=("pt0",))
        P.c("scalar", lambda e, s=s: e.activation(
            sb["wfm"][:, s * 128:(s + 1) * 128], pb["pt0"][0:8, 0:128], AF.Copy),
            reads=("pt0",), writes=G("wfm"))

    for n in range(6):
        for g4 in range(2):
            for g2 in range(2):
                g = n * 4 + g4 * 2 + g2
                slot = g % 2
                buf = "wt%d" % slot
                P.dma("sync", "dW%d" % slot, lambda e, g=g, buf=buf: e.dma_start(
                    out=sb[buf][:, 0:4096], in_=dr["wup_bf"].ap()[g]),
                    reads=("wup_bf",), writes=G(buf))
                for e2 in range(2):
                    ec = (g4 * 2 + g2) * 2 + e2
                    bank = "pf%d" % (ec % 4)
                    for k in range(16):
                        P.c("tensor", lambda e, e2=e2, k=k, buf=buf, bank=bank: e.matmul(
                            pb[bank][:, :], sb[buf][:, k * 256 + e2 * 128:k * 256 + e2 * 128 + 128],
                            sb["xt_fm"][:, k * 512:(k + 1) * 512], start=(k == 0), stop=(k == 15)),
                            reads=G(buf, "xt_fm"), writes=(bank,))
                    P.c("vector", lambda e, n=n, ec=ec, bank=bank: e.tensor_scalar_add(
                        sb["h_fm"][:, ec * 512:(ec + 1) * 512], pb[bank][:, :],
                        sb["bup_cols"][:, n * 8 + ec:n * 8 + ec + 1]),
                        reads=(bank, "bup_cols"), writes=G("h_fm"))
        for g4 in range(4):
            g = n * 4 + g4
            slot = g % 2
            buf = "wt%d" % slot
            P.dma("sync", "dW%d" % slot, lambda e, g=g, buf=buf: e.dma_start(
                out=sb[buf][:, 0:2048], in_=dr["wsw_bf"].ap()[g]),
                reads=("wsw_bf",), writes=G(buf))
            for e2 in range(2):
                ec = g4 * 2 + e2
                bank = "pf%d" % (ec % 4)
                for k in range(8):
                    P.c("tensor", lambda e, e2=e2, k=k, buf=buf, bank=bank: e.matmul(
                        pb[bank][:, :], sb[buf][:, k * 256 + e2 * 128:k * 256 + e2 * 128 + 128],
                        sb["h_fm"][:, k * 512:(k + 1) * 512], start=(k == 0), stop=(k == 7)),
                        reads=G(buf, "h_fm"), writes=(bank,))
                P.c("scalar", lambda e, n=n, ec=ec, bank=bank: e.activation(
                    sb["sil"][:, ec * 512:(ec + 1) * 512], pb[bank][:, :], AF.Silu,
                    bias=sb["bsw_cols"][:, n * 8 + ec:n * 8 + ec + 1]),
                    reads=(bank, "bsw_cols"), writes=G("sil"))
        P.c("tensor", lambda e, n=n: e.matmul(
            pb["pf4"][:, :], sb["selbf_t"][:, n * 128:(n + 1) * 128],
            sb["wfm"][:, 0:512], start=True, stop=True),
            reads=G("selbf_t", "wfm"), writes=("pf4",))
        P.c("scalar", lambda e: e.activation(sb["wbc"][:, 0:512], pb["pf4"][:, :], AF.Copy),
            reads=("pf4",), writes=G("wbc"))
        P.c("gpsimd", lambda e: e.tensor_mul(sb["sil"][:, 0:4096], sb["sil"][:, 0:4096],
                                             sb["h_fm"][:, 0:4096]),
            reads=G("sil", "h_fm"), writes=G("sil"))
        P.c("vector", lambda e, n=n: e.tensor_tensor(
            sb["h2w%d" % n][:].rearrange("p (g c) -> p g c", g=8),
            sb["sil"][:, 0:4096].rearrange("p (g c) -> p g c", g=8),
            sb["wbc"][:, 0:512].unsqueeze(1).broadcast_to([128, 8, 512]), ALU.mult),
            reads=G("sil", "wbc"), writes=G("h2w%d" % n))

    for g in range(8):
        slot = g % 2
        buf = "wt%d" % slot
        P.dma("sync", "dW%d" % slot, lambda e, g=g, buf=buf: e.dma_start(
            out=sb[buf][:, 0:4096], in_=dr["wsu_bf"].ap()[g]),
            reads=("wsu_bf",), writes=G(buf))
        for e2 in range(2):
            sc = g * 2 + e2
            bank = "pf%d" % (sc % 4)
            for k in range(16):
                P.c("tensor", lambda e, e2=e2, k=k, buf=buf, bank=bank: e.matmul(
                    pb[bank][:, :], sb[buf][:, k * 256 + e2 * 128:k * 256 + e2 * 128 + 128],
                    sb["xt_fm"][:, k * 512:(k + 1) * 512], start=(k == 0), stop=(k == 15)),
                    reads=G(buf, "xt_fm"), writes=(bank,))
            P.c("vector", lambda e, sc=sc, bank=bank: e.tensor_scalar_add(
                sb["sh_fm"][:, sc * 512:(sc + 1) * 512], pb[bank][:, :],
                sb["bsu_cols"][:, sc:sc + 1]),
                reads=(bank, "bsu_cols"), writes=G("sh_fm"))
    for g in range(4):
        slot = g % 2
        buf = "wt%d" % slot
        P.dma("sync", "dW%d" % slot, lambda e, g=g, buf=buf: e.dma_start(
            out=sb[buf][:, 0:2048], in_=dr["wsa_bf"].ap()[g]),
            reads=("wsa_bf",), writes=G(buf))
        for sl in range(2):
            for e2 in range(2):
                ec = g * 2 + e2
                bank = "pf%d" % ((sl * 2 + e2) % 4)
                for k in range(8):
                    P.c("tensor", lambda e, sl=sl, e2=e2, k=k, buf=buf, bank=bank: e.matmul(
                        pb[bank][:, :], sb[buf][:, k * 256 + e2 * 128:k * 256 + e2 * 128 + 128],
                        sb["sh_fm"][:, (sl * 8 + k) * 512:(sl * 8 + k + 1) * 512],
                        start=(k == 0), stop=(k == 7)),
                        reads=G(buf, "sh_fm"), writes=(bank,))
                P.c("scalar", lambda e, sl=sl, ec=ec, bank=bank: e.activation(
                    sb["sh2"][:, (sl * 8 + ec) * 512:(sl * 8 + ec + 1) * 512],
                    pb[bank][:, :], AF.Silu, bias=sb["bsa_cols"][:, ec:ec + 1]),
                    reads=(bank, "bsa_cols"), writes=G("sh2"))
    P.c("gpsimd", lambda e: e.tensor_mul(sb["sh2"][:, 0:8192], sb["sh2"][:, 0:8192],
                                         sb["sh_fm"][:, 0:8192]),
        reads=G("sh2", "sh_fm"), writes=G("sh2"))


    last = (j == 1)
    for hg in range(4):
        banks = ["pf%d" % i for i in range(4)]
        first = [True] * 4
        for s in range(4):
            P.dma("scalar", "dRES", lambda e, s=s, hg=hg: e.dma_start(
                out=sb["resfs"][:, s * 512:(s + 1) * 512],
                in_=dr["resf_d"].ap()[t0 + s * 128:t0 + (s + 1) * 128, hg * 512:(hg + 1) * 512]),
                reads=("resf_d",) + G("resfs"), writes=G("resfs"))
            if last:
                P.dma("scalar", "dRES", lambda e, s=s, hg=hg: e.dma_start(
                    out=sb["res0s"][:, s * 512:(s + 1) * 512],
                    in_=dr["res0_d"].ap()[t0 + s * 128:t0 + (s + 1) * 128, hg * 512:(hg + 1) * 512]),
                    reads=("res0_d",) + G("res0s"), writes=G("res0s"))
        for n in range(6):
            slot = n % 2
            buf = "wt%d" % slot
            P.dma("sync", "dW%d" % slot,
                  lambda e, hg=hg, n=n, buf=buf: e.dma_start(
                out=sb[buf][:, 0:4096].rearrange("p (r c) -> p r c", r=8),
                in_=dr["wdn_bf"].ap()[hg, n * 8:(n + 1) * 8].transpose([1, 0, 2])),
                reads=("wdn_bf",), writes=G(buf))
            for s in range(4):
                for r2 in range(8):
                    P.c("tensor", lambda e, n=n, s=s, r2=r2, buf=buf, f=first[s], bank=banks[s]: e.matmul(
                        pb[bank][:, :],
                        sb["h2w%d" % n][:, r2 * 512 + s * 128:r2 * 512 + (s + 1) * 128],
                        sb[buf][:, r2 * 512:(r2 + 1) * 512],
                        start=f and r2 == 0, stop=False),
                        reads=G("h2w%d" % n, buf), writes=(banks[s],))
                first[s] = False
        for half in range(2):
            slot = half % 2
            buf = "wt%d" % slot
            P.dma("sync", "dW%d" % slot,
                  lambda e, hg=hg, half=half, buf=buf: e.dma_start(
                out=sb[buf][:, 0:4096].rearrange("p (r c) -> p r c", r=8),
                in_=dr["wsd_bf"].ap()[hg, half * 8:(half + 1) * 8].transpose([1, 0, 2])),
                reads=("wsd_bf",), writes=G(buf))
            for s in range(4):
                for r2 in range(8):
                    rc = half * 8 + r2
                    P.c("tensor", lambda e, rc=rc, r2=r2, s=s, buf=buf, bank=banks[s]: e.matmul(
                        pb[bank][:, :],
                        sb["sh2"][:, rc * 512 + s * 128:rc * 512 + (s + 1) * 128],
                        sb[buf][:, r2 * 512:(r2 + 1) * 512],
                        start=False, stop=False),
                        reads=G("sh2", buf), writes=(banks[s],))
        for s in range(4):
            P.c("tensor", lambda e, s=s, hg=hg, bank=banks[s]: e.matmul(
                pb[bank][:, :], sb["wfm"][0:6, s * 128:(s + 1) * 128],
                sb["bdn_bf"][:, hg * 512:(hg + 1) * 512],
                start=False, stop=False),
                reads=G("wfm", "bdn_bf"), writes=(banks[s],))
            P.c("tensor", lambda e, s=s, hg=hg, bank=banks[s]: e.matmul(
                pb[bank][:, :], sb["ones_row_bf"][:],
                sb["bsd_bf"][0:1, hg * 512:(hg + 1) * 512],
                start=False, stop=True),
                reads=("ones_row_bf",) + G("bsd_bf"), writes=(banks[s],))
            if not last:
                P.c("vector", lambda e, s=s, bank=banks[s]: e.tensor_add(
                    sb["xt1b"][:, 0:512], pb[bank][:, :], sb["resfs"][:, s * 512:(s + 1) * 512]),
                    reads=(banks[s],) + G("resfs"), writes=G("xt1b"))
                bankt = "pt0"
                for f2 in range(4):
                    P.c("tensor", lambda e, f2=f2, bankt=bankt: e.transpose(
                        pb[bankt][:, f2 * 128:(f2 + 1) * 128],
                        sb["xt1b"][:, f2 * 128:(f2 + 1) * 128], sb["ident_bf"][:]),
                        reads=G("xt1b") + ("ident_bf",), writes=(bankt,))
                P.c("scalar", lambda e, s=s, hg=hg, bankt=bankt: e.activation(
                    bass.AP(sb["xt_fm"], hg * 4 * 512 + s * 128, [[8192, 128], [512, 4], [1, 128]]),
                    pb[bankt][:, 0:512].rearrange("p (f c) -> p f c", f=4), AF.Copy),
                    reads=(bankt,), writes=G("xt_fm"))
            else:
                P.c("vector", lambda e, s=s, bank=banks[s]: e.tensor_add(
                    sb["resfs"][:, s * 512:(s + 1) * 512], pb[bank][:, :],
                    sb["resfs"][:, s * 512:(s + 1) * 512]),
                    reads=(banks[s],) + G("resfs"), writes=G("resfs"))
                P.c("vector", lambda e, s=s: e.tensor_add(
                    sb["resfs"][:, s * 512:(s + 1) * 512],
                    sb["resfs"][:, s * 512:(s + 1) * 512],
                    sb["res0s"][:, s * 512:(s + 1) * 512]),
                    reads=G("resfs", "res0s"), writes=G("resfs"))
                P.dma("sync", "dOUT", lambda e, s=s, hg=hg: e.dma_start(
                    out=cx.y.ap()[t0 + s * 128:t0 + (s + 1) * 128, hg * 512:(hg + 1) * 512],
                    in_=sb["resfs"][:, s * 512:(s + 1) * 512]),
                    reads=G("resfs"), writes=("y",))


def kernel(**inputs):
    nc = build()
    in_maps = [_stage_core(inputs, c) for c in range(8)]
    res = run_bass_kernel_spmd(nc, in_maps, core_ids=list(range(8)))
    out = np.zeros((B, T, D), np.float32)
    for c in range(8):
        b, h = c // 2, c % 2
        yv = np.asarray(res.results[c]["y"], np.float32)
        for tb in range(4):
            out[b, (2 * tb + h) * 512:(2 * tb + h + 1) * 512] = yv[tb * 512:(tb + 1) * 512]
    return out


# version 15
# speedup vs baseline: 1.0974x; 1.0372x over previous
import numpy as np
import ml_dtypes
import concourse.bass as bass
import concourse.mybir as mybir
from concourse.bass_utils import run_bass_kernel_spmd

F32 = mybir.dt.float32
BF16 = mybir.dt.bfloat16
I32 = mybir.dt.int32
AF = mybir.ActivationFunctionType
ALU = mybir.AluOpType
AX = mybir.AxisListType
BFNP = ml_dtypes.bfloat16

B, T, D = 4, 4096, 2048
HV, HK, DK, DV, KCONV = 32, 16, 128, 128, 4
CH = 128
TT = 512
NTILES = T // TT
CPT = TT // CH
NVH, NKH = 16, 8
CONVC = 4096
VCC = 2048
TB = 2048
TTB = 512
NBT = TB // TTB
LN_EPS, L2_EPS, RMS_EPS = 1e-5, 1e-6, 1e-6

ENGS = ("tensor", "vector", "scalar", "gpsimd", "sync")
GRAN = 2048


class Prog:
    def __init__(self):
        self.steps = []
        self.last_write = {}
        self.last_reads = {}
        self.dma_issued = {}
        self.eng_steps = {e: [] for e in ENGS}

    def _dep(self, step, prod):
        if prod is None or prod is step:
            return
        if (prod["eng"] == "tensor" and step["eng"] == "tensor"
                and prod["kind"] == "c" and step["kind"] == "c"):
            return
        if prod not in step["dep_steps"]:
            step["dep_steps"].append(prod)
            if prod["kind"] == "d":
                step["dma_req"][prod["sem"]] = max(
                    step["dma_req"].get(prod["sem"], 0),
                    16 * self.dma_issued[prod["sem"]])

    def _add(self, eng, kind, emit, reads, writes, sem=None):
        step = dict(eng=eng, kind=kind, emit=emit, sem=sem,
                    dep_steps=[], dma_req={}, marked=False, count=None)
        for r in reads:
            self._dep(step, self.last_write.get(r))
        for w in writes:
            self._dep(step, self.last_write.get(w))
            for rd in self.last_reads.get(w, ()):
                self._dep(step, rd)
        if kind == "d":
            self.dma_issued[sem] = self.dma_issued.get(sem, 0) + 1
        for w in writes:
            self.last_write[w] = step
            self.last_reads[w] = []
        for r in reads:
            self.last_reads.setdefault(r, []).append(step)
        self.steps.append(step)
        self.eng_steps[eng].append(step)
        return step

    def c(self, eng, emit, reads=(), writes=()):
        return self._add(eng, "c", emit, reads, writes)

    def dma(self, eng, sem, emit, reads=(), writes=()):
        return self._add(eng, "d", emit, reads, writes, sem=sem)

    def coll(self, emit, reads=(), writes=()):
        return self._add("gpsimd", "cc", emit, reads, writes, sem="_cc")

    def finalize(self, block, sems, eng_sems, final_waits):
        for st in self.steps:
            for p in st["dep_steps"]:
                p["marked"] = True
        for e in ENGS:
            cnt = 0
            for st in self.eng_steps[e]:
                if st["kind"] == "c" and st["marked"]:
                    cnt += 1
                    st["count"] = cnt
        totals = {}
        for st in self.steps:
            if st["kind"] == "d":
                totals[st["sem"]] = totals.get(st["sem"], 0) + 16
        FULL = ("dAUX", "gAUX")
        dcnt = {}
        for st in self.steps:
            if st["kind"] == "d":
                dcnt[st["sem"]] = dcnt.get(st["sem"], 0) + 16
                st["count"] = dcnt[st["sem"]]
                st["marked"] = True
            elif st["kind"] == "cc":
                dcnt[st["sem"]] = dcnt.get(st["sem"], 0) + 1
                st["count"] = dcnt[st["sem"]]
                st["marked"] = True

        def emit_engine(name):
            def run(eng):
                seen = {}
                for st in self.eng_steps[name]:
                    req = {}
                    for p in st["dep_steps"]:
                        if p["kind"] == "c":
                            s, v = eng_sems[p["eng"]], p["count"]
                        elif p["kind"] == "cc":
                            s, v = sems[p["sem"]], p["count"]
                        else:
                            continue
                        if req.get(id(s), (None, -1))[1] < v:
                            req[id(s)] = (s, v)
                    for sname, v in st["dma_req"].items():
                        s = sems[sname]
                        if sname in FULL and not (st["kind"] == "d" and st["sem"] == sname):
                            v = totals[sname]
                        if req.get(id(s), (None, -1))[1] < v:
                            req[id(s)] = (s, v)
                    for s, v in req.values():
                        if seen.get(id(s), -1) < v:
                            eng.wait_ge(s, v)
                            seen[id(s)] = v
                    inst = st["emit"](eng)
                    if st["marked"]:
                        if st["kind"] == "c":
                            inst.then_inc(eng_sems[name], 1)
                        elif st["kind"] == "d":
                            inst.then_inc(sems[st["sem"]], 16)
                        else:
                            inst.then_inc(sems[st["sem"]], 1)
                if name == "sync":
                    for sname, v in final_waits:
                        if v > 0:
                            eng.wait_ge(sems[sname], v)
            return run

        block.sync(emit_engine("sync"))
        block.tensor(emit_engine("tensor"))
        block.vector(emit_engine("vector"))
        block.scalar(emit_engine("scalar"))
        block.gpsimd(emit_engine("gpsimd"))


def _stage_core(inp, c):
    f32 = np.float32
    b, h = c // 2, c % 2
    vh0, kh0 = NVH * h, NKH * h
    W = np.asarray(inp["L0_attn_W_qkv"], f32)
    wqkv = np.ascontiguousarray(np.concatenate(
        [W[:, kh0 * DK:(kh0 + NKH) * DK],
         W[:, 2048 + kh0 * DK:2048 + (kh0 + NKH) * DK],
         W[:, 4096 + vh0 * DV:4096 + (vh0 + NVH) * DV]], axis=1))
    cw = np.asarray(inp["L0_attn_conv_w"], f32)
    cw_sl = np.concatenate(
        [cw[kh0 * DK:(kh0 + NKH) * DK], cw[2048 + kh0 * DK:2048 + (kh0 + NKH) * DK],
         cw[4096 + vh0 * DV:4096 + (vh0 + NVH) * DV]], axis=0)
    convw = np.ascontiguousarray(cw_sl.reshape(32, 128, 4).transpose(1, 0, 2).reshape(128, 128))
    wout = np.ascontiguousarray(np.asarray(inp["L0_attn_W_out"], f32))

    def cols128(v):
        n = v.shape[0]
        return np.ascontiguousarray(v.reshape(n, 8, 128).transpose(2, 0, 1).reshape(128, n * 8))

    x = np.asarray(inp["L0_attn_x"], f32)
    gth = (2 * np.arange(4096, dtype=np.int64) + h).astype(np.int32)
    gth = np.ascontiguousarray(gth.reshape(32, 128).T)
    sel = np.zeros((8, 768), np.float32)
    for n in range(6):
        sel[n, n * 128:(n + 1) * 128] = 1.0

    return dict(
        x=np.ascontiguousarray(x[b]),
        xh=np.ascontiguousarray(np.concatenate(
            [x[b, (2 * tb + h) * 512:(2 * tb + h + 1) * 512] for tb in range(4)])),
        wqkv=wqkv, convw=convw,
        wz=np.ascontiguousarray(np.asarray(inp["L0_attn_W_z"], f32)[:, vh0 * DV:(vh0 + NVH) * DV]),
        wba=np.ascontiguousarray(np.concatenate(
            [np.asarray(inp["L0_attn_W_b"], f32)[:, vh0:vh0 + NVH],
             np.asarray(inp["L0_attn_W_a"], f32)[:, vh0:vh0 + NVH]], axis=1)),
        dtb_row=np.ascontiguousarray(np.asarray(inp["L0_attn_dt_bias"], f32)[vh0:vh0 + NVH].reshape(1, NVH)),
        alog_row=np.ascontiguousarray(np.asarray(inp["L0_attn_A_log"], f32)[vh0:vh0 + NVH].reshape(1, NVH)),
        normw_col=np.asarray(inp["L0_attn_norm_w"], f32).reshape(128, 1),
        wout=wout,
        wg=np.ascontiguousarray(np.asarray(inp["L0_ffn_Wg"], f32).transpose(1, 0, 2).reshape(2048, 16)),
        bg_rows=np.ascontiguousarray(np.asarray(inp["L0_ffn_bg"], f32).reshape(1, 16)),
        wup=np.ascontiguousarray(np.asarray(inp["L0_ffn_Wup"], f32).transpose(1, 0, 2).reshape(2048, 6144)),
        bup_cols=cols128(np.asarray(inp["L0_ffn_bup"], f32)),
        wsw=np.ascontiguousarray(np.asarray(inp["L0_ffn_Wsw"], f32).transpose(1, 0, 2).reshape(1024, 6144)),
        bsw_cols=cols128(np.asarray(inp["L0_ffn_bsw"], f32)),
        wdn=np.ascontiguousarray(np.asarray(inp["L0_ffn_Wdn"], f32).reshape(6144, 2048)),
        bdn_rows=np.ascontiguousarray(np.asarray(inp["L0_ffn_bdn"], f32)),
        wsu=np.asarray(inp["L0_ffn_Wsu"], f32),
        bsu_cols=np.ascontiguousarray(np.asarray(inp["L0_ffn_bsu"], f32).reshape(16, 128).T),
        wsa=np.asarray(inp["L0_ffn_Wsa"], f32),
        bsa_cols=np.ascontiguousarray(np.asarray(inp["L0_ffn_bsa"], f32).reshape(8, 128).T),
        wsd=np.asarray(inp["L0_ffn_Wsd"], f32),
        bsd_row=np.ascontiguousarray(np.asarray(inp["L0_ffn_bsd"], f32).reshape(1, 2048)),
        gther=gth,
        ident_bf=np.eye(128, dtype=np.float32).astype(BFNP),
        identf=np.eye(128, dtype=np.float32),
        mski=np.ascontiguousarray(np.triu(np.ones((128, 128), np.float32))),
        msks=np.ascontiguousarray(np.triu(np.ones((128, 128), np.float32), 1)),
        ones_row=np.ones((1, 128), np.float32),
        ones_row_bf=np.ones((1, 128), np.float32).astype(BFNP),
        ones_col=np.ones((128, 1), np.float32),
        ones_col_bf=np.ones((128, 1), np.float32).astype(BFNP),
        selbf=sel.astype(BFNP),
    )


INPUT_DECL = dict(
    x=([T, D], F32), xh=([TB, D], F32),
    wqkv=([D, CONVC], F32), convw=([128, 128], F32), wz=([D, VCC], F32),
    wba=([D, 32], F32), dtb_row=([1, NVH], F32), alog_row=([1, NVH], F32),
    normw_col=([128, 1], F32), wout=([4096, 2048], F32),
    wg=([2048, 16], F32), bg_rows=([1, 16], F32),
    wup=([2048, 6144], F32), bup_cols=([128, 48], F32),
    wsw=([1024, 6144], F32), bsw_cols=([128, 48], F32),
    wdn=([6144, 2048], F32), bdn_rows=([6, 2048], F32),
    wsu=([2048, 2048], F32), bsu_cols=([128, 16], F32),
    wsa=([1024, 1024], F32), bsa_cols=([128, 8], F32),
    wsd=([2048, 2048], F32), bsd_row=([1, 2048], F32),
    gther=([128, 32], I32),
    ident_bf=([128, 128], BF16), identf=([128, 128], F32),
    mski=([128, 128], F32), msks=([128, 128], F32),
    ones_row=([1, 128], F32), ones_row_bf=([1, 128], BF16),
    ones_col=([128, 1], F32), ones_col_bf=([128, 1], BF16),
    selbf=([8, 768], BF16),
)


ARENA_BASE = 22528
ARENA_TOP = 222000


def _layout():
    L = {}
    off = [ARENA_BASE]

    def al(name, nbytes, at=None):
        a = off[0] if at is None else at
        if at is None:
            off[0] = a + nbytes
        L[name] = (a, nbytes)

    al("xn_fm", 16384)
    al("qkvp", 32768)
    al("sz", 16384)
    al("wt0", 8192)
    al("wt1", 8192)
    al("xtm", 8192)
    al("stg", 6144)
    al("cvt", 2048)
    al("strow", 6144)
    al("q_tm", 2048); al("k_tm", 2048); al("v_tm", 4096)
    al("qs_tm", 2048); al("kn_tm", 2048)
    al("vb", 4096); al("kbg", 4096); al("kdec", 4096)
    al("qgi_fm", 4096)
    al("C", 4096); al("A", 4096); al("C2", 4096); al("A2", 4096)
    al("P", 4096); al("P2", 4096)
    al("vnew", 4096); al("y_tm", 4096); al("nrm_fm", 4096)
    al("mdiff", 2048)
    al("egl", 512); al("gch", 2048); al("gT", 2048); al("ba_all", 512)
    al("lncol", 128); al("rstc", 128)
    al("r_bc", 1024); al("mr_bc", 1024); al("halo", 256); al("S_bf", 4096)
    al("bdn_bf", 4096); al("bsd_bf", 4096); al("selbf_t", 2048)
    al("wgt", 512)
    al("m1", 4096, at=L["xtm"][0])
    al("m0", 4096, at=L["xtm"][0] + 4096)
    al("qs_fm", 2048, at=L["q_tm"][0])
    al("kn_fm", 2048, at=L["k_tm"][0])
    al("qgi_tm", 4096, at=L["v_tm"][0])
    al("vc", 4096, at=L["C2"][0])
    al("kcdT", 4096, at=L["A2"][0])
    al("ysq", 4096, at=L["vnew"][0])
    al("Gm", 2048, at=L["vnew"][0])
    al("qkTm", 2048, at=L["vnew"][0] + 2048)
    al("nrmB", 32768, at=L["qkvp"][0])
    for n in range(4):
        al("h2w%d" % n, 8192, at=L["qkvp"][0] + n * 8192)
    al("h2w4", 8192, at=L["sz"][0])
    al("h2w5", 8192, at=L["sz"][0] + 8192)
    al("xt_fm", 16384, at=L["xn_fm"][0])
    al("res0s", 8192, at=L["xtm"][0])
    al("resfs", 8192, at=L["stg"][0])
    al("h_fm", 8192, at=L["cvt"][0])
    al("sil", 8192, at=L["q_tm"][0])
    al("sh_fm", 16384, at=L["C"][0])
    al("sh2", 16384, at=L["P"][0])
    al("xt1b", 1024, at=L["nrm_fm"][0])
    al("wbc", 1024, at=L["r_bc"][0])
    al("gat", 2048, at=L["mdiff"][0])
    al("wfm", 1024, at=L["egl"][0])
    assert off[0] <= ARENA_TOP, off[0]
    return L


LAY = _layout()
assert LAY["strow"][0] == LAY["cvt"][0] + 2048
assert LAY["k_tm"][0] == LAY["q_tm"][0] + 2048 and LAY["v_tm"][0] == LAY["k_tm"][0] + 2048
assert LAY["gch"][0] == LAY["egl"][0] + 512


def granules(*names):
    out = set()
    for n in names:
        a, sz = LAY[n]
        for g in range(a // GRAN, (a + sz + GRAN - 1) // GRAN):
            out.add("g%d" % g)
    return tuple(out)


ARENA_DECL = dict(
    xn_fm=([128, 8192], BF16), qkvp=([128, 16384], BF16), sz=([128, 8192], BF16),
    wt0=([128, 4096], BF16), wt1=([128, 4096], BF16),
    xtm=([128, 4096], BF16), stg=([2, 1536], F32), cvt=([128, 512], F32),
    strow=([1, 1536], F32),
    q_tm=([128, 1024], BF16), k_tm=([128, 1024], BF16), v_tm=([128, 2048], BF16),
    qs_tm=([128, 1024], BF16), kn_tm=([128, 1024], BF16),
    vb=([128, 2048], BF16), kbg=([128, 2048], BF16), kdec=([128, 2048], BF16),
    qgi_fm=([128, 2048], BF16),
    Gm=([128, 1024], BF16), qkTm=([128, 1024], BF16),
    C=([128, 2048], BF16), A=([128, 2048], BF16),
    C2=([128, 2048], BF16), A2=([128, 2048], BF16),
    P=([128, 2048], BF16), P2=([128, 2048], BF16),
    vnew=([128, 2048], BF16), y_tm=([128, 2048], BF16), nrm_fm=([128, 2048], BF16),
    mdiff=([128, 512], F32),
    egl=([128, 64], F32), gch=([128, 448], F32), gT=([16, 512], F32),
    ba_all=([128, 128], F32), lncol=([128, 16], F32), rstc=([128, 16], F32),
    r_bc=([128, 512], BF16), mr_bc=([128, 512], BF16),
    halo=([128, 96], BF16), S_bf=([128, 2048], BF16),
    bdn_bf=([6, 2048], BF16), bsd_bf=([1, 2048], BF16), selbf_t=([8, 768], BF16),
    wgt=([128, 128], BF16),
    m1=([128, 2048], BF16), m0=([128, 2048], BF16),
    qs_fm=([128, 1024], BF16), kn_fm=([128, 1024], BF16), qgi_tm=([128, 2048], BF16),
    vc=([128, 2048], BF16), kcdT=([128, 2048], BF16), ysq=([128, 2048], BF16),
    nrmB=([128, 16384], BF16),
    h2w0=([128, 4096], BF16), h2w1=([128, 4096], BF16), h2w2=([128, 4096], BF16),
    h2w3=([128, 4096], BF16), h2w4=([128, 4096], BF16), h2w5=([128, 4096], BF16),
    xt_fm=([128, 8192], BF16), res0s=([128, 2048], F32), resfs=([128, 2048], F32),
    h_fm=([128, 4096], BF16), sil=([128, 4096], BF16),
    sh_fm=([128, 8192], BF16), sh2=([128, 8192], BF16),
    xt1b=([128, 512], BF16), wbc=([128, 512], BF16),
    gat=([128, 512], F32), wfm=([8, 512], BF16),
)


class Ctx:
    pass


def build(phase_a_only=False, debug_ccin=False, ntiles=NTILES):
    nc = bass.Bass("TRN2", target_bir_lowering=False)
    din = {n: nc.dram_tensor(n, list(s), d, kind="ExternalInput")
           for n, (s, d) in INPUT_DECL.items()}
    y = nc.dram_tensor("y", [TB, D], F32, kind="ExternalOutput")
    dr = {}
    dr["wqkv_bf"] = nc.dram_tensor("wqkv_bf", [16, 128, 16, 256], BF16)
    dr["wz_bf"] = nc.dram_tensor("wz_bf", [8, 128, 16, 256], BF16)
    dr["wba_bf"] = nc.dram_tensor("wba_bf", [2048, 32], BF16)
    dr["wg_bf"] = nc.dram_tensor("wg_bf", [2048, 16], BF16)
    dr["wup_bf"] = nc.dram_tensor("wup_bf", [24, 128, 16, 256], BF16)
    dr["wsw_bf"] = nc.dram_tensor("wsw_bf", [24, 128, 8, 256], BF16)
    dr["wsu_bf"] = nc.dram_tensor("wsu_bf", [8, 128, 16, 256], BF16)
    dr["wsa_bf"] = nc.dram_tensor("wsa_bf", [4, 128, 8, 256], BF16)
    dr["wdn_bf"] = nc.dram_tensor("wdn_bf", [4, 48, 128, 512], BF16)
    dr["wsd_bf"] = nc.dram_tensor("wsd_bf", [4, 16, 128, 512], BF16)
    dr["wout_bf"] = nc.dram_tensor("wout_bf", [4, 32, 128, 512], BF16)
    dr["cc_in"] = nc.dram_tensor("cc_in", [4, VCC, 1024], BF16,
                                 kind="ExternalOutput" if debug_ccin else "Internal")
    for q in range(4):
        dr["cc_out%d" % q] = nc.dram_tensor("cc_out%d" % q, [2 * VCC, 1024], BF16)
    dr["res0_d"] = nc.dram_tensor("res0_d", [TB, D], F32)
    dr["resf_d"] = nc.dram_tensor("resf_d", [TB, D], F32)

    from contextlib import ExitStack
    with ExitStack() as stk:
        E = stk.enter_context
        sb = {}
        for n, shape, dt in (
                ("ident_bf", [128, 128], BF16), ("identf", [128, 128], F32),
                ("mski", [128, 128], F32), ("msks", [128, 128], F32),
                ("convw", [128, 128], F32),
                ("ones_row", [1, 128], F32), ("ones_row_bf", [1, 128], BF16),
                ("ones_col", [128, 1], F32), ("ones_col_bf", [128, 1], BF16),
                ("normw_col", [128, 1], F32),
                ("bup_cols", [128, 48], F32), ("bsw_cols", [128, 48], F32),
                ("bsu_cols", [128, 16], F32), ("bsa_cols", [128, 8], F32),
                ("bg_rows", [1, 16], F32), ("gth_idx", [128, 32], I32),
                ("dtb_bc", [128, 16], F32), ("acoef_bc", [128, 16], F32),
                ("rowtmp", [1, 16], F32), ("rowtmp2", [1, 16], F32),
                ("halo2", [128, 96], BF16),
                ("wba_t", [128, 512], BF16), ("bg_bf", [1, 16], BF16)):
            sb[n] = nc.alloc_sbuf_tensor("c_" + n, shape, dt, side="right")
        for n, (shape, dt) in ARENA_DECL.items():
            a, sz_ = LAY[n]
            nb = shape[1] * (2 if dt in (BF16,) else 4)
            assert nb <= sz_, (n, nb, sz_)
            sb[n] = nc.alloc_sbuf_tensor_at(n, shape, dt, offset=a)

        pb = {}
        for i in range(6):
            pb["pf%d" % i] = E(nc.psum_tensor("pf%d" % i, [128, 512], F32))
        pb["pt0"] = E(nc.psum_tensor("pt0", [128, 1024], BF16))
        pb["pt1"] = E(nc.psum_tensor("pt1", [128, 1024], BF16))

        sems = {}
        for s in ("dW0", "dW1", "dX", "dST", "dAUX", "dRS", "dGT", "_cc",
                  "dOUT", "dRES", "gAUX", "gW0", "gW1", "gX", "dNF", "gCAST"):
            sems[s] = E(nc.semaphore(s))
        eng_sems = {e: E(nc.semaphore("se_" + e)) for e in ENGS}

        block = E(nc.Block())
        P = Prog()
        cx = Ctx()
        cx.nc, cx.P, cx.sb, cx.pb, cx.din, cx.dr, cx.y = nc, P, sb, pb, din, dr, y
        cx.G = granules

        def make_coll(q):
            return lambda: P.coll(lambda e, q=q: e.collective_compute(
                "AllGather", ALU.bypass,
                replica_groups=[[0, 1], [2, 3], [4, 5], [6, 7]],
                ins=[dr["cc_in"].ap()[q]], outs=[dr["cc_out%d" % q].ap()]),
                reads=("cc_in_q%d" % q,), writes=("cc_out%d" % q,))

        cx.coll_at_tile = {} if phase_a_only else {2 * q + 2: make_coll(q)
                                                  for q in range(3)}
        cx.coll_b0 = make_coll(3)

        emit_prep(cx)
        for t in range(ntiles):
            emit_phaseA_tile(cx, t)
        if not phase_a_only:
            for tb in range(NBT):
                emit_phaseB_tile(cx, tb)
        final = []
        if phase_a_only or debug_ccin:
            final.append(("dST", 16 * P.dma_issued.get("dST", 0)))
            final.append(("_cc", P.dma_issued.get("_cc", 0)))
        if not phase_a_only:
            final.append(("dOUT", 16 * P.dma_issued.get("dOUT", 0)))
        P.finalize(block, sems, eng_sems, final)
    return nc


def emit_prep(cx):
    P, sb, pb, din, dr, G = cx.P, cx.sb, cx.pb, cx.din, cx.dr, cx.G

    for n in ("identf", "mski", "msks", "convw", "ones_row", "ones_col",
              "normw_col", "bup_cols", "bsw_cols", "bsu_cols", "bsa_cols",
              "bg_rows"):
        P.dma("sync", "dAUX", lambda e, n=n: e.dma_start(out=sb[n][:], in_=din[n].ap()),
              reads=(), writes=(n,))
    for n in ("ident_bf", "ones_row_bf", "ones_col_bf", "selbf"):
        dst = "selbf_t" if n == "selbf" else n
        P.dma("sync", "dAUX", lambda e, n=n, dst=dst: e.dma_start(out=sb[dst][:], in_=din[n].ap()),
              reads=(), writes=(dst,) if dst != "selbf_t" else G("selbf_t"))
    P.dma("gpsimd", "gAUX", lambda e: e.dma_start(out=sb["bdn_bf"][:], in_=din["bdn_rows"].ap()),
          reads=(), writes=G("bdn_bf"))
    P.dma("gpsimd", "gAUX", lambda e: e.dma_start(out=sb["bsd_bf"][:], in_=din["bsd_row"].ap()),
          reads=(), writes=G("bsd_bf"))
    P.dma("gpsimd", "gAUX", lambda e: e.dma_start(out=sb["bg_bf"][:], in_=din["bg_rows"].ap()),
          reads=(), writes=("bg_bf",))
    P.dma("sync", "dAUX", lambda e: e.dma_start(
        out=sb["gth_idx"][:], in_=din["gther"].ap()),
        reads=(), writes=("gth_idx",))

    P.dma("sync", "dAUX", lambda e: e.dma_start(out=sb["rowtmp"][:], in_=din["dtb_row"].ap()),
          reads=(), writes=("rowtmp",))
    P.c("tensor", lambda e: e.matmul(pb["pf0"][:, 0:16], sb["ones_row"][:],
                                     sb["rowtmp"][:], start=True, stop=True),
        reads=("rowtmp", "ones_row"), writes=("pf0",))
    P.c("vector", lambda e: e.tensor_copy(sb["dtb_bc"][:], pb["pf0"][:, 0:16]),
        reads=("pf0",), writes=("dtb_bc",))
    P.dma("sync", "dAUX", lambda e: e.dma_start(out=sb["rowtmp2"][:], in_=din["alog_row"].ap()),
          reads=(), writes=("rowtmp2",))
    P.c("scalar", lambda e: e.activation(sb["rowtmp2"][:], sb["rowtmp2"][:], AF.Exp),
        reads=("rowtmp2",), writes=("rowtmp2",))
    P.c("tensor", lambda e: e.matmul(pb["pf0"][:, 0:16], sb["ones_row"][:],
                                     sb["rowtmp2"][:], start=True, stop=True),
        reads=("rowtmp2", "ones_row"), writes=("pf0",))
    P.c("vector", lambda e: e.tensor_scalar_mul(sb["acoef_bc"][:], pb["pf0"][:, 0:16], -1.0),
        reads=("pf0",), writes=("acoef_bc",))

    cx.deferred = []

    def cast_round(dst, emit, defer):
        def rnd():
            st = P.dma("gpsimd", "gCAST", emit, reads=(), writes=())
            P.last_write[dst] = st
        cx.deferred.append(rnd) if defer else rnd()

    def cast_gpkc(src, dst, rows, kdim, gdim, defer=False):
        for k in range(rows // 128):
            for gb in range(0, gdim, 16):
                gn = min(16, gdim - gb)
                cast_round(dst, lambda e, k=k, gb=gb, gn=gn: e.dma_start(
                    out=dr[dst].ap()[gb:gb + gn, :, k, :].transpose([1, 0, 2]),
                    in_=din[src].ap()[k * 128:(k + 1) * 128, gb * 256:(gb + gn) * 256]
                    .rearrange("p (g c) -> p g c", g=gn)), defer)

    def cast_hgrc(src, dst, rows, hg, defer=False):
        for r in range(rows // 128):
            cast_round(dst, lambda e, r=r: e.dma_start(
                out=dr[dst].ap()[:, r, :, :].transpose([1, 0, 2]),
                in_=din[src].ap()[r * 128:(r + 1) * 128, :]
                .rearrange("p (g c) -> p g c", g=hg)), defer)

    cast_gpkc("wqkv", "wqkv_bf", 2048, 16, 16)
    cast_gpkc("wz", "wz_bf", 2048, 16, 8)
    for k in range(16):
        P.dma("gpsimd", "gW0", lambda e, k=k: e.dma_start(
            out=sb["wba_t"][:, k * 32:(k + 1) * 32], in_=din["wba"].ap()[k * 128:(k + 1) * 128, :]),
            reads=("wba_t",), writes=("wba_t",))
    for k in range(16):
        cast_round("wg_bf", lambda e, k=k: e.dma_start(
            out=dr["wg_bf"].ap()[k * 128:(k + 1) * 128, :],
            in_=din["wg"].ap()[k * 128:(k + 1) * 128, :]), False)

    cast_hgrc("wout", "wout_bf", 4096, 4, defer=True)
    cast_gpkc("wup", "wup_bf", 2048, 16, 24, defer=True)
    cast_gpkc("wsw", "wsw_bf", 1024, 8, 24, defer=True)
    cast_gpkc("wsu", "wsu_bf", 2048, 16, 8, defer=True)
    cast_gpkc("wsa", "wsa_bf", 1024, 8, 4, defer=True)
    cast_hgrc("wdn", "wdn_bf", 6144, 4, defer=True)
    cast_hgrc("wsd", "wsd_bf", 2048, 4, defer=True)

    P.c("vector", lambda e: e.memset(sb["S_bf"][:], 0.0), reads=(), writes=G("S_bf"))
    P.c("vector", lambda e: e.memset(sb["halo"][:], 0.0), reads=(), writes=G("halo"))
    P.c("vector", lambda e: e.memset(sb["stg"][0:2, 0:512], 1.0), reads=(), writes=("stgc0",))
    P.c("vector", lambda e: e.memset(sb["stg"][0:2, 512:1536], -1.0),
        reads=(), writes=("stgc1", "stgc2"))


def emit_phaseA_tile(cx, t):
    P, sb, pb, din, dr, G = cx.P, cx.sb, cx.pb, cx.din, cx.dr, cx.G
    t0 = t * TT

    nleft = NTILES - t
    nbatch = (len(cx.deferred) + nleft - 1) // nleft if cx.deferred else 0
    batch = cx.deferred[:nbatch]
    del cx.deferred[:nbatch]
    cx.tile_casts = batch[(len(batch) + 1) // 2:]

    for rnd in range(2):
        for s2 in range(2):
            s = rnd * 2 + s2
            P.dma("gpsimd", "gX", lambda e, s=s, s2=s2: e.dma_start(
                out=sb["xtm"][:, s2 * 2048:(s2 + 1) * 2048],
                in_=din["x"].ap()[t0 + s * 128:t0 + (s + 1) * 128, :]),
                reads=G("xtm"), writes=G("xtm"))
        for s2 in range(2):
            s = rnd * 2 + s2
            for fg in range(2):
                bank = "pt%d" % fg
                for f2 in range(8):
                    f = fg * 8 + f2
                    P.c("tensor", lambda e, s2=s2, f=f, f2=f2, bank=bank: e.transpose(
                        pb[bank][:, f2 * 128:(f2 + 1) * 128],
                        sb["xtm"][:, s2 * 2048 + f * 128:s2 * 2048 + (f + 1) * 128],
                        sb["ident_bf"][:]),
                        reads=G("xtm") + ("ident_bf",), writes=(bank,))
                P.c("scalar", lambda e, s=s, fg=fg, bank=bank: e.activation(
                    bass.AP(sb["xn_fm"], fg * 8 * 512 + s * 128, [[8192, 128], [512, 8], [1, 128]]),
                    pb[bank][:].rearrange("p (f c) -> p f c", f=8), AF.Copy),
                    reads=(bank,), writes=G("xn_fm"))
    for cast_rnd in batch[:(len(batch) + 1) // 2]:
        cast_rnd()

    for k in range(16):
        P.c("tensor", lambda e, k=k: e.matmul(
            pb["pf4"][0:1, :], sb["ones_col_bf"][:], sb["xn_fm"][:, k * 512:(k + 1) * 512],
            start=(k == 0), stop=(k == 15)),
            reads=G("xn_fm") + ("ones_col_bf",), writes=("pf4",))
    for k in range(16):
        P.c("scalar", lambda e, k=k: e.activation(
            sb["r_bc"][:, 0:512], sb["xn_fm"][:, k * 512:(k + 1) * 512], AF.Square),
            reads=G("xn_fm"), writes=G("r_bc"))
        P.c("tensor", lambda e, k=k: e.matmul(
            pb["pf5"][0:1, :], sb["ones_col_bf"][:], sb["r_bc"][:, 0:512],
            start=(k == 0), stop=(k == 15)),
            reads=G("r_bc") + ("ones_col_bf",), writes=("pf5",))
    P.c("vector", lambda e: e.tensor_scalar_mul(sb["strow"][0:1, 0:512], pb["pf4"][0:1, :], 1.0 / D),
        reads=("pf4",), writes=G("strow"))
    P.c("vector", lambda e: e.tensor_scalar_mul(sb["strow"][0:1, 512:1024], pb["pf5"][0:1, :], 1.0 / D),
        reads=("pf5",), writes=G("strow"))
    P.c("vector", lambda e: e.scalar_tensor_tensor(
        sb["strow"][0:1, 1024:1536], sb["strow"][0:1, 0:512], -1.0, sb["strow"][0:1, 0:512],
        ALU.mult, ALU.mult), reads=G("strow"), writes=G("strow"))
    P.c("vector", lambda e: e.tensor_add(sb["strow"][0:1, 1024:1536],
                                         sb["strow"][0:1, 1024:1536], sb["strow"][0:1, 512:1024]),
        reads=G("strow"), writes=G("strow"))
    P.c("vector", lambda e: e.tensor_scalar_add(sb["strow"][0:1, 1024:1536],
                                                sb["strow"][0:1, 1024:1536], LN_EPS),
        reads=G("strow"), writes=G("strow"))
    P.c("scalar", lambda e: e.activation(sb["strow"][0:1, 512:1024], sb["strow"][0:1, 1024:1536],
                                         AF.Ln), reads=G("strow"), writes=G("strow"))
    P.c("scalar", lambda e: e.activation(sb["strow"][0:1, 512:1024], sb["strow"][0:1, 512:1024],
                                         AF.Exp, scale=-0.5), reads=G("strow"), writes=G("strow"))
    P.c("vector", lambda e: e.tensor_mul(sb["strow"][0:1, 0:512], sb["strow"][0:1, 0:512],
                                         sb["strow"][0:1, 512:1024]),
        reads=G("strow"), writes=G("strow"))
    P.c("tensor", lambda e: e.matmul(pb["pf4"][:, :], sb["ones_row"][:],
                                     sb["strow"][0:1, 512:1024], start=True, stop=True),
        reads=G("strow") + ("ones_row",), writes=("pf4",))
    P.c("scalar", lambda e: e.activation(sb["r_bc"][:, 0:512], pb["pf4"][:, :], AF.Copy),
        reads=("pf4",), writes=G("r_bc"))
    P.c("tensor", lambda e: e.matmul(pb["pf5"][:, :], sb["ones_row"][:],
                                     sb["strow"][0:1, 0:512], start=True, stop=True),
        reads=G("strow") + ("ones_row",), writes=("pf5",))
    P.c("scalar", lambda e: e.activation(sb["mr_bc"][:, 0:512], pb["pf5"][:, :], AF.Copy),
        reads=("pf5",), writes=G("mr_bc"))
    for k in range(16):
        eng = "gpsimd" if k % 2 == 0 else "vector"
        P.c(eng, lambda e, k=k: e.tensor_mul(
            sb["xn_fm"][:, k * 512:(k + 1) * 512], sb["xn_fm"][:, k * 512:(k + 1) * 512],
            sb["r_bc"][:, 0:512]),
            reads=G("xn_fm", "r_bc"), writes=G("xn_fm"))
        P.c(eng, lambda e, k=k: e.tensor_sub(
            sb["xn_fm"][:, k * 512:(k + 1) * 512], sb["xn_fm"][:, k * 512:(k + 1) * 512],
            sb["mr_bc"][:, 0:512]),
            reads=G("xn_fm", "mr_bc"), writes=G("xn_fm"))

    for g in range(16):
        slot = g % 2
        buf = "wt%d" % slot
        P.dma("sync", "dW%d" % slot, lambda e, g=g, buf=buf: e.dma_start(
            out=sb[buf][:, 0:4096], in_=dr["wqkv_bf"].ap()[g]),
            reads=("wqkv_bf",), writes=G(buf))
        for oc2 in range(2):
            oc = g * 2 + oc2
            bank = "pf%d" % (oc % 4)
            for k in range(16):
                P.c("tensor", lambda e, oc2=oc2, k=k, buf=buf, bank=bank: e.matmul(
                    pb[bank][:, :], sb[buf][:, k * 256 + oc2 * 128:k * 256 + oc2 * 128 + 128],
                    sb["xn_fm"][:, k * 512:(k + 1) * 512], start=(k == 0), stop=(k == 15)),
                    reads=G(buf, "xn_fm"), writes=(bank,))
            P.c("scalar", lambda e, oc=oc, bank=bank: e.activation(
                sb["qkvp"][:, oc * 512:(oc + 1) * 512], pb[bank][:, :], AF.Copy),
                reads=(bank,), writes=G("qkvp"))

    for g in range(8):
        slot = g % 2
        buf = "wt%d" % slot
        P.dma("sync", "dW%d" % slot, lambda e, g=g, buf=buf: e.dma_start(
            out=sb[buf][:, 0:4096], in_=dr["wz_bf"].ap()[g]),
            reads=("wz_bf",), writes=G(buf))
        for s in range(4):
            bank = "pf%d" % (s % 4)
            for k in range(16):
                P.c("tensor", lambda e, s=s, k=k, buf=buf, bank=bank: e.matmul(
                    pb[bank][:, 0:256], sb["xn_fm"][:, k * 512 + s * 128:k * 512 + (s + 1) * 128],
                    sb[buf][:, k * 256:(k + 1) * 256], start=(k == 0), stop=(k == 15)),
                    reads=G(buf, "xn_fm"), writes=(bank,))
            P.c("scalar", lambda e, s=s, g=g, bank=bank: e.activation(
                sb["sz"][:, s * 2048 + g * 256:s * 2048 + (g + 1) * 256],
                pb[bank][:, 0:256], AF.Silu),
                reads=(bank,), writes=G("sz"))

    for s in range(4):
        for k in range(16):
            P.c("tensor", lambda e, s=s, k=k: e.matmul(
                pb["pf4"][:, 0:32], sb["xn_fm"][:, k * 512 + s * 128:k * 512 + (s + 1) * 128],
                sb["wba_t"][:, k * 32:(k + 1) * 32], start=(k == 0), stop=(k == 15)),
                reads=("wba_t",) + G("xn_fm"), writes=("pf4",))
        P.c("vector", lambda e, s=s: e.tensor_copy(sb["ba_all"][:, s * 32:(s + 1) * 32],
                                                   pb["pf4"][:, 0:32]),
            reads=("pf4",), writes=G("ba_all"))

    for rnd in cx.tile_casts:
        rnd()
    cx.tile_casts = []
    if t in cx.coll_at_tile:
        cx.coll_at_tile.pop(t)()

    P.c("gpsimd", lambda e: e.tensor_copy(
        sb["halo2"][:].rearrange("p (c k) -> p c k", c=32),
        bass.AP(sb["qkvp"], 509, [[16384, 128], [512, 32], [1, 3]])),
        reads=G("qkvp"), writes=("halo2",))
    for cc in range(32):
        eng = "vector"
        scr = "cvt" if cc % 2 == 0 else "mdiff"
        base = cc * 512
        P.c(eng, lambda e, cc=cc, base=base, scr=scr: e.tensor_scalar_mul(
            sb[scr][:, 0:512], sb["qkvp"][:, base:base + 512],
            sb["convw"][:, cc * 4 + 3:cc * 4 + 4]),
            reads=G("qkvp") + ("convw",), writes=G(scr))
        for k in range(3):
            nh = 3 - k
            P.c(eng, lambda e, cc=cc, base=base, k=k, nh=nh, scr=scr: e.scalar_tensor_tensor(
                sb[scr][:, nh:512], sb["qkvp"][:, base:base + 512 - nh],
                sb["convw"][:, cc * 4 + k:cc * 4 + k + 1], sb[scr][:, nh:512],
                ALU.mult, ALU.add),
                reads=G("qkvp", scr) + ("convw",), writes=G(scr))
            P.c(eng, lambda e, cc=cc, k=k, nh=nh, scr=scr: e.scalar_tensor_tensor(
                sb[scr][:, 0:nh], sb["halo"][:, cc * 3 + k:cc * 3 + k + nh],
                sb["convw"][:, cc * 4 + k:cc * 4 + k + 1], sb[scr][:, 0:nh],
                ALU.mult, ALU.add),
                reads=G("halo", scr) + ("convw",), writes=G(scr))
        P.c(eng, lambda e, cc=cc, base=base, scr=scr: e.tensor_copy(
            sb["qkvp"][:, base:base + 512], sb[scr][:, 0:512]),
            reads=G(scr) + ("halo2",), writes=G("qkvp"))
    P.c("gpsimd", lambda e: e.tensor_copy(sb["halo"][:], sb["halo2"][:]),
        reads=("halo2",) + G("halo"), writes=G("halo"))

    gch = sb["gch"]
    bvw = bass.AP(sb["ba_all"], 0, [[128, 128], [32, 4], [1, 16]])
    avw = bass.AP(sb["ba_all"], 16, [[128, 128], [32, 4], [1, 16]])
    LNB, GC, R1, BGF, BET, KDF, EG = 0, 64, 128, 192, 256, 320, 384

    def gv(off):
        return gch[:, off:off + 64]
    P.c("vector", lambda e: e.tensor_scalar_min(gv(LNB), bvw, 0.0),
        reads=G("ba_all"), writes=G("gch"))
    P.c("scalar", lambda e: e.activation(gv(R1), bvw, AF.Abs),
        reads=G("ba_all"), writes=G("gch"))
    P.c("scalar", lambda e: e.activation(gv(R1), gv(R1), AF.Exp, scale=-1.0),
        reads=G("gch"), writes=G("gch"))
    P.c("scalar", lambda e: e.activation(gv(R1), gv(R1), AF.Ln, bias=1.0),
        reads=G("gch"), writes=G("gch"))
    P.c("vector", lambda e: e.tensor_sub(gv(LNB), gv(LNB), gv(R1)),
        reads=G("gch"), writes=G("gch"))
    P.c("vector", lambda e: e.tensor_tensor(
        gv(GC).rearrange("p (c v) -> p c v", c=4), avw,
        sb["dtb_bc"][:].unsqueeze(1).broadcast_to([128, 4, 16]), ALU.add),
        reads=G("ba_all") + ("dtb_bc",), writes=G("gch"))
    P.c("scalar", lambda e: e.activation(gv(BGF), gv(GC), AF.Abs),
        reads=G("gch"), writes=G("gch"))
    P.c("scalar", lambda e: e.activation(gv(BGF), gv(BGF), AF.Exp, scale=-1.0),
        reads=G("gch"), writes=G("gch"))
    P.c("scalar", lambda e: e.activation(gv(BGF), gv(BGF), AF.Ln, bias=1.0),
        reads=G("gch"), writes=G("gch"))
    P.c("vector", lambda e: e.tensor_scalar_max(gv(GC), gv(GC), 0.0),
        reads=G("gch"), writes=G("gch"))
    P.c("vector", lambda e: e.tensor_add(gv(GC), gv(GC), gv(BGF)),
        reads=G("gch"), writes=G("gch"))
    P.c("vector", lambda e: e.tensor_tensor(
        gv(GC).rearrange("p (c v) -> p c v", c=4), gv(GC).rearrange("p (c v) -> p c v", c=4),
        sb["acoef_bc"][:].unsqueeze(1).broadcast_to([128, 4, 16]), ALU.mult),
        reads=G("gch") + ("acoef_bc",), writes=G("gch"))
    for ci in range(CPT):
        P.c("tensor", lambda e, ci=ci: e.matmul(
            pb["pf5"][:, ci * 16:(ci + 1) * 16], sb["mski"][:],
            gch[:, GC + ci * 16:GC + (ci + 1) * 16], start=True, stop=True),
            reads=G("gch") + ("mski",), writes=("pf5",))
    P.c("vector", lambda e: e.tensor_copy(gv(GC), pb["pf5"][:, 0:64]),
        reads=("pf5",), writes=G("gch"))
    P.c("vector", lambda e: e.tensor_add(gv(R1), gv(GC), gv(LNB)),
        reads=G("gch"), writes=G("gch"))
    P.c("scalar", lambda e: e.activation(gv(BGF), gv(R1), AF.Exp),
        reads=G("gch"), writes=G("gch"))
    P.c("scalar", lambda e: e.activation(gv(BET), gv(LNB), AF.Exp),
        reads=G("gch"), writes=G("gch"))
    P.c("scalar", lambda e: e.activation(gv(EG), gv(GC), AF.Exp),
        reads=G("gch"), writes=G("gch"))

    for ci in range(CPT):
        emit_chunk(cx, t, ci)


def trans_group(cx, srcs, dst, evac_af=AF.Copy, evac_engs=("scalar",)):
    P, sb, pb, G = cx.P, cx.sb, cx.pb, cx.G
    for grp in range((len(srcs) + 7) // 8):
        bank = "pt%d" % (grp % 2)
        blk = srcs[grp * 8:(grp + 1) * 8]
        for j2, (tn, co) in enumerate(blk):
            P.c("tensor", lambda e, tn=tn, co=co, j2=j2, bank=bank: e.transpose(
                pb[bank][:, j2 * 128:(j2 + 1) * 128], sb[tn][:, co:co + 128],
                sb["ident_bf"][:]),
                reads=G(tn) + ("ident_bf",), writes=(bank,))
        eng = evac_engs[grp % len(evac_engs)]
        if eng == "scalar":
            P.c("scalar", lambda e, grp=grp, n=len(blk), bank=bank, dst=dst, af=evac_af: e.activation(
                sb[dst][:, grp * 1024:grp * 1024 + n * 128], pb[bank][:, 0:n * 128], af),
                reads=(bank,), writes=G(dst))
        else:
            P.c(eng, lambda e, grp=grp, n=len(blk), bank=bank, dst=dst: e.tensor_copy(
                sb[dst][:, grp * 1024:grp * 1024 + n * 128], pb[bank][:, 0:n * 128]),
                reads=(bank,), writes=G(dst))


def emit_chunk(cx, t, ci):
    P, sb, pb, G = cx.P, cx.sb, cx.pb, cx.G
    t0 = t * TT + ci * CH
    gch = sb["gch"]
    LNB, GC, R1, BGF, BET, KDF, EG = 0, 64, 128, 192, 256, 320, 384
    cs = ci * 16

    P.c("tensor", lambda e, cs=cs: e.transpose(pb["pf5"][0:16, 128:256],
                                               gch[:, GC + cs:GC + cs + 16], sb["identf"][:]),
        reads=G("gch") + ("identf",), writes=("pf5",))
    P.c("vector", lambda e: e.tensor_copy(sb["gT"][:, 0:128], pb["pf5"][0:16, 128:256]),
        reads=("pf5",), writes=G("gT"))
    P.c("tensor", lambda e, cs=cs: e.transpose(pb["pf5"][0:16, 256:384],
                                               gch[:, R1 + cs:R1 + cs + 16], sb["identf"][:]),
        reads=G("gch") + ("identf",), writes=("pf5",))
    P.c("vector", lambda e: e.tensor_copy(sb["gT"][:, 128:256], pb["pf5"][0:16, 256:384]),
        reads=("pf5",), writes=G("gT"))
    P.c("tensor", lambda e: e.matmul(
        pb["pf5"][0:1, 0:16], sb["gT"][:, 127:128], sb["identf"][0:16, 0:16],
        start=True, stop=True),
        reads=G("gT") + ("identf",), writes=("pf5",))
    P.c("vector", lambda e: e.tensor_copy(sb["rowtmp"][:], pb["pf5"][0:1, 0:16]),
        reads=("pf5",), writes=("rowtmp",))
    P.c("tensor", lambda e: e.matmul(
        pb["pf5"][:, 0:16], sb["ones_row"][:], sb["rowtmp"][:], start=True, stop=True),
        reads=("rowtmp", "ones_row"), writes=("pf5",))
    P.c("scalar", lambda e, ci=ci: e.activation(
        sb["egl"][:, ci * 16:(ci + 1) * 16], pb["pf5"][:, 0:16], AF.Exp),
        reads=("pf5",), writes=G("egl"))
    P.c("vector", lambda e, cs=cs: e.tensor_sub(gch[:, KDF + cs:KDF + cs + 16],
                                                pb["pf5"][:, 0:16], gch[:, GC + cs:GC + cs + 16]),
        reads=("pf5",) + G("gch"), writes=G("gch"))
    P.c("scalar", lambda e, cs=cs: e.activation(gch[:, KDF + cs:KDF + cs + 16],
                                                gch[:, KDF + cs:KDF + cs + 16], AF.Exp),
        reads=G("gch"), writes=G("gch"))

    trans_group(cx, [("qkvp", j * 512 + ci * 128) for j in range(0, 8)], "q_tm", AF.Silu)
    trans_group(cx, [("qkvp", j * 512 + ci * 128) for j in range(8, 16)], "k_tm", AF.Silu)
    trans_group(cx, [("qkvp", j * 512 + ci * 128) for j in range(16, 32)], "v_tm", AF.Silu)

    for (src, col0) in (("q_tm", 0), ("k_tm", 8)):
        P.c("scalar", lambda e, src=src: e.activation(sb["ysq"][:, 0:1024], sb[src][:], AF.Square),
            reads=G(src), writes=G("ysq"))
        P.c("vector", lambda e, col0=col0: e.tensor_reduce(
            sb["lncol"][:, col0:col0 + 8],
            sb["ysq"][:, 0:1024].rearrange("p (h d) -> p h d", h=8), AX.X, ALU.add),
            reads=G("ysq"), writes=G("lncol"))
    P.c("vector", lambda e: e.tensor_scalar_add(sb["lncol"][:, 0:16], sb["lncol"][:, 0:16],
                                                L2_EPS),
        reads=G("lncol"), writes=G("lncol"))
    P.c("scalar", lambda e: e.activation(sb["lncol"][:, 0:16], sb["lncol"][:, 0:16],
                                         AF.Ln), reads=G("lncol"), writes=G("lncol"))
    P.c("scalar", lambda e: e.activation(sb["lncol"][:, 0:16], sb["lncol"][:, 0:16],
                                         AF.Exp, scale=-0.5), reads=G("lncol"), writes=G("lncol"))
    P.c("vector", lambda e: e.tensor_scalar_mul(sb["lncol"][:, 0:8], sb["lncol"][:, 0:8],
                                                float(128.0 ** -0.5)),
        reads=G("lncol"), writes=G("lncol"))
    P.c("vector", lambda e: e.tensor_tensor(
        sb["qs_tm"][:].rearrange("p (h d) -> p h d", h=8),
        sb["q_tm"][:].rearrange("p (h d) -> p h d", h=8),
        sb["lncol"][:, 0:8].unsqueeze(2).broadcast_to([128, 8, 128]), ALU.mult),
        reads=G("q_tm", "lncol"), writes=G("qs_tm"))
    P.c("vector", lambda e: e.tensor_tensor(
        sb["kn_tm"][:].rearrange("p (h d) -> p h d", h=8),
        sb["k_tm"][:].rearrange("p (h d) -> p h d", h=8),
        sb["lncol"][:, 8:16].unsqueeze(2).broadcast_to([128, 8, 128]), ALU.mult),
        reads=G("k_tm", "lncol"), writes=G("kn_tm"))
    P.c("gpsimd", lambda e, cs=cs: e.tensor_tensor(
        sb["vb"][:].rearrange("p (v d) -> p v d", v=16),
        sb["v_tm"][:].rearrange("p (v d) -> p v d", v=16),
        gch[:, BET + cs:BET + cs + 16].unsqueeze(2).broadcast_to([128, 16, 128]), ALU.mult),
        reads=G("v_tm", "gch"), writes=G("vb"))
    P.c("gpsimd", lambda e, cs=cs: e.tensor_tensor(
        sb["kbg"][:].rearrange("p (h u d) -> p h u d", h=8, u=2),
        sb["kn_tm"][:].rearrange("p (h d) -> p h d", h=8)
        .unsqueeze(2).broadcast_to([128, 8, 2, 128]),
        gch[:, BGF + cs:BGF + cs + 16].rearrange("p (h u) -> p h u", h=8)
        .unsqueeze(3).broadcast_to([128, 8, 2, 128]), ALU.mult),
        reads=G("kn_tm", "gch"), writes=G("kbg"))
    P.c("vector", lambda e, cs=cs: e.tensor_tensor(
        sb["kdec"][:].rearrange("p (h u d) -> p h u d", h=8, u=2),
        sb["kn_tm"][:].rearrange("p (h d) -> p h d", h=8)
        .unsqueeze(2).broadcast_to([128, 8, 2, 128]),
        gch[:, KDF + cs:KDF + cs + 16].rearrange("p (h u) -> p h u", h=8)
        .unsqueeze(3).broadcast_to([128, 8, 2, 128]), ALU.mult),
        reads=G("kn_tm", "gch"), writes=G("kdec"))
    P.c("vector", lambda e, cs=cs: e.tensor_tensor(
        sb["qgi_tm"][:].rearrange("p (h u d) -> p h u d", h=8, u=2),
        sb["qs_tm"][:].rearrange("p (h d) -> p h d", h=8)
        .unsqueeze(2).broadcast_to([128, 8, 2, 128]),
        gch[:, EG + cs:EG + cs + 16].rearrange("p (h u) -> p h u", h=8)
        .unsqueeze(3).broadcast_to([128, 8, 2, 128]), ALU.mult),
        reads=G("qs_tm", "gch"), writes=G("qgi_tm"))

    trans_group(cx, [("qs_tm", j * 128) for j in range(8)], "qs_fm")
    trans_group(cx, [("kn_tm", j * 128) for j in range(8)], "kn_fm",
                evac_engs=("vector",))
    trans_group(cx, [("qgi_tm", j * 128) for j in range(16)], "qgi_fm",
                evac_engs=("scalar", "vector"))

    for rd in range(4):
        P.dma("scalar", "dRS", lambda e, rd=rd: e.dma_start(
            out=sb["stg"][1:2, 0:512], in_=sb["gT"][rd * 4:rd * 4 + 4, 0:128]),
            reads=G("gT"), writes=("stg_l",))
        P.dma("scalar", "dRS", lambda e, rd=rd: e.dma_start(
            out=sb["stg"][0:1, 512:1024], in_=sb["gT"][rd * 4:rd * 4 + 4, 0:128]),
            reads=G("gT"), writes=("stg_r0",))
        P.dma("scalar", "dRS", lambda e, rd=rd: e.dma_start(
            out=sb["stg"][0:1, 1024:1536], in_=sb["gT"][rd * 4:rd * 4 + 4, 128:256]),
            reads=G("gT"), writes=("stg_r1",))
        for v4 in range(4):
            P.c("tensor", lambda e, v4=v4: e.matmul(
                pb["pf0"][:, v4 * 128:(v4 + 1) * 128],
                bass.AP(sb["stg"], v4 * 128, [[1536, 2], [1, 128]]),
                bass.AP(sb["stg"], 1024 + v4 * 128, [[1536, 2], [1, 128]]),
                start=True, stop=True),
                reads=("stg_l", "stg_r1", "stgc0", "stgc2"), writes=("pf0",))
            P.c("tensor", lambda e, v4=v4: e.matmul(
                pb["pf1"][:, v4 * 128:(v4 + 1) * 128],
                bass.AP(sb["stg"], v4 * 128, [[1536, 2], [1, 128]]),
                bass.AP(sb["stg"], 512 + v4 * 128, [[1536, 2], [1, 128]]),
                start=True, stop=True),
                reads=("stg_l", "stg_r0", "stgc0", "stgc1"), writes=("pf1",))
        P.c("vector", lambda e: e.tensor_tensor(
            sb["mdiff"][:].rearrange("p (v c) -> p v c", v=4),
            pb["pf0"][:].rearrange("p (v c) -> p v c", v=4),
            sb["msks"][:].unsqueeze(1).broadcast_to([128, 4, 128]), ALU.mult),
            reads=("pf0", "msks"), writes=G("mdiff"))
        P.c("vector", lambda e: e.tensor_tensor(
            sb["cvt"][:].rearrange("p (v c) -> p v c", v=4),
            pb["pf1"][:].rearrange("p (v c) -> p v c", v=4),
            sb["mski"][:].unsqueeze(1).broadcast_to([128, 4, 128]), ALU.mult),
            reads=("pf1", "mski"), writes=G("cvt"))
        P.c("scalar", lambda e, rd=rd: e.activation(
            sb["m1"][:, rd * 512:(rd + 1) * 512], sb["mdiff"][:, 0:512], AF.Exp),
            reads=G("mdiff"), writes=G("m1"))
        P.c("scalar", lambda e, rd=rd: e.activation(
            sb["m0"][:, rd * 512:(rd + 1) * 512], sb["cvt"][:, 0:512], AF.Exp),
            reads=G("cvt"), writes=G("m0"))

    for rnd in range(2):
        for k4 in range(4):
            kh = rnd * 4 + k4
            P.c("tensor", lambda e, kh=kh, k4=k4: e.matmul(
                pb["pf2"][:, k4 * 128:(k4 + 1) * 128], sb["kn_fm"][:, kh * 128:(kh + 1) * 128],
                sb["kn_fm"][:, kh * 128:(kh + 1) * 128], start=True, stop=True),
                reads=G("kn_fm"), writes=("pf2",))
            P.c("tensor", lambda e, kh=kh, k4=k4: e.matmul(
                pb["pf3"][:, k4 * 128:(k4 + 1) * 128], sb["kn_fm"][:, kh * 128:(kh + 1) * 128],
                sb["qs_fm"][:, kh * 128:(kh + 1) * 128], start=True, stop=True),
                reads=G("kn_fm", "qs_fm"), writes=("pf3",))
        P.c("vector", lambda e, rnd=rnd: e.scalar_tensor_tensor(
            sb["Gm"][:, rnd * 512:(rnd + 1) * 512].rearrange("p (v c) -> p v c", v=4),
            pb["pf2"][:].rearrange("p (v c) -> p v c", v=4), -1.0,
            sb["msks"][:].unsqueeze(1).broadcast_to([128, 4, 128]), ALU.mult, ALU.mult),
            reads=("pf2", "msks"), writes=G("Gm"))
        P.c("vector", lambda e, rnd=rnd: e.tensor_tensor(
            sb["qkTm"][:, rnd * 512:(rnd + 1) * 512].rearrange("p (v c) -> p v c", v=4),
            pb["pf3"][:].rearrange("p (v c) -> p v c", v=4),
            sb["mski"][:].unsqueeze(1).broadcast_to([128, 4, 128]), ALU.mult),
            reads=("pf3", "mski"), writes=G("qkTm"))
    P.c("gpsimd", lambda e: e.tensor_tensor(
        sb["C"][:].rearrange("p (h u d) -> p h u d", h=8, u=2),
        sb["Gm"][:].rearrange("p (h d) -> p h d", h=8)
        .unsqueeze(2).broadcast_to([128, 8, 2, 128]),
        sb["m1"][:].rearrange("p (h u d) -> p h u d", h=8, u=2), ALU.mult),
        reads=G("Gm", "m1"), writes=G("C"))
    P.c("vector", lambda e: e.tensor_tensor(
        sb["m0"][:].rearrange("p (h u d) -> p h u d", h=8, u=2),
        sb["qkTm"][:].rearrange("p (h d) -> p h d", h=8)
        .unsqueeze(2).broadcast_to([128, 8, 2, 128]),
        sb["m0"][:].rearrange("p (h u d) -> p h u d", h=8, u=2), ALU.mult),
        reads=G("qkTm", "m0"), writes=G("m0"))

    trans_group(cx, [("C", j * 128) for j in range(16)], "A",
                evac_engs=("scalar", "vector"))
    P.c("gpsimd", lambda e: e.tensor_tensor(
        sb["P"][:].rearrange("p (v c) -> p v c", v=16),
        sb["C"][:].rearrange("p (v c) -> p v c", v=16),
        sb["ident_bf"][:].unsqueeze(1).broadcast_to([128, 16, 128]), ALU.add),
        reads=G("C") + ("ident_bf",), writes=G("P"))

    def quad_evac(grp, bank, dst):
        if grp % 2 == 0:
            P.c("scalar", lambda e, grp=grp, bank=bank, dst=dst: e.activation(
                sb[dst][:, grp * 512:(grp + 1) * 512], pb[bank][:], AF.Copy),
                reads=(bank,), writes=G(dst))
        else:
            P.c("vector", lambda e, grp=grp, bank=bank, dst=dst: e.tensor_copy(
                sb[dst][:, grp * 512:(grp + 1) * 512], pb[bank][:]),
                reads=(bank,), writes=G(dst))

    def quad_mm(lhs_of, rhs_of, dst):
        for grp in range(4):
            bank = "pf%d" % (grp % 4)
            rd = set()
            for v2 in range(4):
                vh = grp * 4 + v2
                ln, lo = lhs_of(vh)
                rn, ro = rhs_of(vh)
                rd.add(ln)
                rd.add(rn)
                P.c("tensor", lambda e, v2=v2, ln=ln, lo=lo, rn=rn, ro=ro, bank=bank: e.matmul(
                    pb[bank][:, v2 * 128:(v2 + 1) * 128],
                    sb[ln][:, lo:lo + 128], sb[rn][:, ro:ro + 128],
                    start=True, stop=True),
                    reads=G(*rd), writes=(bank,))
            quad_evac(grp, bank, dst)

    def quad_mm_addp(a_name, p_name, dst):
        for grp in range(4):
            bank = "pf%d" % (grp % 4)
            for v2 in range(4):
                vh = grp * 4 + v2
                P.c("tensor", lambda e, v2=v2, vh=vh, bank=bank, a=a_name, p=p_name: e.matmul(
                    pb[bank][:, v2 * 128:(v2 + 1) * 128],
                    sb[a][:, vh * 128:(vh + 1) * 128], sb[p][:, vh * 128:(vh + 1) * 128],
                    start=True, stop=False),
                    reads=G(a_name, p_name), writes=(bank,))
                P.c("tensor", lambda e, v2=v2, vh=vh, bank=bank, p=p_name: e.matmul(
                    pb[bank][:, v2 * 128:(v2 + 1) * 128],
                    sb["ident_bf"][:], sb[p][:, vh * 128:(vh + 1) * 128],
                    start=False, stop=True),
                    reads=("ident_bf",) + G(p_name), writes=(bank,))
            quad_evac(grp, bank, dst)

    cur_c, cur_a, cur_p = "C", "A", "P"
    oth_c, oth_a, oth_p = "C2", "A2", "P2"
    for lvl in range(6):
        if lvl < 5:
            quad_mm(lambda vh, a=cur_a: (a, vh * 128), lambda vh, c=cur_c: (c, vh * 128), oth_c)
        quad_mm(lambda vh, c=cur_c: (c, vh * 128), lambda vh, a=cur_a: (a, vh * 128), oth_a)
        quad_mm_addp(oth_a, cur_p, oth_p)
        cur_c, oth_c = oth_c, cur_c
        cur_a, oth_a = oth_a, cur_a
        cur_p, oth_p = oth_p, cur_p
    Pfin = cur_p

    quad_mm(lambda vh, p=Pfin: (p, vh * 128), lambda vh: ("vb", vh * 128), "vc")
    quad_mm(lambda vh: ("kbg", vh * 128), lambda vh, p=Pfin: (p, vh * 128), "kcdT")

    for grp in range(4):
        bs = "pf%d" % (2 + grp % 2)
        bo = "pf%d" % (4 + grp % 2)
        bd = "pf%d" % (grp % 2)
        for v2 in range(4):
            vh = grp * 4 + v2
            P.c("tensor", lambda e, vh=vh, v2=v2, bs=bs: e.matmul(
                pb[bs][:, v2 * 128:(v2 + 1) * 128], sb["kcdT"][:, vh * 128:(vh + 1) * 128],
                sb["S_bf"][:, vh * 128:(vh + 1) * 128], start=True, stop=True),
                reads=G("kcdT", "S_bf"), writes=(bs,))
        P.c("vector", lambda e, grp=grp, bs=bs: e.tensor_sub(
            sb["vnew"][:, grp * 512:(grp + 1) * 512], sb["vc"][:, grp * 512:(grp + 1) * 512],
            pb[bs][:]),
            reads=G("vc") + (bs,), writes=G("vnew"))
        for v2 in range(4):
            vh = grp * 4 + v2
            P.c("tensor", lambda e, vh=vh, v2=v2, bo=bo: e.matmul(
                pb[bo][:, v2 * 128:(v2 + 1) * 128], sb["qgi_fm"][:, vh * 128:(vh + 1) * 128],
                sb["S_bf"][:, vh * 128:(vh + 1) * 128], start=True, stop=False),
                reads=G("qgi_fm", "S_bf"), writes=(bo,))
            P.c("tensor", lambda e, vh=vh, v2=v2, bo=bo: e.matmul(
                pb[bo][:, v2 * 128:(v2 + 1) * 128], sb["m0"][:, vh * 128:(vh + 1) * 128],
                sb["vnew"][:, vh * 128:(vh + 1) * 128], start=False, stop=True),
                reads=G("m0", "vnew"), writes=(bo,))
            P.c("tensor", lambda e, vh=vh, v2=v2, bd=bd: e.matmul(
                pb[bd][:, v2 * 128:(v2 + 1) * 128], sb["kdec"][:, vh * 128:(vh + 1) * 128],
                sb["vnew"][:, vh * 128:(vh + 1) * 128], start=True, stop=True),
                reads=G("kdec", "vnew"), writes=(bd,))
        P.c("vector", lambda e, grp=grp, bo=bo, ci=ci: e.tensor_mul(
            sb["y_tm"][:, grp * 512:(grp + 1) * 512], pb[bo][:],
            sb["sz"][:, ci * 2048 + grp * 512:ci * 2048 + (grp + 1) * 512]),
            reads=(bo,) + G("sz"), writes=G("y_tm"))
        for v2 in range(4):
            vh = grp * 4 + v2
            P.c("vector", lambda e, vh=vh, v2=v2, bd=bd, ci=ci: e.scalar_tensor_tensor(
                sb["S_bf"][:, vh * 128:(vh + 1) * 128], sb["S_bf"][:, vh * 128:(vh + 1) * 128],
                sb["egl"][:, ci * 16 + vh:ci * 16 + vh + 1], pb[bd][:, v2 * 128:(v2 + 1) * 128],
                ALU.mult, ALU.add),
                reads=G("S_bf", "egl") + (bd,), writes=G("S_bf"))

    P.c("scalar", lambda e: e.activation(sb["ysq"][:, 0:2048], sb["y_tm"][:], AF.Square),
        reads=G("y_tm"), writes=G("ysq"))
    P.c("vector", lambda e: e.tensor_reduce(
        sb["rstc"][:, 0:16], sb["ysq"][:].rearrange("p (v d) -> p v d", v=16), AX.X, ALU.add),
        reads=G("ysq"), writes=G("rstc"))
    P.c("vector", lambda e: e.tensor_scalar_add(sb["rstc"][:, 0:16], sb["rstc"][:, 0:16],
                                                RMS_EPS * 128.0),
        reads=G("rstc"), writes=G("rstc"))
    P.c("scalar", lambda e: e.activation(sb["rstc"][:, 0:16], sb["rstc"][:, 0:16],
                                         AF.Ln), reads=G("rstc"), writes=G("rstc"))
    P.c("scalar", lambda e: e.activation(sb["rstc"][:, 0:16], sb["rstc"][:, 0:16],
                                         AF.Exp, scale=-0.5), reads=G("rstc"), writes=G("rstc"))
    P.c("vector", lambda e: e.tensor_scalar_mul(sb["rstc"][:, 0:16], sb["rstc"][:, 0:16],
                                                float(128.0 ** 0.5)),
        reads=G("rstc"), writes=G("rstc"))
    P.c("vector", lambda e: e.tensor_tensor(
        sb["y_tm"][:].rearrange("p (v d) -> p v d", v=16),
        sb["y_tm"][:].rearrange("p (v d) -> p v d", v=16),
        sb["rstc"][:, 0:16].unsqueeze(2).broadcast_to([128, 16, 128]), ALU.mult),
        reads=G("y_tm", "rstc"), writes=G("y_tm"))
    trans_group(cx, [("y_tm", j * 128) for j in range(16)], "nrm_fm",
                evac_engs=("scalar", "vector"))
    P.c("vector", lambda e: e.tensor_scalar_mul(
        sb["nrm_fm"][:], sb["nrm_fm"][:], sb["normw_col"][:]),
        reads=G("nrm_fm") + ("normw_col",), writes=G("nrm_fm"))
    q, col = t0 // 1024, t0 % 1024
    for half in range(2):
        P.dma("scalar", "dST", lambda e, half=half, q=q, col=col: e.dma_start(
            out=cx.dr["cc_in"].ap()[q, half * 1024:(half + 1) * 1024, col:col + 128]
            .rearrange("(v p) c -> v p c", p=128).transpose([1, 0, 2]),
            in_=sb["nrm_fm"][:, half * 1024:(half + 1) * 1024]
            .rearrange("p (v c) -> p v c", v=8)),
            reads=G("nrm_fm"), writes=("cc_in_q%d" % q,))


def emit_gathers(cx, tb):
    P, sb, dr, G = cx.P, cx.sb, cx.dr, cx.G
    srcq = dr["cc_out%d" % tb].ap().rearrange("r (u c) -> (r u) c", u=2)
    for g in range(32):
        P.dma("gpsimd", "dGT", lambda e, g=g, srcq=srcq: e.indirect_dma_start(
            sb["nrmB"][:, g * 512:(g + 1) * 512], None, srcq,
            bass.IndirectOffsetOnAxis(sb["gth_idx"][:, g:g + 1], 0)),
            reads=("cc_out%d" % tb, "gth_idx") + G("nrmB"), writes=G("nrmB"))


def emit_phaseB_tile(cx, tb):
    P, sb, pb, din, dr, G = cx.P, cx.sb, cx.pb, cx.din, cx.dr, cx.G
    t0 = tb * TTB

    emit_gathers(cx, tb)
    if tb == 0:
        cx.coll_b0()

    for hg in range(4):
        P.dma("scalar", "dX", lambda e, hg=hg: e.dma_start(
            out=sb["res0s"][:, 0:2048].rearrange("p (s c) -> p s c", s=4),
            in_=din["xh"].ap()[t0:t0 + 512, hg * 512:(hg + 1) * 512]
            .rearrange("(s p) c -> p s c", p=128)),
            reads=G("res0s"), writes=G("res0s"))
        for rg in range(4):
            slot = rg % 2
            buf = "wt%d" % slot
            P.dma("sync", "dW%d" % slot,
                  lambda e, hg=hg, rg=rg, buf=buf: e.dma_start(
                out=sb[buf][:, 0:4096].rearrange("p (r c) -> p r c", r=8),
                in_=dr["wout_bf"].ap()[hg, rg * 8:(rg + 1) * 8].transpose([1, 0, 2])),
                reads=("wout_bf",), writes=G(buf))
            for s in range(4):
                bank = "pf%d" % s
                for r2 in range(8):
                    rc = rg * 8 + r2
                    P.c("tensor", lambda e, s=s, rc=rc, r2=r2, buf=buf, bank=bank: e.matmul(
                        pb[bank][:, :],
                        sb["nrmB"][:, rc * 512 + s * 128:rc * 512 + (s + 1) * 128],
                        sb[buf][:, r2 * 512:(r2 + 1) * 512],
                        start=(rc == 0), stop=(rc == 31)),
                        reads=G("nrmB", buf), writes=(bank,))
        for s in range(4):
            bank = "pf%d" % s
            P.c("vector", lambda e, s=s, hg=hg, bank=bank: e.tensor_add(
                sb["resfs"][:, s * 512:(s + 1) * 512], pb[bank][:, :],
                sb["res0s"][:, s * 512:(s + 1) * 512]),
                reads=(bank,) + G("res0s", "resfs"), writes=G("resfs"))
        P.dma("scalar", "dRES", lambda e, hg=hg: e.dma_start(
            out=dr["res0_d"].ap()[t0:t0 + 512, hg * 512:(hg + 1) * 512]
            .rearrange("(s p) c -> p s c", p=128),
            in_=sb["resfs"][:, 0:2048].rearrange("p (s c) -> p s c", s=4)),
            reads=G("resfs"), writes=("res0_d",))

    for s in range(4):
        P.dma("sync", "dRES", lambda e, s=s: e.dma_start(
            out=sb["res0s"][:, 0:2048], in_=dr["res0_d"].ap()[t0 + s * 128:t0 + (s + 1) * 128, :]),
            reads=("res0_d",) + G("res0s"), writes=G("res0s"))
        P.c("vector", lambda e: e.tensor_reduce(
            sb["rstc"][:, 0:1], sb["res0s"][:, 0:2048], AX.X, ALU.add),
            reads=G("res0s"), writes=G("rstc"))
        P.c("scalar", lambda e: e.activation(sb["resfs"][:, 0:2048], sb["res0s"][:, 0:2048],
                                             AF.Square),
            reads=G("res0s"), writes=G("resfs"))
        P.c("vector", lambda e: e.tensor_reduce(
            sb["rstc"][:, 1:2], sb["resfs"][:, 0:2048], AX.X, ALU.add),
            reads=G("resfs"), writes=G("rstc"))
        P.c("vector", lambda e: e.tensor_scalar_mul(sb["rstc"][:, 0:2], sb["rstc"][:, 0:2], 1.0 / D),
            reads=G("rstc"), writes=G("rstc"))
        P.c("vector", lambda e: e.scalar_tensor_tensor(
            sb["rstc"][:, 2:3], sb["rstc"][:, 0:1], -1.0, sb["rstc"][:, 0:1],
            ALU.mult, ALU.mult), reads=G("rstc"), writes=G("rstc"))
        P.c("vector", lambda e: e.tensor_add(sb["rstc"][:, 2:3], sb["rstc"][:, 2:3],
                                             sb["rstc"][:, 1:2]),
            reads=G("rstc"), writes=G("rstc"))
        P.c("vector", lambda e: e.tensor_scalar_add(sb["rstc"][:, 2:3], sb["rstc"][:, 2:3],
                                                    LN_EPS), reads=G("rstc"), writes=G("rstc"))
        P.c("scalar", lambda e: e.activation(sb["rstc"][:, 2:3], sb["rstc"][:, 2:3],
                                             AF.Ln), reads=G("rstc"), writes=G("rstc"))
        P.c("scalar", lambda e: e.activation(sb["rstc"][:, 2:3], sb["rstc"][:, 2:3],
                                             AF.Exp, scale=-0.5), reads=G("rstc"), writes=G("rstc"))
        P.c("vector", lambda e: e.tensor_mul(sb["rstc"][:, 3:4], sb["rstc"][:, 0:1],
                                             sb["rstc"][:, 2:3]),
            reads=G("rstc"), writes=G("rstc"))
        P.c("vector", lambda e: e.tensor_scalar(
            sb["resfs"][:, 0:2048], sb["res0s"][:, 0:2048], sb["rstc"][:, 2:3],
            sb["rstc"][:, 3:4], ALU.mult, ALU.subtract),
            reads=G("res0s", "rstc"), writes=G("resfs"))
        P.dma("scalar", "dRES", lambda e, s=s: e.dma_start(
            out=dr["resf_d"].ap()[t0 + s * 128:t0 + (s + 1) * 128, :], in_=sb["resfs"][:, 0:2048]),
            reads=G("resfs"), writes=("resf_d",))
        P.c("vector", lambda e: e.tensor_copy(sb["xt1b"][:, 0:512] if False else
                                              sb["h2w0"][:, 0:2048], sb["resfs"][:, 0:2048]),
            reads=G("resfs"), writes=G("h2w0"))
        trans_to_xtfm(cx, "h2w0", 0, s)

    emit_ffn_iter(cx, tb, 0)
    emit_ffn_iter(cx, tb, 1)


def trans_to_xtfm(cx, srcname, src_off, s):
    P, sb, pb, G = cx.P, cx.sb, cx.pb, cx.G
    for grp in range(2):
        bank = "pt%d" % grp
        for f2 in range(8):
            f = grp * 8 + f2
            P.c("tensor", lambda e, f=f, f2=f2, bank=bank, srcname=srcname, src_off=src_off: e.transpose(
                pb[bank][:, f2 * 128:(f2 + 1) * 128],
                sb[srcname][:, src_off + f * 128:src_off + (f + 1) * 128], sb["ident_bf"][:]),
                reads=G(srcname) + ("ident_bf",), writes=(bank,))
        P.c("scalar", lambda e, grp=grp, s=s, bank=bank: e.activation(
            bass.AP(sb["xt_fm"], grp * 8 * 512 + s * 128, [[8192, 128], [512, 8], [1, 128]]),
            pb[bank][:].rearrange("p (f c) -> p f c", f=8), AF.Copy),
            reads=(bank,), writes=G("xt_fm"))


def emit_ffn_iter(cx, tb, j):
    P, sb, pb, din, dr, G = cx.P, cx.sb, cx.pb, cx.din, cx.dr, cx.G
    t0 = tb * TTB

    P.dma("sync", "dW0", lambda e, j=j: e.dma_start(
        out=sb["wgt"][:, 0:128].rearrange("p (k c) -> p k c", k=16),
        in_=dr["wg_bf"].ap()[:, j * 8:(j + 1) * 8].rearrange("(k p) c -> k p c", p=128)
        .transpose([1, 0, 2])),
        reads=("wg_bf",), writes=("wgt",))
    for s in range(4):
        for k in range(16):
            P.c("tensor", lambda e, s=s, k=k: e.matmul(
                pb["pf4"][:, 0:8], sb["xt_fm"][:, k * 512 + s * 128:k * 512 + (s + 1) * 128],
                sb["wgt"][:, k * 8:(k + 1) * 8], start=(k == 0), stop=False),
                reads=("wgt",) + G("xt_fm"), writes=("pf4",))
        P.c("tensor", lambda e, s=s, j=j: e.matmul(
            pb["pf4"][:, 0:8], sb["ones_row_bf"][:], sb["bg_bf"][0:1, j * 8:j * 8 + 8],
            start=False, stop=True),
            reads=("bg_bf", "ones_row_bf"), writes=("pf4",))
        P.c("vector", lambda e, s=s: e.tensor_copy(sb["gat"][:, s * 8:s * 8 + 8],
                                                   pb["pf4"][:, 0:8]),
            reads=("pf4",), writes=G("gat"))
    gat = sb["gat"]
    gv = gat[:, 0:32].rearrange("p (s e) -> p s e", s=4)

    def col4(c0):
        return gat[:, c0:c0 + 4].unsqueeze(2).broadcast_to([128, 4, 8])

    def t32(c0):
        return gat[:, c0:c0 + 32].rearrange("p (s e) -> p s e", s=4)
    P.c("vector", lambda e: e.tensor_reduce(gat[:, 32:36], gv, AX.X, ALU.min),
        reads=G("gat"), writes=G("gat"))
    P.c("vector", lambda e: e.tensor_tensor(t32(64), gv, col4(32), ALU.subtract),
        reads=G("gat"), writes=G("gat"))
    P.c("scalar", lambda e: e.activation(gat[:, 64:96], gat[:, 64:96], AF.Sign),
        reads=G("gat"), writes=G("gat"))
    P.c("vector", lambda e: e.tensor_mul(gat[:, 96:128], gat[:, 64:96], gat[:, 0:32]),
        reads=G("gat"), writes=G("gat"))
    P.c("vector", lambda e: e.tensor_scalar(
        gat[:, 64:96], gat[:, 64:96], -1e30, 1e30, ALU.mult, ALU.add),
        reads=G("gat"), writes=G("gat"))
    P.c("vector", lambda e: e.tensor_add(gat[:, 96:128], gat[:, 96:128], gat[:, 64:96]),
        reads=G("gat"), writes=G("gat"))
    P.c("vector", lambda e: e.tensor_reduce(gat[:, 36:40], t32(96), AX.X, ALU.min),
        reads=G("gat"), writes=G("gat"))
    P.c("vector", lambda e: e.tensor_tensor(t32(64), gv, col4(36), ALU.subtract),
        reads=G("gat"), writes=G("gat"))
    P.c("scalar", lambda e: e.activation(gat[:, 64:96], gat[:, 64:96], AF.Sign),
        reads=G("gat"), writes=G("gat"))
    P.c("vector", lambda e: e.tensor_scalar_max(gat[:, 64:96], gat[:, 64:96], 0.0),
        reads=G("gat"), writes=G("gat"))
    P.c("vector", lambda e: e.tensor_reduce(gat[:, 40:44], gv, AX.X, ALU.max),
        reads=G("gat"), writes=G("gat"))
    P.c("vector", lambda e: e.tensor_tensor(t32(96), gv, col4(40), ALU.subtract),
        reads=G("gat"), writes=G("gat"))
    P.c("scalar", lambda e: e.activation(gat[:, 96:128], gat[:, 96:128], AF.Exp),
        reads=G("gat"), writes=G("gat"))
    P.c("vector", lambda e: e.tensor_mul(gat[:, 96:128], gat[:, 96:128], gat[:, 64:96]),
        reads=G("gat"), writes=G("gat"))
    P.c("vector", lambda e: e.tensor_reduce(gat[:, 44:48], t32(96), AX.X, ALU.add),
        reads=G("gat"), writes=G("gat"))
    P.c("vector", lambda e: e.reciprocal(gat[:, 48:52], gat[:, 44:48]),
        reads=G("gat"), writes=G("gat"))
    P.c("vector", lambda e: e.tensor_tensor(t32(128), t32(96), col4(48), ALU.mult),
        reads=G("gat"), writes=G("gat"))
    P.c("vector", lambda e: e.tensor_copy(sb["wbc"][:, 0:32], gat[:, 128:160]),
        reads=G("gat"), writes=G("wbc"))
    for s in range(4):
        P.c("tensor", lambda e, s=s: e.transpose(
            pb["pt0"][0:8, s * 128:(s + 1) * 128], sb["wbc"][:, s * 8:(s + 1) * 8],
            sb["ident_bf"][:]),
            reads=G("wbc") + ("ident_bf",), writes=("pt0",))
    P.c("scalar", lambda e: e.activation(
        sb["wfm"][:, 0:512], pb["pt0"][0:8, 0:512], AF.Copy),
        reads=("pt0",), writes=G("wfm"))

    for n in range(6):
        for g4 in range(2):
            for g2 in range(2):
                g = n * 4 + g4 * 2 + g2
                slot = g % 2
                buf = "wt%d" % slot
                P.dma("sync", "dW%d" % slot, lambda e, g=g, buf=buf: e.dma_start(
                    out=sb[buf][:, 0:4096], in_=dr["wup_bf"].ap()[g]),
                    reads=("wup_bf",), writes=G(buf))
                for e2 in range(2):
                    ec = (g4 * 2 + g2) * 2 + e2
                    bank = "pf%d" % (ec % 4)
                    for k in range(16):
                        P.c("tensor", lambda e, e2=e2, k=k, buf=buf, bank=bank: e.matmul(
                            pb[bank][:, :], sb[buf][:, k * 256 + e2 * 128:k * 256 + e2 * 128 + 128],
                            sb["xt_fm"][:, k * 512:(k + 1) * 512], start=(k == 0), stop=(k == 15)),
                            reads=G(buf, "xt_fm"), writes=(bank,))
                    P.c("vector", lambda e, n=n, ec=ec, bank=bank: e.tensor_scalar_add(
                        sb["h_fm"][:, ec * 512:(ec + 1) * 512], pb[bank][:, :],
                        sb["bup_cols"][:, n * 8 + ec:n * 8 + ec + 1]),
                        reads=(bank, "bup_cols"), writes=G("h_fm"))
        for g4 in range(4):
            g = n * 4 + g4
            slot = g % 2
            buf = "wt%d" % slot
            P.dma("sync", "dW%d" % slot, lambda e, g=g, buf=buf: e.dma_start(
                out=sb[buf][:, 0:2048], in_=dr["wsw_bf"].ap()[g]),
                reads=("wsw_bf",), writes=G(buf))
            for e2 in range(2):
                ec = g4 * 2 + e2
                bank = "pf%d" % (ec % 4)
                for k in range(8):
                    P.c("tensor", lambda e, e2=e2, k=k, buf=buf, bank=bank: e.matmul(
                        pb[bank][:, :], sb[buf][:, k * 256 + e2 * 128:k * 256 + e2 * 128 + 128],
                        sb["h_fm"][:, k * 512:(k + 1) * 512], start=(k == 0), stop=(k == 7)),
                        reads=G(buf, "h_fm"), writes=(bank,))
                P.c("scalar", lambda e, n=n, ec=ec, bank=bank: e.activation(
                    sb["sil"][:, ec * 512:(ec + 1) * 512], pb[bank][:, :], AF.Silu,
                    bias=sb["bsw_cols"][:, n * 8 + ec:n * 8 + ec + 1]),
                    reads=(bank, "bsw_cols"), writes=G("sil"))
        P.c("tensor", lambda e, n=n: e.matmul(
            pb["pf4"][:, :], sb["selbf_t"][:, n * 128:(n + 1) * 128],
            sb["wfm"][:, 0:512], start=True, stop=True),
            reads=G("selbf_t", "wfm"), writes=("pf4",))
        P.c("scalar", lambda e: e.activation(sb["wbc"][:, 0:512], pb["pf4"][:, :], AF.Copy),
            reads=("pf4",), writes=G("wbc"))
        P.c("gpsimd", lambda e: e.tensor_mul(sb["sil"][:, 0:4096], sb["sil"][:, 0:4096],
                                             sb["h_fm"][:, 0:4096]),
            reads=G("sil", "h_fm"), writes=G("sil"))
        P.c("vector", lambda e, n=n: e.tensor_tensor(
            sb["h2w%d" % n][:].rearrange("p (g c) -> p g c", g=8),
            sb["sil"][:, 0:4096].rearrange("p (g c) -> p g c", g=8),
            sb["wbc"][:, 0:512].unsqueeze(1).broadcast_to([128, 8, 512]), ALU.mult),
            reads=G("sil", "wbc"), writes=G("h2w%d" % n))

    for g in range(8):
        slot = g % 2
        buf = "wt%d" % slot
        P.dma("sync", "dW%d" % slot, lambda e, g=g, buf=buf: e.dma_start(
            out=sb[buf][:, 0:4096], in_=dr["wsu_bf"].ap()[g]),
            reads=("wsu_bf",), writes=G(buf))
        for e2 in range(2):
            sc = g * 2 + e2
            bank = "pf%d" % (sc % 4)
            for k in range(16):
                P.c("tensor", lambda e, e2=e2, k=k, buf=buf, bank=bank: e.matmul(
                    pb[bank][:, :], sb[buf][:, k * 256 + e2 * 128:k * 256 + e2 * 128 + 128],
                    sb["xt_fm"][:, k * 512:(k + 1) * 512], start=(k == 0), stop=(k == 15)),
                    reads=G(buf, "xt_fm"), writes=(bank,))
            P.c("vector", lambda e, sc=sc, bank=bank: e.tensor_scalar_add(
                sb["sh_fm"][:, sc * 512:(sc + 1) * 512], pb[bank][:, :],
                sb["bsu_cols"][:, sc:sc + 1]),
                reads=(bank, "bsu_cols"), writes=G("sh_fm"))
    for g in range(4):
        slot = g % 2
        buf = "wt%d" % slot
        P.dma("sync", "dW%d" % slot, lambda e, g=g, buf=buf: e.dma_start(
            out=sb[buf][:, 0:2048], in_=dr["wsa_bf"].ap()[g]),
            reads=("wsa_bf",), writes=G(buf))
        for sl in range(2):
            for e2 in range(2):
                ec = g * 2 + e2
                bank = "pf%d" % ((sl * 2 + e2) % 4)
                for k in range(8):
                    P.c("tensor", lambda e, sl=sl, e2=e2, k=k, buf=buf, bank=bank: e.matmul(
                        pb[bank][:, :], sb[buf][:, k * 256 + e2 * 128:k * 256 + e2 * 128 + 128],
                        sb["sh_fm"][:, (sl * 8 + k) * 512:(sl * 8 + k + 1) * 512],
                        start=(k == 0), stop=(k == 7)),
                        reads=G(buf, "sh_fm"), writes=(bank,))
                P.c("scalar", lambda e, sl=sl, ec=ec, bank=bank: e.activation(
                    sb["sh2"][:, (sl * 8 + ec) * 512:(sl * 8 + ec + 1) * 512],
                    pb[bank][:, :], AF.Silu, bias=sb["bsa_cols"][:, ec:ec + 1]),
                    reads=(bank, "bsa_cols"), writes=G("sh2"))
    P.c("gpsimd", lambda e: e.tensor_mul(sb["sh2"][:, 0:8192], sb["sh2"][:, 0:8192],
                                         sb["sh_fm"][:, 0:8192]),
        reads=G("sh2", "sh_fm"), writes=G("sh2"))


    last = (j == 1)
    for hg in range(4):
        banks = ["pf%d" % i for i in range(4)]
        first = [True] * 4
        P.dma("scalar", "dRES", lambda e, hg=hg: e.dma_start(
            out=sb["resfs"][:, 0:2048].rearrange("p (s c) -> p s c", s=4),
            in_=dr["resf_d"].ap()[t0:t0 + 512, hg * 512:(hg + 1) * 512]
            .rearrange("(s p) c -> p s c", p=128)),
            reads=("resf_d",) + G("resfs"), writes=G("resfs"))
        if last:
            P.dma("scalar", "dRES", lambda e, hg=hg: e.dma_start(
                out=sb["res0s"][:, 0:2048].rearrange("p (s c) -> p s c", s=4),
                in_=dr["res0_d"].ap()[t0:t0 + 512, hg * 512:(hg + 1) * 512]
                .rearrange("(s p) c -> p s c", p=128)),
                reads=("res0_d",) + G("res0s"), writes=G("res0s"))
        for n in range(6):
            slot = n % 2
            buf = "wt%d" % slot
            P.dma("sync", "dW%d" % slot,
                  lambda e, hg=hg, n=n, buf=buf: e.dma_start(
                out=sb[buf][:, 0:4096].rearrange("p (r c) -> p r c", r=8),
                in_=dr["wdn_bf"].ap()[hg, n * 8:(n + 1) * 8].transpose([1, 0, 2])),
                reads=("wdn_bf",), writes=G(buf))
            for s in range(4):
                for r2 in range(8):
                    P.c("tensor", lambda e, n=n, s=s, r2=r2, buf=buf, f=first[s], bank=banks[s]: e.matmul(
                        pb[bank][:, :],
                        sb["h2w%d" % n][:, r2 * 512 + s * 128:r2 * 512 + (s + 1) * 128],
                        sb[buf][:, r2 * 512:(r2 + 1) * 512],
                        start=f and r2 == 0, stop=False),
                        reads=G("h2w%d" % n, buf), writes=(banks[s],))
                first[s] = False
        for half in range(2):
            slot = half % 2
            buf = "wt%d" % slot
            P.dma("sync", "dW%d" % slot,
                  lambda e, hg=hg, half=half, buf=buf: e.dma_start(
                out=sb[buf][:, 0:4096].rearrange("p (r c) -> p r c", r=8),
                in_=dr["wsd_bf"].ap()[hg, half * 8:(half + 1) * 8].transpose([1, 0, 2])),
                reads=("wsd_bf",), writes=G(buf))
            for s in range(4):
                for r2 in range(8):
                    rc = half * 8 + r2
                    P.c("tensor", lambda e, rc=rc, r2=r2, s=s, buf=buf, bank=banks[s]: e.matmul(
                        pb[bank][:, :],
                        sb["sh2"][:, rc * 512 + s * 128:rc * 512 + (s + 1) * 128],
                        sb[buf][:, r2 * 512:(r2 + 1) * 512],
                        start=False, stop=False),
                        reads=G("sh2", buf), writes=(banks[s],))
        for s in range(4):
            P.c("tensor", lambda e, s=s, hg=hg, bank=banks[s]: e.matmul(
                pb[bank][:, :], sb["wfm"][0:6, s * 128:(s + 1) * 128],
                sb["bdn_bf"][:, hg * 512:(hg + 1) * 512],
                start=False, stop=False),
                reads=G("wfm", "bdn_bf"), writes=(banks[s],))
            P.c("tensor", lambda e, s=s, hg=hg, bank=banks[s]: e.matmul(
                pb[bank][:, :], sb["ones_row_bf"][:],
                sb["bsd_bf"][0:1, hg * 512:(hg + 1) * 512],
                start=False, stop=True),
                reads=("ones_row_bf",) + G("bsd_bf"), writes=(banks[s],))
            if not last:
                P.c("vector", lambda e, s=s, bank=banks[s]: e.tensor_add(
                    sb["xt1b"][:, 0:512], pb[bank][:, :], sb["resfs"][:, s * 512:(s + 1) * 512]),
                    reads=(banks[s],) + G("resfs"), writes=G("xt1b"))
                bankt = "pt0"
                for f2 in range(4):
                    P.c("tensor", lambda e, f2=f2, bankt=bankt: e.transpose(
                        pb[bankt][:, f2 * 128:(f2 + 1) * 128],
                        sb["xt1b"][:, f2 * 128:(f2 + 1) * 128], sb["ident_bf"][:]),
                        reads=G("xt1b") + ("ident_bf",), writes=(bankt,))
                P.c("scalar", lambda e, s=s, hg=hg, bankt=bankt: e.activation(
                    bass.AP(sb["xt_fm"], hg * 4 * 512 + s * 128, [[8192, 128], [512, 4], [1, 128]]),
                    pb[bankt][:, 0:512].rearrange("p (f c) -> p f c", f=4), AF.Copy),
                    reads=(bankt,), writes=G("xt_fm"))
            else:
                P.c("vector", lambda e, s=s, bank=banks[s]: e.tensor_add(
                    sb["resfs"][:, s * 512:(s + 1) * 512], pb[bank][:, :],
                    sb["resfs"][:, s * 512:(s + 1) * 512]),
                    reads=(banks[s],) + G("resfs"), writes=G("resfs"))
                P.c("vector", lambda e, s=s: e.tensor_add(
                    sb["resfs"][:, s * 512:(s + 1) * 512],
                    sb["resfs"][:, s * 512:(s + 1) * 512],
                    sb["res0s"][:, s * 512:(s + 1) * 512]),
                    reads=G("resfs", "res0s"), writes=G("resfs"))
        if last:
            P.dma("sync", "dOUT", lambda e, hg=hg: e.dma_start(
                out=cx.y.ap()[t0:t0 + 512, hg * 512:(hg + 1) * 512]
                .rearrange("(s p) c -> p s c", p=128),
                in_=sb["resfs"][:, 0:2048].rearrange("p (s c) -> p s c", s=4)),
                reads=G("resfs"), writes=("y",))


def kernel(**inputs):
    nc = build()
    in_maps = [_stage_core(inputs, c) for c in range(8)]
    res = run_bass_kernel_spmd(nc, in_maps, core_ids=list(range(8)))
    out = np.zeros((B, T, D), np.float32)
    for c in range(8):
        b, h = c // 2, c % 2
        yv = np.asarray(res.results[c]["y"], np.float32)
        for tb in range(4):
            out[b, (2 * tb + h) * 512:(2 * tb + h + 1) * 512] = yv[tb * 512:(tb + 1) * 512]
    return out
